# version 02
# baseline (speedup 1.0000x reference)
import contextlib
import math
import numpy as np
import ml_dtypes
import concourse.bass as bass
import concourse.mybir as mybir
from concourse.bass_utils import run_bass_kernel_spmd

F32 = mybir.dt.float32
BF16 = mybir.dt.bfloat16
AF = mybir.ActivationFunctionType
ALU = mybir.AluOpType
AX = mybir.AxisListType

NCORES = 8
D = 1024
B = 4
S = 4096
TC = 2048
ND = D // 128
LN_EPS = 1e-5
WKV_EPS = 1e-8

ENGINES = ("tensor", "vector", "scalar", "gpsimd", "sync")
FENCE_ALL = True
_DMA_ENG = ("gpsimd", "sync", "scalar")


class Ev:
    __slots__ = ("sem", "n", "eng")

    def __init__(self, sem, n, eng):
        self.sem, self.n, self.eng = sem, n, eng


class Sem:
    def __init__(self, handle, name):
        self.h = handle
        self.name = name
        self.n = 0


class Buf:
    __slots__ = ("t", "ready", "users", "name", "dma_sem")

    def __init__(self, t, name=""):
        self.t = t
        self.name = name
        self.ready = None
        self.users = {}
        self.dma_sem = None

    def __getitem__(self, idx):
        return self.t[idx]

    @property
    def ap(self):
        return self.t[:]


class Prog:
    def __init__(self, nc):
        self.nc = nc
        self.q = {e: [] for e in ENGINES}
        self.ctx = contextlib.ExitStack()
        self._waited = {e: {} for e in ENGINES}
        self._fence_cache = {}
        self.sems = {}
        for e in ENGINES:
            self.sems[e] = self.sem("sm_" + e)
        self._fence_scratch = self.ctx.enter_context(self.nc.sbuf_tensor("fence_scr", [1, 1], F32))
        self.n_ops = 0
        _gs = self.sems["gpsimd"]
        _gs.n += 1
        scr = self._fence_scratch
        self._emit("gpsimd", lambda h, s=_gs.h, scr=scr: h.memset(scr[:], 0.0).then_inc(s, 1))
        self._fence_init_ev = Ev(_gs, _gs.n, "gpsimd")
        self._fence_init_done = set()
        self._dma_pool = [self.sem(f"dma{i}") for i in range(64)]
        self._dma_i = 0

    def _dma_sem_for(self, buf):
        if buf.dma_sem is None:
            buf.dma_sem = self._dma_pool[self._dma_i % len(self._dma_pool)]
            self._dma_i += 1
        return buf.dma_sem

    def sem(self, name):
        h = self.ctx.enter_context(self.nc.semaphore(name))
        s = Sem(h, name)
        if not hasattr(self, "_misc_sems"):
            self._misc_sems = []
        if name.startswith("cc_"):
            self._misc_sems.append(s)
        return s

    def sbuf(self, name, shape, dtype):
        st = getattr(self, "_cur_stack", None) or self.ctx
        name = getattr(self, "_prefix", "") + name
        return Buf(st.enter_context(self.nc.sbuf_tensor(name, shape, dtype)), name)

    def psum(self, name, shape, dtype=F32):
        st = getattr(self, "_cur_stack", None) or self.ctx
        name = getattr(self, "_prefix", "") + name
        return Buf(st.enter_context(self.nc.psum_tensor(name, shape, dtype)), name)

    def barrier(self):
        evs = []
        for e in ENGINES:
            s = self.sems[e]
            if s.n:
                evs.append(Ev(s, s.n, e))
        for s in self._dma_pool:
            if s.n:
                evs.append(Ev(s, s.n, "dma"))
        for s in getattr(self, "_misc_sems", []):
            if s.n:
                evs.append(Ev(s, s.n, "cc"))
        for e in ENGINES:
            for ev in evs:
                self._emit_wait(e, ev)

    @contextlib.contextmanager
    def stage(self, name):
        st = contextlib.ExitStack()
        old = getattr(self, "_cur_stack", None)
        oldp = getattr(self, "_prefix", "")
        self._cur_stack = st
        self._prefix = name + "_"
        try:
            yield
        finally:
            self._cur_stack = old
            self._prefix = oldp
            self.barrier()
            st.close()

    def raw(self, eng, fn):
        self._emit(eng, fn)

    def _emit(self, eng, fn):
        self.q[eng].append(fn)
        self.n_ops += 1

    def _emit_wait(self, eng, ev):
        if ev is None or ev.eng == eng:
            return
        w = self._waited[eng]
        if w.get(ev.sem, 0) >= ev.n:
            return
        w[ev.sem] = ev.n
        self._emit(eng, lambda h, s=ev.sem.h, n=ev.n: h.wait_ge(s, n))

    def _fenced(self, ev):
        if ev.eng not in ("vector", "scalar"):
            return ev
        key = (ev.sem, ev.n, ev.eng)
        f = self._fence_cache.get(key)
        if f is not None:
            return f
        scr = self._fence_scratch
        if ev.eng not in self._fence_init_done:
            self._fence_init_done.add(ev.eng)
            self._emit_wait(ev.eng, self._fence_init_ev)
        sem = self.sems[ev.eng]
        sem.n += 1
        n = sem.n
        if ev.eng == "vector":
            self._emit("vector", lambda h, s=sem.h, scr=scr: h.tensor_copy(scr[:], scr[:]).then_inc(s, 1))
        else:
            self._emit("scalar", lambda h, s=sem.h, scr=scr: h.copy(scr[:], scr[:]).then_inc(s, 1))
        f = Ev(sem, n, ev.eng)
        self._fence_cache[key] = f
        return f

    def op(self, eng, fn, reads=(), writes=(), sem=None, k=1, dma=False, dma_key=None):
        ev_eng = eng
        if dma:
            assert dma_key is not None
            sem = self._dma_sem_for(dma_key)
            k = 16
            ev_eng = "dma"
            self._emit_wait(eng, Ev(sem, sem.n, "dma"))
        elif sem is None:
            sem = self.sems[eng]
        for b in reads:
            ev = b.ready
            if ev is not None and (dma or FENCE_ALL) and ev.eng != eng:
                ev = self._fenced(ev)
            self._emit_wait(eng, ev)
        for b in writes:
            self._emit_wait(eng, b.ready)
            for uev in b.users.values():
                self._emit_wait(eng, uev)
        sem.n += k
        n = sem.n
        self._emit(eng, lambda h, fn=fn, s=sem.h, k=k: fn(h).then_inc(s, k))
        ev = Ev(sem, n, ev_eng)
        for b in writes:
            b.ready = ev
            b.users = {}
        for b in reads:
            b.users[eng if not dma else "dma:" + eng] = ev
        return ev

    def chain(self, eng, fns, reads=(), writes=(), sem=None):
        if sem is None:
            sem = self.sems[eng]
        for b in reads:
            ev = b.ready
            if ev is not None and FENCE_ALL and ev.eng != eng:
                ev = self._fenced(ev)
            self._emit_wait(eng, ev)
        for b in writes:
            self._emit_wait(eng, b.ready)
            for uev in b.users.values():
                self._emit_wait(eng, uev)
        for fn in fns[:-1]:
            self._emit(eng, fn)
        sem.n += 1
        n = sem.n
        self._emit(eng, lambda h, fn=fns[-1], s=sem.h: fn(h).then_inc(s, 1))
        ev = Ev(sem, n, eng)
        for b in writes:
            b.ready = ev
            b.users = {}
        for b in reads:
            b.users[eng] = ev
        return ev

    def dma_in(self, buf, dram_ap, eng="gpsimd", dst=None):
        d = buf.ap if dst is None else dst
        return self.op(eng, lambda h, d=d, s=dram_ap: h.dma_start(d, s), writes=(buf,), dma=True,
                       dma_key=buf)

    def dma_out(self, dram_ap, buf, eng="sync", src=None, accum=False):
        s = buf.ap if src is None else src
        if accum:
            return self.op(eng, lambda h, d=dram_ap, s=s: h.dma_start(d, s, accum_op=ALU.add),
                           reads=(buf,), dma=True, dma_key=buf)
        return self.op(eng, lambda h, d=dram_ap, s=s: h.dma_start(d, s), reads=(buf,), dma=True,
                       dma_key=buf)

    def barrier_ev(self, eng):
        sem = self.sems[eng]
        return Ev(sem, sem.n, eng)

    def finalize(self):
        with self.nc.Block() as block:
            for eng in ENGINES:
                ops = self.q[eng]
                if not ops:
                    continue
                section = getattr(block, eng)

                def body(handle, ops=ops):
                    for fn in ops:
                        fn(handle)
                section(body)
        self.ctx.close()
        return self.nc


class PsumPool:
    def __init__(self, p, name, n, cols=512, parts=128, dtype=F32):
        self.bufs = [p.psum(f"{name}{i}", [parts, cols], dtype) for i in range(n)]
        self.i = 0

    def next(self):
        b = self.bufs[self.i % len(self.bufs)]
        self.i += 1
        return b



class Common:

    def __init__(self, p):
        self.ones_col_f = p.sbuf("ones_col_f", [128, 1], F32)
        self.ones_row_f = p.sbuf("ones_row_f", [1, 128], F32)
        p.op("gpsimd", lambda h, t=self.ones_col_f: h.memset(t.ap, 1.0), writes=(self.ones_col_f,))
        p.op("gpsimd", lambda h, t=self.ones_row_f: h.memset(t.ap, 1.0), writes=(self.ones_row_f,))


def emit_ln(p, cm, T, src_tile, dst_tile, scratch, stat, ps_s0, ps_s1, ps_bc0, ps_bc1,
           gb=None, nd=ND, eps=LN_EPS, feat=None):
    src_bufs, dst_bufs = feat
    nfeat = nd * 128
    for j in range(nd):
        p.op("scalar", lambda h, j=j: h.activation(scratch[0:128, j * T:(j + 1) * T], src_tile(j), AF.Square),
             reads=tuple(src_bufs), writes=(scratch,))
    for j in range(nd):
        p.op("tensor", lambda h, j=j: h.matmul(ps_s0[0:1, 0:T], cm.ones_col_f[0:128, 0:1], src_tile(j),
                                               start=(j == 0), stop=(j == nd - 1)),
             reads=tuple(src_bufs) + (cm.ones_col_f,), writes=(ps_s0,))
    for j in range(nd):
        p.op("tensor", lambda h, j=j: h.matmul(ps_s1[0:1, 0:T], cm.ones_col_f[0:128, 0:1],
                                               scratch[0:128, j * T:(j + 1) * T],
                                               start=(j == 0), stop=(j == nd - 1)),
             reads=(scratch, cm.ones_col_f), writes=(ps_s1,))
    p.op("vector", lambda h: h.tensor_scalar(stat[0:1, 0:T], ps_s0[0:1, 0:T], 1.0 / nfeat, None, op0=ALU.mult),
         reads=(ps_s0,), writes=(stat,))
    p.chain("vector", [
        lambda h: h.tensor_scalar(stat[0:1, T:2 * T], ps_s1[0:1, 0:T], 1.0 / nfeat, eps, op0=ALU.mult, op1=ALU.add),
        lambda h: h.tensor_tensor(stat[0:1, 2 * T:3 * T], stat[0:1, 0:T], stat[0:1, 0:T], op=ALU.mult),
        lambda h: h.tensor_tensor(stat[0:1, T:2 * T], stat[0:1, T:2 * T], stat[0:1, 2 * T:3 * T], op=ALU.subtract),
    ], reads=(ps_s1,), writes=(stat,))
    p.chain("scalar", [
        lambda h: h.activation(stat[0:1, 2 * T:3 * T], stat[0:1, T:2 * T], AF.Ln),
        lambda h: h.activation(stat[0:1, 2 * T:3 * T], stat[0:1, 2 * T:3 * T], AF.Exp, scale=-0.5),
    ], reads=(stat,), writes=(stat,))
    p.op("tensor", lambda h: h.matmul(ps_bc0[0:128, 0:T], cm.ones_row_f[0:1, 0:128], stat[0:1, 0:T],
                                      start=True, stop=True),
         reads=(stat, cm.ones_row_f), writes=(ps_bc0,))
    p.op("tensor", lambda h: h.matmul(ps_bc1[0:128, 0:T], cm.ones_row_f[0:1, 0:128], stat[0:1, 2 * T:3 * T],
                                      start=True, stop=True),
         reads=(stat, cm.ones_row_f), writes=(ps_bc1,))
    for j in range(nd):
        p.op("vector", lambda h, j=j: h.tensor_tensor(scratch[0:128, j * T:(j + 1) * T], src_tile(j),
                                                      ps_bc0[0:128, 0:T], op=ALU.subtract),
             reads=tuple(src_bufs) + (ps_bc0,), writes=(scratch,))
        if gb is None:
            p.op("vector", lambda h, j=j: h.tensor_tensor(dst_tile(j), scratch[0:128, j * T:(j + 1) * T],
                                                          ps_bc1[0:128, 0:T], op=ALU.mult),
                 reads=(scratch, ps_bc1), writes=tuple(dst_bufs))
        else:
            g_ap, b_ap, gb_bufs = gb
            p.op("vector", lambda h, j=j: h.tensor_tensor(scratch[0:128, j * T:(j + 1) * T],
                                                          scratch[0:128, j * T:(j + 1) * T],
                                                          ps_bc1[0:128, 0:T], op=ALU.mult),
                 reads=(ps_bc1,), writes=(scratch,))
            p.op("vector", lambda h, j=j: h.tensor_scalar(dst_tile(j), scratch[0:128, j * T:(j + 1) * T],
                                                          g_ap(j), b_ap(j), op0=ALU.mult, op1=ALU.add),
                 reads=(scratch,) + tuple(gb_bufs), writes=tuple(dst_bufs))


def emit_mm(p, pool, n_k, rhs_tile, w_tile, w_bufs, rhs_bufs, evac, m_tiles, T):
    for mj in m_tiles:
        ps = pool.next()
        fns = [
            (lambda h, k=k, mj=mj, ps=ps: h.matmul(
                ps[0:128, 0:T], w_tile(k, mj), rhs_tile(k), start=(k == 0), stop=(k == n_k - 1)))
            for k in range(n_k)
        ]
        p.chain("tensor", fns, reads=tuple(w_bufs) + tuple(rhs_bufs), writes=(ps,))
        evac(mj, ps)


CH_A = 256


def fm3(dram_ap, c0, cw):
    return dram_ap.rearrange("(j p) t -> p j t", p=128)[:, :, c0:c0 + cw]


def build_stageA(p=None, io=None, debug=False):
    own = p is None
    if own:
        nc = bass.Bass("TRN2", target_bir_lowering=False)
        p = Prog(nc)
    else:
        nc = p.nc
    io = io or {}
    def dram(name, shape, dt, kind):
        if name in io:
            return io[name]
        return nc.dram_tensor(name, shape, dt, kind=kind).ap()
    TCE = TC + 1
    xe = dram("xe", [D, TCE], F32, "ExternalInput")
    pval = dram("pval", [128, 1], F32, "ExternalInput")
    fmask = dram("fmask", [128, 1], F32, "ExternalInput")
    vecs = dram("vecs", [128, 8 * 6], F32, "ExternalInput")
    wk = dram("wk", [D, D], F32, "ExternalInput")
    wv = dram("wv", [D, D], F32, "ExternalInput")
    wr = dram("wr", [D, D], F32, "ExternalInput")
    ln0_o = dram("ln0_o", [D, TC], F32, "ExternalOutput")
    u_o = dram("u_o", [D, TC], F32, "ExternalOutput")
    ek_o = dram("ek_o", [D, TC], F32, "ExternalOutput")
    r_o = dram("r_o", [D, TC], F32, "ExternalOutput")

    cm = io.get("cm") or Common(p)
    CH = CH_A
    NCH = TC // CH

    wkb = [p.sbuf(f"wk{k}", [128, D], BF16) for k in range(ND)]
    wvb = [p.sbuf(f"wv{k}", [128, D], BF16) for k in range(ND)]
    wrb = [p.sbuf(f"wr{k}", [128, D], BF16) for k in range(ND)]
    vec = p.sbuf("vec", [128, 8 * 6], F32)
    pv = p.sbuf("pv", [128, 1], F32)
    fm = p.sbuf("fm", [128, 1], F32)
    deriv = p.sbuf("deriv", [128, 8 * 4], F32)
    CHP = CH_A + 1
    xn_ch = p.sbuf("xn_ch", [128, ND * CHP], F32)

    CHE = CH + 1
    xb = p.sbuf("xb", [128, ND * CHE], F32)
    ln0b = p.sbuf("ln0b", [128, ND * CHE], F32)
    sqs = p.sbuf("sqs", [128, ND * CHE], F32)
    stat = p.sbuf("stat", [1, 3 * CHE], F32)
    ekf = p.sbuf("ekf", [128, ND * CH], F32)
    vf = p.sbuf("vf", [128, ND * CH], F32)
    uf = p.sbuf("uf", [128, ND * CH], F32)
    rf = p.sbuf("rf", [128, ND * CH], F32)
    kin = p.sbuf("kin", [128, ND * CH], BF16)
    vin = p.sbuf("vin", [128, ND * CH], BF16)
    rin = p.sbuf("rin", [128, ND * CH], BF16)
    tmp_f = p.sbuf("tmp_f", [128, CH], F32)

    ps_s0 = p.psum("ps_s0", [1, 512])
    ps_s1 = p.psum("ps_s1", [1, 512])
    ps_bc0 = p.psum("ps_bc0", [128, 512])
    ps_bc1 = p.psum("ps_bc1", [128, 512])
    pool = PsumPool(p, "pp", 4)

    for k in range(ND):
        p.dma_in(wkb[k], wk[k * 128:(k + 1) * 128, :])
        p.dma_in(wvb[k], wv[k * 128:(k + 1) * 128, :])
        p.dma_in(wrb[k], wr[k * 128:(k + 1) * 128, :])
    p.dma_in(vec, vecs)
    p.dma_in(pv, pval)
    p.dma_in(fm, fmask)
    for i, src_off in enumerate((16, 24, 32)):
        p.op("vector", lambda h, i=i, o=src_off: h.tensor_scalar(
            deriv[0:128, i * 8:(i + 1) * 8], vec[0:128, o:o + 8], -1.0, 1.0, op0=ALU.mult, op1=ALU.add),
            reads=(vec,), writes=(deriv,))
    p.op("vector", lambda h: h.tensor_scalar(deriv[0:128, 24:32], vec[0:128, 40:48], fm[0:128, 0:1], None, op0=ALU.mult),
         reads=(vec, fm), writes=(deriv,))

    g_ap = lambda j: vec[0:128, j:j + 1]
    b_ap = lambda j: vec[0:128, 8 + j:8 + j + 1]

    def do_ln_pair(T, xsrc_buf, xsrc_tile, ln0_dst_buf, ln0_dst_tile, xn_dst_tile, sq_b, st_b):
        emit_ln(p, cm, T, xsrc_tile, ln0_dst_tile, sq_b, st_b, ps_s0, ps_s1, ps_bc0, ps_bc1,
                gb=None, feat=((xsrc_buf,), (ln0_dst_buf,)))
        emit_ln(p, cm, T, ln0_dst_tile, xn_dst_tile, sq_b, st_b, ps_s0, ps_s1, ps_bc0, ps_bc1,
                gb=(g_ap, b_ap, (vec,)), feat=((ln0_dst_buf,), (xn_ch,)))

    mix_specs = [
        (kin, 16, 0), (vin, 24, 8), (rin, 32, 16)]

    for ci in range(NCH):
        c0 = ci * CH
        e0 = 0 if ci == 0 else ci * CH + 1
        W = CH + 1 if ci == 0 else CH
        lo = 1 if ci == 0 else 0
        xo = 0 if ci == 0 else 1
        if ci > 0:
            for j in range(ND):
                p.op("vector", lambda h, j=j: h.tensor_copy(
                    xn_ch[0:128, j * CHP:j * CHP + 1], xn_ch[0:128, j * CHP + CH:j * CHP + CH + 1]),
                    reads=(), writes=(xn_ch,))
            p.raw("vector", lambda h: h.drain())
        p.dma_in(xb, fm3(xe, e0, W), dst=xb.t[:, 0:ND * W].rearrange("p (j t) -> p j t", j=ND))
        do_ln_pair(W, xb, lambda j, W=W: xb[0:128, j * W:(j + 1) * W],
                   ln0b, lambda j, W=W: ln0b[0:128, j * W:(j + 1) * W],
                   lambda j, W=W, xo=xo: xn_ch[0:128, j * CHP + xo:j * CHP + xo + W], sqs, stat)
        if ci == 0:
            for j in range(ND):
                p.op("vector", lambda h, j=j: h.tensor_scalar(
                    xn_ch[0:128, j * CHP:j * CHP + 1], xn_ch[0:128, j * CHP:j * CHP + 1],
                    pv[0:128, 0:1], None, op0=ALU.mult),
                    reads=(pv,), writes=(xn_ch,))
            p.raw("vector", lambda h: h.drain())
        p.dma_out(fm3(ln0_o, c0, CH), ln0b,
                  src=ln0b.t[:, 0:ND * W].rearrange("p (j t) -> p j t", j=ND)[:, :, lo:lo + CH])
        for dst, mixo, invo in mix_specs:
            for j in range(ND):
                cur = xn_ch[0:128, j * CHP + 1:j * CHP + 1 + CH]
                prv = xn_ch[0:128, j * CHP:j * CHP + CH]
                p.op("vector", lambda h, prv=prv, invo=invo, j=j: h.tensor_scalar(
                    tmp_f[0:128, 0:CH], prv, deriv[0:128, invo + j:invo + j + 1], None, op0=ALU.mult),
                    reads=(xn_ch, deriv), writes=(tmp_f,))
                p.op("vector", lambda h, cur=cur, mixo=mixo, j=j, dst=dst: h.scalar_tensor_tensor(
                    dst[0:128, j * CH:(j + 1) * CH], cur, vec[0:128, mixo + j:mixo + j + 1],
                    tmp_f[0:128, 0:CH], op0=ALU.mult, op1=ALU.add),
                    reads=(xn_ch, vec, tmp_f), writes=(dst,))

        def evac_k(mj, ps):
            if ci == 0:
                p.op("vector", lambda h, mj=mj, ps=ps: h.tensor_tensor(
                    ps[0:128, 0:1], ps[0:128, 0:1], deriv[0:128, 24 + mj:24 + mj + 1], op=ALU.add),
                    reads=(deriv, ps), writes=(ps,))
            p.op("scalar", lambda h, mj=mj, ps=ps: h.activation(
                ekf[0:128, mj * CH:(mj + 1) * CH], ps[0:128, 0:CH], AF.Exp),
                reads=(ps,), writes=(ekf,))
        emit_mm(p, pool, ND, lambda k: kin[0:128, k * CH:(k + 1) * CH],
                lambda k, mj: wkb[k][0:128, mj * 128:(mj + 1) * 128],
                wkb, (kin,), evac_k, range(ND), CH)

        def evac_v(mj, ps):
            p.op("vector", lambda h, mj=mj, ps=ps: h.tensor_copy(
                vf[0:128, mj * CH:(mj + 1) * CH], ps[0:128, 0:CH]), reads=(ps,), writes=(vf,))
        emit_mm(p, pool, ND, lambda k: vin[0:128, k * CH:(k + 1) * CH],
                lambda k, mj: wvb[k][0:128, mj * 128:(mj + 1) * 128],
                wvb, (vin,), evac_v, range(ND), CH)

        def evac_r(mj, ps):
            p.op("scalar", lambda h, mj=mj, ps=ps: h.activation(
                rf[0:128, mj * CH:(mj + 1) * CH], ps[0:128, 0:CH], AF.Sigmoid),
                reads=(ps,), writes=(rf,))
        emit_mm(p, pool, ND, lambda k: rin[0:128, k * CH:(k + 1) * CH],
                lambda k, mj: wrb[k][0:128, mj * 128:(mj + 1) * 128],
                wrb, (rin,), evac_r, range(ND), CH)

        for j in range(ND):
            p.op("vector", lambda h, j=j: h.tensor_tensor(
                uf[0:128, j * CH:(j + 1) * CH], ekf[0:128, j * CH:(j + 1) * CH],
                vf[0:128, j * CH:(j + 1) * CH], op=ALU.mult), reads=(ekf, vf), writes=(uf,))

        for dram, buf in ((u_o, uf), (ek_o, ekf), (r_o, rf)):
            p.dma_out(fm3(dram, c0, CH), buf, src=buf.t[:].rearrange("p (j t) -> p j t", j=ND))

        if debug and ci == 0:
            xn_o = dram("xn_o", [D, CH + 1], F32, "ExternalOutput")
            kin_o = dram("kin_o", [D, CH], F32, "ExternalOutput")
            xnf = p.sbuf("xnf_dbg", [128, ND * (CH + 1)], F32)
            kinf = p.sbuf("kinf_dbg", [128, ND * CH], F32)
            for j in range(ND):
                p.op("vector", lambda h, j=j: h.tensor_copy(
                    xnf[0:128, j * (CH + 1):(j + 1) * (CH + 1)], xn_ch[0:128, j * CHP:j * CHP + CH + 1]),
                    reads=(xn_ch,), writes=(xnf,))
                p.op("vector", lambda h, j=j: h.tensor_copy(
                    kinf[0:128, j * CH:(j + 1) * CH], kin[0:128, j * CH:(j + 1) * CH]),
                    reads=(kin,), writes=(kinf,))
            p.dma_out(fm3(xn_o, 0, CH + 1), xnf, src=xnf.t[:].rearrange("p (j t) -> p j t", j=ND))
            p.dma_out(fm3(kin_o, 0, CH), kinf, src=kinf.t[:].rearrange("p (j t) -> p j t", j=ND))

    return p


def _finish(p):
    for s in p._dma_pool:
        if s.n > 0:
            p._emit("sync", lambda h, s=s.h, n=s.n: h.wait_ge(s, n))
    return p.finalize()


def build_stageB(p=None, io=None):
    own = p is None
    if own:
        nc = bass.Bass("TRN2", target_bir_lowering=False)
        p = Prog(nc)
    else:
        nc = p.nc
    io = io or {}
    def dram(name, shape, dt, kind):
        if name in io:
            return io[name]
        return nc.dram_tensor(name, shape, dt, kind=kind).ap()
    NCH_T = 4
    u_s = dram("u_s", [512, S], F32, "ExternalInput")
    ek_s = dram("ek_s", [512, S], F32, "ExternalInput")
    td_s = dram("td_s", [128, 4], F32, "ExternalInput")
    wkv_o = dram("wkv_o", [512, S], F32, "ExternalOutput")

    ones_t = p.sbuf("ones_t", [128, S], F32)
    p.op("gpsimd", lambda h: h.memset(ones_t.ap, 1.0), writes=(ones_t,))
    td = p.sbuf("td", [128, 4], F32)
    ew_col = p.sbuf("ew_col", [128, 4], F32)
    p.dma_in(td, td_s)
    p.chain("scalar", [
        lambda h: h.activation(ew_col[0:128, 0:4], td[0:128, 0:4], AF.Exp),
        lambda h: h.activation(ew_col[0:128, 0:4], ew_col[0:128, 0:4], AF.Exp, scale=-1.0),
    ], reads=(td,), writes=(ew_col,))

    ew_t = p.sbuf("ew_t", [128, S], F32)
    ut = p.sbuf("ut", [128, S], F32)
    ekt = p.sbuf("ekt", [128, S], F32)
    at = p.sbuf("at", [128, S], F32)
    bt = p.sbuf("bt", [128, S], F32)
    wt = p.sbuf("wt", [128, S], F32)

    for i in range(NCH_T):
        p.dma_in(ut, u_s[i * 128:(i + 1) * 128, :])
        p.dma_in(ekt, ek_s[i * 128:(i + 1) * 128, :])
        p.op("vector", lambda h, i=i: h.tensor_scalar(
            ew_t.ap, ones_t.ap, ew_col[0:128, i:i + 1], None, op0=ALU.mult),
            reads=(ones_t, ew_col), writes=(ew_t,))
        p.op("vector", lambda h: h.tensor_tensor_scan(
            at.ap, ew_t.ap, ut.ap, 0.0, op0=ALU.mult, op1=ALU.add),
            reads=(ew_t, ut), writes=(at,))
        p.op("vector", lambda h: h.tensor_tensor_scan(
            bt.ap, ew_t.ap, ekt.ap, 0.0, op0=ALU.mult, op1=ALU.add),
            reads=(ew_t, ekt), writes=(bt,))
        p.op("vector", lambda h: h.tensor_scalar(bt.ap, bt.ap, WKV_EPS, None, op0=ALU.add),
             reads=(), writes=(bt,))
        p.chain("scalar", [
            lambda h: h.activation(bt.ap, bt.ap, AF.Ln),
            lambda h: h.activation(bt.ap, bt.ap, AF.Exp, scale=-1.0),
        ], reads=(), writes=(bt,))
        p.op("vector", lambda h: h.tensor_tensor(wt.ap, at.ap, bt.ap, op=ALU.mult),
             reads=(at, bt), writes=(wt,))
        p.dma_out(wkv_o[i * 128:(i + 1) * 128, :], wt)
    return p


def build_stageC1(p=None, io=None):
    own = p is None
    if own:
        nc = bass.Bass("TRN2", target_bir_lowering=False)
        p = Prog(nc)
    else:
        nc = p.nc
    io = io or {}
    def dram(name, shape, dt, kind):
        if name in io:
            return io[name]
        return nc.dram_tensor(name, shape, dt, kind=kind).ap()
    TCE = TC + 1
    CH = 256
    ln0_e = dram("ln0_e", [D, TCE], F32, "ExternalInput")
    r_e = dram("r_e", [D, TCE], F32, "ExternalInput")
    wkv_e = dram("wkv_e", [D, TCE], F32, "ExternalInput")
    wo = dram("wo", [D, D], F32, "ExternalInput")
    x1_o = dram("x1_o", [D, TCE], F32, "ExternalOutput")

    wob = [p.sbuf(f"wo{k}", [128, D], BF16) for k in range(ND)]
    for k in range(ND):
        p.dma_in(wob[k], wo[k * 128:(k + 1) * 128, :])
    rb = p.sbuf("rb", [128, ND * (CH + 1)], F32)
    wb = p.sbuf("wb", [128, ND * (CH + 1)], F32)
    l0b = p.sbuf("l0b", [128, ND * (CH + 1)], F32)
    zb = p.sbuf("zb", [128, ND * (CH + 1)], BF16)
    x1b = p.sbuf("x1b", [128, ND * (CH + 1)], F32)
    pool = PsumPool(p, "pp", 4)

    nch = TCE // CH
    for ci in range(nch):
        c0 = ci * CH
        W = CH + 1 if ci == nch - 1 else CH
        p.dma_in(rb, fm3(r_e, c0, W), dst=rb.t[:, 0:ND * W].rearrange("p (j t) -> p j t", j=ND))
        p.dma_in(wb, fm3(wkv_e, c0, W), dst=wb.t[:, 0:ND * W].rearrange("p (j t) -> p j t", j=ND))
        p.dma_in(l0b, fm3(ln0_e, c0, W), dst=l0b.t[:, 0:ND * W].rearrange("p (j t) -> p j t", j=ND))
        for j in range(ND):
            p.op("vector", lambda h, j=j, W=W: h.tensor_tensor(
                zb[0:128, j * W:(j + 1) * W], rb[0:128, j * W:(j + 1) * W],
                wb[0:128, j * W:(j + 1) * W], op=ALU.mult), reads=(rb, wb), writes=(zb,))

        def evac(mj, ps, W=W):
            p.op("vector", lambda h, mj=mj, ps=ps, W=W: h.tensor_tensor(
                x1b[0:128, mj * W:(mj + 1) * W], ps[0:128, 0:W],
                l0b[0:128, mj * W:(mj + 1) * W], op=ALU.add), reads=(ps, l0b), writes=(x1b,))
        emit_mm(p, pool, ND, lambda k, W=W: zb[0:128, k * W:(k + 1) * W],
                lambda k, mj: wob[k][0:128, mj * 128:(mj + 1) * 128],
                wob, (zb,), evac, range(ND), W)
        p.dma_out(fm3(x1_o, c0, W), x1b, src=x1b.t[:, 0:ND * W].rearrange("p (j t) -> p j t", j=ND))
    return p


def build_stageC2(p=None, io=None):
    own = p is None
    if own:
        nc = bass.Bass("TRN2", target_bir_lowering=False)
        p = Prog(nc)
    else:
        nc = p.nc
    io = io or {}
    def dram(name, shape, dt, kind):
        if name in io:
            return io[name]
        return nc.dram_tensor(name, shape, dt, kind=kind).ap()
    TCE = TC + 1
    CH = 256
    CHP = CH + 1
    x1_e = dram("x1_e", [D, TCE], F32, "ExternalInput")
    x_fm = dram("x_fm", [D, TC], F32, "ExternalInput")
    pval = dram("pval", [128, 1], F32, "ExternalInput")
    vecs = dram("vecs", [128, 16], F32, "ExternalInput")
    wfk = dram("wfk", [D, 4 * D], F32, "ExternalInput")
    wfr = dram("wfr", [D, D], F32, "ExternalInput")
    wfv = dram("wfv", [4 * D, D], F32, "ExternalInput")
    x2_o = dram("x2_o", [D, TC], F32, "ExternalOutput")

    cm = io.get("cm") or Common(p)
    wfkb = [p.sbuf(f"wfk{k}", [128, 4 * D], BF16) for k in range(ND)]
    wfrb = [p.sbuf(f"wfr{k}", [128, D], BF16) for k in range(ND)]
    for k in range(ND):
        p.dma_in(wfkb[k], wfk[k * 128:(k + 1) * 128, :])
        p.dma_in(wfrb[k], wfr[k * 128:(k + 1) * 128, :])
    wfvb = [p.sbuf("wfv_a", [128, D], BF16), p.sbuf("wfv_b", [128, D], BF16)]
    vec = p.sbuf("vec", [128, 16], F32)
    pv = p.sbuf("pv", [128, 1], F32)
    p.dma_in(vec, vecs)
    p.dma_in(pv, pval)
    g_ap = lambda j: vec[0:128, j:j + 1]
    b_ap = lambda j: vec[0:128, 8 + j:8 + j + 1]

    x1w = p.sbuf("x1w", [128, ND * CHP], F32)
    yn_ch = p.sbuf("yn_ch", [128, ND * CHP], F32)
    sqs = p.sbuf("sqs", [128, ND * CHP], F32)
    stat = p.sbuf("stat", [1, 3 * CHP], F32)
    ynb = p.sbuf("ynb", [128, ND * CH], BF16)
    z2b = p.sbuf("z2b", [128, ND * CH], BF16)
    tmp_f = p.sbuf("tmp_f", [128, CH], F32)
    rrf = p.sbuf("rrf", [128, ND * CH], F32)
    kkt = p.sbuf("kkt", [128, CH], F32)
    kkb = p.sbuf("kkb", [128, 32 * CH], BF16)
    xc = p.sbuf("xc", [128, ND * CH], F32)
    bo = p.sbuf("bo", [128, ND * CH], F32)
    x2b = p.sbuf("x2b", [128, ND * CH], F32)

    ps_s0 = p.psum("ps_s0", [1, 512])
    ps_s1 = p.psum("ps_s1", [1, 512])
    ps_bc0 = p.psum("ps_bc0", [128, 512])
    ps_bc1 = p.psum("ps_bc1", [128, 512])
    pool = PsumPool(p, "pp", 4)

    NCH = TC // CH
    for ci in range(NCH):
        c0 = ci * CH
        e0 = 0 if ci == 0 else c0 + 1
        W = CH + 1 if ci == 0 else CH
        xo = 0 if ci == 0 else 1
        if ci > 0:
            for j in range(ND):
                p.op("vector", lambda h, j=j: h.tensor_copy(
                    yn_ch[0:128, j * CHP:j * CHP + 1], yn_ch[0:128, j * CHP + CH:j * CHP + CH + 1]),
                    reads=(), writes=(yn_ch,))
            p.raw("vector", lambda h: h.drain())
        p.dma_in(x1w, fm3(x1_e, e0, W), dst=x1w.t[:, 0:ND * W].rearrange("p (j t) -> p j t", j=ND))
        emit_ln(p, cm, W, lambda j, W=W: x1w[0:128, j * W:(j + 1) * W],
                lambda j, W=W, xo=xo: yn_ch[0:128, j * CHP + xo:j * CHP + xo + W],
                sqs, stat, ps_s0, ps_s1, ps_bc0, ps_bc1,
                gb=(g_ap, b_ap, (vec,)), feat=((x1w,), (yn_ch,)))
        if ci == 0:
            for j in range(ND):
                p.op("vector", lambda h, j=j: h.tensor_scalar(
                    yn_ch[0:128, j * CHP:j * CHP + 1], yn_ch[0:128, j * CHP:j * CHP + 1],
                    pv[0:128, 0:1], None, op0=ALU.mult), reads=(pv,), writes=(yn_ch,))
            p.raw("vector", lambda h: h.drain())
        for j in range(ND):
            p.op("vector", lambda h, j=j: h.tensor_copy(
                ynb[0:128, j * CH:(j + 1) * CH], yn_ch[0:128, j * CHP + 1:j * CHP + 1 + CH]),
                reads=(yn_ch,), writes=(ynb,))
            p.op("vector", lambda h, j=j: h.tensor_scalar(
                tmp_f[0:128, 0:CH], yn_ch[0:128, j * CHP:j * CHP + CH], 0.5, None, op0=ALU.mult),
                reads=(yn_ch,), writes=(tmp_f,))
            p.op("vector", lambda h, j=j: h.scalar_tensor_tensor(
                z2b[0:128, j * CH:(j + 1) * CH], yn_ch[0:128, j * CHP + 1:j * CHP + 1 + CH],
                0.5, tmp_f[0:128, 0:CH], op0=ALU.mult, op1=ALU.add),
                reads=(yn_ch, tmp_f), writes=(z2b,))

        def evac_rr(mj, ps):
            p.op("scalar", lambda h, mj=mj, ps=ps: h.activation(
                rrf[0:128, mj * CH:(mj + 1) * CH], ps[0:128, 0:CH], AF.Sigmoid),
                reads=(ps,), writes=(rrf,))
        emit_mm(p, pool, ND, lambda k: ynb[0:128, k * CH:(k + 1) * CH],
                lambda k, mj: wfrb[k][0:128, mj * 128:(mj + 1) * 128],
                wfrb, (ynb,), evac_rr, range(ND), CH)

        def evac_kk(mj, ps):
            p.op("scalar", lambda h, mj=mj, ps=ps: h.activation(
                kkt[0:128, 0:CH], ps[0:128, 0:CH], AF.Relu), reads=(ps,), writes=(kkt,))
            p.op("scalar", lambda h, mj=mj: h.activation(
                kkb[0:128, mj * CH:(mj + 1) * CH], kkt[0:128, 0:CH], AF.Square),
                reads=(kkt,), writes=(kkb,))
        emit_mm(p, pool, ND, lambda k: z2b[0:128, k * CH:(k + 1) * CH],
                lambda k, mj: wfkb[k][0:128, mj * 128:(mj + 1) * 128],
                wfkb, (z2b,), evac_kk, range(32), CH)

        p.dma_in(xc, fm3(x1_e, c0 + 1, CH), dst=xc.t[:].rearrange("p (j t) -> p j t", j=ND))
        for half in range(2):
            mjs = range(half * 4, half * 4 + 4)
            pss = {}
            for mj in mjs:
                pss[mj] = pool.next()
            for k in range(32):
                wb_ = wfvb[k % 2]
                p.dma_in(wb_, wfv[k * 128:(k + 1) * 128, :])
                for mj in mjs:
                    p.chain("tensor", [
                        lambda h, k=k, mj=mj, wb_=wb_, ps_=pss[mj]: h.matmul(
                            ps_[0:128, 0:CH], wb_[0:128, mj * 128:(mj + 1) * 128],
                            kkb[0:128, k * CH:(k + 1) * CH], start=(k == 0), stop=(k == 31))],
                        reads=(wb_, kkb), writes=(pss[mj],))
            for mj in mjs:
                p.op("vector", lambda h, mj=mj, ps_=pss[mj]: h.tensor_tensor(
                    bo[0:128, mj * CH:(mj + 1) * CH], rrf[0:128, mj * CH:(mj + 1) * CH],
                    ps_[0:128, 0:CH], op=ALU.mult), reads=(rrf, pss[mj]), writes=(bo,))
                p.op("vector", lambda h, mj=mj: h.tensor_tensor(
                    bo[0:128, mj * CH:(mj + 1) * CH], bo[0:128, mj * CH:(mj + 1) * CH],
                    xc[0:128, mj * CH:(mj + 1) * CH], op=ALU.add), reads=(xc,), writes=(bo,))
        emit_ln(p, cm, CH, lambda j: bo[0:128, j * CH:(j + 1) * CH],
                lambda j: x2b[0:128, j * CH:(j + 1) * CH],
                sqs, stat, ps_s0, ps_s1, ps_bc0, ps_bc1, gb=None, feat=((bo,), (x2b,)))
        p.dma_in(xc, fm3(x_fm, c0, CH), dst=xc.t[:].rearrange("p (j t) -> p j t", j=ND))
        for j in range(ND):
            p.op("vector", lambda h, j=j: h.tensor_tensor(
                x2b[0:128, j * CH:(j + 1) * CH], x2b[0:128, j * CH:(j + 1) * CH],
                xc[0:128, j * CH:(j + 1) * CH], op=ALU.add), reads=(xc,), writes=(x2b,))
        p.dma_out(fm3(x2_o, c0, CH), x2b, src=x2b.t[:].rearrange("p (j t) -> p j t", j=ND))
    return p


def build_stageD(p=None, io=None):
    own = p is None
    if own:
        nc = bass.Bass("TRN2", target_bir_lowering=False)
        p = Prog(nc)
    else:
        nc = p.nc
    io = io or {}
    def dram(name, shape, dt, kind):
        if name in io:
            return io[name]
        return nc.dram_tensor(name, shape, dt, kind=kind).ap()
    CH = 256
    DH = 4 * D
    x2_i = dram("x2_i", [D, TC], F32, "ExternalInput")
    w1t = dram("w1t", [D, DH], F32, "ExternalInput")
    w2t = dram("w2t", [D, DH], F32, "ExternalInput")
    h1t = dram("h1t", [DH, DH], F32, "ExternalInput")
    h2t = dram("h2t", [DH, DH], F32, "ExternalInput")
    w3t = dram("w3t", [DH, D], F32, "ExternalInput")
    scl = dram("scl", [128, 32 * 4 + 8], F32, "ExternalInput")
    x3_o = dram("x3_o", [D, TC], F32, "ExternalOutput")

    cm = io.get("cm") or Common(p)
    slab = [p.sbuf(f"slab{i}", [128, 512], BF16) for i in range(4)]
    sc = p.sbuf("sc", [128, 32 * 4 + 8], F32)
    p.dma_in(sc, scl)

    xcb = p.sbuf("xcb", [128, ND * CH], F32)
    lnb = p.sbuf("lnb", [128, ND * CH], BF16)
    lnf = p.sbuf("lnf", [128, ND * CH], F32)
    sqs = p.sbuf("sqs", [128, ND * CH], F32)
    stat = p.sbuf("stat", [1, 3 * CH], F32)
    slf = p.sbuf("slf", [128, 32 * CH], F32)
    t2f = p.sbuf("t2f", [128, CH], F32)
    hb = p.sbuf("hb", [128, 32 * CH], BF16)
    g1b = p.sbuf("g1b", [128, 32 * CH], BF16)
    ob = p.sbuf("ob", [128, ND * CH], F32)
    x3b = p.sbuf("x3b", [128, ND * CH], F32)

    ps_s0 = p.psum("ps_s0", [1, 512])
    ps_s1 = p.psum("ps_s1", [1, 512])
    ps_bc0 = p.psum("ps_bc0", [128, 512])
    ps_bc1 = p.psum("ps_bc1", [128, 512])
    pool = PsumPool(p, "pp", 4)

    def big_mm(src_b, n_k, wsrc, dst_fn, scale_off, act):
        for g in range(8):
            pss = [pool.next() for _ in range(4)]
            for k in range(n_k):
                sb = slab[k % 2 + 2 * 0] if False else slab[k % 2]
                p.dma_in(sb, wsrc[k * 128:(k + 1) * 128, g * 512:(g + 1) * 512])
                for mi in range(4):
                    mj = g * 4 + mi
                    p.chain("tensor", [
                        lambda h, k=k, mi=mi, sb=sb, ps_=pss[mi]: h.matmul(
                            ps_[0:128, 0:CH], sb[0:128, mi * 128:(mi + 1) * 128],
                            src_b[0:128, k * CH:(k + 1) * CH], start=(k == 0), stop=(k == n_k - 1))],
                        reads=(sb, src_b), writes=(pss[mi],))
            for mi in range(4):
                mj = g * 4 + mi
                dst_fn(mj, pss[mi], scale_off, act)

    def evac_gelu(dst_b):
        def f(mj, ps, so, act):
            p.op("scalar", lambda h, mj=mj, ps=ps, so=so: h.activation(
                dst_b[0:128, mj * CH:(mj + 1) * CH], ps[0:128, 0:CH], AF.Gelu,
                scale=sc[0:128, so + mj:so + mj + 1]), reads=(ps, sc), writes=(dst_b,))
        return f

    NCH = TC // CH
    for ci in range(NCH):
        c0 = ci * CH
        p.dma_in(xcb, fm3(x2_i, c0, CH), dst=xcb.t[:].rearrange("p (j t) -> p j t", j=ND))
        emit_ln(p, cm, CH, lambda j: xcb[0:128, j * CH:(j + 1) * CH],
                lambda j: lnf[0:128, j * CH:(j + 1) * CH],
                sqs, stat, ps_s0, ps_s1, ps_bc0, ps_bc1, gb=None, feat=((xcb,), (lnf,)))
        for j in range(ND):
            p.op("vector", lambda h, j=j: h.tensor_copy(
                lnb[0:128, j * CH:(j + 1) * CH], lnf[0:128, j * CH:(j + 1) * CH]),
                reads=(lnf,), writes=(lnb,))
        def evac_t1(mj, ps, so=None, act=None):
            p.op("scalar", lambda h, mj=mj, ps=ps: h.activation(
                slf[0:128, mj * CH:(mj + 1) * CH], ps[0:128, 0:CH], AF.Silu,
                scale=sc[0:128, mj:mj + 1]), reads=(ps, sc), writes=(slf,))
        big_mm(lnb, ND, w1t, evac_t1, 0, AF.Silu)
        def evac_t2(mj, ps, so=None, act=None):
            p.op("vector", lambda h, mj=mj, ps=ps: h.tensor_scalar(
                t2f[0:128, 0:CH], ps[0:128, 0:CH], sc[0:128, 32 + mj:32 + mj + 1], None, op0=ALU.mult),
                reads=(ps, sc), writes=(t2f,))
            p.op("vector", lambda h, mj=mj: h.tensor_tensor(
                hb[0:128, mj * CH:(mj + 1) * CH], slf[0:128, mj * CH:(mj + 1) * CH],
                t2f[0:128, 0:CH], op=ALU.mult), reads=(slf, t2f), writes=(hb,))
        big_mm(lnb, ND, w2t, evac_t2, 32, AF.Silu)
        big_mm(hb, 32, h1t, evac_gelu(g1b), 64, AF.Gelu)
        big_mm(g1b, 32, h2t, evac_gelu(hb), 96, AF.Gelu)
        for g in range(2):
            pss = [pool.next() for _ in range(4)]
            for k in range(32):
                sb = slab[2 + k % 2]
                p.dma_in(sb, w3t[k * 128:(k + 1) * 128, g * 512:(g + 1) * 512])
                for mi in range(4):
                    mj = g * 4 + mi
                    p.chain("tensor", [
                        lambda h, k=k, mi=mi, sb=sb, ps_=pss[mi]: h.matmul(
                            ps_[0:128, 0:CH], sb[0:128, mi * 128:(mi + 1) * 128],
                            hb[0:128, k * CH:(k + 1) * CH], start=(k == 0), stop=(k == 31))],
                        reads=(sb, hb), writes=(pss[mi],))
            for mi in range(4):
                mj = g * 4 + mi
                p.op("vector", lambda h, mj=mj, ps=pss[mi]: h.tensor_scalar(
                    ob[0:128, mj * CH:(mj + 1) * CH], ps[0:128, 0:CH],
                    sc[0:128, 128 + mj:128 + mj + 1], None, op0=ALU.mult),
                    reads=(pss[mi], sc), writes=(ob,))
        emit_ln(p, cm, CH, lambda j: ob[0:128, j * CH:(j + 1) * CH],
                lambda j: x3b[0:128, j * CH:(j + 1) * CH],
                sqs, stat, ps_s0, ps_s1, ps_bc0, ps_bc1, gb=None, feat=((ob,), (x3b,)))
        for j in range(ND):
            p.op("vector", lambda h, j=j: h.tensor_tensor(
                x3b[0:128, j * CH:(j + 1) * CH], x3b[0:128, j * CH:(j + 1) * CH],
                xcb[0:128, j * CH:(j + 1) * CH], op=ALU.add), reads=(xcb,), writes=(x3b,))
        p.dma_out(fm3(x3_o, c0, CH), x3b, src=x3b.t[:].rearrange("p (j t) -> p j t", j=ND))
    return p


def build_stageE(p=None, io=None):
    own = p is None
    if own:
        nc = bass.Bass("TRN2", target_bir_lowering=False)
        p = Prog(nc)
    else:
        nc = p.nc
    io = io or {}
    def dram(name, shape, dt, kind):
        if name in io:
            return io[name]
        return nc.dram_tensor(name, shape, dt, kind=kind).ap()
    CH = 256
    x3_i = dram("x3_i", [D, TC], F32, "ExternalInput")
    wqkv = dram("wqkv", [D, 3 * D], F32, "ExternalInput")
    cost = dram("cost", [128, TC], F32, "ExternalInput")
    sint = dram("sint", [128, TC], F32, "ExternalInput")
    selm = dram("selm", [128, 16 * ND], F32, "ExternalInput")
    selt = dram("selt", [16, 128 * ND], F32, "ExternalInput")
    qb_o = dram("qb_o", [D, TC], BF16, "ExternalOutput")
    kb_o = dram("kb_o", [D, TC], BF16, "ExternalOutput")
    vb_o = dram("vb_o", [D, TC], BF16, "ExternalOutput")

    cm = io.get("cm") or Common(p)
    wb = [p.sbuf(f"wq{k}", [128, 3 * D], BF16) for k in range(ND)]
    for k in range(ND):
        p.dma_in(wb[k], wqkv[k * 128:(k + 1) * 128, :])
    cosb = p.sbuf("cosb", [128, TC], F32)
    sinb = p.sbuf("sinb", [128, TC], F32)
    p.dma_in(cosb, cost)
    p.dma_in(sinb, sint)
    selb = p.sbuf("selb", [128, 16 * ND], F32)
    seltb = p.sbuf("seltb", [16, 128 * ND], F32)
    p.dma_in(selb, selm)
    p.dma_in(seltb, selt)

    xcb = p.sbuf("xcb", [128, ND * CH], F32)
    lnb = p.sbuf("lnb", [128, ND * CH], BF16)
    lnf = p.sbuf("lnf", [128, ND * CH], F32)
    sqs = p.sbuf("sqs", [128, ND * CH], F32)
    stat = p.sbuf("stat", [1, 3 * CH], F32)
    qf = p.sbuf("qf", [128, ND * CH], F32)
    kf = p.sbuf("kf", [128, ND * CH], F32)
    qr = p.sbuf("qr", [128, ND * CH], F32)
    rt = p.sbuf("rt", [128, 2 * CH], F32)
    sqv = p.sbuf("sqv", [128, CH], F32)
    inv = p.sbuf("inv", [16, 512], F32)
    outb = p.sbuf("outb", [128, ND * CH], BF16)

    ps_s0 = p.psum("ps_s0", [1, 512])
    ps_s1 = p.psum("ps_s1", [1, 512])
    ps_bc0 = p.psum("ps_bc0", [128, 512])
    ps_bc1 = p.psum("ps_bc1", [128, 512])
    ps_n = p.psum("ps_n", [16, 512])
    pool = PsumPool(p, "pp", 3)

    NCH = TC // CH
    for ci in range(NCH):
        c0 = ci * CH
        p.dma_in(xcb, fm3(x3_i, c0, CH), dst=xcb.t[:].rearrange("p (j t) -> p j t", j=ND))
        emit_ln(p, cm, CH, lambda j: xcb[0:128, j * CH:(j + 1) * CH],
                lambda j: lnf[0:128, j * CH:(j + 1) * CH],
                sqs, stat, ps_s0, ps_s1, ps_bc0, ps_bc1, gb=None, feat=((xcb,), (lnf,)))
        for j in range(ND):
            p.op("vector", lambda h, j=j: h.tensor_copy(
                lnb[0:128, j * CH:(j + 1) * CH], lnf[0:128, j * CH:(j + 1) * CH]),
                reads=(lnf,), writes=(lnb,))

        def evac_qkv(mj, ps):
            if mj < 8:
                p.op("vector", lambda h, mj=mj, ps=ps: h.tensor_copy(
                    qf[0:128, mj * CH:(mj + 1) * CH], ps[0:128, 0:CH]), reads=(ps,), writes=(qf,))
            elif mj < 16:
                p.op("vector", lambda h, mj=mj, ps=ps: h.tensor_copy(
                    kf[0:128, (mj - 8) * CH:(mj - 7) * CH], ps[0:128, 0:CH]), reads=(ps,), writes=(kf,))
            else:
                p.op("vector", lambda h, mj=mj, ps=ps: h.tensor_copy(
                    outb[0:128, (mj - 16) * CH:(mj - 15) * CH], ps[0:128, 0:CH]), reads=(ps,), writes=(outb,))
        emit_mm(p, pool, ND, lambda k: lnb[0:128, k * CH:(k + 1) * CH],
                lambda k, mj: wb[k][0:128, mj * 128:(mj + 1) * 128],
                wb, (lnb,), evac_qkv, range(24), CH)
        p.dma_out(fm3(vb_o, c0, CH), outb, src=outb.t[:].rearrange("p (j t) -> p j t", j=ND))

        for src_f, dst_o in ((qf, qb_o), (kf, kb_o)):
            for j in range(ND):
                q1 = src_f[0:64, j * CH:(j + 1) * CH]
                q2 = src_f[64:128, j * CH:(j + 1) * CH]
                cs0 = cosb[0:64, c0:c0 + CH]
                sn0 = sinb[0:64, c0:c0 + CH]
                cs1 = cosb[64:128, c0:c0 + CH]
                sn1 = sinb[64:128, c0:c0 + CH]
                p.chain("vector", [
                    lambda h, q1=q1, cs0=cs0: h.tensor_tensor(rt[0:64, 0:CH], q1, cs0, op=ALU.mult),
                    lambda h, q2=q2, sn1=sn1: h.tensor_tensor(rt[0:64, CH:2 * CH], q2, sn1, op=ALU.mult),
                    lambda h, j=j: h.tensor_tensor(qr[0:64, j * CH:(j + 1) * CH],
                                                   rt[0:64, 0:CH], rt[0:64, CH:2 * CH], op=ALU.subtract),
                    lambda h, q1=q1, sn0=sn0: h.tensor_tensor(rt[64:128, 0:CH], q1, sn0, op=ALU.mult),
                    lambda h, q2=q2, cs1=cs1: h.tensor_tensor(rt[64:128, CH:2 * CH], q2, cs1, op=ALU.mult),
                    lambda h, j=j: h.tensor_tensor(qr[64:128, j * CH:(j + 1) * CH],
                                                   rt[64:128, 0:CH], rt[64:128, CH:2 * CH], op=ALU.add),
                ], reads=(src_f, cosb, sinb), writes=(qr, rt))
            for j in range(ND):
                p.op("scalar", lambda h, j=j: h.activation(sqv[0:128, 0:CH],
                                                           qr[0:128, j * CH:(j + 1) * CH], AF.Square),
                     reads=(qr,), writes=(sqv,))
                p.chain("tensor", [lambda h, j=j: h.matmul(
                    ps_n[0:16, 0:CH], selb[0:128, j * 16:(j + 1) * 16], sqv[0:128, 0:CH],
                    start=(j == 0), stop=(j == ND - 1))], reads=(selb, sqv), writes=(ps_n,))
            p.op("vector", lambda h: h.tensor_scalar(inv[0:16, 0:CH], ps_n[0:16, 0:CH], 1e-6, None, op0=ALU.add),
                 reads=(ps_n,), writes=(inv,))
            p.chain("scalar", [
                lambda h: h.activation(inv[0:16, 0:CH], inv[0:16, 0:CH], AF.Ln),
                lambda h: h.activation(inv[0:16, 0:CH], inv[0:16, 0:CH], AF.Exp, scale=-0.5),
            ], reads=(), writes=(inv,))
            for j in range(ND):
                p.op("tensor", lambda h, j=j: h.matmul(
                    ps_bc0[0:128, 0:CH], seltb[0:16, j * 128:(j + 1) * 128], inv[0:16, 0:CH],
                    start=True, stop=True), reads=(seltb, inv), writes=(ps_bc0,))
                p.op("vector", lambda h, j=j: h.tensor_tensor(
                    outb[0:128, j * CH:(j + 1) * CH], qr[0:128, j * CH:(j + 1) * CH],
                    ps_bc0[0:128, 0:CH], op=ALU.mult), reads=(qr, ps_bc0), writes=(outb,))
            p.dma_out(fm3(dst_o, c0, CH), outb, src=outb.t[:].rearrange("p (j t) -> p j t", j=ND))
    return p


def build_stageF(p=None, io=None):
    own = p is None
    if own:
        nc = bass.Bass("TRN2", target_bir_lowering=False)
        p = Prog(nc)
    else:
        nc = p.nc
    io = io or {}
    def dram(name, shape, dt, kind):
        if name in io:
            return io[name]
        return nc.dram_tensor(name, shape, dt, kind=kind).ap()
    TQ = 512
    qh_i = dram("qh_i", [512, S], BF16, "ExternalInput")
    kh_i = dram("kh_i", [512, S], BF16, "ExternalInput")
    vx_i = dram("vx_i", [128, 8 * 32 * 65], BF16, "ExternalInput")
    ao_o = dram("ao_o", [512, S], BF16, "ExternalOutput")

    cm = io.get("cm") or Common(p)
    kb = p.sbuf("kb", [64, S], BF16)
    qb = p.sbuf("qb", [64, S], BF16)
    vxb = p.sbuf("vxb", [128, 32 * 65], BF16)
    pb = [p.sbuf(f"pb{i}", [128, TQ], BF16) for i in range(2)]
    dn = p.sbuf("dn", [1, TQ], F32)
    af = p.sbuf("af", [64, TQ], F32)
    aob = p.sbuf("aob", [64, TQ], BF16)
    ps_qk = PsumPool(p, "qk", 2)
    ps_av = p.psum("ps_av", [65, 512])
    ps_bc = p.psum("ps_bc", [64, 512])

    for hh in range(8):
        p.dma_in(kb, kh_i[hh * 64:(hh + 1) * 64, :])
        p.dma_in(qb, qh_i[hh * 64:(hh + 1) * 64, :])
        p.dma_in(vxb, vx_i[0:128, hh * 2080:(hh + 1) * 2080])
        for tq in range(S // TQ):
            q_ap = qb[0:64, tq * TQ:(tq + 1) * TQ]
            for ts in range(32):
                ps = ps_qk.next()
                p.chain("tensor", [lambda h, ts=ts, q_ap=q_ap, ps=ps: h.matmul(
                    ps[0:128, 0:TQ], kb[0:64, ts * 128:(ts + 1) * 128], q_ap,
                    start=True, stop=True)], reads=(kb, qb), writes=(ps,))
                pbuf = pb[ts % 2]
                p.op("scalar", lambda h, ps=ps, pbuf=pbuf: h.activation(
                    pbuf.ap, ps[0:128, 0:TQ], AF.Exp, scale=0.125), reads=(ps,), writes=(pbuf,))
                p.chain("tensor", [lambda h, ts=ts, pbuf=pbuf: h.matmul(
                    ps_av[0:65, 0:TQ], vxb[0:128, ts * 65:(ts + 1) * 65], pbuf.ap,
                    start=(ts == 0), stop=(ts == 31))], reads=(vxb, pbuf), writes=(ps_av,))
            p.op("vector", lambda h: h.tensor_copy(dn.ap, ps_av[64:65, 0:TQ]), reads=(ps_av,), writes=(dn,))
            p.chain("scalar", [
                lambda h: h.activation(dn.ap, dn.ap, AF.Ln),
                lambda h: h.activation(dn.ap, dn.ap, AF.Exp, scale=-1.0),
            ], reads=(), writes=(dn,))
            p.op("scalar", lambda h: h.copy(af.ap, ps_av[0:64, 0:TQ]), reads=(ps_av,), writes=(af,))
            p.op("tensor", lambda h: h.matmul(ps_bc[0:64, 0:TQ], cm.ones_row_f[0:1, 0:64], dn.ap,
                                              start=True, stop=True),
                 reads=(cm.ones_row_f, dn), writes=(ps_bc,))
            p.op("vector", lambda h: h.tensor_tensor(aob.ap, af.ap, ps_bc[0:64, 0:TQ], op=ALU.mult),
                 reads=(af, ps_bc), writes=(aob,))
            p.dma_out(ao_o[hh * 64:(hh + 1) * 64, tq * TQ:(tq + 1) * TQ], aob)
    return p


def build_stageG(p=None, io=None):
    own = p is None
    if own:
        nc = bass.Bass("TRN2", target_bir_lowering=False)
        p = Prog(nc)
    else:
        nc = p.nc
    io = io or {}
    def dram(name, shape, dt, kind):
        if name in io:
            return io[name]
        return nc.dram_tensor(name, shape, dt, kind=kind).ap()
    CH = 256
    ao_i = dram("ao_i", [D, TC], BF16, "ExternalInput")
    x3_i = dram("x3_i", [D, TC], F32, "ExternalInput")
    wo = dram("wo", [D, D], F32, "ExternalInput")
    bo_v = dram("bo_v", [128, 8], F32, "ExternalInput")
    x4_o = dram("x4_o", [D, TC], F32, "ExternalOutput")
    h0_o = dram("h0_o", [D, TC], F32, "ExternalOutput")

    cm = io.get("cm") or Common(p)
    wob = [p.sbuf(f"wo{k}", [128, D], BF16) for k in range(ND)]
    for k in range(ND):
        p.dma_in(wob[k], wo[k * 128:(k + 1) * 128, :])
    bov = p.sbuf("bov", [128, 8], F32)
    p.dma_in(bov, bo_v)
    aob = p.sbuf("aob", [128, ND * CH], BF16)
    xcb = p.sbuf("xcb", [128, ND * CH], F32)
    of = p.sbuf("of", [128, ND * CH], F32)
    x4b = p.sbuf("x4b", [128, ND * CH], F32)
    h0b = p.sbuf("h0b", [128, ND * CH], F32)
    sqs = p.sbuf("sqs", [128, ND * CH], F32)
    stat = p.sbuf("stat", [1, 3 * CH], F32)
    ps_s0 = p.psum("ps_s0", [1, 512])
    ps_s1 = p.psum("ps_s1", [1, 512])
    ps_bc0 = p.psum("ps_bc0", [128, 512])
    ps_bc1 = p.psum("ps_bc1", [128, 512])
    pool = PsumPool(p, "pp", 4)

    NCH = TC // CH
    for ci in range(NCH):
        c0 = ci * CH
        p.dma_in(aob, fm3(ao_i, c0, CH), dst=aob.t[:].rearrange("p (j t) -> p j t", j=ND))
        p.dma_in(xcb, fm3(x3_i, c0, CH), dst=xcb.t[:].rearrange("p (j t) -> p j t", j=ND))

        def evac_o(mj, ps):
            p.op("vector", lambda h, mj=mj, ps=ps: h.tensor_scalar(
                of[0:128, mj * CH:(mj + 1) * CH], ps[0:128, 0:CH],
                bov[0:128, mj:mj + 1], None, op0=ALU.add), reads=(ps, bov), writes=(of,))
        emit_mm(p, pool, ND, lambda k: aob[0:128, k * CH:(k + 1) * CH],
                lambda k, mj: wob[k][0:128, mj * 128:(mj + 1) * 128],
                wob, (aob,), evac_o, range(ND), CH)
        emit_ln(p, cm, CH, lambda j: of[0:128, j * CH:(j + 1) * CH],
                lambda j: x4b[0:128, j * CH:(j + 1) * CH],
                sqs, stat, ps_s0, ps_s1, ps_bc0, ps_bc1, gb=None, feat=((of,), (x4b,)))
        for j in range(ND):
            p.op("vector", lambda h, j=j: h.tensor_tensor(
                x4b[0:128, j * CH:(j + 1) * CH], x4b[0:128, j * CH:(j + 1) * CH],
                xcb[0:128, j * CH:(j + 1) * CH], op=ALU.add), reads=(xcb,), writes=(x4b,))
        emit_ln(p, cm, CH, lambda j: x4b[0:128, j * CH:(j + 1) * CH],
                lambda j: h0b[0:128, j * CH:(j + 1) * CH],
                sqs, stat, ps_s0, ps_s1, ps_bc0, ps_bc1, gb=None, feat=((x4b,), (h0b,)))
        p.dma_out(fm3(x4_o, c0, CH), x4b, src=x4b.t[:].rearrange("p (j t) -> p j t", j=ND))
        p.dma_out(fm3(h0_o, c0, CH), h0b, src=h0b.t[:].rearrange("p (j t) -> p j t", j=ND))
    return p


def build_stageH(p=None, io=None, debug=False):
    own = p is None
    if own:
        nc = bass.Bass("TRN2", target_bir_lowering=False)
        p = Prog(nc)
    else:
        nc = p.nc
    io = io or {}
    def dram(name, shape, dt, kind):
        if name in io:
            return io[name]
        return nc.dram_tensor(name, shape, dt, kind=kind).ap()
    CH = 256
    DH = 4096
    xh_i = dram("xh_i", [D, TC], F32, "ExternalInput")
    rwt = dram("rwt", [128, 32], F32, "ExternalInput")
    rbt = dram("rbt", [1, 4], F32, "ExternalInput")
    lgb = dram("lgb", [128, 8 * 8], F32, "ExternalInput")
    b1t = dram("b1t", [128, 32 * 4], F32, "ExternalInput")
    b2t = dram("b2t", [128, 8 * 4], F32, "ExternalInput")
    w1_i = dram("w1_i", [4 * D, DH], F32, "ExternalInput")
    w2_i = dram("w2_i", [4 * DH, D], F32, "ExternalInput")
    idt_i = dram("idt_i", [128, 128], F32, "ExternalInput")
    xo_o = dram("xo_o", [D, TC], F32, "ExternalOutput")
    xhat_d = dram("xhat_d", [D, TC], BF16, "Internal")
    xhat_db = Buf(None, "xhat_d")

    cm = io.get("cm") or Common(p)
    rwb = p.sbuf("rwb", [128, 32], F32)
    rbb = p.sbuf("rbb", [128, 4], F32)
    rbs = p.sbuf("rbs", [1, 4], F32)
    lgbv = p.sbuf("lgbv", [128, 8 * 8], F32)
    b1v = p.sbuf("b1v", [128, 32 * 4], F32)
    b2v = p.sbuf("b2v", [128, 8 * 4], F32)
    idt = p.sbuf("idt", [128, 128], F32)
    for buf, src in ((rwb, rwt), (rbs, rbt), (lgbv, lgb), (b1v, b1t), (b2v, b2t), (idt, idt_i)):
        p.dma_in(buf, src)
    crow = [p.sbuf(f"crow{e}", [1, TC], BF16) for e in range(4)]
    ones_row_b = p.sbuf("ones_row_b", [1, 128], BF16)
    p.op("gpsimd", lambda h: h.memset(ones_row_b.ap, 1.0), writes=(ones_row_b,))

    xcb = p.sbuf("xcb", [128, ND * CH], F32)
    xhatb = p.sbuf("xhatb", [128, ND * CH], BF16)
    sqs = p.sbuf("sqs", [128, ND * CH], F32)
    stat = p.sbuf("stat", [1, 3 * CH], F32)
    lg = p.sbuf("lg", [128, 4], F32)
    m1 = p.sbuf("m1", [128, 1], F32)
    el = p.sbuf("el", [128, 4], F32)
    msk = p.sbuf("msk", [128, 4], F32)
    p2t = p.sbuf("p2t", [128, 4], F32)
    comb = p.sbuf("comb", [128, 4], F32)
    den = p.sbuf("den", [128, 1], F32)

    w1b = [p.sbuf(f"w1b{k}", [128, DH], BF16) for k in range(ND)]
    w2b = [p.sbuf(f"w2b{k}", [128, D], BF16) for k in range(32)]
    xeb = p.sbuf("xeb", [128, ND * CH], BF16)
    hb = p.sbuf("hb", [128, 32 * CH], BF16)
    gout = p.sbuf("gout", [128, ND * CH], F32)
    gateb = p.sbuf("gateb", [128, CH], F32)

    ps_s0 = p.psum("ps_s0", [1, 512])
    ps_s1 = p.psum("ps_s1", [1, 512])
    ps_bc0 = p.psum("ps_bc0", [128, 512])
    ps_bc1 = p.psum("ps_bc1", [128, 512])
    pool = PsumPool(p, "pp", 4)

    psb = pool.next()
    p.op("tensor", lambda h, psb=psb: h.matmul(psb[0:128, 0:4], cm.ones_row_f[0:1, 0:128],
                                               rbs[0:1, 0:4], start=True, stop=True),
         reads=(cm.ones_row_f, rbs), writes=(psb,))
    p.op("vector", lambda h, psb=psb: h.tensor_copy(rbb.ap, psb[0:128, 0:4]), reads=(psb,), writes=(rbb,))

    NCH = TC // CH
    for ci in range(NCH):
        c0 = ci * CH
        p.dma_in(xcb, fm3(xh_i, c0, CH), dst=xcb.t[:].rearrange("p (j t) -> p j t", j=ND))
        emit_ln(p, cm, CH, lambda j: xcb[0:128, j * CH:(j + 1) * CH],
                lambda j: xhatb[0:128, j * CH:(j + 1) * CH],
                sqs, stat, ps_s0, ps_s1, ps_bc0, ps_bc1, gb=None, feat=((xcb,), (xhatb,)))
        p.dma_out(fm3(xhat_d, c0, CH), xhatb, src=xhatb.t[:].rearrange("p (j t) -> p j t", j=ND))
        xhat_db.ready = xhatb.users.get("dma:sync") or xhat_db.ready
        for tt in range(CH // 128):
            ps_l = pool.next()
            fns = [(lambda h, ki=ki, tt=tt, ps_l=ps_l: h.matmul(
                ps_l[0:128, 0:4], xcb[0:128, ki * CH + tt * 128:ki * CH + (tt + 1) * 128],
                rwb[0:128, ki * 4:(ki + 1) * 4], start=(ki == 0), stop=(ki == ND - 1)))
                for ki in range(ND)]
            p.chain("tensor", fns, reads=(xcb, rwb), writes=(ps_l,))
            p.op("vector", lambda h, ps_l=ps_l: h.tensor_tensor(lg.ap, ps_l[0:128, 0:4], rbb.ap, op=ALU.add),
                 reads=(ps_l, rbb), writes=(lg,))
            p.raw("vector", lambda h: h.drain())
            p.op("vector", lambda h: h.tensor_reduce(m1.ap, lg.ap, axis=AX.X, op=ALU.max),
                 reads=(lg,), writes=(m1,))
            p.raw("vector", lambda h: h.drain())
            p.op("vector", lambda h: h.tensor_scalar(msk.ap, lg.ap, m1.ap, None, op0=ALU.subtract),
                 reads=(lg, m1), writes=(msk,))
            p.raw("vector", lambda h: h.drain())
            if debug and ci == 0 and tt == 0:
                dbg_r = dram("dbg_r", [128, 8], F32, "ExternalOutput")
                dbgr = p.sbuf("dbgr", [128, 8], F32)
                p.op("vector", lambda h: h.tensor_copy(dbgr[0:128, 0:1], m1.ap), reads=(m1,), writes=(dbgr,))
                p.op("vector", lambda h: h.tensor_copy(dbgr[0:128, 1:5], msk.ap), reads=(msk,), writes=(dbgr,))
                p.dma_out(dbg_r, dbgr)
            p.op("scalar", lambda h: h.activation(el.ap, msk.ap, AF.Exp), reads=(msk,), writes=(el,))
            p.chain("scalar", [
                lambda h: h.activation(msk.ap, msk.ap, AF.Relu, scale=-1.0),
                lambda h: h.activation(msk.ap, msk.ap, AF.Sign),
            ], reads=(), writes=(msk,))
            p.op("vector", lambda h: h.tensor_tensor(p2t.ap, el.ap, msk.ap, op=ALU.mult),
                 reads=(el, msk), writes=(p2t,))
            p.raw("vector", lambda h: h.drain())
            p.op("vector", lambda h: h.tensor_reduce(m1.ap, p2t.ap, axis=AX.X, op=ALU.max),
                 reads=(p2t,), writes=(m1,))
            p.raw("vector", lambda h: h.drain())
            p.op("vector", lambda h: h.tensor_scalar(msk.ap, el.ap, m1.ap, None, op0=ALU.subtract),
                 reads=(el, m1), writes=(msk,))
            p.raw("vector", lambda h: h.drain())
            p.chain("scalar", [
                lambda h: h.activation(msk.ap, msk.ap, AF.Relu, scale=-1.0),
                lambda h: h.activation(msk.ap, msk.ap, AF.Sign),
            ], reads=(), writes=(msk,))
            p.op("vector", lambda h: h.tensor_tensor(p2t.ap, el.ap, msk.ap, op=ALU.mult),
                 reads=(el, msk), writes=(p2t,))
            p.raw("vector", lambda h: h.drain())
            p.op("vector", lambda h: h.tensor_tensor(comb.ap, el.ap, p2t.ap, op=ALU.subtract),
                 reads=(el, p2t), writes=(comb,))
            p.raw("vector", lambda h: h.drain())
            p.op("vector", lambda h: h.tensor_reduce(den.ap, comb.ap, axis=AX.X, op=ALU.add),
                 reads=(comb,), writes=(den,))
            p.raw("vector", lambda h: h.drain())
            p.chain("scalar", [
                lambda h: h.activation(den.ap, den.ap, AF.Ln),
                lambda h: h.activation(den.ap, den.ap, AF.Exp, scale=-1.0),
            ], reads=(), writes=(den,))
            p.op("vector", lambda h: h.tensor_scalar(comb.ap, comb.ap, den.ap, None, op0=ALU.mult),
                 reads=(den,), writes=(comb,))
            p.raw("vector", lambda h: h.drain())
            if debug and ci == 0 and tt == 0:
                lg_o = dram("lg_o", [128, 16], F32, "ExternalOutput")
                dbg0 = p.sbuf("dbg0", [128, 16], F32)
                p.op("vector", lambda h: h.tensor_copy(dbg0[0:128, 0:4], lg.ap), reads=(lg,), writes=(dbg0,))
                p.op("vector", lambda h: h.tensor_copy(dbg0[0:128, 4:8], el.ap), reads=(el,), writes=(dbg0,))
                p.op("vector", lambda h: h.tensor_copy(dbg0[0:128, 8:12], comb.ap), reads=(comb,), writes=(dbg0,))
                p.op("vector", lambda h: h.tensor_copy(dbg0[0:128, 12:13], den.ap), reads=(den,), writes=(dbg0,))
                p.dma_out(lg_o, dbg0)
            for e in range(4):
                ps_t = pool.next()
                p.op("tensor", lambda h, e=e, ps_t=ps_t: h.matmul(
                    ps_t[0:1, 0:128], comb[0:128, e:e + 1], idt.ap, start=True, stop=True),
                    reads=(comb, idt), writes=(ps_t,))
                p.op("vector", lambda h, e=e, ps_t=ps_t, c0=c0, tt=tt: h.tensor_copy(
                    crow[e][0:1, c0 + tt * 128:c0 + (tt + 1) * 128], ps_t[0:1, 0:128]),
                    reads=(ps_t,), writes=(crow[e],))

    for e in range(4):
        for k in range(ND):
            p.dma_in(w1b[k], w1_i[e * D + k * 128:e * D + (k + 1) * 128, :])
        for k in range(32):
            p.dma_in(w2b[k], w2_i[e * DH + k * 128:e * DH + (k + 1) * 128, :])
        for ci in range(NCH):
            c0 = ci * CH
            p.dma_in(xhatb, fm3(xhat_d, c0, CH), dst=xhatb.t[:].rearrange("p (j t) -> p j t", j=ND))
            p.op("gpsimd", lambda h: h.engine_nop(), reads=(xhat_db,)) if False else None
            p.dma_in(xcb, fm3(xh_i, c0, CH), dst=xcb.t[:].rearrange("p (j t) -> p j t", j=ND))
            for j in range(ND):
                p.op("vector", lambda h, j=j, e=e: h.tensor_scalar(
                    xeb[0:128, j * CH:(j + 1) * CH], xhatb[0:128, j * CH:(j + 1) * CH],
                    lgbv[0:128, (2 * e) * 8 + j:(2 * e) * 8 + j + 1],
                    lgbv[0:128, (2 * e + 1) * 8 + j:(2 * e + 1) * 8 + j + 1],
                    op0=ALU.mult, op1=ALU.add), reads=(xhatb, lgbv), writes=(xeb,))

            def evac_h(mj, ps, e=e):
                p.op("scalar", lambda h, mj=mj, ps=ps, e=e: h.activation(
                    hb[0:128, mj * CH:(mj + 1) * CH], ps[0:128, 0:CH], AF.Gelu,
                    bias=b1v[0:128, e * 32 + mj:e * 32 + mj + 1]), reads=(ps, b1v), writes=(hb,))
            emit_mm(p, pool, ND, lambda k: xeb[0:128, k * CH:(k + 1) * CH],
                    lambda k, mj: w1b[k][0:128, mj * 128:(mj + 1) * 128],
                    w1b, (xeb,), evac_h, range(32), CH)
            ps_g = pool.next()
            p.op("tensor", lambda h, e=e, c0=c0, ps_g=ps_g: h.matmul(
                ps_g[0:128, 0:CH], ones_row_b[0:1, 0:128], crow[e][0:1, c0:c0 + CH],
                start=True, stop=True), reads=(ones_row_b, crow[e]), writes=(ps_g,))
            p.op("vector", lambda h, ps_g=ps_g: h.tensor_copy(gateb[0:128, 0:CH], ps_g[0:128, 0:CH]),
                 reads=(ps_g,), writes=(gateb,))

            def evac_y(mj, ps, e=e):
                p.op("vector", lambda h, mj=mj, ps=ps, e=e: h.tensor_scalar(
                    gout[0:128, mj * CH:(mj + 1) * CH], ps[0:128, 0:CH],
                    b2v[0:128, e * 8 + mj:e * 8 + mj + 1], None, op0=ALU.add),
                    reads=(ps, b2v), writes=(gout,))
                p.op("vector", lambda h, mj=mj: h.tensor_tensor(
                    gout[0:128, mj * CH:(mj + 1) * CH], gout[0:128, mj * CH:(mj + 1) * CH],
                    xcb[0:128, mj * CH:(mj + 1) * CH], op=ALU.add), reads=(xcb,), writes=(gout,))
                p.op("vector", lambda h, mj=mj: h.tensor_tensor(
                    gout[0:128, mj * CH:(mj + 1) * CH], gout[0:128, mj * CH:(mj + 1) * CH],
                    gateb[0:128, 0:CH], op=ALU.mult), reads=(gateb,), writes=(gout,))
            emit_mm(p, pool, 32, lambda k: hb[0:128, k * CH:(k + 1) * CH],
                    lambda k, mj: w2b[k][0:128, mj * 128:(mj + 1) * 128],
                    w2b, (hb,), evac_y, range(ND), CH)
            p.dma_out(fm3(xo_o, c0, CH), gout, src=gout.t[:].rearrange("p (j t) -> p j t", j=ND),
                      accum=(e > 0), eng="gpsimd" if e > 0 else "sync")
            if debug and e == 0 and ci == 0:
                xhd_o = dram("xhd_o", [D, CH], F32, "ExternalOutput")
                xeb_o = dram("xeb_o", [D, CH], F32, "ExternalOutput")
                hb_o = dram("hb_o", [DH, CH], F32, "ExternalOutput")
                gate_o = dram("gate_o", [128, CH], F32, "ExternalOutput")
                dbg1 = p.sbuf("dbg1", [128, ND * CH], F32)

                for j in range(ND):
                    p.op("vector", lambda h, j=j: h.tensor_copy(
                        dbg1[0:128, j * CH:(j + 1) * CH], xhatb[0:128, j * CH:(j + 1) * CH]),
                        reads=(xhatb,), writes=(dbg1,))
                p.dma_out(fm3(xhd_o, 0, CH), dbg1, src=dbg1.t[:].rearrange("p (j t) -> p j t", j=ND))
                for j in range(ND):
                    p.op("vector", lambda h, j=j: h.tensor_copy(
                        dbg1[0:128, j * CH:(j + 1) * CH], xeb[0:128, j * CH:(j + 1) * CH]),
                        reads=(xeb,), writes=(dbg1,))
                p.dma_out(fm3(xeb_o, 0, CH), dbg1, src=dbg1.t[:].rearrange("p (j t) -> p j t", j=ND))
                for j in range(4):
                    p.op("vector", lambda h, j=j: h.tensor_copy(
                        dbg1[0:128, j * CH:(j + 1) * CH], hb[0:128, j * CH:(j + 1) * CH]),
                        reads=(hb,), writes=(dbg1,))
                p.dma_out(hb_o.rearrange("(j p) t -> p j t", p=128)[:, 0:4, :], dbg1,
                          src=dbg1.t[:, 0:4 * CH].rearrange("p (j t) -> p j t", j=4))
                p.dma_out(gate_o, gateb)
    return p


def build_stageI(p=None, io=None):
    own = p is None
    if own:
        nc = bass.Bass("TRN2", target_bir_lowering=False)
        p = Prog(nc)
    else:
        nc = p.nc
    io = io or {}
    def dram(name, shape, dt, kind):
        if name in io:
            return io[name]
        return nc.dram_tensor(name, shape, dt, kind=kind).ap()
    CH = 512
    x4_i = dram("x4_i", [D, TC], F32, "ExternalInput")
    h4_i = dram("h4_i", [D, TC], F32, "ExternalInput")
    x5_o = dram("x5_o", [D, TC], F32, "ExternalOutput")
    cm = io.get("cm") or Common(p)
    x4b = p.sbuf("x4b", [128, ND * CH], F32)
    h4b = p.sbuf("h4b", [128, ND * CH], F32)
    ob = p.sbuf("ob", [128, ND * CH], F32)
    sqs = p.sbuf("sqs", [128, ND * CH], F32)
    stat = p.sbuf("stat", [1, 3 * CH], F32)
    ps_s0 = p.psum("ps_s0", [1, 512])
    ps_s1 = p.psum("ps_s1", [1, 512])
    ps_bc0 = p.psum("ps_bc0", [128, 512])
    ps_bc1 = p.psum("ps_bc1", [128, 512])
    for ci in range(TC // CH):
        c0 = ci * CH
        p.dma_in(x4b, fm3(x4_i, c0, CH), dst=x4b.t[:].rearrange("p (j t) -> p j t", j=ND))
        p.dma_in(h4b, fm3(h4_i, c0, CH), dst=h4b.t[:].rearrange("p (j t) -> p j t", j=ND))
        emit_ln(p, cm, CH, lambda j: h4b[0:128, j * CH:(j + 1) * CH],
                lambda j: ob[0:128, j * CH:(j + 1) * CH],
                sqs, stat, ps_s0, ps_s1, ps_bc0, ps_bc1, gb=None, feat=((h4b,), (ob,)))
        for j in range(ND):
            p.op("vector", lambda h, j=j: h.tensor_tensor(
                ob[0:128, j * CH:(j + 1) * CH], ob[0:128, j * CH:(j + 1) * CH],
                x4b[0:128, j * CH:(j + 1) * CH], op=ALU.add), reads=(x4b,), writes=(ob,))
        p.dma_out(fm3(x5_o, c0, CH), ob, src=ob.t[:].rearrange("p (j t) -> p j t", j=ND))
    return p


def _perm1024():
    perm = np.zeros(1024, np.int64)
    for g in range(8):
        for r in range(128):
            if r < 32:
                h, d = 2 * g, r
            elif r < 64:
                h, d = 2 * g + 1, r - 32
            elif r < 96:
                h, d = 2 * g, 32 + r - 64
            else:
                h, d = 2 * g + 1, 32 + r - 96
            perm[g * 128 + r] = h * 64 + d
    return perm


def _head_rows(h):
    g, sub = h // 2, h % 2
    return np.concatenate([np.arange(g * 128 + sub * 32, g * 128 + sub * 32 + 32),
                           np.arange(g * 128 + 64 + sub * 32, g * 128 + 64 + sub * 32 + 32)])


def _rope_tables(s0):
    inv = 1.0 / (10000.0 ** (np.arange(32, dtype=np.float64) / 32.0))
    pos = np.arange(s0, s0 + TC, dtype=np.float64)
    ang = (pos[None, :] * inv[:, None])
    cost = np.cos(ang).astype(np.float32)
    sint = np.sin(ang).astype(np.float32)
    cost = np.concatenate([cost, cost, cost, cost], axis=0)
    sint = np.concatenate([sint, sint, sint, sint], axis=0)
    return np.ascontiguousarray(cost), np.ascontiguousarray(sint)


def _v8(v):
    return np.ascontiguousarray(np.asarray(v, np.float32).reshape(8, 128).T)


def _v32(v):
    return np.ascontiguousarray(np.asarray(v, np.float32).reshape(32, 128).T)


def _run(build_fn, maps):
    p = build_fn()
    nc = _finish(p)
    return run_bass_kernel_spmd(nc, maps, core_ids=list(range(NCORES))).results


def kernel_unfused(**inp):
    f32 = lambda a: np.ascontiguousarray(np.asarray(a, np.float32))
    x = f32(inp["L0_attn_x"])

    vecs = np.concatenate([_v8(inp[f"L0_attn_{n}"]) for n in
                           ("ln1_g", "ln1_b", "mix_k", "mix_v", "mix_r", "time_first")], axis=1)
    wk, wv, wr = f32(inp["L0_attn_Wk"]), f32(inp["L0_attn_Wv"]), f32(inp["L0_attn_Wr"])
    mapsA = []
    for c in range(NCORES):
        b, hh = c // 2, c % 2
        xs = x[b, hh * TC:(hh + 1) * TC, :].T
        bound = np.zeros((D, 1), np.float32) if hh == 0 else x[b, hh * TC - 1, :].reshape(D, 1)
        pval = np.full((128, 1), float(hh), np.float32)
        fmask = np.full((128, 1), 1.0 - hh, np.float32)
        mapsA.append(dict(xe=np.ascontiguousarray(np.concatenate([bound, xs], axis=1)),
                          pval=pval, fmask=fmask, vecs=np.ascontiguousarray(vecs),
                          wk=wk, wv=wv, wr=wr))
    resA = _run(build_stageA, mapsA)

    td = f32(inp["L0_attn_time_decay"])
    mapsB = []
    for c in range(NCORES):
        b, hh = c // 2, c % 2
        u_full = np.concatenate([resA[2 * b]["u_o"], resA[2 * b + 1]["u_o"]], axis=1)
        ek_full = np.concatenate([resA[2 * b]["ek_o"], resA[2 * b + 1]["ek_o"]], axis=1)
        lo = hh * 512
        mapsB.append(dict(u_s=np.ascontiguousarray(u_full[lo:lo + 512]),
                          ek_s=np.ascontiguousarray(ek_full[lo:lo + 512]),
                          td_s=np.ascontiguousarray(td[lo:lo + 512].reshape(4, 128).T)))
    resB = _run(build_stageB, mapsB)

    wo0 = f32(inp["L0_attn_Wo"])
    wkv_fm = {}
    for c in range(NCORES):
        b, hh = c // 2, c % 2
        full = np.concatenate([resB[2 * b]["wkv_o"], resB[2 * b + 1]["wkv_o"]], axis=0)
        wkv_fm[c] = full[:, hh * TC:(hh + 1) * TC]
    mapsC1 = []
    for c in range(NCORES):
        b, hh = c // 2, c % 2
        z = np.zeros((D, 1), np.float32)
        if hh == 0:
            l0e = np.concatenate([z, resA[c]["ln0_o"]], axis=1)
            re_ = np.concatenate([z, resA[c]["r_o"]], axis=1)
            we_ = np.concatenate([z, wkv_fm[c]], axis=1)
        else:
            l0e = np.concatenate([resA[c - 1]["ln0_o"][:, -1:], resA[c]["ln0_o"]], axis=1)
            re_ = np.concatenate([resA[c - 1]["r_o"][:, -1:], resA[c]["r_o"]], axis=1)
            we_ = np.concatenate([wkv_fm[c - 1][:, -1:], wkv_fm[c]], axis=1)
        mapsC1.append(dict(ln0_e=np.ascontiguousarray(l0e), r_e=np.ascontiguousarray(re_),
                           wkv_e=np.ascontiguousarray(we_), wo=wo0))
    resC1 = _run(build_stageC1, mapsC1)

    vecs2 = np.concatenate([_v8(inp["L0_attn_ln2_g"]), _v8(inp["L0_attn_ln2_b"])], axis=1)
    wfk, wfr, wfv = f32(inp["L0_attn_Wfk"]), f32(inp["L0_attn_Wfr"]), f32(inp["L0_attn_Wfv"])
    mapsC2 = []
    for c in range(NCORES):
        b, hh = c // 2, c % 2
        mapsC2.append(dict(x1_e=np.ascontiguousarray(resC1[c]["x1_o"]),
                           x_fm=np.ascontiguousarray(x[b, hh * TC:(hh + 1) * TC, :].T),
                           pval=np.full((128, 1), float(hh), np.float32),
                           vecs=np.ascontiguousarray(vecs2), wfk=wfk, wfr=wfr, wfv=wfv))
    resC2 = _run(build_stageC2, mapsC2)

    scl = np.concatenate([_v32(inp["L0_ffn_w1_s"]), _v32(inp["L0_ffn_w2_s"]),
                          _v32(inp["L0_ffn_h1_s"]), _v32(inp["L0_ffn_h2_s"]),
                          _v8(inp["L0_ffn_w3_s"])], axis=1)
    w1t = np.ascontiguousarray(f32(inp["L0_ffn_w1_w"]).T)
    w2t = np.ascontiguousarray(f32(inp["L0_ffn_w2_w"]).T)
    h1t = np.ascontiguousarray(f32(inp["L0_ffn_h1_w"]).T)
    h2t = np.ascontiguousarray(f32(inp["L0_ffn_h2_w"]).T)
    w3t = np.ascontiguousarray(f32(inp["L0_ffn_w3_w"]).T)
    mapsD = [dict(x2_i=np.ascontiguousarray(resC2[c]["x2_o"]), w1t=w1t, w2t=w2t,
                  h1t=h1t, h2t=h2t, w3t=w3t, scl=np.ascontiguousarray(scl))
             for c in range(NCORES)]
    resD = _run(build_stageD, mapsD)

    wqkv = f32(inp["L1_attn_Wqkv"])
    perm = _perm1024()
    wperm = np.ascontiguousarray(np.concatenate(
        [wqkv[:, 0:1024][:, perm], wqkv[:, 1024:2048][:, perm], wqkv[:, 2048:3072]], axis=1))
    sel = np.zeros((128, 16 * 8), np.float32)
    selt = np.zeros((16, 128 * 8), np.float32)
    for j in range(8):
        for r in range(128):
            h = 2 * j + (1 if (32 <= r < 64 or r >= 96) else 0)
            sel[r, j * 16 + h] = 1.0
            selt[h, j * 128 + r] = 1.0
    mapsE = []
    for c in range(NCORES):
        hh = c % 2
        cost, sint = _rope_tables(hh * TC)
        mapsE.append(dict(x3_i=np.ascontiguousarray(resD[c]["x3_o"]), wqkv=wperm,
                          cost=cost, sint=sint, selm=sel, selt=selt))
    resE = _run(build_stageE, mapsE)

    mapsF = []
    for c in range(NCORES):
        b, hg = c // 2, c % 2
        qfull = np.concatenate([np.asarray(resE[2 * b]["qb_o"]), np.asarray(resE[2 * b + 1]["qb_o"])], axis=1)
        kfull = np.concatenate([np.asarray(resE[2 * b]["kb_o"]), np.asarray(resE[2 * b + 1]["kb_o"])], axis=1)
        vfull = np.concatenate([np.asarray(resE[2 * b]["vb_o"]), np.asarray(resE[2 * b + 1]["vb_o"])], axis=1)
        qh = np.zeros((512, S), ml_dtypes.bfloat16)
        kh = np.zeros((512, S), ml_dtypes.bfloat16)
        vx = np.zeros((128, 8 * 32 * 65), ml_dtypes.bfloat16)
        ones_col = np.ones((128, 1), ml_dtypes.bfloat16)
        for i in range(8):
            h = hg * 8 + i
            rows = _head_rows(h)
            qh[i * 64:(i + 1) * 64] = qfull[rows]
            kh[i * 64:(i + 1) * 64] = kfull[rows]
            vh = vfull[h * 64:(h + 1) * 64]
            for ts in range(32):
                base = (i * 32 + ts) * 65
                vx[:, base:base + 64] = vh[:, ts * 128:(ts + 1) * 128].T
                vx[:, base + 64:base + 65] = ones_col
        mapsF.append(dict(qh_i=qh, kh_i=kh, vx_i=vx))
    resF = _run(build_stageF, mapsF)

    wo1 = f32(inp["L1_attn_Wo"])
    bo1 = _v8(inp["L1_attn_bo"])
    mapsG = []
    for c in range(NCORES):
        b, hh = c // 2, c % 2
        ao_fm = np.concatenate([np.asarray(resF[2 * b]["ao_o"]), np.asarray(resF[2 * b + 1]["ao_o"])], axis=0)
        mapsG.append(dict(ao_i=np.ascontiguousarray(ao_fm[:, hh * TC:(hh + 1) * TC]),
                          x3_i=np.ascontiguousarray(resD[c]["x3_o"]), wo=wo1, bo_v=bo1))
    resG = _run(build_stageG, mapsG)

    idt = np.eye(128, dtype=np.float32)
    xh = {c: resG[c]["h0_o"] for c in range(NCORES)}
    for l in range(4):
        rw = f32(inp["L1_ffn_router_w"])[l]
        rb = f32(inp["L1_ffn_router_b"])[l]
        g_ = f32(inp["L1_ffn_ln_g"])[l]
        b_ = f32(inp["L1_ffn_ln_b"])[l]
        w1 = f32(inp["L1_ffn_w1"])[l]
        b1 = f32(inp["L1_ffn_b1"])[l]
        w2 = f32(inp["L1_ffn_w2"])[l]
        b2 = f32(inp["L1_ffn_b2"])[l]
        rwt = np.ascontiguousarray(rw.reshape(8, 128, 4).transpose(1, 0, 2).reshape(128, 32))
        lgb = np.zeros((128, 64), np.float32)
        for e in range(4):
            lgb[:, (2 * e) * 8:(2 * e) * 8 + 8] = _v8(g_[e])
            lgb[:, (2 * e + 1) * 8:(2 * e + 1) * 8 + 8] = _v8(b_[e])
        b1t = np.ascontiguousarray(np.concatenate([_v32(b1[e]) for e in range(4)], axis=1))
        b2t = np.ascontiguousarray(np.concatenate([_v8(b2[e]) for e in range(4)], axis=1))
        w1_i = np.ascontiguousarray(w1.reshape(4 * D, 4096))
        w2_i = np.ascontiguousarray(w2.reshape(4 * 4096, D))
        mapsH = [dict(xh_i=np.ascontiguousarray(xh[c]), rwt=rwt,
                      rbt=np.ascontiguousarray(rb.reshape(1, 4)), lgb=lgb, b1t=b1t, b2t=b2t,
                      w1_i=w1_i, w2_i=w2_i, idt_i=idt) for c in range(NCORES)]
        resH = _run(build_stageH, mapsH)
        xh = {c: resH[c]["xo_o"] for c in range(NCORES)}

    mapsI = [dict(x4_i=np.ascontiguousarray(resG[c]["x4_o"]), h4_i=np.ascontiguousarray(xh[c]))
             for c in range(NCORES)]
    resI = _run(build_stageI, mapsI)

    out = np.empty((B, S, D), np.float32)
    for c in range(NCORES):
        b, hh = c // 2, c % 2
        out[b, hh * TC:(hh + 1) * TC, :] = resI[c]["x5_o"].T
    return out


RG_PAIR = [[0, 1], [2, 3], [4, 5], [6, 7]]
RG_ALL = [[0, 1, 2, 3, 4, 5, 6, 7]]


def emit_scan_fused(p, cm, io):
    nc = p.nc
    u_d, ek_d, tdv, pval = io["u_d"], io["ek_d"], io["tdv"], io["pval"]
    pay_d, payg, payoff = io["pay_d"], io["payg"], io["payoff"]
    ln0_e, r_e, wkv_e = io["ln0_e"], io["r_e"], io["wkv_e"]
    a_d, b_d = io["a_d"], io["b_d"]
    cc_sem = p.sem("cc_scan")

    td = p.sbuf("sc_td", [128, 8], F32)
    pv = p.sbuf("sc_pv", [128, 1], F32)
    ew8 = p.sbuf("sc_ew8", [128, 8], F32)
    ntd = p.sbuf("sc_ntd", [128, 8], F32)
    pay = p.sbuf("sc_pay", [128, 32], F32)
    ones_t = p.sbuf("sc_ones", [128, TC], F32)
    ew_t = p.sbuf("sc_ewt", [128, TC], F32)
    ut = p.sbuf("sc_u", [128, TC], F32)
    ekt = p.sbuf("sc_ek", [128, TC], F32)
    at = p.sbuf("sc_a", [128, TC], F32)
    bt = p.sbuf("sc_b", [128, TC], F32)

    p.dma_in(td, tdv)
    p.dma_in(pv, pval)
    p.op("gpsimd", lambda h: h.memset(ones_t.ap, 1.0), writes=(ones_t,))
    p.chain("scalar", [
        lambda h: h.activation(ntd.ap, td.ap, AF.Exp),
    ], reads=(td,), writes=(ntd,))
    p.chain("scalar", [
        lambda h: h.activation(ew8.ap, ntd.ap, AF.Exp, scale=-1.0),
    ], reads=(ntd,), writes=(ew8,))
    p.op("vector", lambda h: h.tensor_scalar(ntd.ap, ntd.ap, -1.0, None, op0=ALU.mult),
         reads=(), writes=(ntd,))
    p.raw("vector", lambda h: h.drain())

    for j in range(ND):
        p.dma_in(ut, u_d[j * 128:(j + 1) * 128, :])
        p.dma_in(ekt, ek_d[j * 128:(j + 1) * 128, :])
        p.op("vector", lambda h, j=j: h.tensor_scalar(ew_t.ap, ones_t.ap, ew8[0:128, j:j + 1], None, op0=ALU.mult),
             reads=(ones_t, ew8), writes=(ew_t,))
        p.op("vector", lambda h: h.tensor_tensor_scan(at.ap, ew_t.ap, ut.ap, 0.0, op0=ALU.mult, op1=ALU.add),
             reads=(ew_t, ut), writes=(at,))
        p.op("vector", lambda h: h.tensor_tensor_scan(bt.ap, ew_t.ap, ekt.ap, 0.0, op0=ALU.mult, op1=ALU.add),
             reads=(ew_t, ekt), writes=(bt,))
        p.op("vector", lambda h, j=j: h.tensor_copy(pay[0:128, j:j + 1], at[0:128, TC - 1:TC]),
             reads=(at,), writes=(pay,))
        p.op("vector", lambda h, j=j: h.tensor_copy(pay[0:128, 8 + j:8 + j + 1], bt[0:128, TC - 1:TC]),
             reads=(bt,), writes=(pay,))
        p.dma_out(a_d[j * 128:(j + 1) * 128, :], at)
        p.dma_out(b_d[j * 128:(j + 1) * 128, :], bt)
    lr = p.sbuf("sc_lr", [128, 16], F32)
    p.op("gpsimd", lambda h: h.dma_start(
        lr[0:128, 0:8], ln0_e.rearrange("(j p) t -> p j t", p=128)[:, :, TC:TC + 1].rearrange("p j o -> p (j o)"),
        allow_slow_non_contiguous=True), writes=(lr,), dma=True, dma_key=lr)
    p.op("gpsimd", lambda h: h.dma_start(
        lr[0:128, 8:16], r_e.rearrange("(j p) t -> p j t", p=128)[:, :, TC:TC + 1].rearrange("p j o -> p (j o)"),
        allow_slow_non_contiguous=True), writes=(lr,), dma=True, dma_key=lr)
    p.op("vector", lambda h: h.tensor_copy(pay[0:128, 16:32], lr.ap), reads=(lr,), writes=(pay,))
    p.raw("vector", lambda h: h.drain())
    ev_pay = p.dma_out(pay_d, pay)
    p._emit_wait("gpsimd", ev_pay)
    p.op("gpsimd", lambda h: h.collective_compute(
        "AllGather", ALU.bypass, replica_groups=RG_ALL, ins=[pay_d.opt()], outs=[payg.opt()]),
        sem=cc_sem, k=1)
    fl_in, fl_out = io["fence1_in"], io["fence1_out"]
    p.op("gpsimd", lambda h: h.collective_compute(
        "AllReduce", ALU.add, replica_groups=RG_ALL, ins=[fl_in.opt()], outs=[fl_out.opt()]),
        sem=cc_sem, k=1)
    ev_cc = Ev(cc_sem, cc_sem.n, "cc")
    pin = p.sbuf("sc_pin", [128, 32], F32)
    pofft = p.sbuf("sc_poff", [1, 1], mybir.dt.int32)
    p.dma_in(pofft, payoff)
    p._emit_wait("gpsimd", ev_cc)

    def _pin_dma(h):
        with h.register("sc_r0") as r0:
            h.reg_load(r0, pofft[0:1, 0:1])
            return h.dma_start(pin.ap, bass.AP(payg.tensor, r0, [[32, 128], [1, 32]]))
    p.op("gpsimd", _pin_dma, reads=(pofft,), writes=(pin,), dma=True, dma_key=pin)
    pvinv = p.sbuf("sc_pvinv", [128, 1], F32)
    p.op("vector", lambda h: h.tensor_scalar(pvinv.ap, pv.ap, -1.0, 1.0, op0=ALU.mult, op1=ALU.add),
         reads=(pv,), writes=(pvinv,))
    p.raw("vector", lambda h: h.drain())
    bsafe = p.sbuf("sc_bsafe", [128, 8], F32)
    p.op("vector", lambda h: h.tensor_scalar(bsafe.ap, pin[0:128, 8:16], pv[0:128, 0:1], pvinv[0:128, 0:1],
                                             op0=ALU.mult, op1=ALU.add),
         reads=(pin, pv, pvinv), writes=(bsafe,))
    p.op("vector", lambda h: h.tensor_scalar(pin.ap, pin.ap, pv[0:128, 0:1], None, op0=ALU.mult),
         reads=(pv,), writes=(pin,))
    p.raw("vector", lambda h: h.drain())
    p.op("sync", lambda h: h.dma_start(
        ln0_e.rearrange("(j p) t -> p j t", p=128)[:, :, 0:1].rearrange("p j o -> p (j o)"),
        pin[0:128, 16:24], allow_slow_non_contiguous=True), reads=(pin,), dma=True, dma_key=pin)
    p.op("sync", lambda h: h.dma_start(
        r_e.rearrange("(j p) t -> p j t", p=128)[:, :, 0:1].rearrange("p j o -> p (j o)"),
        pin[0:128, 8 + 16:16 + 16], allow_slow_non_contiguous=True), reads=(pin,), dma=True, dma_key=pin)
    wb0 = p.sbuf("sc_wb0", [128, 8], F32)
    p.op("vector", lambda h: h.tensor_scalar(wb0.ap, bsafe.ap, WKV_EPS, None, op0=ALU.add),
         reads=(bsafe,), writes=(wb0,))
    p.raw("vector", lambda h: h.drain())
    p.chain("scalar", [
        lambda h: h.activation(wb0.ap, wb0.ap, AF.Ln),
        lambda h: h.activation(wb0.ap, wb0.ap, AF.Exp, scale=-1.0),
    ], reads=(), writes=(wb0,))
    p.op("vector", lambda h: h.tensor_tensor(wb0.ap, wb0.ap, pin[0:128, 0:8], op=ALU.mult),
         reads=(pin,), writes=(wb0,))
    p.raw("vector", lambda h: h.drain())
    p.op("sync", lambda h: h.dma_start(
        wkv_e.rearrange("(j p) t -> p j t", p=128)[:, :, 0:1].rearrange("p j o -> p (j o)"),
        wb0.ap, allow_slow_non_contiguous=True), reads=(wb0,), dma=True, dma_key=wb0)
    if io.get("debug_taps"):
        pay_tap = io["pay_tap"]
        payg_tap = io["payg_tap"]
        p.dma_out(pay_tap, pay)
        pgt = p.sbuf("sc_pgt", [128, 8 * 32], F32)
        for r in range(8):
            p.dma_in(pgt, payg[r * 128:(r + 1) * 128, :], dst=pgt.t[0:128, r * 32:(r + 1) * 32])
        p.dma_out(payg_tap, pgt)
        pin_tap = io["pin_tap"]
        p.dma_out(pin_tap, pin)
    iot = p.sbuf("sc_iot", [128, TC], F32)
    p.op("gpsimd", lambda h: h.iota(iot.ap, pattern=[[1, TC]], base=1, channel_multiplier=0,
                                    allow_small_or_imprecise_dtypes=True), writes=(iot,))
    etab = p.sbuf("sc_etab", [128, TC], F32)
    ca = p.sbuf("sc_ca", [128, 16], F32)
    p.op("vector", lambda h: h.tensor_copy(ca.ap, pin[0:128, 0:16]), reads=(pin,), writes=(ca,))
    p.raw("vector", lambda h: h.drain())
    wt = p.sbuf("sc_wkv", [128, TC], F32)
    for j in range(ND):
        p.dma_in(at, a_d[j * 128:(j + 1) * 128, :])
        p.dma_in(bt, b_d[j * 128:(j + 1) * 128, :])
        p.op("vector", lambda h, j=j: h.tensor_scalar(etab.ap, iot.ap, ntd[0:128, j:j + 1], None, op0=ALU.mult),
             reads=(iot, ntd), writes=(etab,))
        p.op("scalar", lambda h: h.activation(etab.ap, etab.ap, AF.Exp), reads=(), writes=(etab,))
        p.op("vector", lambda h, j=j: h.scalar_tensor_tensor(
            at.ap, etab.ap, ca[0:128, j:j + 1], at.ap, op0=ALU.mult, op1=ALU.add),
            reads=(etab, ca), writes=(at,))
        p.op("vector", lambda h, j=j: h.scalar_tensor_tensor(
            bt.ap, etab.ap, ca[0:128, 8 + j:8 + j + 1], bt.ap, op0=ALU.mult, op1=ALU.add),
            reads=(etab, ca), writes=(bt,))
        p.op("vector", lambda h: h.tensor_scalar(bt.ap, bt.ap, WKV_EPS, None, op0=ALU.add),
             reads=(), writes=(bt,))
        p.chain("scalar", [
            lambda h: h.activation(bt.ap, bt.ap, AF.Ln),
            lambda h: h.activation(bt.ap, bt.ap, AF.Exp, scale=-1.0),
        ], reads=(), writes=(bt,))
        p.op("vector", lambda h: h.tensor_tensor(wt.ap, at.ap, bt.ap, op=ALU.mult),
             reads=(at, bt), writes=(wt,))
        p.dma_out(wkv_e[j * 128:(j + 1) * 128, 1:1 + TC], wt)


def emit_attnF2(p, cm, io):
    nc = p.nc
    qb_d, kvm, ao_d, idt64 = io["qb_d"], io["kv_my"], io["ao_d"], io["idt64"]
    TQ = 512
    idb = p.sbuf("f2_id", [64, 64], BF16)
    p.dma_in(idb, idt64)
    kb = p.sbuf("f2_k", [64, S], BF16)
    qb = p.sbuf("f2_q", [64, TC], BF16)
    vf = p.sbuf("f2_v", [64, S], BF16)
    vext = p.sbuf("f2_vx", [128, 32 * 65], BF16)
    pb = [p.sbuf(f"f2_p{i}", [128, TQ], BF16) for i in range(2)]
    dn = p.sbuf("f2_dn", [1, TQ], F32)
    af = p.sbuf("f2_af", [64, TQ], F32)
    aob = p.sbuf("f2_ao", [64, TQ], BF16)
    ps_qk = PsumPool(p, "f2qk", 2)
    ps_tr = PsumPool(p, "f2tr", 2)
    ps_av = p.psum("f2_av", [65, 512])
    ps_bc = p.psum("f2_bc", [64, 512])

    for hh in range(16):
        g, sub = hh // 2, hh % 2
        r0 = g * 128 + sub * 32
        r1 = g * 128 + 64 + sub * 32
        p.dma_in(kb, kvm[r0:r0 + 32, :], dst=kb.t[0:32, :])
        p.op("gpsimd", lambda h, r1=r1: h.dma_start(kb.t[32:64, :], kvm[r1:r1 + 32, :]),
             writes=(kb,), dma=True, dma_key=kb)
        p.dma_in(vf, kvm[1024 + hh * 64:1024 + (hh + 1) * 64, :])
        p.dma_in(qb, qb_d[r0:r0 + 32, :], dst=qb.t[0:32, :])
        p.op("gpsimd", lambda h, r1=r1: h.dma_start(qb.t[32:64, :], qb_d[r1:r1 + 32, :]),
             writes=(qb,), dma=True, dma_key=qb)
        p.op("gpsimd", lambda h: h.memset(vext.ap, 1.0), writes=(vext,))
        for ts in range(32):
            ps = ps_tr.next()
            p.chain("tensor", [lambda h, ts=ts, ps=ps: h.matmul(
                ps[0:128, 0:64], vf[0:64, ts * 128:(ts + 1) * 128], idb.ap, start=True, stop=True)],
                reads=(vf, idb), writes=(ps,))
            p.op("vector", lambda h, ts=ts, ps=ps: h.tensor_copy(
                vext[0:128, ts * 65:ts * 65 + 64], ps[0:128, 0:64]), reads=(ps,), writes=(vext,))
        for tq in range(TC // TQ):
            q_ap = qb[0:64, tq * TQ:(tq + 1) * TQ]
            for ts in range(32):
                ps = ps_qk.next()
                p.chain("tensor", [lambda h, ts=ts, q_ap=q_ap, ps=ps: h.matmul(
                    ps[0:128, 0:TQ], kb[0:64, ts * 128:(ts + 1) * 128], q_ap,
                    start=True, stop=True)], reads=(kb, qb), writes=(ps,))
                pbuf = pb[ts % 2]
                p.op("scalar", lambda h, ps=ps, pbuf=pbuf: h.activation(
                    pbuf.ap, ps[0:128, 0:TQ], AF.Exp, scale=0.125), reads=(ps,), writes=(pbuf,))
                p.chain("tensor", [lambda h, ts=ts, pbuf=pbuf: h.matmul(
                    ps_av[0:65, 0:TQ], vext[0:128, ts * 65:(ts + 1) * 65], pbuf.ap,
                    start=(ts == 0), stop=(ts == 31))], reads=(vext, pbuf), writes=(ps_av,))
            p.op("vector", lambda h: h.tensor_copy(dn.ap, ps_av[64:65, 0:TQ]), reads=(ps_av,), writes=(dn,))
            p.chain("scalar", [
                lambda h: h.activation(dn.ap, dn.ap, AF.Ln),
                lambda h: h.activation(dn.ap, dn.ap, AF.Exp, scale=-1.0),
            ], reads=(), writes=(dn,))
            p.op("scalar", lambda h: h.copy(af.ap, ps_av[0:64, 0:TQ]), reads=(ps_av,), writes=(af,))
            p.op("tensor", lambda h: h.matmul(ps_bc[0:64, 0:TQ], cm.ones_row_f[0:1, 0:64], dn.ap,
                                              start=True, stop=True),
                 reads=(cm.ones_row_f, dn), writes=(ps_bc,))
            p.op("vector", lambda h: h.tensor_tensor(aob.ap, af.ap, ps_bc[0:64, 0:TQ], op=ALU.mult),
                 reads=(af, ps_bc), writes=(aob,))
            p.dma_out(ao_d[hh * 64:(hh + 1) * 64, tq * TQ:(tq + 1) * TQ], aob)


def build_fused(debug=False):
    nc = bass.Bass("TRN2", target_bir_lowering=False)
    p = Prog(nc)
    cm = Common(p)
    TCE = TC + 1
    DH = 4096
    dI = lambda n, s, dt=F32: nc.dram_tensor(n, s, dt, kind="ExternalInput").ap()
    _no_tap = {"pay_d", "payg", "kv_d", "kvg", "fence1_in", "fence2_in"}
    dN = lambda n, s, dt=F32: nc.dram_tensor(
        n, s, dt, kind="ExternalOutput" if (debug and n not in _no_tap) else "Internal").ap()
    xe = dI("xe", [D, TCE])
    pval = dI("pval", [128, 1])
    fmask = dI("fmask", [128, 1])
    vecs = dI("vecs", [128, 48])
    wk, wv, wr = dI("wk", [D, D]), dI("wv", [D, D]), dI("wr", [D, D])
    tdv = dI("tdv", [128, 8])
    wo0 = dI("wo0", [D, D])
    vecs2 = dI("vecs2", [128, 16])
    wfk, wfr, wfv = dI("wfk", [D, 4 * D]), dI("wfr", [D, D]), dI("wfv", [4 * D, D])
    w1t, w2t = dI("w1t", [D, DH]), dI("w2t", [D, DH])
    h1t, h2t = dI("h1t", [DH, DH]), dI("h2t", [DH, DH])
    w3t = dI("w3t", [DH, D])
    scl = dI("scl", [128, 136])
    wqkv = dI("wqkv", [D, 3 * D])
    cost, sint = dI("cost", [128, TC]), dI("sint", [128, TC])
    selm, selt = dI("selm", [128, 128]), dI("selt", [16, 1024])
    idt64 = dI("idt64", [64, 64], BF16)
    wo1 = dI("wo1", [D, D])
    bo_v = dI("bo_v", [128, 8])
    idt_i = dI("idt_i", [128, 128])
    payoff = dI("payoff", [1, 1], mybir.dt.int32)
    kvoff = dI("kvoff", [1, 2], mybir.dt.int32)
    hw = []
    for l in range(4):
        hw.append(dict(rwt=dI(f"rwt{l}", [128, 32]), rbt=dI(f"rbt{l}", [1, 4]),
                       lgb=dI(f"lgb{l}", [128, 64]), b1t=dI(f"b1t{l}", [128, 128]),
                       b2t=dI(f"b2t{l}", [128, 32]), w1_i=dI(f"w1i{l}", [4 * D, DH]),
                       w2_i=dI(f"w2i{l}", [4 * DH, D])))
    x5_o = nc.dram_tensor("x5_o", [D, TC], F32, kind="ExternalOutput").ap()
    ln0_e, r_e, wkv_e = dN("ln0_e", [D, TCE]), dN("r_e", [D, TCE]), dN("wkv_e", [D, TCE])
    u_d, ek_d = dN("u_d", [D, TC]), dN("ek_d", [D, TC])
    a_d, b_d = dN("a_d", [D, TC]), dN("b_d", [D, TC])
    pay_d = dN("pay_d", [128, 32])
    payg = nc.dram_tensor("payg", [1024, 32], F32, kind="Internal", addr_space="Shared").ap()
    x1_d = dN("x1_d", [D, TCE])
    x2_d, x3_d = dN("x2_d", [D, TC]), dN("x3_d", [D, TC])
    q_d = dN("q_d", [D, TC], BF16)
    kv_d = dN("kv_d", [2 * D, TC], BF16)
    kvg = nc.dram_tensor("kvg", [16 * D, TC], BF16, kind="Internal", addr_space="Shared").ap()
    kv_my = dN("kv_my", [2 * D, S], BF16)
    fence1_in = dN("fence1_in", [128, 1])
    fence1_out = nc.dram_tensor("fence1_out", [128, 1], F32, kind="Internal", addr_space="Shared").ap()
    fence2_in = dN("fence2_in", [128, 1])
    fence2_out = nc.dram_tensor("fence2_out", [128, 1], F32, kind="Internal", addr_space="Shared").ap()
    ao_d = dN("ao_d", [D, TC], BF16)
    x4_d = dN("x4_d", [D, TC])
    xhat_d = dN("xhat_d", [D, TC], BF16)
    hh_d = [dN(f"hh{l}", [D, TC]) for l in range(5)]
    cc2_sem = p.sem("cc_kv")

    with p.stage("A"):
        build_stageA(p, io=dict(cm=cm, xe=xe, pval=pval, fmask=fmask, vecs=vecs,
                                wk=wk, wv=wv, wr=wr,
                                ln0_o=ln0_e[:, 1:1 + TC], u_o=u_d, ek_o=ek_d,
                                r_o=r_e[:, 1:1 + TC]))
    with p.stage("scan"):
        sc_io = dict(u_d=u_d, ek_d=ek_d, tdv=tdv, pval=pval,
                     pay_d=pay_d, payg=payg, payoff=payoff,
                     fence1_in=fence1_in, fence1_out=fence1_out,
                     ln0_e=ln0_e, r_e=r_e,
                     wkv_e=wkv_e, a_d=a_d, b_d=b_d)
        if debug:
            sc_io.update(debug_taps=True,
                         pay_tap=nc.dram_tensor("pay_tap", [128, 32], F32, kind="ExternalOutput").ap(),
                         payg_tap=nc.dram_tensor("payg_tap", [128, 256], F32, kind="ExternalOutput").ap(),
                         pin_tap=nc.dram_tensor("pin_tap", [128, 32], F32, kind="ExternalOutput").ap())
        emit_scan_fused(p, cm, sc_io)
    with p.stage("C1"):
        build_stageC1(p, io=dict(ln0_e=ln0_e, r_e=r_e, wkv_e=wkv_e, wo=wo0, x1_o=x1_d))
    with p.stage("C2"):
        build_stageC2(p, io=dict(cm=cm, x1_e=x1_d, x_fm=xe[:, 1:1 + TC], pval=pval,
                                 vecs=vecs2, wfk=wfk, wfr=wfr, wfv=wfv, x2_o=x2_d))
    with p.stage("D"):
        build_stageD(p, io=dict(cm=cm, x2_i=x2_d, w1t=w1t, w2t=w2t, h1t=h1t, h2t=h2t,
                                w3t=w3t, scl=scl, x3_o=x3_d))
    with p.stage("E"):
        build_stageE(p, io=dict(cm=cm, x3_i=x3_d, wqkv=wqkv, cost=cost, sint=sint,
                                selm=selm, selt=selt, qb_o=q_d,
                                kb_o=kv_d[0:D, :], vb_o=kv_d[D:2 * D, :]))
    with p.stage("cc2"):
        p.op("gpsimd", lambda h: h.collective_compute(
            "AllGather", ALU.bypass, replica_groups=RG_ALL, ins=[kv_d.opt()], outs=[kvg.opt()]),
            sem=cc2_sem, k=1)
        p.op("gpsimd", lambda h: h.collective_compute(
            "AllReduce", ALU.add, replica_groups=RG_ALL, ins=[fence2_in.opt()], outs=[fence2_out.opt()]),
            sem=cc2_sem, k=1)
        p._emit_wait("gpsimd", Ev(cc2_sem, cc2_sem.n, "cc"))
        kofft = p.sbuf("cc2_koff", [1, 2], mybir.dt.int32)
        p.dma_in(kofft, kvoff)
        kvm_buf = Buf(None, "kv_my")
        for half in range(2):
            def cpy(h, half=half):
                with h.register(f"cc2_r{half}") as r0:
                    h.reg_load(r0, kofft[0:1, half:half + 1])
                    return h.dma_start(
                        bass.AP(kv_my.tensor, half * TC, [[2 * TC, 2 * D], [1, TC]]),
                        bass.AP(kvg.tensor, r0, [[TC, 2 * D], [1, TC]]))
            p.op("gpsimd", cpy, reads=(kofft,), writes=(kvm_buf,), dma=True, dma_key=kvm_buf)
    with p.stage("F"):
        emit_attnF2(p, cm, dict(qb_d=q_d, kv_my=kv_my, ao_d=ao_d, idt64=idt64))
    with p.stage("G"):
        build_stageG(p, io=dict(cm=cm, ao_i=ao_d, x3_i=x3_d, wo=wo1, bo_v=bo_v,
                                x4_o=x4_d, h0_o=hh_d[0]))
    for l in range(4):
        with p.stage(f"H{l}"):
            build_stageH(p, io=dict(cm=cm, xh_i=hh_d[l], xo_o=hh_d[l + 1], xhat_d=xhat_d,
                                    idt_i=idt_i, **hw[l]))
    with p.stage("I"):
        build_stageI(p, io=dict(cm=cm, x4_i=x4_d, h4_i=hh_d[4], x5_o=x5_o))
    return p


def _fused_maps(inp):
    f32 = lambda a: np.ascontiguousarray(np.asarray(a, np.float32))
    x = f32(inp["L0_attn_x"])
    vecs = np.concatenate([_v8(inp[f"L0_attn_{n}"]) for n in
                           ("ln1_g", "ln1_b", "mix_k", "mix_v", "mix_r", "time_first")], axis=1)
    vecs2 = np.concatenate([_v8(inp["L0_attn_ln2_g"]), _v8(inp["L0_attn_ln2_b"])], axis=1)
    scl = np.concatenate([_v32(inp["L0_ffn_w1_s"]), _v32(inp["L0_ffn_w2_s"]),
                          _v32(inp["L0_ffn_h1_s"]), _v32(inp["L0_ffn_h2_s"]),
                          _v8(inp["L0_ffn_w3_s"])], axis=1)
    wqkv = f32(inp["L1_attn_Wqkv"])
    perm = _perm1024()
    wperm = np.ascontiguousarray(np.concatenate(
        [wqkv[:, 0:1024][:, perm], wqkv[:, 1024:2048][:, perm], wqkv[:, 2048:3072]], axis=1))
    sel = np.zeros((128, 16 * 8), np.float32)
    selt = np.zeros((16, 128 * 8), np.float32)
    for j in range(8):
        for r in range(128):
            h = 2 * j + (1 if (32 <= r < 64 or r >= 96) else 0)
            sel[r, j * 16 + h] = 1.0
            selt[h, j * 128 + r] = 1.0
    common = dict(
        vecs=np.ascontiguousarray(vecs), vecs2=np.ascontiguousarray(vecs2),
        wk=f32(inp["L0_attn_Wk"]), wv=f32(inp["L0_attn_Wv"]), wr=f32(inp["L0_attn_Wr"]),
        tdv=_v8(inp["L0_attn_time_decay"]), wo0=f32(inp["L0_attn_Wo"]),
        wfk=f32(inp["L0_attn_Wfk"]), wfr=f32(inp["L0_attn_Wfr"]), wfv=f32(inp["L0_attn_Wfv"]),
        w1t=np.ascontiguousarray(f32(inp["L0_ffn_w1_w"]).T),
        w2t=np.ascontiguousarray(f32(inp["L0_ffn_w2_w"]).T),
        h1t=np.ascontiguousarray(f32(inp["L0_ffn_h1_w"]).T),
        h2t=np.ascontiguousarray(f32(inp["L0_ffn_h2_w"]).T),
        w3t=np.ascontiguousarray(f32(inp["L0_ffn_w3_w"]).T),
        scl=np.ascontiguousarray(scl), wqkv=wperm, selm=sel, selt=selt,
        idt64=np.eye(64, dtype=ml_dtypes.bfloat16),
        wo1=f32(inp["L1_attn_Wo"]), bo_v=_v8(inp["L1_attn_bo"]),
        idt_i=np.eye(128, dtype=np.float32),
    )
    for l in range(4):
        rw = f32(inp["L1_ffn_router_w"])[l]
        g_ = f32(inp["L1_ffn_ln_g"])[l]
        b_ = f32(inp["L1_ffn_ln_b"])[l]
        lgb = np.zeros((128, 64), np.float32)
        for e in range(4):
            lgb[:, (2 * e) * 8:(2 * e) * 8 + 8] = _v8(g_[e])
            lgb[:, (2 * e + 1) * 8:(2 * e + 1) * 8 + 8] = _v8(b_[e])
        common.update({
            f"rwt{l}": np.ascontiguousarray(rw.reshape(8, 128, 4).transpose(1, 0, 2).reshape(128, 32)),
            f"rbt{l}": np.ascontiguousarray(f32(inp["L1_ffn_router_b"])[l].reshape(1, 4)),
            f"lgb{l}": lgb,
            f"b1t{l}": np.ascontiguousarray(np.concatenate(
                [_v32(f32(inp["L1_ffn_b1"])[l][e]) for e in range(4)], axis=1)),
            f"b2t{l}": np.ascontiguousarray(np.concatenate(
                [_v8(f32(inp["L1_ffn_b2"])[l][e]) for e in range(4)], axis=1)),
            f"w1i{l}": np.ascontiguousarray(f32(inp["L1_ffn_w1"])[l].reshape(4 * D, 4096)),
            f"w2i{l}": np.ascontiguousarray(f32(inp["L1_ffn_w2"])[l].reshape(4 * 4096, D)),
        })
    maps = []
    for c in range(NCORES):
        b, hh = c // 2, c % 2
        xs = x[b, hh * TC:(hh + 1) * TC, :].T
        bound = np.zeros((D, 1), np.float32) if hh == 0 else x[b, hh * TC - 1, :].reshape(D, 1)
        cost, sint = _rope_tables(hh * TC)
        m = dict(common)
        m.update(xe=np.ascontiguousarray(np.concatenate([bound, xs], axis=1)),
                 pval=np.full((128, 1), float(hh), np.float32),
                 fmask=np.full((128, 1), 1.0 - hh, np.float32),
                 cost=cost, sint=sint,
                 payoff=np.array([[(c ^ 1) * 128 * 32]], np.int32),
                 kvoff=np.array([[(2 * b) * 2 * D * TC, (2 * b + 1) * 2 * D * TC]], np.int32))
        maps.append(m)
    return maps


def kernel(**inp):
    maps = _fused_maps(inp)
    p = build_fused()
    nc = _finish(p)
    res = run_bass_kernel_spmd(nc, maps, core_ids=list(range(NCORES))).results
    out = np.empty((B, S, D), np.float32)
    for c in range(NCORES):
        b, hh = c // 2, c % 2
        out[b, hh * TC:(hh + 1) * TC, :] = res[c]["x5_o"].T
    return out


# version 03
# speedup vs baseline: 1.0201x; 1.0201x over previous
import contextlib
import math
import numpy as np
import ml_dtypes
import concourse.bass as bass
import concourse.mybir as mybir
from concourse.bass_utils import run_bass_kernel_spmd

F32 = mybir.dt.float32
BF16 = mybir.dt.bfloat16
AF = mybir.ActivationFunctionType
ALU = mybir.AluOpType
AX = mybir.AxisListType

NCORES = 8
D = 1024
B = 4
S = 4096
TC = 2048
ND = D // 128
LN_EPS = 1e-5
WKV_EPS = 1e-8

ENGINES = ("tensor", "vector", "scalar", "gpsimd", "sync")
FENCE_ALL = False
_DMA_ENG = ("gpsimd", "sync", "scalar")


class Ev:
    __slots__ = ("sem", "n", "eng")

    def __init__(self, sem, n, eng):
        self.sem, self.n, self.eng = sem, n, eng


class Sem:
    def __init__(self, handle, name):
        self.h = handle
        self.name = name
        self.n = 0


class Buf:
    __slots__ = ("t", "ready", "users", "name", "dma_sem")

    def __init__(self, t, name=""):
        self.t = t
        self.name = name
        self.ready = None
        self.users = {}
        self.dma_sem = None

    def __getitem__(self, idx):
        return self.t[idx]

    @property
    def ap(self):
        return self.t[:]


class Prog:
    def __init__(self, nc):
        self.nc = nc
        self.q = {e: [] for e in ENGINES}
        self.ctx = contextlib.ExitStack()
        self._waited = {e: {} for e in ENGINES}
        self._fence_cache = {}
        self.sems = {}
        for e in ENGINES:
            self.sems[e] = self.sem("sm_" + e)
        self._fence_scratch = self.ctx.enter_context(self.nc.sbuf_tensor("fence_scr", [1, 1], F32))
        self.n_ops = 0
        _gs = self.sems["gpsimd"]
        _gs.n += 1
        scr = self._fence_scratch
        self._emit("gpsimd", lambda h, s=_gs.h, scr=scr: h.memset(scr[:], 0.0).then_inc(s, 1))
        self._fence_init_ev = Ev(_gs, _gs.n, "gpsimd")
        self._fence_init_done = set()
        self._dma_pool = [self.sem(f"dma{i}") for i in range(64)]
        self._dma_i = 0

    def _dma_sem_for(self, buf):
        if buf.dma_sem is None:
            buf.dma_sem = self._dma_pool[self._dma_i % len(self._dma_pool)]
            self._dma_i += 1
        return buf.dma_sem

    def sem(self, name):
        h = self.ctx.enter_context(self.nc.semaphore(name))
        s = Sem(h, name)
        if not hasattr(self, "_misc_sems"):
            self._misc_sems = []
        if name.startswith("cc_"):
            self._misc_sems.append(s)
        return s

    def sbuf(self, name, shape, dtype):
        st = getattr(self, "_cur_stack", None) or self.ctx
        name = getattr(self, "_prefix", "") + name
        return Buf(st.enter_context(self.nc.sbuf_tensor(name, shape, dtype)), name)

    def psum(self, name, shape, dtype=F32):
        st = getattr(self, "_cur_stack", None) or self.ctx
        name = getattr(self, "_prefix", "") + name
        return Buf(st.enter_context(self.nc.psum_tensor(name, shape, dtype)), name)

    def barrier(self):
        evs = []
        for e in ENGINES:
            s = self.sems[e]
            if s.n:
                evs.append(Ev(s, s.n, e))
        for s in self._dma_pool:
            if s.n:
                evs.append(Ev(s, s.n, "dma"))
        for s in getattr(self, "_misc_sems", []):
            if s.n:
                evs.append(Ev(s, s.n, "cc"))
        for e in ENGINES:
            for ev in evs:
                self._emit_wait(e, ev)

    @contextlib.contextmanager
    def stage(self, name):
        st = contextlib.ExitStack()
        old = getattr(self, "_cur_stack", None)
        oldp = getattr(self, "_prefix", "")
        self._cur_stack = st
        self._prefix = name + "_"
        try:
            yield
        finally:
            self._cur_stack = old
            self._prefix = oldp
            self.barrier()
            st.close()

    def raw(self, eng, fn):
        self._emit(eng, fn)

    def _emit(self, eng, fn):
        self.q[eng].append(fn)
        self.n_ops += 1

    def _emit_wait(self, eng, ev):
        if ev is None or ev.eng == eng:
            return
        w = self._waited[eng]
        if w.get(ev.sem, 0) >= ev.n:
            return
        w[ev.sem] = ev.n
        self._emit(eng, lambda h, s=ev.sem.h, n=ev.n: h.wait_ge(s, n))

    def _fenced(self, ev):
        if ev.eng not in ("vector", "scalar"):
            return ev
        key = (ev.sem, ev.n, ev.eng)
        f = self._fence_cache.get(key)
        if f is not None:
            return f
        scr = self._fence_scratch
        if ev.eng not in self._fence_init_done:
            self._fence_init_done.add(ev.eng)
            self._emit_wait(ev.eng, self._fence_init_ev)
        sem = self.sems[ev.eng]
        sem.n += 1
        n = sem.n
        if ev.eng == "vector":
            self._emit("vector", lambda h, s=sem.h, scr=scr: h.tensor_copy(scr[:], scr[:]).then_inc(s, 1))
        else:
            self._emit("scalar", lambda h, s=sem.h, scr=scr: h.copy(scr[:], scr[:]).then_inc(s, 1))
        f = Ev(sem, n, ev.eng)
        self._fence_cache[key] = f
        return f

    def op(self, eng, fn, reads=(), writes=(), sem=None, k=1, dma=False, dma_key=None):
        ev_eng = eng
        if dma:
            assert dma_key is not None
            sem = self._dma_sem_for(dma_key)
            k = 16
            ev_eng = "dma"
            self._emit_wait(eng, Ev(sem, sem.n, "dma"))
        elif sem is None:
            sem = self.sems[eng]
        for b in reads:
            ev = b.ready
            if ev is not None and (dma or FENCE_ALL) and ev.eng != eng:
                ev = self._fenced(ev)
            self._emit_wait(eng, ev)
        for b in writes:
            self._emit_wait(eng, b.ready)
            for uev in b.users.values():
                self._emit_wait(eng, uev)
        sem.n += k
        n = sem.n
        self._emit(eng, lambda h, fn=fn, s=sem.h, k=k: fn(h).then_inc(s, k))
        ev = Ev(sem, n, ev_eng)
        for b in writes:
            b.ready = ev
            b.users = {}
        for b in reads:
            b.users[eng if not dma else "dma:" + eng] = ev
        return ev

    def chain(self, eng, fns, reads=(), writes=(), sem=None):
        if sem is None:
            sem = self.sems[eng]
        for b in reads:
            ev = b.ready
            if ev is not None and FENCE_ALL and ev.eng != eng:
                ev = self._fenced(ev)
            self._emit_wait(eng, ev)
        for b in writes:
            self._emit_wait(eng, b.ready)
            for uev in b.users.values():
                self._emit_wait(eng, uev)
        for fn in fns[:-1]:
            self._emit(eng, fn)
        sem.n += 1
        n = sem.n
        self._emit(eng, lambda h, fn=fns[-1], s=sem.h: fn(h).then_inc(s, 1))
        ev = Ev(sem, n, eng)
        for b in writes:
            b.ready = ev
            b.users = {}
        for b in reads:
            b.users[eng] = ev
        return ev

    def dma_in(self, buf, dram_ap, eng="gpsimd", dst=None):
        d = buf.ap if dst is None else dst
        return self.op(eng, lambda h, d=d, s=dram_ap: h.dma_start(d, s), writes=(buf,), dma=True,
                       dma_key=buf)

    def dma_out(self, dram_ap, buf, eng="sync", src=None, accum=False):
        s = buf.ap if src is None else src
        if accum:
            return self.op(eng, lambda h, d=dram_ap, s=s: h.dma_start(d, s, accum_op=ALU.add),
                           reads=(buf,), dma=True, dma_key=buf)
        return self.op(eng, lambda h, d=dram_ap, s=s: h.dma_start(d, s), reads=(buf,), dma=True,
                       dma_key=buf)

    def barrier_ev(self, eng):
        sem = self.sems[eng]
        return Ev(sem, sem.n, eng)

    def finalize(self):
        with self.nc.Block() as block:
            for eng in ENGINES:
                ops = self.q[eng]
                if not ops:
                    continue
                section = getattr(block, eng)

                def body(handle, ops=ops):
                    for fn in ops:
                        fn(handle)
                section(body)
        self.ctx.close()
        return self.nc


class PsumPool:
    def __init__(self, p, name, n, cols=512, parts=128, dtype=F32):
        self.bufs = [p.psum(f"{name}{i}", [parts, cols], dtype) for i in range(n)]
        self.i = 0

    def next(self):
        b = self.bufs[self.i % len(self.bufs)]
        self.i += 1
        return b



class Common:

    def __init__(self, p):
        self.ones_col_f = p.sbuf("ones_col_f", [128, 1], F32)
        self.ones_row_f = p.sbuf("ones_row_f", [1, 128], F32)
        p.op("gpsimd", lambda h, t=self.ones_col_f: h.memset(t.ap, 1.0), writes=(self.ones_col_f,))
        p.op("gpsimd", lambda h, t=self.ones_row_f: h.memset(t.ap, 1.0), writes=(self.ones_row_f,))


def emit_ln(p, cm, T, src_tile, dst_tile, scratch, stat, ps_s0, ps_s1, ps_bc0, ps_bc1,
           gb=None, nd=ND, eps=LN_EPS, feat=None):
    src_bufs, dst_bufs = feat
    nfeat = nd * 128
    for j in range(nd):
        p.op("scalar", lambda h, j=j: h.activation(scratch[0:128, j * T:(j + 1) * T], src_tile(j), AF.Square),
             reads=tuple(src_bufs), writes=(scratch,))
    for j in range(nd):
        p.op("tensor", lambda h, j=j: h.matmul(ps_s0[0:1, 0:T], cm.ones_col_f[0:128, 0:1], src_tile(j),
                                               start=(j == 0), stop=(j == nd - 1)),
             reads=tuple(src_bufs) + (cm.ones_col_f,), writes=(ps_s0,))
    for j in range(nd):
        p.op("tensor", lambda h, j=j: h.matmul(ps_s1[0:1, 0:T], cm.ones_col_f[0:128, 0:1],
                                               scratch[0:128, j * T:(j + 1) * T],
                                               start=(j == 0), stop=(j == nd - 1)),
             reads=(scratch, cm.ones_col_f), writes=(ps_s1,))
    p.op("vector", lambda h: h.tensor_scalar(stat[0:1, 0:T], ps_s0[0:1, 0:T], 1.0 / nfeat, None, op0=ALU.mult),
         reads=(ps_s0,), writes=(stat,))
    p.chain("vector", [
        lambda h: h.tensor_scalar(stat[0:1, T:2 * T], ps_s1[0:1, 0:T], 1.0 / nfeat, eps, op0=ALU.mult, op1=ALU.add),
        lambda h: h.tensor_tensor(stat[0:1, 2 * T:3 * T], stat[0:1, 0:T], stat[0:1, 0:T], op=ALU.mult),
        lambda h: h.tensor_tensor(stat[0:1, T:2 * T], stat[0:1, T:2 * T], stat[0:1, 2 * T:3 * T], op=ALU.subtract),
    ], reads=(ps_s1,), writes=(stat,))
    p.chain("scalar", [
        lambda h: h.activation(stat[0:1, 2 * T:3 * T], stat[0:1, T:2 * T], AF.Ln),
        lambda h: h.activation(stat[0:1, 2 * T:3 * T], stat[0:1, 2 * T:3 * T], AF.Exp, scale=-0.5),
    ], reads=(stat,), writes=(stat,))
    p.op("tensor", lambda h: h.matmul(ps_bc0[0:128, 0:T], cm.ones_row_f[0:1, 0:128], stat[0:1, 0:T],
                                      start=True, stop=True),
         reads=(stat, cm.ones_row_f), writes=(ps_bc0,))
    p.op("tensor", lambda h: h.matmul(ps_bc1[0:128, 0:T], cm.ones_row_f[0:1, 0:128], stat[0:1, 2 * T:3 * T],
                                      start=True, stop=True),
         reads=(stat, cm.ones_row_f), writes=(ps_bc1,))
    for j in range(nd):
        p.op("vector", lambda h, j=j: h.tensor_tensor(scratch[0:128, j * T:(j + 1) * T], src_tile(j),
                                                      ps_bc0[0:128, 0:T], op=ALU.subtract),
             reads=tuple(src_bufs) + (ps_bc0,), writes=(scratch,))
        if gb is None:
            p.op("vector", lambda h, j=j: h.tensor_tensor(dst_tile(j), scratch[0:128, j * T:(j + 1) * T],
                                                          ps_bc1[0:128, 0:T], op=ALU.mult),
                 reads=(scratch, ps_bc1), writes=tuple(dst_bufs))
        else:
            g_ap, b_ap, gb_bufs = gb
            p.op("vector", lambda h, j=j: h.tensor_tensor(scratch[0:128, j * T:(j + 1) * T],
                                                          scratch[0:128, j * T:(j + 1) * T],
                                                          ps_bc1[0:128, 0:T], op=ALU.mult),
                 reads=(ps_bc1,), writes=(scratch,))
            p.op("vector", lambda h, j=j: h.tensor_scalar(dst_tile(j), scratch[0:128, j * T:(j + 1) * T],
                                                          g_ap(j), b_ap(j), op0=ALU.mult, op1=ALU.add),
                 reads=(scratch,) + tuple(gb_bufs), writes=tuple(dst_bufs))


def emit_mm(p, pool, n_k, rhs_tile, w_tile, w_bufs, rhs_bufs, evac, m_tiles, T):
    for mj in m_tiles:
        ps = pool.next()
        fns = [
            (lambda h, k=k, mj=mj, ps=ps: h.matmul(
                ps[0:128, 0:T], w_tile(k, mj), rhs_tile(k), start=(k == 0), stop=(k == n_k - 1)))
            for k in range(n_k)
        ]
        p.chain("tensor", fns, reads=tuple(w_bufs) + tuple(rhs_bufs), writes=(ps,))
        evac(mj, ps)


CH_A = 256


def fm3(dram_ap, c0, cw):
    return dram_ap.rearrange("(j p) t -> p j t", p=128)[:, :, c0:c0 + cw]


def build_stageA(p=None, io=None, debug=False):
    own = p is None
    if own:
        nc = bass.Bass("TRN2", target_bir_lowering=False)
        p = Prog(nc)
    else:
        nc = p.nc
    io = io or {}
    def dram(name, shape, dt, kind):
        if name in io:
            return io[name]
        return nc.dram_tensor(name, shape, dt, kind=kind).ap()
    TCE = TC + 1
    xe = dram("xe", [D, TCE], F32, "ExternalInput")
    pval = dram("pval", [128, 1], F32, "ExternalInput")
    fmask = dram("fmask", [128, 1], F32, "ExternalInput")
    vecs = dram("vecs", [128, 8 * 6], F32, "ExternalInput")
    wk = dram("wk", [D, D], F32, "ExternalInput")
    wv = dram("wv", [D, D], F32, "ExternalInput")
    wr = dram("wr", [D, D], F32, "ExternalInput")
    ln0_o = dram("ln0_o", [D, TC], F32, "ExternalOutput")
    u_o = dram("u_o", [D, TC], F32, "ExternalOutput")
    ek_o = dram("ek_o", [D, TC], F32, "ExternalOutput")
    r_o = dram("r_o", [D, TC], F32, "ExternalOutput")

    cm = io.get("cm") or Common(p)
    CH = CH_A
    NCH = TC // CH

    wkb = [p.sbuf(f"wk{k}", [128, D], BF16) for k in range(ND)]
    wvb = [p.sbuf(f"wv{k}", [128, D], BF16) for k in range(ND)]
    wrb = [p.sbuf(f"wr{k}", [128, D], BF16) for k in range(ND)]
    vec = p.sbuf("vec", [128, 8 * 6], F32)
    pv = p.sbuf("pv", [128, 1], F32)
    fm = p.sbuf("fm", [128, 1], F32)
    deriv = p.sbuf("deriv", [128, 8 * 4], F32)
    CHP = CH_A + 1
    xn_ch = p.sbuf("xn_ch", [128, ND * CHP], F32)

    CHE = CH + 1
    xb = p.sbuf("xb", [128, ND * CHE], F32)
    ln0b = p.sbuf("ln0b", [128, ND * CHE], F32)
    sqs = p.sbuf("sqs", [128, ND * CHE], F32)
    stat = p.sbuf("stat", [1, 3 * CHE], F32)
    ekf = p.sbuf("ekf", [128, ND * CH], F32)
    vf = p.sbuf("vf", [128, ND * CH], F32)
    uf = p.sbuf("uf", [128, ND * CH], F32)
    rf = p.sbuf("rf", [128, ND * CH], F32)
    kin = p.sbuf("kin", [128, ND * CH], BF16)
    vin = p.sbuf("vin", [128, ND * CH], BF16)
    rin = p.sbuf("rin", [128, ND * CH], BF16)
    tmp_f = p.sbuf("tmp_f", [128, CH], F32)

    ps_s0 = p.psum("ps_s0", [1, 512])
    ps_s1 = p.psum("ps_s1", [1, 512])
    ps_bc0 = p.psum("ps_bc0", [128, 512])
    ps_bc1 = p.psum("ps_bc1", [128, 512])
    pool = PsumPool(p, "pp", 4)

    for k in range(ND):
        p.dma_in(wkb[k], wk[k * 128:(k + 1) * 128, :])
        p.dma_in(wvb[k], wv[k * 128:(k + 1) * 128, :])
        p.dma_in(wrb[k], wr[k * 128:(k + 1) * 128, :])
    p.dma_in(vec, vecs)
    p.dma_in(pv, pval)
    p.dma_in(fm, fmask)
    for i, src_off in enumerate((16, 24, 32)):
        p.op("vector", lambda h, i=i, o=src_off: h.tensor_scalar(
            deriv[0:128, i * 8:(i + 1) * 8], vec[0:128, o:o + 8], -1.0, 1.0, op0=ALU.mult, op1=ALU.add),
            reads=(vec,), writes=(deriv,))
    p.op("vector", lambda h: h.tensor_scalar(deriv[0:128, 24:32], vec[0:128, 40:48], fm[0:128, 0:1], None, op0=ALU.mult),
         reads=(vec, fm), writes=(deriv,))

    g_ap = lambda j: vec[0:128, j:j + 1]
    b_ap = lambda j: vec[0:128, 8 + j:8 + j + 1]

    def do_ln_pair(T, xsrc_buf, xsrc_tile, ln0_dst_buf, ln0_dst_tile, xn_dst_tile, sq_b, st_b):
        emit_ln(p, cm, T, xsrc_tile, ln0_dst_tile, sq_b, st_b, ps_s0, ps_s1, ps_bc0, ps_bc1,
                gb=None, feat=((xsrc_buf,), (ln0_dst_buf,)))
        emit_ln(p, cm, T, ln0_dst_tile, xn_dst_tile, sq_b, st_b, ps_s0, ps_s1, ps_bc0, ps_bc1,
                gb=(g_ap, b_ap, (vec,)), feat=((ln0_dst_buf,), (xn_ch,)))

    mix_specs = [
        (kin, 16, 0), (vin, 24, 8), (rin, 32, 16)]

    for ci in range(NCH):
        c0 = ci * CH
        e0 = 0 if ci == 0 else ci * CH + 1
        W = CH + 1 if ci == 0 else CH
        lo = 1 if ci == 0 else 0
        xo = 0 if ci == 0 else 1
        if ci > 0:
            for j in range(ND):
                p.op("vector", lambda h, j=j: h.tensor_copy(
                    xn_ch[0:128, j * CHP:j * CHP + 1], xn_ch[0:128, j * CHP + CH:j * CHP + CH + 1]),
                    reads=(), writes=(xn_ch,))
            p.raw("vector", lambda h: h.drain())
        p.dma_in(xb, fm3(xe, e0, W), dst=xb.t[:, 0:ND * W].rearrange("p (j t) -> p j t", j=ND))
        do_ln_pair(W, xb, lambda j, W=W: xb[0:128, j * W:(j + 1) * W],
                   ln0b, lambda j, W=W: ln0b[0:128, j * W:(j + 1) * W],
                   lambda j, W=W, xo=xo: xn_ch[0:128, j * CHP + xo:j * CHP + xo + W], sqs, stat)
        if ci == 0:
            for j in range(ND):
                p.op("vector", lambda h, j=j: h.tensor_scalar(
                    xn_ch[0:128, j * CHP:j * CHP + 1], xn_ch[0:128, j * CHP:j * CHP + 1],
                    pv[0:128, 0:1], None, op0=ALU.mult),
                    reads=(pv,), writes=(xn_ch,))
            p.raw("vector", lambda h: h.drain())
        p.dma_out(fm3(ln0_o, c0, CH), ln0b,
                  src=ln0b.t[:, 0:ND * W].rearrange("p (j t) -> p j t", j=ND)[:, :, lo:lo + CH])
        for dst, mixo, invo in mix_specs:
            for j in range(ND):
                cur = xn_ch[0:128, j * CHP + 1:j * CHP + 1 + CH]
                prv = xn_ch[0:128, j * CHP:j * CHP + CH]
                p.op("vector", lambda h, prv=prv, invo=invo, j=j: h.tensor_scalar(
                    tmp_f[0:128, 0:CH], prv, deriv[0:128, invo + j:invo + j + 1], None, op0=ALU.mult),
                    reads=(xn_ch, deriv), writes=(tmp_f,))
                p.op("vector", lambda h, cur=cur, mixo=mixo, j=j, dst=dst: h.scalar_tensor_tensor(
                    dst[0:128, j * CH:(j + 1) * CH], cur, vec[0:128, mixo + j:mixo + j + 1],
                    tmp_f[0:128, 0:CH], op0=ALU.mult, op1=ALU.add),
                    reads=(xn_ch, vec, tmp_f), writes=(dst,))

        def evac_k(mj, ps):
            if ci == 0:
                p.op("vector", lambda h, mj=mj, ps=ps: h.tensor_tensor(
                    ps[0:128, 0:1], ps[0:128, 0:1], deriv[0:128, 24 + mj:24 + mj + 1], op=ALU.add),
                    reads=(deriv, ps), writes=(ps,))
            p.op("scalar", lambda h, mj=mj, ps=ps: h.activation(
                ekf[0:128, mj * CH:(mj + 1) * CH], ps[0:128, 0:CH], AF.Exp),
                reads=(ps,), writes=(ekf,))
        emit_mm(p, pool, ND, lambda k: kin[0:128, k * CH:(k + 1) * CH],
                lambda k, mj: wkb[k][0:128, mj * 128:(mj + 1) * 128],
                wkb, (kin,), evac_k, range(ND), CH)

        def evac_v(mj, ps):
            p.op("vector", lambda h, mj=mj, ps=ps: h.tensor_copy(
                vf[0:128, mj * CH:(mj + 1) * CH], ps[0:128, 0:CH]), reads=(ps,), writes=(vf,))
        emit_mm(p, pool, ND, lambda k: vin[0:128, k * CH:(k + 1) * CH],
                lambda k, mj: wvb[k][0:128, mj * 128:(mj + 1) * 128],
                wvb, (vin,), evac_v, range(ND), CH)

        def evac_r(mj, ps):
            p.op("scalar", lambda h, mj=mj, ps=ps: h.activation(
                rf[0:128, mj * CH:(mj + 1) * CH], ps[0:128, 0:CH], AF.Sigmoid),
                reads=(ps,), writes=(rf,))
        emit_mm(p, pool, ND, lambda k: rin[0:128, k * CH:(k + 1) * CH],
                lambda k, mj: wrb[k][0:128, mj * 128:(mj + 1) * 128],
                wrb, (rin,), evac_r, range(ND), CH)

        for j in range(ND):
            p.op("vector", lambda h, j=j: h.tensor_tensor(
                uf[0:128, j * CH:(j + 1) * CH], ekf[0:128, j * CH:(j + 1) * CH],
                vf[0:128, j * CH:(j + 1) * CH], op=ALU.mult), reads=(ekf, vf), writes=(uf,))

        for dram, buf in ((u_o, uf), (ek_o, ekf), (r_o, rf)):
            p.dma_out(fm3(dram, c0, CH), buf, src=buf.t[:].rearrange("p (j t) -> p j t", j=ND))

        if debug and ci == 0:
            xn_o = dram("xn_o", [D, CH + 1], F32, "ExternalOutput")
            kin_o = dram("kin_o", [D, CH], F32, "ExternalOutput")
            xnf = p.sbuf("xnf_dbg", [128, ND * (CH + 1)], F32)
            kinf = p.sbuf("kinf_dbg", [128, ND * CH], F32)
            for j in range(ND):
                p.op("vector", lambda h, j=j: h.tensor_copy(
                    xnf[0:128, j * (CH + 1):(j + 1) * (CH + 1)], xn_ch[0:128, j * CHP:j * CHP + CH + 1]),
                    reads=(xn_ch,), writes=(xnf,))
                p.op("vector", lambda h, j=j: h.tensor_copy(
                    kinf[0:128, j * CH:(j + 1) * CH], kin[0:128, j * CH:(j + 1) * CH]),
                    reads=(kin,), writes=(kinf,))
            p.dma_out(fm3(xn_o, 0, CH + 1), xnf, src=xnf.t[:].rearrange("p (j t) -> p j t", j=ND))
            p.dma_out(fm3(kin_o, 0, CH), kinf, src=kinf.t[:].rearrange("p (j t) -> p j t", j=ND))

    return p


def _finish(p):
    for s in p._dma_pool:
        if s.n > 0:
            p._emit("sync", lambda h, s=s.h, n=s.n: h.wait_ge(s, n))
    return p.finalize()


def build_stageB(p=None, io=None):
    own = p is None
    if own:
        nc = bass.Bass("TRN2", target_bir_lowering=False)
        p = Prog(nc)
    else:
        nc = p.nc
    io = io or {}
    def dram(name, shape, dt, kind):
        if name in io:
            return io[name]
        return nc.dram_tensor(name, shape, dt, kind=kind).ap()
    NCH_T = 4
    u_s = dram("u_s", [512, S], F32, "ExternalInput")
    ek_s = dram("ek_s", [512, S], F32, "ExternalInput")
    td_s = dram("td_s", [128, 4], F32, "ExternalInput")
    wkv_o = dram("wkv_o", [512, S], F32, "ExternalOutput")

    ones_t = p.sbuf("ones_t", [128, S], F32)
    p.op("gpsimd", lambda h: h.memset(ones_t.ap, 1.0), writes=(ones_t,))
    td = p.sbuf("td", [128, 4], F32)
    ew_col = p.sbuf("ew_col", [128, 4], F32)
    p.dma_in(td, td_s)
    p.chain("scalar", [
        lambda h: h.activation(ew_col[0:128, 0:4], td[0:128, 0:4], AF.Exp),
        lambda h: h.activation(ew_col[0:128, 0:4], ew_col[0:128, 0:4], AF.Exp, scale=-1.0),
    ], reads=(td,), writes=(ew_col,))

    ew_t = p.sbuf("ew_t", [128, S], F32)
    ut = p.sbuf("ut", [128, S], F32)
    ekt = p.sbuf("ekt", [128, S], F32)
    at = p.sbuf("at", [128, S], F32)
    bt = p.sbuf("bt", [128, S], F32)
    wt = p.sbuf("wt", [128, S], F32)

    for i in range(NCH_T):
        p.dma_in(ut, u_s[i * 128:(i + 1) * 128, :])
        p.dma_in(ekt, ek_s[i * 128:(i + 1) * 128, :])
        p.op("vector", lambda h, i=i: h.tensor_scalar(
            ew_t.ap, ones_t.ap, ew_col[0:128, i:i + 1], None, op0=ALU.mult),
            reads=(ones_t, ew_col), writes=(ew_t,))
        p.op("vector", lambda h: h.tensor_tensor_scan(
            at.ap, ew_t.ap, ut.ap, 0.0, op0=ALU.mult, op1=ALU.add),
            reads=(ew_t, ut), writes=(at,))
        p.op("vector", lambda h: h.tensor_tensor_scan(
            bt.ap, ew_t.ap, ekt.ap, 0.0, op0=ALU.mult, op1=ALU.add),
            reads=(ew_t, ekt), writes=(bt,))
        p.op("vector", lambda h: h.tensor_scalar(bt.ap, bt.ap, WKV_EPS, None, op0=ALU.add),
             reads=(), writes=(bt,))
        p.chain("scalar", [
            lambda h: h.activation(bt.ap, bt.ap, AF.Ln),
            lambda h: h.activation(bt.ap, bt.ap, AF.Exp, scale=-1.0),
        ], reads=(), writes=(bt,))
        p.op("vector", lambda h: h.tensor_tensor(wt.ap, at.ap, bt.ap, op=ALU.mult),
             reads=(at, bt), writes=(wt,))
        p.dma_out(wkv_o[i * 128:(i + 1) * 128, :], wt)
    return p


def build_stageC1(p=None, io=None):
    own = p is None
    if own:
        nc = bass.Bass("TRN2", target_bir_lowering=False)
        p = Prog(nc)
    else:
        nc = p.nc
    io = io or {}
    def dram(name, shape, dt, kind):
        if name in io:
            return io[name]
        return nc.dram_tensor(name, shape, dt, kind=kind).ap()
    TCE = TC + 1
    CH = 256
    ln0_e = dram("ln0_e", [D, TCE], F32, "ExternalInput")
    r_e = dram("r_e", [D, TCE], F32, "ExternalInput")
    wkv_e = dram("wkv_e", [D, TCE], F32, "ExternalInput")
    wo = dram("wo", [D, D], F32, "ExternalInput")
    x1_o = dram("x1_o", [D, TCE], F32, "ExternalOutput")

    wob = [p.sbuf(f"wo{k}", [128, D], BF16) for k in range(ND)]
    for k in range(ND):
        p.dma_in(wob[k], wo[k * 128:(k + 1) * 128, :])
    rb = p.sbuf("rb", [128, ND * (CH + 1)], F32)
    wb = p.sbuf("wb", [128, ND * (CH + 1)], F32)
    l0b = p.sbuf("l0b", [128, ND * (CH + 1)], F32)
    zb = p.sbuf("zb", [128, ND * (CH + 1)], BF16)
    x1b = p.sbuf("x1b", [128, ND * (CH + 1)], F32)
    pool = PsumPool(p, "pp", 4)

    nch = TCE // CH
    for ci in range(nch):
        c0 = ci * CH
        W = CH + 1 if ci == nch - 1 else CH
        p.dma_in(rb, fm3(r_e, c0, W), dst=rb.t[:, 0:ND * W].rearrange("p (j t) -> p j t", j=ND))
        p.dma_in(wb, fm3(wkv_e, c0, W), dst=wb.t[:, 0:ND * W].rearrange("p (j t) -> p j t", j=ND))
        p.dma_in(l0b, fm3(ln0_e, c0, W), dst=l0b.t[:, 0:ND * W].rearrange("p (j t) -> p j t", j=ND))
        for j in range(ND):
            p.op("vector", lambda h, j=j, W=W: h.tensor_tensor(
                zb[0:128, j * W:(j + 1) * W], rb[0:128, j * W:(j + 1) * W],
                wb[0:128, j * W:(j + 1) * W], op=ALU.mult), reads=(rb, wb), writes=(zb,))

        def evac(mj, ps, W=W):
            p.op("vector", lambda h, mj=mj, ps=ps, W=W: h.tensor_tensor(
                x1b[0:128, mj * W:(mj + 1) * W], ps[0:128, 0:W],
                l0b[0:128, mj * W:(mj + 1) * W], op=ALU.add), reads=(ps, l0b), writes=(x1b,))
        emit_mm(p, pool, ND, lambda k, W=W: zb[0:128, k * W:(k + 1) * W],
                lambda k, mj: wob[k][0:128, mj * 128:(mj + 1) * 128],
                wob, (zb,), evac, range(ND), W)
        p.dma_out(fm3(x1_o, c0, W), x1b, src=x1b.t[:, 0:ND * W].rearrange("p (j t) -> p j t", j=ND))
    return p


def build_stageC2(p=None, io=None):
    own = p is None
    if own:
        nc = bass.Bass("TRN2", target_bir_lowering=False)
        p = Prog(nc)
    else:
        nc = p.nc
    io = io or {}
    def dram(name, shape, dt, kind):
        if name in io:
            return io[name]
        return nc.dram_tensor(name, shape, dt, kind=kind).ap()
    TCE = TC + 1
    CH = 256
    CHP = CH + 1
    x1_e = dram("x1_e", [D, TCE], F32, "ExternalInput")
    x_fm = dram("x_fm", [D, TC], F32, "ExternalInput")
    pval = dram("pval", [128, 1], F32, "ExternalInput")
    vecs = dram("vecs", [128, 16], F32, "ExternalInput")
    wfk = dram("wfk", [D, 4 * D], F32, "ExternalInput")
    wfr = dram("wfr", [D, D], F32, "ExternalInput")
    wfv = dram("wfv", [4 * D, D], F32, "ExternalInput")
    x2_o = dram("x2_o", [D, TC], F32, "ExternalOutput")

    cm = io.get("cm") or Common(p)
    wfkb = [p.sbuf(f"wfk{k}", [128, 4 * D], BF16) for k in range(ND)]
    wfrb = [p.sbuf(f"wfr{k}", [128, D], BF16) for k in range(ND)]
    for k in range(ND):
        p.dma_in(wfkb[k], wfk[k * 128:(k + 1) * 128, :])
        p.dma_in(wfrb[k], wfr[k * 128:(k + 1) * 128, :])
    wfvb = [p.sbuf("wfv_a", [128, D], BF16), p.sbuf("wfv_b", [128, D], BF16)]
    vec = p.sbuf("vec", [128, 16], F32)
    pv = p.sbuf("pv", [128, 1], F32)
    p.dma_in(vec, vecs)
    p.dma_in(pv, pval)
    g_ap = lambda j: vec[0:128, j:j + 1]
    b_ap = lambda j: vec[0:128, 8 + j:8 + j + 1]

    x1w = p.sbuf("x1w", [128, ND * CHP], F32)
    yn_ch = p.sbuf("yn_ch", [128, ND * CHP], F32)
    sqs = p.sbuf("sqs", [128, ND * CHP], F32)
    stat = p.sbuf("stat", [1, 3 * CHP], F32)
    ynb = p.sbuf("ynb", [128, ND * CH], BF16)
    z2b = p.sbuf("z2b", [128, ND * CH], BF16)
    tmp_f = p.sbuf("tmp_f", [128, CH], F32)
    rrf = p.sbuf("rrf", [128, ND * CH], F32)
    kkt = p.sbuf("kkt", [128, CH], F32)
    kkb = p.sbuf("kkb", [128, 32 * CH], BF16)
    xc = p.sbuf("xc", [128, ND * CH], F32)
    bo = p.sbuf("bo", [128, ND * CH], F32)
    x2b = p.sbuf("x2b", [128, ND * CH], F32)

    ps_s0 = p.psum("ps_s0", [1, 512])
    ps_s1 = p.psum("ps_s1", [1, 512])
    ps_bc0 = p.psum("ps_bc0", [128, 512])
    ps_bc1 = p.psum("ps_bc1", [128, 512])
    pool = PsumPool(p, "pp", 4)

    NCH = TC // CH
    for ci in range(NCH):
        c0 = ci * CH
        e0 = 0 if ci == 0 else c0 + 1
        W = CH + 1 if ci == 0 else CH
        xo = 0 if ci == 0 else 1
        if ci > 0:
            for j in range(ND):
                p.op("vector", lambda h, j=j: h.tensor_copy(
                    yn_ch[0:128, j * CHP:j * CHP + 1], yn_ch[0:128, j * CHP + CH:j * CHP + CH + 1]),
                    reads=(), writes=(yn_ch,))
            p.raw("vector", lambda h: h.drain())
        p.dma_in(x1w, fm3(x1_e, e0, W), dst=x1w.t[:, 0:ND * W].rearrange("p (j t) -> p j t", j=ND))
        emit_ln(p, cm, W, lambda j, W=W: x1w[0:128, j * W:(j + 1) * W],
                lambda j, W=W, xo=xo: yn_ch[0:128, j * CHP + xo:j * CHP + xo + W],
                sqs, stat, ps_s0, ps_s1, ps_bc0, ps_bc1,
                gb=(g_ap, b_ap, (vec,)), feat=((x1w,), (yn_ch,)))
        if ci == 0:
            for j in range(ND):
                p.op("vector", lambda h, j=j: h.tensor_scalar(
                    yn_ch[0:128, j * CHP:j * CHP + 1], yn_ch[0:128, j * CHP:j * CHP + 1],
                    pv[0:128, 0:1], None, op0=ALU.mult), reads=(pv,), writes=(yn_ch,))
            p.raw("vector", lambda h: h.drain())
        for j in range(ND):
            p.op("vector", lambda h, j=j: h.tensor_copy(
                ynb[0:128, j * CH:(j + 1) * CH], yn_ch[0:128, j * CHP + 1:j * CHP + 1 + CH]),
                reads=(yn_ch,), writes=(ynb,))
            p.op("vector", lambda h, j=j: h.tensor_scalar(
                tmp_f[0:128, 0:CH], yn_ch[0:128, j * CHP:j * CHP + CH], 0.5, None, op0=ALU.mult),
                reads=(yn_ch,), writes=(tmp_f,))
            p.op("vector", lambda h, j=j: h.scalar_tensor_tensor(
                z2b[0:128, j * CH:(j + 1) * CH], yn_ch[0:128, j * CHP + 1:j * CHP + 1 + CH],
                0.5, tmp_f[0:128, 0:CH], op0=ALU.mult, op1=ALU.add),
                reads=(yn_ch, tmp_f), writes=(z2b,))

        def evac_rr(mj, ps):
            p.op("scalar", lambda h, mj=mj, ps=ps: h.activation(
                rrf[0:128, mj * CH:(mj + 1) * CH], ps[0:128, 0:CH], AF.Sigmoid),
                reads=(ps,), writes=(rrf,))
        emit_mm(p, pool, ND, lambda k: ynb[0:128, k * CH:(k + 1) * CH],
                lambda k, mj: wfrb[k][0:128, mj * 128:(mj + 1) * 128],
                wfrb, (ynb,), evac_rr, range(ND), CH)

        def evac_kk(mj, ps):
            p.op("scalar", lambda h, mj=mj, ps=ps: h.activation(
                kkt[0:128, 0:CH], ps[0:128, 0:CH], AF.Relu), reads=(ps,), writes=(kkt,))
            p.op("scalar", lambda h, mj=mj: h.activation(
                kkb[0:128, mj * CH:(mj + 1) * CH], kkt[0:128, 0:CH], AF.Square),
                reads=(kkt,), writes=(kkb,))
        emit_mm(p, pool, ND, lambda k: z2b[0:128, k * CH:(k + 1) * CH],
                lambda k, mj: wfkb[k][0:128, mj * 128:(mj + 1) * 128],
                wfkb, (z2b,), evac_kk, range(32), CH)

        p.dma_in(xc, fm3(x1_e, c0 + 1, CH), dst=xc.t[:].rearrange("p (j t) -> p j t", j=ND))
        for half in range(2):
            mjs = range(half * 4, half * 4 + 4)
            pss = {}
            for mj in mjs:
                pss[mj] = pool.next()
            for k in range(32):
                wb_ = wfvb[k % 2]
                p.dma_in(wb_, wfv[k * 128:(k + 1) * 128, :])
                for mj in mjs:
                    p.chain("tensor", [
                        lambda h, k=k, mj=mj, wb_=wb_, ps_=pss[mj]: h.matmul(
                            ps_[0:128, 0:CH], wb_[0:128, mj * 128:(mj + 1) * 128],
                            kkb[0:128, k * CH:(k + 1) * CH], start=(k == 0), stop=(k == 31))],
                        reads=(wb_, kkb), writes=(pss[mj],))
            for mj in mjs:
                p.op("vector", lambda h, mj=mj, ps_=pss[mj]: h.tensor_tensor(
                    bo[0:128, mj * CH:(mj + 1) * CH], rrf[0:128, mj * CH:(mj + 1) * CH],
                    ps_[0:128, 0:CH], op=ALU.mult), reads=(rrf, pss[mj]), writes=(bo,))
                p.op("vector", lambda h, mj=mj: h.tensor_tensor(
                    bo[0:128, mj * CH:(mj + 1) * CH], bo[0:128, mj * CH:(mj + 1) * CH],
                    xc[0:128, mj * CH:(mj + 1) * CH], op=ALU.add), reads=(xc,), writes=(bo,))
        emit_ln(p, cm, CH, lambda j: bo[0:128, j * CH:(j + 1) * CH],
                lambda j: x2b[0:128, j * CH:(j + 1) * CH],
                sqs, stat, ps_s0, ps_s1, ps_bc0, ps_bc1, gb=None, feat=((bo,), (x2b,)))
        p.dma_in(xc, fm3(x_fm, c0, CH), dst=xc.t[:].rearrange("p (j t) -> p j t", j=ND))
        for j in range(ND):
            p.op("vector", lambda h, j=j: h.tensor_tensor(
                x2b[0:128, j * CH:(j + 1) * CH], x2b[0:128, j * CH:(j + 1) * CH],
                xc[0:128, j * CH:(j + 1) * CH], op=ALU.add), reads=(xc,), writes=(x2b,))
        p.dma_out(fm3(x2_o, c0, CH), x2b, src=x2b.t[:].rearrange("p (j t) -> p j t", j=ND))
    return p


def build_stageD(p=None, io=None):
    own = p is None
    if own:
        nc = bass.Bass("TRN2", target_bir_lowering=False)
        p = Prog(nc)
    else:
        nc = p.nc
    io = io or {}
    def dram(name, shape, dt, kind):
        if name in io:
            return io[name]
        return nc.dram_tensor(name, shape, dt, kind=kind).ap()
    CH = 256
    DH = 4 * D
    x2_i = dram("x2_i", [D, TC], F32, "ExternalInput")
    w1t = dram("w1t", [D, DH], F32, "ExternalInput")
    w2t = dram("w2t", [D, DH], F32, "ExternalInput")
    h1t = dram("h1t", [DH, DH], F32, "ExternalInput")
    h2t = dram("h2t", [DH, DH], F32, "ExternalInput")
    w3t = dram("w3t", [DH, D], F32, "ExternalInput")
    scl = dram("scl", [128, 32 * 4 + 8], F32, "ExternalInput")
    x3_o = dram("x3_o", [D, TC], F32, "ExternalOutput")

    cm = io.get("cm") or Common(p)
    slab = [p.sbuf(f"slab{i}", [128, 512], BF16) for i in range(4)]
    sc = p.sbuf("sc", [128, 32 * 4 + 8], F32)
    p.dma_in(sc, scl)

    xcb = p.sbuf("xcb", [128, ND * CH], F32)
    lnb = p.sbuf("lnb", [128, ND * CH], BF16)
    lnf = p.sbuf("lnf", [128, ND * CH], F32)
    sqs = p.sbuf("sqs", [128, ND * CH], F32)
    stat = p.sbuf("stat", [1, 3 * CH], F32)
    slf = p.sbuf("slf", [128, 32 * CH], F32)
    t2f = p.sbuf("t2f", [128, CH], F32)
    hb = p.sbuf("hb", [128, 32 * CH], BF16)
    g1b = p.sbuf("g1b", [128, 32 * CH], BF16)
    ob = p.sbuf("ob", [128, ND * CH], F32)
    x3b = p.sbuf("x3b", [128, ND * CH], F32)

    ps_s0 = p.psum("ps_s0", [1, 512])
    ps_s1 = p.psum("ps_s1", [1, 512])
    ps_bc0 = p.psum("ps_bc0", [128, 512])
    ps_bc1 = p.psum("ps_bc1", [128, 512])
    pool = PsumPool(p, "pp", 4)

    def big_mm(src_b, n_k, wsrc, dst_fn, scale_off, act):
        for g in range(8):
            pss = [pool.next() for _ in range(4)]
            for k in range(n_k):
                sb = slab[k % 2 + 2 * 0] if False else slab[k % 2]
                p.dma_in(sb, wsrc[k * 128:(k + 1) * 128, g * 512:(g + 1) * 512])
                for mi in range(4):
                    mj = g * 4 + mi
                    p.chain("tensor", [
                        lambda h, k=k, mi=mi, sb=sb, ps_=pss[mi]: h.matmul(
                            ps_[0:128, 0:CH], sb[0:128, mi * 128:(mi + 1) * 128],
                            src_b[0:128, k * CH:(k + 1) * CH], start=(k == 0), stop=(k == n_k - 1))],
                        reads=(sb, src_b), writes=(pss[mi],))
            for mi in range(4):
                mj = g * 4 + mi
                dst_fn(mj, pss[mi], scale_off, act)

    def evac_gelu(dst_b):
        def f(mj, ps, so, act):
            p.op("scalar", lambda h, mj=mj, ps=ps, so=so: h.activation(
                dst_b[0:128, mj * CH:(mj + 1) * CH], ps[0:128, 0:CH], AF.Gelu,
                scale=sc[0:128, so + mj:so + mj + 1]), reads=(ps, sc), writes=(dst_b,))
        return f

    NCH = TC // CH
    for ci in range(NCH):
        c0 = ci * CH
        p.dma_in(xcb, fm3(x2_i, c0, CH), dst=xcb.t[:].rearrange("p (j t) -> p j t", j=ND))
        emit_ln(p, cm, CH, lambda j: xcb[0:128, j * CH:(j + 1) * CH],
                lambda j: lnf[0:128, j * CH:(j + 1) * CH],
                sqs, stat, ps_s0, ps_s1, ps_bc0, ps_bc1, gb=None, feat=((xcb,), (lnf,)))
        for j in range(ND):
            p.op("vector", lambda h, j=j: h.tensor_copy(
                lnb[0:128, j * CH:(j + 1) * CH], lnf[0:128, j * CH:(j + 1) * CH]),
                reads=(lnf,), writes=(lnb,))
        def evac_t1(mj, ps, so=None, act=None):
            p.op("scalar", lambda h, mj=mj, ps=ps: h.activation(
                slf[0:128, mj * CH:(mj + 1) * CH], ps[0:128, 0:CH], AF.Silu,
                scale=sc[0:128, mj:mj + 1]), reads=(ps, sc), writes=(slf,))
        big_mm(lnb, ND, w1t, evac_t1, 0, AF.Silu)
        def evac_t2(mj, ps, so=None, act=None):
            p.op("vector", lambda h, mj=mj, ps=ps: h.tensor_scalar(
                t2f[0:128, 0:CH], ps[0:128, 0:CH], sc[0:128, 32 + mj:32 + mj + 1], None, op0=ALU.mult),
                reads=(ps, sc), writes=(t2f,))
            p.op("vector", lambda h, mj=mj: h.tensor_tensor(
                hb[0:128, mj * CH:(mj + 1) * CH], slf[0:128, mj * CH:(mj + 1) * CH],
                t2f[0:128, 0:CH], op=ALU.mult), reads=(slf, t2f), writes=(hb,))
        big_mm(lnb, ND, w2t, evac_t2, 32, AF.Silu)
        big_mm(hb, 32, h1t, evac_gelu(g1b), 64, AF.Gelu)
        big_mm(g1b, 32, h2t, evac_gelu(hb), 96, AF.Gelu)
        for g in range(2):
            pss = [pool.next() for _ in range(4)]
            for k in range(32):
                sb = slab[2 + k % 2]
                p.dma_in(sb, w3t[k * 128:(k + 1) * 128, g * 512:(g + 1) * 512])
                for mi in range(4):
                    mj = g * 4 + mi
                    p.chain("tensor", [
                        lambda h, k=k, mi=mi, sb=sb, ps_=pss[mi]: h.matmul(
                            ps_[0:128, 0:CH], sb[0:128, mi * 128:(mi + 1) * 128],
                            hb[0:128, k * CH:(k + 1) * CH], start=(k == 0), stop=(k == 31))],
                        reads=(sb, hb), writes=(pss[mi],))
            for mi in range(4):
                mj = g * 4 + mi
                p.op("vector", lambda h, mj=mj, ps=pss[mi]: h.tensor_scalar(
                    ob[0:128, mj * CH:(mj + 1) * CH], ps[0:128, 0:CH],
                    sc[0:128, 128 + mj:128 + mj + 1], None, op0=ALU.mult),
                    reads=(pss[mi], sc), writes=(ob,))
        emit_ln(p, cm, CH, lambda j: ob[0:128, j * CH:(j + 1) * CH],
                lambda j: x3b[0:128, j * CH:(j + 1) * CH],
                sqs, stat, ps_s0, ps_s1, ps_bc0, ps_bc1, gb=None, feat=((ob,), (x3b,)))
        for j in range(ND):
            p.op("vector", lambda h, j=j: h.tensor_tensor(
                x3b[0:128, j * CH:(j + 1) * CH], x3b[0:128, j * CH:(j + 1) * CH],
                xcb[0:128, j * CH:(j + 1) * CH], op=ALU.add), reads=(xcb,), writes=(x3b,))
        p.dma_out(fm3(x3_o, c0, CH), x3b, src=x3b.t[:].rearrange("p (j t) -> p j t", j=ND))
    return p


def build_stageE(p=None, io=None):
    own = p is None
    if own:
        nc = bass.Bass("TRN2", target_bir_lowering=False)
        p = Prog(nc)
    else:
        nc = p.nc
    io = io or {}
    def dram(name, shape, dt, kind):
        if name in io:
            return io[name]
        return nc.dram_tensor(name, shape, dt, kind=kind).ap()
    CH = 256
    x3_i = dram("x3_i", [D, TC], F32, "ExternalInput")
    wqkv = dram("wqkv", [D, 3 * D], F32, "ExternalInput")
    cost = dram("cost", [128, TC], F32, "ExternalInput")
    sint = dram("sint", [128, TC], F32, "ExternalInput")
    selm = dram("selm", [128, 16 * ND], F32, "ExternalInput")
    selt = dram("selt", [16, 128 * ND], F32, "ExternalInput")
    qb_o = dram("qb_o", [D, TC], BF16, "ExternalOutput")
    kb_o = dram("kb_o", [D, TC], BF16, "ExternalOutput")
    vb_o = dram("vb_o", [D, TC], BF16, "ExternalOutput")

    cm = io.get("cm") or Common(p)
    wb = [p.sbuf(f"wq{k}", [128, 3 * D], BF16) for k in range(ND)]
    for k in range(ND):
        p.dma_in(wb[k], wqkv[k * 128:(k + 1) * 128, :])
    cosb = p.sbuf("cosb", [128, TC], F32)
    sinb = p.sbuf("sinb", [128, TC], F32)
    p.dma_in(cosb, cost)
    p.dma_in(sinb, sint)
    selb = p.sbuf("selb", [128, 16 * ND], F32)
    seltb = p.sbuf("seltb", [16, 128 * ND], F32)
    p.dma_in(selb, selm)
    p.dma_in(seltb, selt)

    xcb = p.sbuf("xcb", [128, ND * CH], F32)
    lnb = p.sbuf("lnb", [128, ND * CH], BF16)
    lnf = p.sbuf("lnf", [128, ND * CH], F32)
    sqs = p.sbuf("sqs", [128, ND * CH], F32)
    stat = p.sbuf("stat", [1, 3 * CH], F32)
    qf = p.sbuf("qf", [128, ND * CH], F32)
    kf = p.sbuf("kf", [128, ND * CH], F32)
    qr = p.sbuf("qr", [128, ND * CH], F32)
    rt = p.sbuf("rt", [128, 2 * CH], F32)
    sqv = p.sbuf("sqv", [128, CH], F32)
    inv = p.sbuf("inv", [16, 512], F32)
    outb = p.sbuf("outb", [128, ND * CH], BF16)

    ps_s0 = p.psum("ps_s0", [1, 512])
    ps_s1 = p.psum("ps_s1", [1, 512])
    ps_bc0 = p.psum("ps_bc0", [128, 512])
    ps_bc1 = p.psum("ps_bc1", [128, 512])
    ps_n = p.psum("ps_n", [16, 512])
    pool = PsumPool(p, "pp", 3)

    NCH = TC // CH
    for ci in range(NCH):
        c0 = ci * CH
        p.dma_in(xcb, fm3(x3_i, c0, CH), dst=xcb.t[:].rearrange("p (j t) -> p j t", j=ND))
        emit_ln(p, cm, CH, lambda j: xcb[0:128, j * CH:(j + 1) * CH],
                lambda j: lnf[0:128, j * CH:(j + 1) * CH],
                sqs, stat, ps_s0, ps_s1, ps_bc0, ps_bc1, gb=None, feat=((xcb,), (lnf,)))
        for j in range(ND):
            p.op("vector", lambda h, j=j: h.tensor_copy(
                lnb[0:128, j * CH:(j + 1) * CH], lnf[0:128, j * CH:(j + 1) * CH]),
                reads=(lnf,), writes=(lnb,))

        def evac_qkv(mj, ps):
            if mj < 8:
                p.op("vector", lambda h, mj=mj, ps=ps: h.tensor_copy(
                    qf[0:128, mj * CH:(mj + 1) * CH], ps[0:128, 0:CH]), reads=(ps,), writes=(qf,))
            elif mj < 16:
                p.op("vector", lambda h, mj=mj, ps=ps: h.tensor_copy(
                    kf[0:128, (mj - 8) * CH:(mj - 7) * CH], ps[0:128, 0:CH]), reads=(ps,), writes=(kf,))
            else:
                p.op("vector", lambda h, mj=mj, ps=ps: h.tensor_copy(
                    outb[0:128, (mj - 16) * CH:(mj - 15) * CH], ps[0:128, 0:CH]), reads=(ps,), writes=(outb,))
        emit_mm(p, pool, ND, lambda k: lnb[0:128, k * CH:(k + 1) * CH],
                lambda k, mj: wb[k][0:128, mj * 128:(mj + 1) * 128],
                wb, (lnb,), evac_qkv, range(24), CH)
        p.dma_out(fm3(vb_o, c0, CH), outb, src=outb.t[:].rearrange("p (j t) -> p j t", j=ND))

        for src_f, dst_o in ((qf, qb_o), (kf, kb_o)):
            for j in range(ND):
                q1 = src_f[0:64, j * CH:(j + 1) * CH]
                q2 = src_f[64:128, j * CH:(j + 1) * CH]
                cs0 = cosb[0:64, c0:c0 + CH]
                sn0 = sinb[0:64, c0:c0 + CH]
                cs1 = cosb[64:128, c0:c0 + CH]
                sn1 = sinb[64:128, c0:c0 + CH]
                p.chain("vector", [
                    lambda h, q1=q1, cs0=cs0: h.tensor_tensor(rt[0:64, 0:CH], q1, cs0, op=ALU.mult),
                    lambda h, q2=q2, sn1=sn1: h.tensor_tensor(rt[0:64, CH:2 * CH], q2, sn1, op=ALU.mult),
                    lambda h, j=j: h.tensor_tensor(qr[0:64, j * CH:(j + 1) * CH],
                                                   rt[0:64, 0:CH], rt[0:64, CH:2 * CH], op=ALU.subtract),
                    lambda h, q1=q1, sn0=sn0: h.tensor_tensor(rt[64:128, 0:CH], q1, sn0, op=ALU.mult),
                    lambda h, q2=q2, cs1=cs1: h.tensor_tensor(rt[64:128, CH:2 * CH], q2, cs1, op=ALU.mult),
                    lambda h, j=j: h.tensor_tensor(qr[64:128, j * CH:(j + 1) * CH],
                                                   rt[64:128, 0:CH], rt[64:128, CH:2 * CH], op=ALU.add),
                ], reads=(src_f, cosb, sinb), writes=(qr, rt))
            for j in range(ND):
                p.op("scalar", lambda h, j=j: h.activation(sqv[0:128, 0:CH],
                                                           qr[0:128, j * CH:(j + 1) * CH], AF.Square),
                     reads=(qr,), writes=(sqv,))
                p.chain("tensor", [lambda h, j=j: h.matmul(
                    ps_n[0:16, 0:CH], selb[0:128, j * 16:(j + 1) * 16], sqv[0:128, 0:CH],
                    start=(j == 0), stop=(j == ND - 1))], reads=(selb, sqv), writes=(ps_n,))
            p.op("vector", lambda h: h.tensor_scalar(inv[0:16, 0:CH], ps_n[0:16, 0:CH], 1e-6, None, op0=ALU.add),
                 reads=(ps_n,), writes=(inv,))
            p.chain("scalar", [
                lambda h: h.activation(inv[0:16, 0:CH], inv[0:16, 0:CH], AF.Ln),
                lambda h: h.activation(inv[0:16, 0:CH], inv[0:16, 0:CH], AF.Exp, scale=-0.5),
            ], reads=(), writes=(inv,))
            for j in range(ND):
                p.op("tensor", lambda h, j=j: h.matmul(
                    ps_bc0[0:128, 0:CH], seltb[0:16, j * 128:(j + 1) * 128], inv[0:16, 0:CH],
                    start=True, stop=True), reads=(seltb, inv), writes=(ps_bc0,))
                p.op("vector", lambda h, j=j: h.tensor_tensor(
                    outb[0:128, j * CH:(j + 1) * CH], qr[0:128, j * CH:(j + 1) * CH],
                    ps_bc0[0:128, 0:CH], op=ALU.mult), reads=(qr, ps_bc0), writes=(outb,))
            p.dma_out(fm3(dst_o, c0, CH), outb, src=outb.t[:].rearrange("p (j t) -> p j t", j=ND))
    return p


def build_stageF(p=None, io=None):
    own = p is None
    if own:
        nc = bass.Bass("TRN2", target_bir_lowering=False)
        p = Prog(nc)
    else:
        nc = p.nc
    io = io or {}
    def dram(name, shape, dt, kind):
        if name in io:
            return io[name]
        return nc.dram_tensor(name, shape, dt, kind=kind).ap()
    TQ = 512
    qh_i = dram("qh_i", [512, S], BF16, "ExternalInput")
    kh_i = dram("kh_i", [512, S], BF16, "ExternalInput")
    vx_i = dram("vx_i", [128, 8 * 32 * 65], BF16, "ExternalInput")
    ao_o = dram("ao_o", [512, S], BF16, "ExternalOutput")

    cm = io.get("cm") or Common(p)
    kb = p.sbuf("kb", [64, S], BF16)
    qb = p.sbuf("qb", [64, S], BF16)
    vxb = p.sbuf("vxb", [128, 32 * 65], BF16)
    pb = [p.sbuf(f"pb{i}", [128, TQ], BF16) for i in range(2)]
    dn = p.sbuf("dn", [1, TQ], F32)
    af = p.sbuf("af", [64, TQ], F32)
    aob = p.sbuf("aob", [64, TQ], BF16)
    ps_qk = PsumPool(p, "qk", 2)
    ps_av = p.psum("ps_av", [65, 512])
    ps_bc = p.psum("ps_bc", [64, 512])

    for hh in range(8):
        p.dma_in(kb, kh_i[hh * 64:(hh + 1) * 64, :])
        p.dma_in(qb, qh_i[hh * 64:(hh + 1) * 64, :])
        p.dma_in(vxb, vx_i[0:128, hh * 2080:(hh + 1) * 2080])
        for tq in range(S // TQ):
            q_ap = qb[0:64, tq * TQ:(tq + 1) * TQ]
            for ts in range(32):
                ps = ps_qk.next()
                p.chain("tensor", [lambda h, ts=ts, q_ap=q_ap, ps=ps: h.matmul(
                    ps[0:128, 0:TQ], kb[0:64, ts * 128:(ts + 1) * 128], q_ap,
                    start=True, stop=True)], reads=(kb, qb), writes=(ps,))
                pbuf = pb[ts % 2]
                p.op("scalar", lambda h, ps=ps, pbuf=pbuf: h.activation(
                    pbuf.ap, ps[0:128, 0:TQ], AF.Exp, scale=0.125), reads=(ps,), writes=(pbuf,))
                p.chain("tensor", [lambda h, ts=ts, pbuf=pbuf: h.matmul(
                    ps_av[0:65, 0:TQ], vxb[0:128, ts * 65:(ts + 1) * 65], pbuf.ap,
                    start=(ts == 0), stop=(ts == 31))], reads=(vxb, pbuf), writes=(ps_av,))
            p.op("vector", lambda h: h.tensor_copy(dn.ap, ps_av[64:65, 0:TQ]), reads=(ps_av,), writes=(dn,))
            p.chain("scalar", [
                lambda h: h.activation(dn.ap, dn.ap, AF.Ln),
                lambda h: h.activation(dn.ap, dn.ap, AF.Exp, scale=-1.0),
            ], reads=(), writes=(dn,))
            p.op("scalar", lambda h: h.copy(af.ap, ps_av[0:64, 0:TQ]), reads=(ps_av,), writes=(af,))
            p.op("tensor", lambda h: h.matmul(ps_bc[0:64, 0:TQ], cm.ones_row_f[0:1, 0:64], dn.ap,
                                              start=True, stop=True),
                 reads=(cm.ones_row_f, dn), writes=(ps_bc,))
            p.op("vector", lambda h: h.tensor_tensor(aob.ap, af.ap, ps_bc[0:64, 0:TQ], op=ALU.mult),
                 reads=(af, ps_bc), writes=(aob,))
            p.dma_out(ao_o[hh * 64:(hh + 1) * 64, tq * TQ:(tq + 1) * TQ], aob)
    return p


def build_stageG(p=None, io=None):
    own = p is None
    if own:
        nc = bass.Bass("TRN2", target_bir_lowering=False)
        p = Prog(nc)
    else:
        nc = p.nc
    io = io or {}
    def dram(name, shape, dt, kind):
        if name in io:
            return io[name]
        return nc.dram_tensor(name, shape, dt, kind=kind).ap()
    CH = 256
    ao_i = dram("ao_i", [D, TC], BF16, "ExternalInput")
    x3_i = dram("x3_i", [D, TC], F32, "ExternalInput")
    wo = dram("wo", [D, D], F32, "ExternalInput")
    bo_v = dram("bo_v", [128, 8], F32, "ExternalInput")
    x4_o = dram("x4_o", [D, TC], F32, "ExternalOutput")
    h0_o = dram("h0_o", [D, TC], F32, "ExternalOutput")

    cm = io.get("cm") or Common(p)
    wob = [p.sbuf(f"wo{k}", [128, D], BF16) for k in range(ND)]
    for k in range(ND):
        p.dma_in(wob[k], wo[k * 128:(k + 1) * 128, :])
    bov = p.sbuf("bov", [128, 8], F32)
    p.dma_in(bov, bo_v)
    aob = p.sbuf("aob", [128, ND * CH], BF16)
    xcb = p.sbuf("xcb", [128, ND * CH], F32)
    of = p.sbuf("of", [128, ND * CH], F32)
    x4b = p.sbuf("x4b", [128, ND * CH], F32)
    h0b = p.sbuf("h0b", [128, ND * CH], F32)
    sqs = p.sbuf("sqs", [128, ND * CH], F32)
    stat = p.sbuf("stat", [1, 3 * CH], F32)
    ps_s0 = p.psum("ps_s0", [1, 512])
    ps_s1 = p.psum("ps_s1", [1, 512])
    ps_bc0 = p.psum("ps_bc0", [128, 512])
    ps_bc1 = p.psum("ps_bc1", [128, 512])
    pool = PsumPool(p, "pp", 4)

    NCH = TC // CH
    for ci in range(NCH):
        c0 = ci * CH
        p.dma_in(aob, fm3(ao_i, c0, CH), dst=aob.t[:].rearrange("p (j t) -> p j t", j=ND))
        p.dma_in(xcb, fm3(x3_i, c0, CH), dst=xcb.t[:].rearrange("p (j t) -> p j t", j=ND))

        def evac_o(mj, ps):
            p.op("vector", lambda h, mj=mj, ps=ps: h.tensor_scalar(
                of[0:128, mj * CH:(mj + 1) * CH], ps[0:128, 0:CH],
                bov[0:128, mj:mj + 1], None, op0=ALU.add), reads=(ps, bov), writes=(of,))
        emit_mm(p, pool, ND, lambda k: aob[0:128, k * CH:(k + 1) * CH],
                lambda k, mj: wob[k][0:128, mj * 128:(mj + 1) * 128],
                wob, (aob,), evac_o, range(ND), CH)
        emit_ln(p, cm, CH, lambda j: of[0:128, j * CH:(j + 1) * CH],
                lambda j: x4b[0:128, j * CH:(j + 1) * CH],
                sqs, stat, ps_s0, ps_s1, ps_bc0, ps_bc1, gb=None, feat=((of,), (x4b,)))
        for j in range(ND):
            p.op("vector", lambda h, j=j: h.tensor_tensor(
                x4b[0:128, j * CH:(j + 1) * CH], x4b[0:128, j * CH:(j + 1) * CH],
                xcb[0:128, j * CH:(j + 1) * CH], op=ALU.add), reads=(xcb,), writes=(x4b,))
        emit_ln(p, cm, CH, lambda j: x4b[0:128, j * CH:(j + 1) * CH],
                lambda j: h0b[0:128, j * CH:(j + 1) * CH],
                sqs, stat, ps_s0, ps_s1, ps_bc0, ps_bc1, gb=None, feat=((x4b,), (h0b,)))
        p.dma_out(fm3(x4_o, c0, CH), x4b, src=x4b.t[:].rearrange("p (j t) -> p j t", j=ND))
        p.dma_out(fm3(h0_o, c0, CH), h0b, src=h0b.t[:].rearrange("p (j t) -> p j t", j=ND))
    return p


def build_stageH(p=None, io=None, debug=False):
    own = p is None
    if own:
        nc = bass.Bass("TRN2", target_bir_lowering=False)
        p = Prog(nc)
    else:
        nc = p.nc
    io = io or {}
    def dram(name, shape, dt, kind):
        if name in io:
            return io[name]
        return nc.dram_tensor(name, shape, dt, kind=kind).ap()
    CH = 256
    DH = 4096
    xh_i = dram("xh_i", [D, TC], F32, "ExternalInput")
    rwt = dram("rwt", [128, 32], F32, "ExternalInput")
    rbt = dram("rbt", [1, 4], F32, "ExternalInput")
    lgb = dram("lgb", [128, 8 * 8], F32, "ExternalInput")
    b1t = dram("b1t", [128, 32 * 4], F32, "ExternalInput")
    b2t = dram("b2t", [128, 8 * 4], F32, "ExternalInput")
    w1_i = dram("w1_i", [4 * D, DH], F32, "ExternalInput")
    w2_i = dram("w2_i", [4 * DH, D], F32, "ExternalInput")
    idt_i = dram("idt_i", [128, 128], F32, "ExternalInput")
    xo_o = dram("xo_o", [D, TC], F32, "ExternalOutput")
    xhat_d = dram("xhat_d", [D, TC], BF16, "Internal")
    xhat_db = Buf(None, "xhat_d")

    cm = io.get("cm") or Common(p)
    rwb = p.sbuf("rwb", [128, 32], F32)
    rbb = p.sbuf("rbb", [128, 4], F32)
    rbs = p.sbuf("rbs", [1, 4], F32)
    lgbv = p.sbuf("lgbv", [128, 8 * 8], F32)
    b1v = p.sbuf("b1v", [128, 32 * 4], F32)
    b2v = p.sbuf("b2v", [128, 8 * 4], F32)
    idt = p.sbuf("idt", [128, 128], F32)
    for buf, src in ((rwb, rwt), (rbs, rbt), (lgbv, lgb), (b1v, b1t), (b2v, b2t), (idt, idt_i)):
        p.dma_in(buf, src)
    crow = [p.sbuf(f"crow{e}", [1, TC], BF16) for e in range(4)]
    ones_row_b = p.sbuf("ones_row_b", [1, 128], BF16)
    p.op("gpsimd", lambda h: h.memset(ones_row_b.ap, 1.0), writes=(ones_row_b,))

    xcb = p.sbuf("xcb", [128, ND * CH], F32)
    xhatb = p.sbuf("xhatb", [128, ND * CH], BF16)
    sqs = p.sbuf("sqs", [128, ND * CH], F32)
    stat = p.sbuf("stat", [1, 3 * CH], F32)
    lg = p.sbuf("lg", [128, 4], F32)
    m1 = p.sbuf("m1", [128, 1], F32)
    el = p.sbuf("el", [128, 4], F32)
    msk = p.sbuf("msk", [128, 4], F32)
    p2t = p.sbuf("p2t", [128, 4], F32)
    comb = p.sbuf("comb", [128, 4], F32)
    den = p.sbuf("den", [128, 1], F32)

    w1b = [p.sbuf(f"w1b{k}", [128, DH], BF16) for k in range(ND)]
    w2b = [p.sbuf(f"w2b{k}", [128, D], BF16) for k in range(32)]
    xeb = p.sbuf("xeb", [128, ND * CH], BF16)
    hb = p.sbuf("hb", [128, 32 * CH], BF16)
    gout = p.sbuf("gout", [128, ND * CH], F32)
    gateb = p.sbuf("gateb", [128, CH], F32)

    ps_s0 = p.psum("ps_s0", [1, 512])
    ps_s1 = p.psum("ps_s1", [1, 512])
    ps_bc0 = p.psum("ps_bc0", [128, 512])
    ps_bc1 = p.psum("ps_bc1", [128, 512])
    pool = PsumPool(p, "pp", 4)

    psb = pool.next()
    p.op("tensor", lambda h, psb=psb: h.matmul(psb[0:128, 0:4], cm.ones_row_f[0:1, 0:128],
                                               rbs[0:1, 0:4], start=True, stop=True),
         reads=(cm.ones_row_f, rbs), writes=(psb,))
    p.op("vector", lambda h, psb=psb: h.tensor_copy(rbb.ap, psb[0:128, 0:4]), reads=(psb,), writes=(rbb,))

    NCH = TC // CH
    for ci in range(NCH):
        c0 = ci * CH
        p.dma_in(xcb, fm3(xh_i, c0, CH), dst=xcb.t[:].rearrange("p (j t) -> p j t", j=ND))
        emit_ln(p, cm, CH, lambda j: xcb[0:128, j * CH:(j + 1) * CH],
                lambda j: xhatb[0:128, j * CH:(j + 1) * CH],
                sqs, stat, ps_s0, ps_s1, ps_bc0, ps_bc1, gb=None, feat=((xcb,), (xhatb,)))
        p.dma_out(fm3(xhat_d, c0, CH), xhatb, src=xhatb.t[:].rearrange("p (j t) -> p j t", j=ND))
        xhat_db.ready = xhatb.users.get("dma:sync") or xhat_db.ready
        for tt in range(CH // 128):
            ps_l = pool.next()
            fns = [(lambda h, ki=ki, tt=tt, ps_l=ps_l: h.matmul(
                ps_l[0:128, 0:4], xcb[0:128, ki * CH + tt * 128:ki * CH + (tt + 1) * 128],
                rwb[0:128, ki * 4:(ki + 1) * 4], start=(ki == 0), stop=(ki == ND - 1)))
                for ki in range(ND)]
            p.chain("tensor", fns, reads=(xcb, rwb), writes=(ps_l,))
            p.op("vector", lambda h, ps_l=ps_l: h.tensor_tensor(lg.ap, ps_l[0:128, 0:4], rbb.ap, op=ALU.add),
                 reads=(ps_l, rbb), writes=(lg,))
            p.raw("vector", lambda h: h.drain())
            p.op("vector", lambda h: h.tensor_reduce(m1.ap, lg.ap, axis=AX.X, op=ALU.max),
                 reads=(lg,), writes=(m1,))
            p.raw("vector", lambda h: h.drain())
            p.op("vector", lambda h: h.tensor_scalar(msk.ap, lg.ap, m1.ap, None, op0=ALU.subtract),
                 reads=(lg, m1), writes=(msk,))
            p.raw("vector", lambda h: h.drain())
            if debug and ci == 0 and tt == 0:
                dbg_r = dram("dbg_r", [128, 8], F32, "ExternalOutput")
                dbgr = p.sbuf("dbgr", [128, 8], F32)
                p.op("vector", lambda h: h.tensor_copy(dbgr[0:128, 0:1], m1.ap), reads=(m1,), writes=(dbgr,))
                p.op("vector", lambda h: h.tensor_copy(dbgr[0:128, 1:5], msk.ap), reads=(msk,), writes=(dbgr,))
                p.dma_out(dbg_r, dbgr)
            p.op("scalar", lambda h: h.activation(el.ap, msk.ap, AF.Exp), reads=(msk,), writes=(el,))
            p.chain("scalar", [
                lambda h: h.activation(msk.ap, msk.ap, AF.Relu, scale=-1.0),
                lambda h: h.activation(msk.ap, msk.ap, AF.Sign),
            ], reads=(), writes=(msk,))
            p.op("vector", lambda h: h.tensor_tensor(p2t.ap, el.ap, msk.ap, op=ALU.mult),
                 reads=(el, msk), writes=(p2t,))
            p.raw("vector", lambda h: h.drain())
            p.op("vector", lambda h: h.tensor_reduce(m1.ap, p2t.ap, axis=AX.X, op=ALU.max),
                 reads=(p2t,), writes=(m1,))
            p.raw("vector", lambda h: h.drain())
            p.op("vector", lambda h: h.tensor_scalar(msk.ap, el.ap, m1.ap, None, op0=ALU.subtract),
                 reads=(el, m1), writes=(msk,))
            p.raw("vector", lambda h: h.drain())
            p.chain("scalar", [
                lambda h: h.activation(msk.ap, msk.ap, AF.Relu, scale=-1.0),
                lambda h: h.activation(msk.ap, msk.ap, AF.Sign),
            ], reads=(), writes=(msk,))
            p.op("vector", lambda h: h.tensor_tensor(p2t.ap, el.ap, msk.ap, op=ALU.mult),
                 reads=(el, msk), writes=(p2t,))
            p.raw("vector", lambda h: h.drain())
            p.op("vector", lambda h: h.tensor_tensor(comb.ap, el.ap, p2t.ap, op=ALU.subtract),
                 reads=(el, p2t), writes=(comb,))
            p.raw("vector", lambda h: h.drain())
            p.op("vector", lambda h: h.tensor_reduce(den.ap, comb.ap, axis=AX.X, op=ALU.add),
                 reads=(comb,), writes=(den,))
            p.raw("vector", lambda h: h.drain())
            p.chain("scalar", [
                lambda h: h.activation(den.ap, den.ap, AF.Ln),
                lambda h: h.activation(den.ap, den.ap, AF.Exp, scale=-1.0),
            ], reads=(), writes=(den,))
            p.op("vector", lambda h: h.tensor_scalar(comb.ap, comb.ap, den.ap, None, op0=ALU.mult),
                 reads=(den,), writes=(comb,))
            p.raw("vector", lambda h: h.drain())
            if debug and ci == 0 and tt == 0:
                lg_o = dram("lg_o", [128, 16], F32, "ExternalOutput")
                dbg0 = p.sbuf("dbg0", [128, 16], F32)
                p.op("vector", lambda h: h.tensor_copy(dbg0[0:128, 0:4], lg.ap), reads=(lg,), writes=(dbg0,))
                p.op("vector", lambda h: h.tensor_copy(dbg0[0:128, 4:8], el.ap), reads=(el,), writes=(dbg0,))
                p.op("vector", lambda h: h.tensor_copy(dbg0[0:128, 8:12], comb.ap), reads=(comb,), writes=(dbg0,))
                p.op("vector", lambda h: h.tensor_copy(dbg0[0:128, 12:13], den.ap), reads=(den,), writes=(dbg0,))
                p.dma_out(lg_o, dbg0)
            for e in range(4):
                ps_t = pool.next()
                p.op("tensor", lambda h, e=e, ps_t=ps_t: h.matmul(
                    ps_t[0:1, 0:128], comb[0:128, e:e + 1], idt.ap, start=True, stop=True),
                    reads=(comb, idt), writes=(ps_t,))
                p.op("vector", lambda h, e=e, ps_t=ps_t, c0=c0, tt=tt: h.tensor_copy(
                    crow[e][0:1, c0 + tt * 128:c0 + (tt + 1) * 128], ps_t[0:1, 0:128]),
                    reads=(ps_t,), writes=(crow[e],))

    for e in range(4):
        for k in range(ND):
            p.dma_in(w1b[k], w1_i[e * D + k * 128:e * D + (k + 1) * 128, :])
        for k in range(32):
            p.dma_in(w2b[k], w2_i[e * DH + k * 128:e * DH + (k + 1) * 128, :])
        for ci in range(NCH):
            c0 = ci * CH
            p.dma_in(xhatb, fm3(xhat_d, c0, CH), dst=xhatb.t[:].rearrange("p (j t) -> p j t", j=ND))
            p.op("gpsimd", lambda h: h.engine_nop(), reads=(xhat_db,)) if False else None
            p.dma_in(xcb, fm3(xh_i, c0, CH), dst=xcb.t[:].rearrange("p (j t) -> p j t", j=ND))
            for j in range(ND):
                p.op("vector", lambda h, j=j, e=e: h.tensor_scalar(
                    xeb[0:128, j * CH:(j + 1) * CH], xhatb[0:128, j * CH:(j + 1) * CH],
                    lgbv[0:128, (2 * e) * 8 + j:(2 * e) * 8 + j + 1],
                    lgbv[0:128, (2 * e + 1) * 8 + j:(2 * e + 1) * 8 + j + 1],
                    op0=ALU.mult, op1=ALU.add), reads=(xhatb, lgbv), writes=(xeb,))

            def evac_h(mj, ps, e=e):
                p.op("scalar", lambda h, mj=mj, ps=ps, e=e: h.activation(
                    hb[0:128, mj * CH:(mj + 1) * CH], ps[0:128, 0:CH], AF.Gelu,
                    bias=b1v[0:128, e * 32 + mj:e * 32 + mj + 1]), reads=(ps, b1v), writes=(hb,))
            emit_mm(p, pool, ND, lambda k: xeb[0:128, k * CH:(k + 1) * CH],
                    lambda k, mj: w1b[k][0:128, mj * 128:(mj + 1) * 128],
                    w1b, (xeb,), evac_h, range(32), CH)
            ps_g = pool.next()
            p.op("tensor", lambda h, e=e, c0=c0, ps_g=ps_g: h.matmul(
                ps_g[0:128, 0:CH], ones_row_b[0:1, 0:128], crow[e][0:1, c0:c0 + CH],
                start=True, stop=True), reads=(ones_row_b, crow[e]), writes=(ps_g,))
            p.op("vector", lambda h, ps_g=ps_g: h.tensor_copy(gateb[0:128, 0:CH], ps_g[0:128, 0:CH]),
                 reads=(ps_g,), writes=(gateb,))

            def evac_y(mj, ps, e=e):
                p.op("vector", lambda h, mj=mj, ps=ps, e=e: h.tensor_scalar(
                    gout[0:128, mj * CH:(mj + 1) * CH], ps[0:128, 0:CH],
                    b2v[0:128, e * 8 + mj:e * 8 + mj + 1], None, op0=ALU.add),
                    reads=(ps, b2v), writes=(gout,))
                p.op("vector", lambda h, mj=mj: h.tensor_tensor(
                    gout[0:128, mj * CH:(mj + 1) * CH], gout[0:128, mj * CH:(mj + 1) * CH],
                    xcb[0:128, mj * CH:(mj + 1) * CH], op=ALU.add), reads=(xcb,), writes=(gout,))
                p.op("vector", lambda h, mj=mj: h.tensor_tensor(
                    gout[0:128, mj * CH:(mj + 1) * CH], gout[0:128, mj * CH:(mj + 1) * CH],
                    gateb[0:128, 0:CH], op=ALU.mult), reads=(gateb,), writes=(gout,))
            emit_mm(p, pool, 32, lambda k: hb[0:128, k * CH:(k + 1) * CH],
                    lambda k, mj: w2b[k][0:128, mj * 128:(mj + 1) * 128],
                    w2b, (hb,), evac_y, range(ND), CH)
            p.dma_out(fm3(xo_o, c0, CH), gout, src=gout.t[:].rearrange("p (j t) -> p j t", j=ND),
                      accum=(e > 0), eng="gpsimd" if e > 0 else "sync")
            if debug and e == 0 and ci == 0:
                xhd_o = dram("xhd_o", [D, CH], F32, "ExternalOutput")
                xeb_o = dram("xeb_o", [D, CH], F32, "ExternalOutput")
                hb_o = dram("hb_o", [DH, CH], F32, "ExternalOutput")
                gate_o = dram("gate_o", [128, CH], F32, "ExternalOutput")
                dbg1 = p.sbuf("dbg1", [128, ND * CH], F32)

                for j in range(ND):
                    p.op("vector", lambda h, j=j: h.tensor_copy(
                        dbg1[0:128, j * CH:(j + 1) * CH], xhatb[0:128, j * CH:(j + 1) * CH]),
                        reads=(xhatb,), writes=(dbg1,))
                p.dma_out(fm3(xhd_o, 0, CH), dbg1, src=dbg1.t[:].rearrange("p (j t) -> p j t", j=ND))
                for j in range(ND):
                    p.op("vector", lambda h, j=j: h.tensor_copy(
                        dbg1[0:128, j * CH:(j + 1) * CH], xeb[0:128, j * CH:(j + 1) * CH]),
                        reads=(xeb,), writes=(dbg1,))
                p.dma_out(fm3(xeb_o, 0, CH), dbg1, src=dbg1.t[:].rearrange("p (j t) -> p j t", j=ND))
                for j in range(4):
                    p.op("vector", lambda h, j=j: h.tensor_copy(
                        dbg1[0:128, j * CH:(j + 1) * CH], hb[0:128, j * CH:(j + 1) * CH]),
                        reads=(hb,), writes=(dbg1,))
                p.dma_out(hb_o.rearrange("(j p) t -> p j t", p=128)[:, 0:4, :], dbg1,
                          src=dbg1.t[:, 0:4 * CH].rearrange("p (j t) -> p j t", j=4))
                p.dma_out(gate_o, gateb)
    return p


def build_stageI(p=None, io=None):
    own = p is None
    if own:
        nc = bass.Bass("TRN2", target_bir_lowering=False)
        p = Prog(nc)
    else:
        nc = p.nc
    io = io or {}
    def dram(name, shape, dt, kind):
        if name in io:
            return io[name]
        return nc.dram_tensor(name, shape, dt, kind=kind).ap()
    CH = 512
    x4_i = dram("x4_i", [D, TC], F32, "ExternalInput")
    h4_i = dram("h4_i", [D, TC], F32, "ExternalInput")
    x5_o = dram("x5_o", [D, TC], F32, "ExternalOutput")
    cm = io.get("cm") or Common(p)
    x4b = p.sbuf("x4b", [128, ND * CH], F32)
    h4b = p.sbuf("h4b", [128, ND * CH], F32)
    ob = p.sbuf("ob", [128, ND * CH], F32)
    sqs = p.sbuf("sqs", [128, ND * CH], F32)
    stat = p.sbuf("stat", [1, 3 * CH], F32)
    ps_s0 = p.psum("ps_s0", [1, 512])
    ps_s1 = p.psum("ps_s1", [1, 512])
    ps_bc0 = p.psum("ps_bc0", [128, 512])
    ps_bc1 = p.psum("ps_bc1", [128, 512])
    for ci in range(TC // CH):
        c0 = ci * CH
        p.dma_in(x4b, fm3(x4_i, c0, CH), dst=x4b.t[:].rearrange("p (j t) -> p j t", j=ND))
        p.dma_in(h4b, fm3(h4_i, c0, CH), dst=h4b.t[:].rearrange("p (j t) -> p j t", j=ND))
        emit_ln(p, cm, CH, lambda j: h4b[0:128, j * CH:(j + 1) * CH],
                lambda j: ob[0:128, j * CH:(j + 1) * CH],
                sqs, stat, ps_s0, ps_s1, ps_bc0, ps_bc1, gb=None, feat=((h4b,), (ob,)))
        for j in range(ND):
            p.op("vector", lambda h, j=j: h.tensor_tensor(
                ob[0:128, j * CH:(j + 1) * CH], ob[0:128, j * CH:(j + 1) * CH],
                x4b[0:128, j * CH:(j + 1) * CH], op=ALU.add), reads=(x4b,), writes=(ob,))
        p.dma_out(fm3(x5_o, c0, CH), ob, src=ob.t[:].rearrange("p (j t) -> p j t", j=ND))
    return p


def _perm1024():
    perm = np.zeros(1024, np.int64)
    for g in range(8):
        for r in range(128):
            if r < 32:
                h, d = 2 * g, r
            elif r < 64:
                h, d = 2 * g + 1, r - 32
            elif r < 96:
                h, d = 2 * g, 32 + r - 64
            else:
                h, d = 2 * g + 1, 32 + r - 96
            perm[g * 128 + r] = h * 64 + d
    return perm


def _head_rows(h):
    g, sub = h // 2, h % 2
    return np.concatenate([np.arange(g * 128 + sub * 32, g * 128 + sub * 32 + 32),
                           np.arange(g * 128 + 64 + sub * 32, g * 128 + 64 + sub * 32 + 32)])


def _rope_tables(s0):
    inv = 1.0 / (10000.0 ** (np.arange(32, dtype=np.float64) / 32.0))
    pos = np.arange(s0, s0 + TC, dtype=np.float64)
    ang = (pos[None, :] * inv[:, None])
    cost = np.cos(ang).astype(np.float32)
    sint = np.sin(ang).astype(np.float32)
    cost = np.concatenate([cost, cost, cost, cost], axis=0)
    sint = np.concatenate([sint, sint, sint, sint], axis=0)
    return np.ascontiguousarray(cost), np.ascontiguousarray(sint)


def _v8(v):
    return np.ascontiguousarray(np.asarray(v, np.float32).reshape(8, 128).T)


def _v32(v):
    return np.ascontiguousarray(np.asarray(v, np.float32).reshape(32, 128).T)


def _run(build_fn, maps):
    p = build_fn()
    nc = _finish(p)
    return run_bass_kernel_spmd(nc, maps, core_ids=list(range(NCORES))).results


def kernel_unfused(**inp):
    f32 = lambda a: np.ascontiguousarray(np.asarray(a, np.float32))
    x = f32(inp["L0_attn_x"])

    vecs = np.concatenate([_v8(inp[f"L0_attn_{n}"]) for n in
                           ("ln1_g", "ln1_b", "mix_k", "mix_v", "mix_r", "time_first")], axis=1)
    wk, wv, wr = f32(inp["L0_attn_Wk"]), f32(inp["L0_attn_Wv"]), f32(inp["L0_attn_Wr"])
    mapsA = []
    for c in range(NCORES):
        b, hh = c // 2, c % 2
        xs = x[b, hh * TC:(hh + 1) * TC, :].T
        bound = np.zeros((D, 1), np.float32) if hh == 0 else x[b, hh * TC - 1, :].reshape(D, 1)
        pval = np.full((128, 1), float(hh), np.float32)
        fmask = np.full((128, 1), 1.0 - hh, np.float32)
        mapsA.append(dict(xe=np.ascontiguousarray(np.concatenate([bound, xs], axis=1)),
                          pval=pval, fmask=fmask, vecs=np.ascontiguousarray(vecs),
                          wk=wk, wv=wv, wr=wr))
    resA = _run(build_stageA, mapsA)

    td = f32(inp["L0_attn_time_decay"])
    mapsB = []
    for c in range(NCORES):
        b, hh = c // 2, c % 2
        u_full = np.concatenate([resA[2 * b]["u_o"], resA[2 * b + 1]["u_o"]], axis=1)
        ek_full = np.concatenate([resA[2 * b]["ek_o"], resA[2 * b + 1]["ek_o"]], axis=1)
        lo = hh * 512
        mapsB.append(dict(u_s=np.ascontiguousarray(u_full[lo:lo + 512]),
                          ek_s=np.ascontiguousarray(ek_full[lo:lo + 512]),
                          td_s=np.ascontiguousarray(td[lo:lo + 512].reshape(4, 128).T)))
    resB = _run(build_stageB, mapsB)

    wo0 = f32(inp["L0_attn_Wo"])
    wkv_fm = {}
    for c in range(NCORES):
        b, hh = c // 2, c % 2
        full = np.concatenate([resB[2 * b]["wkv_o"], resB[2 * b + 1]["wkv_o"]], axis=0)
        wkv_fm[c] = full[:, hh * TC:(hh + 1) * TC]
    mapsC1 = []
    for c in range(NCORES):
        b, hh = c // 2, c % 2
        z = np.zeros((D, 1), np.float32)
        if hh == 0:
            l0e = np.concatenate([z, resA[c]["ln0_o"]], axis=1)
            re_ = np.concatenate([z, resA[c]["r_o"]], axis=1)
            we_ = np.concatenate([z, wkv_fm[c]], axis=1)
        else:
            l0e = np.concatenate([resA[c - 1]["ln0_o"][:, -1:], resA[c]["ln0_o"]], axis=1)
            re_ = np.concatenate([resA[c - 1]["r_o"][:, -1:], resA[c]["r_o"]], axis=1)
            we_ = np.concatenate([wkv_fm[c - 1][:, -1:], wkv_fm[c]], axis=1)
        mapsC1.append(dict(ln0_e=np.ascontiguousarray(l0e), r_e=np.ascontiguousarray(re_),
                           wkv_e=np.ascontiguousarray(we_), wo=wo0))
    resC1 = _run(build_stageC1, mapsC1)

    vecs2 = np.concatenate([_v8(inp["L0_attn_ln2_g"]), _v8(inp["L0_attn_ln2_b"])], axis=1)
    wfk, wfr, wfv = f32(inp["L0_attn_Wfk"]), f32(inp["L0_attn_Wfr"]), f32(inp["L0_attn_Wfv"])
    mapsC2 = []
    for c in range(NCORES):
        b, hh = c // 2, c % 2
        mapsC2.append(dict(x1_e=np.ascontiguousarray(resC1[c]["x1_o"]),
                           x_fm=np.ascontiguousarray(x[b, hh * TC:(hh + 1) * TC, :].T),
                           pval=np.full((128, 1), float(hh), np.float32),
                           vecs=np.ascontiguousarray(vecs2), wfk=wfk, wfr=wfr, wfv=wfv))
    resC2 = _run(build_stageC2, mapsC2)

    scl = np.concatenate([_v32(inp["L0_ffn_w1_s"]), _v32(inp["L0_ffn_w2_s"]),
                          _v32(inp["L0_ffn_h1_s"]), _v32(inp["L0_ffn_h2_s"]),
                          _v8(inp["L0_ffn_w3_s"])], axis=1)
    w1t = np.ascontiguousarray(f32(inp["L0_ffn_w1_w"]).T)
    w2t = np.ascontiguousarray(f32(inp["L0_ffn_w2_w"]).T)
    h1t = np.ascontiguousarray(f32(inp["L0_ffn_h1_w"]).T)
    h2t = np.ascontiguousarray(f32(inp["L0_ffn_h2_w"]).T)
    w3t = np.ascontiguousarray(f32(inp["L0_ffn_w3_w"]).T)
    mapsD = [dict(x2_i=np.ascontiguousarray(resC2[c]["x2_o"]), w1t=w1t, w2t=w2t,
                  h1t=h1t, h2t=h2t, w3t=w3t, scl=np.ascontiguousarray(scl))
             for c in range(NCORES)]
    resD = _run(build_stageD, mapsD)

    wqkv = f32(inp["L1_attn_Wqkv"])
    perm = _perm1024()
    wperm = np.ascontiguousarray(np.concatenate(
        [wqkv[:, 0:1024][:, perm], wqkv[:, 1024:2048][:, perm], wqkv[:, 2048:3072]], axis=1))
    sel = np.zeros((128, 16 * 8), np.float32)
    selt = np.zeros((16, 128 * 8), np.float32)
    for j in range(8):
        for r in range(128):
            h = 2 * j + (1 if (32 <= r < 64 or r >= 96) else 0)
            sel[r, j * 16 + h] = 1.0
            selt[h, j * 128 + r] = 1.0
    mapsE = []
    for c in range(NCORES):
        hh = c % 2
        cost, sint = _rope_tables(hh * TC)
        mapsE.append(dict(x3_i=np.ascontiguousarray(resD[c]["x3_o"]), wqkv=wperm,
                          cost=cost, sint=sint, selm=sel, selt=selt))
    resE = _run(build_stageE, mapsE)

    mapsF = []
    for c in range(NCORES):
        b, hg = c // 2, c % 2
        qfull = np.concatenate([np.asarray(resE[2 * b]["qb_o"]), np.asarray(resE[2 * b + 1]["qb_o"])], axis=1)
        kfull = np.concatenate([np.asarray(resE[2 * b]["kb_o"]), np.asarray(resE[2 * b + 1]["kb_o"])], axis=1)
        vfull = np.concatenate([np.asarray(resE[2 * b]["vb_o"]), np.asarray(resE[2 * b + 1]["vb_o"])], axis=1)
        qh = np.zeros((512, S), ml_dtypes.bfloat16)
        kh = np.zeros((512, S), ml_dtypes.bfloat16)
        vx = np.zeros((128, 8 * 32 * 65), ml_dtypes.bfloat16)
        ones_col = np.ones((128, 1), ml_dtypes.bfloat16)
        for i in range(8):
            h = hg * 8 + i
            rows = _head_rows(h)
            qh[i * 64:(i + 1) * 64] = qfull[rows]
            kh[i * 64:(i + 1) * 64] = kfull[rows]
            vh = vfull[h * 64:(h + 1) * 64]
            for ts in range(32):
                base = (i * 32 + ts) * 65
                vx[:, base:base + 64] = vh[:, ts * 128:(ts + 1) * 128].T
                vx[:, base + 64:base + 65] = ones_col
        mapsF.append(dict(qh_i=qh, kh_i=kh, vx_i=vx))
    resF = _run(build_stageF, mapsF)

    wo1 = f32(inp["L1_attn_Wo"])
    bo1 = _v8(inp["L1_attn_bo"])
    mapsG = []
    for c in range(NCORES):
        b, hh = c // 2, c % 2
        ao_fm = np.concatenate([np.asarray(resF[2 * b]["ao_o"]), np.asarray(resF[2 * b + 1]["ao_o"])], axis=0)
        mapsG.append(dict(ao_i=np.ascontiguousarray(ao_fm[:, hh * TC:(hh + 1) * TC]),
                          x3_i=np.ascontiguousarray(resD[c]["x3_o"]), wo=wo1, bo_v=bo1))
    resG = _run(build_stageG, mapsG)

    idt = np.eye(128, dtype=np.float32)
    xh = {c: resG[c]["h0_o"] for c in range(NCORES)}
    for l in range(4):
        rw = f32(inp["L1_ffn_router_w"])[l]
        rb = f32(inp["L1_ffn_router_b"])[l]
        g_ = f32(inp["L1_ffn_ln_g"])[l]
        b_ = f32(inp["L1_ffn_ln_b"])[l]
        w1 = f32(inp["L1_ffn_w1"])[l]
        b1 = f32(inp["L1_ffn_b1"])[l]
        w2 = f32(inp["L1_ffn_w2"])[l]
        b2 = f32(inp["L1_ffn_b2"])[l]
        rwt = np.ascontiguousarray(rw.reshape(8, 128, 4).transpose(1, 0, 2).reshape(128, 32))
        lgb = np.zeros((128, 64), np.float32)
        for e in range(4):
            lgb[:, (2 * e) * 8:(2 * e) * 8 + 8] = _v8(g_[e])
            lgb[:, (2 * e + 1) * 8:(2 * e + 1) * 8 + 8] = _v8(b_[e])
        b1t = np.ascontiguousarray(np.concatenate([_v32(b1[e]) for e in range(4)], axis=1))
        b2t = np.ascontiguousarray(np.concatenate([_v8(b2[e]) for e in range(4)], axis=1))
        w1_i = np.ascontiguousarray(w1.reshape(4 * D, 4096))
        w2_i = np.ascontiguousarray(w2.reshape(4 * 4096, D))
        mapsH = [dict(xh_i=np.ascontiguousarray(xh[c]), rwt=rwt,
                      rbt=np.ascontiguousarray(rb.reshape(1, 4)), lgb=lgb, b1t=b1t, b2t=b2t,
                      w1_i=w1_i, w2_i=w2_i, idt_i=idt) for c in range(NCORES)]
        resH = _run(build_stageH, mapsH)
        xh = {c: resH[c]["xo_o"] for c in range(NCORES)}

    mapsI = [dict(x4_i=np.ascontiguousarray(resG[c]["x4_o"]), h4_i=np.ascontiguousarray(xh[c]))
             for c in range(NCORES)]
    resI = _run(build_stageI, mapsI)

    out = np.empty((B, S, D), np.float32)
    for c in range(NCORES):
        b, hh = c // 2, c % 2
        out[b, hh * TC:(hh + 1) * TC, :] = resI[c]["x5_o"].T
    return out


RG_PAIR = [[0, 1], [2, 3], [4, 5], [6, 7]]
RG_ALL = [[0, 1, 2, 3, 4, 5, 6, 7]]


def emit_scan_fused(p, cm, io):
    nc = p.nc
    u_d, ek_d, tdv, pval = io["u_d"], io["ek_d"], io["tdv"], io["pval"]
    pay_d, payg, payoff = io["pay_d"], io["payg"], io["payoff"]
    ln0_e, r_e, wkv_e = io["ln0_e"], io["r_e"], io["wkv_e"]
    a_d, b_d = io["a_d"], io["b_d"]
    cc_sem = p.sem("cc_scan")

    td = p.sbuf("sc_td", [128, 8], F32)
    pv = p.sbuf("sc_pv", [128, 1], F32)
    ew8 = p.sbuf("sc_ew8", [128, 8], F32)
    ntd = p.sbuf("sc_ntd", [128, 8], F32)
    pay = p.sbuf("sc_pay", [128, 32], F32)
    ones_t = p.sbuf("sc_ones", [128, TC], F32)
    ew_t = p.sbuf("sc_ewt", [128, TC], F32)
    ut = p.sbuf("sc_u", [128, TC], F32)
    ekt = p.sbuf("sc_ek", [128, TC], F32)
    at = p.sbuf("sc_a", [128, TC], F32)
    bt = p.sbuf("sc_b", [128, TC], F32)

    p.dma_in(td, tdv)
    p.dma_in(pv, pval)
    p.op("gpsimd", lambda h: h.memset(ones_t.ap, 1.0), writes=(ones_t,))
    p.chain("scalar", [
        lambda h: h.activation(ntd.ap, td.ap, AF.Exp),
    ], reads=(td,), writes=(ntd,))
    p.chain("scalar", [
        lambda h: h.activation(ew8.ap, ntd.ap, AF.Exp, scale=-1.0),
    ], reads=(ntd,), writes=(ew8,))
    p.op("vector", lambda h: h.tensor_scalar(ntd.ap, ntd.ap, -1.0, None, op0=ALU.mult),
         reads=(), writes=(ntd,))
    p.raw("vector", lambda h: h.drain())

    for j in range(ND):
        p.dma_in(ut, u_d[j * 128:(j + 1) * 128, :])
        p.dma_in(ekt, ek_d[j * 128:(j + 1) * 128, :])
        p.op("vector", lambda h, j=j: h.tensor_scalar(ew_t.ap, ones_t.ap, ew8[0:128, j:j + 1], None, op0=ALU.mult),
             reads=(ones_t, ew8), writes=(ew_t,))
        p.op("vector", lambda h: h.tensor_tensor_scan(at.ap, ew_t.ap, ut.ap, 0.0, op0=ALU.mult, op1=ALU.add),
             reads=(ew_t, ut), writes=(at,))
        p.op("vector", lambda h: h.tensor_tensor_scan(bt.ap, ew_t.ap, ekt.ap, 0.0, op0=ALU.mult, op1=ALU.add),
             reads=(ew_t, ekt), writes=(bt,))
        p.op("vector", lambda h, j=j: h.tensor_copy(pay[0:128, j:j + 1], at[0:128, TC - 1:TC]),
             reads=(at,), writes=(pay,))
        p.op("vector", lambda h, j=j: h.tensor_copy(pay[0:128, 8 + j:8 + j + 1], bt[0:128, TC - 1:TC]),
             reads=(bt,), writes=(pay,))
        p.dma_out(a_d[j * 128:(j + 1) * 128, :], at)
        p.dma_out(b_d[j * 128:(j + 1) * 128, :], bt)
    lr = p.sbuf("sc_lr", [128, 16], F32)
    p.op("gpsimd", lambda h: h.dma_start(
        lr[0:128, 0:8], ln0_e.rearrange("(j p) t -> p j t", p=128)[:, :, TC:TC + 1].rearrange("p j o -> p (j o)"),
        allow_slow_non_contiguous=True), writes=(lr,), dma=True, dma_key=lr)
    p.op("gpsimd", lambda h: h.dma_start(
        lr[0:128, 8:16], r_e.rearrange("(j p) t -> p j t", p=128)[:, :, TC:TC + 1].rearrange("p j o -> p (j o)"),
        allow_slow_non_contiguous=True), writes=(lr,), dma=True, dma_key=lr)
    p.op("vector", lambda h: h.tensor_copy(pay[0:128, 16:32], lr.ap), reads=(lr,), writes=(pay,))
    p.raw("vector", lambda h: h.drain())
    ev_pay = p.dma_out(pay_d, pay)
    p._emit_wait("gpsimd", ev_pay)
    p.op("gpsimd", lambda h: h.collective_compute(
        "AllGather", ALU.bypass, replica_groups=RG_ALL, ins=[pay_d.opt()], outs=[payg.opt()]),
        sem=cc_sem, k=1)
    fl_in, fl_out = io["fence1_in"], io["fence1_out"]
    p.op("gpsimd", lambda h: h.collective_compute(
        "AllReduce", ALU.add, replica_groups=RG_ALL, ins=[fl_in.opt()], outs=[fl_out.opt()]),
        sem=cc_sem, k=1)
    ev_cc = Ev(cc_sem, cc_sem.n, "cc")
    pin = p.sbuf("sc_pin", [128, 32], F32)
    pofft = p.sbuf("sc_poff", [1, 1], mybir.dt.int32)
    p.dma_in(pofft, payoff)
    p._emit_wait("gpsimd", ev_cc)

    def _pin_dma(h):
        with h.register("sc_r0") as r0:
            h.reg_load(r0, pofft[0:1, 0:1])
            return h.dma_start(pin.ap, bass.AP(payg.tensor, r0, [[32, 128], [1, 32]]))
    p.op("gpsimd", _pin_dma, reads=(pofft,), writes=(pin,), dma=True, dma_key=pin)
    pvinv = p.sbuf("sc_pvinv", [128, 1], F32)
    p.op("vector", lambda h: h.tensor_scalar(pvinv.ap, pv.ap, -1.0, 1.0, op0=ALU.mult, op1=ALU.add),
         reads=(pv,), writes=(pvinv,))
    p.raw("vector", lambda h: h.drain())
    bsafe = p.sbuf("sc_bsafe", [128, 8], F32)
    p.op("vector", lambda h: h.tensor_scalar(bsafe.ap, pin[0:128, 8:16], pv[0:128, 0:1], pvinv[0:128, 0:1],
                                             op0=ALU.mult, op1=ALU.add),
         reads=(pin, pv, pvinv), writes=(bsafe,))
    p.op("vector", lambda h: h.tensor_scalar(pin.ap, pin.ap, pv[0:128, 0:1], None, op0=ALU.mult),
         reads=(pv,), writes=(pin,))
    p.raw("vector", lambda h: h.drain())
    p.op("sync", lambda h: h.dma_start(
        ln0_e.rearrange("(j p) t -> p j t", p=128)[:, :, 0:1].rearrange("p j o -> p (j o)"),
        pin[0:128, 16:24], allow_slow_non_contiguous=True), reads=(pin,), dma=True, dma_key=pin)
    p.op("sync", lambda h: h.dma_start(
        r_e.rearrange("(j p) t -> p j t", p=128)[:, :, 0:1].rearrange("p j o -> p (j o)"),
        pin[0:128, 8 + 16:16 + 16], allow_slow_non_contiguous=True), reads=(pin,), dma=True, dma_key=pin)
    wb0 = p.sbuf("sc_wb0", [128, 8], F32)
    p.op("vector", lambda h: h.tensor_scalar(wb0.ap, bsafe.ap, WKV_EPS, None, op0=ALU.add),
         reads=(bsafe,), writes=(wb0,))
    p.raw("vector", lambda h: h.drain())
    p.chain("scalar", [
        lambda h: h.activation(wb0.ap, wb0.ap, AF.Ln),
        lambda h: h.activation(wb0.ap, wb0.ap, AF.Exp, scale=-1.0),
    ], reads=(), writes=(wb0,))
    p.op("vector", lambda h: h.tensor_tensor(wb0.ap, wb0.ap, pin[0:128, 0:8], op=ALU.mult),
         reads=(pin,), writes=(wb0,))
    p.raw("vector", lambda h: h.drain())
    p.op("sync", lambda h: h.dma_start(
        wkv_e.rearrange("(j p) t -> p j t", p=128)[:, :, 0:1].rearrange("p j o -> p (j o)"),
        wb0.ap, allow_slow_non_contiguous=True), reads=(wb0,), dma=True, dma_key=wb0)
    if io.get("debug_taps"):
        pay_tap = io["pay_tap"]
        payg_tap = io["payg_tap"]
        p.dma_out(pay_tap, pay)
        pgt = p.sbuf("sc_pgt", [128, 8 * 32], F32)
        for r in range(8):
            p.dma_in(pgt, payg[r * 128:(r + 1) * 128, :], dst=pgt.t[0:128, r * 32:(r + 1) * 32])
        p.dma_out(payg_tap, pgt)
        pin_tap = io["pin_tap"]
        p.dma_out(pin_tap, pin)
    iot = p.sbuf("sc_iot", [128, TC], F32)
    p.op("gpsimd", lambda h: h.iota(iot.ap, pattern=[[1, TC]], base=1, channel_multiplier=0,
                                    allow_small_or_imprecise_dtypes=True), writes=(iot,))
    etab = p.sbuf("sc_etab", [128, TC], F32)
    ca = p.sbuf("sc_ca", [128, 16], F32)
    p.op("vector", lambda h: h.tensor_copy(ca.ap, pin[0:128, 0:16]), reads=(pin,), writes=(ca,))
    p.raw("vector", lambda h: h.drain())
    wt = p.sbuf("sc_wkv", [128, TC], F32)
    for j in range(ND):
        p.dma_in(at, a_d[j * 128:(j + 1) * 128, :])
        p.dma_in(bt, b_d[j * 128:(j + 1) * 128, :])
        p.op("vector", lambda h, j=j: h.tensor_scalar(etab.ap, iot.ap, ntd[0:128, j:j + 1], None, op0=ALU.mult),
             reads=(iot, ntd), writes=(etab,))
        p.op("scalar", lambda h: h.activation(etab.ap, etab.ap, AF.Exp), reads=(), writes=(etab,))
        p.op("vector", lambda h, j=j: h.scalar_tensor_tensor(
            at.ap, etab.ap, ca[0:128, j:j + 1], at.ap, op0=ALU.mult, op1=ALU.add),
            reads=(etab, ca), writes=(at,))
        p.op("vector", lambda h, j=j: h.scalar_tensor_tensor(
            bt.ap, etab.ap, ca[0:128, 8 + j:8 + j + 1], bt.ap, op0=ALU.mult, op1=ALU.add),
            reads=(etab, ca), writes=(bt,))
        p.op("vector", lambda h: h.tensor_scalar(bt.ap, bt.ap, WKV_EPS, None, op0=ALU.add),
             reads=(), writes=(bt,))
        p.chain("scalar", [
            lambda h: h.activation(bt.ap, bt.ap, AF.Ln),
            lambda h: h.activation(bt.ap, bt.ap, AF.Exp, scale=-1.0),
        ], reads=(), writes=(bt,))
        p.op("vector", lambda h: h.tensor_tensor(wt.ap, at.ap, bt.ap, op=ALU.mult),
             reads=(at, bt), writes=(wt,))
        p.dma_out(wkv_e[j * 128:(j + 1) * 128, 1:1 + TC], wt)


def emit_attnF2(p, cm, io):
    nc = p.nc
    qb_d, kvm, ao_d, idt64 = io["qb_d"], io["kv_my"], io["ao_d"], io["idt64"]
    TQ = 512
    idb = p.sbuf("f2_id", [64, 64], BF16)
    p.dma_in(idb, idt64)
    kb = p.sbuf("f2_k", [64, S], BF16)
    qb = p.sbuf("f2_q", [64, TC], BF16)
    vf = p.sbuf("f2_v", [64, S], BF16)
    vext = p.sbuf("f2_vx", [128, 32 * 65], BF16)
    pb = [p.sbuf(f"f2_p{i}", [128, TQ], BF16) for i in range(2)]
    dn = p.sbuf("f2_dn", [1, TQ], F32)
    af = p.sbuf("f2_af", [64, TQ], F32)
    aob = p.sbuf("f2_ao", [64, TQ], BF16)
    ps_qk = PsumPool(p, "f2qk", 2)
    ps_tr = PsumPool(p, "f2tr", 2)
    ps_av = p.psum("f2_av", [65, 512])
    ps_bc = p.psum("f2_bc", [64, 512])

    for hh in range(16):
        g, sub = hh // 2, hh % 2
        r0 = g * 128 + sub * 32
        r1 = g * 128 + 64 + sub * 32
        p.dma_in(kb, kvm[r0:r0 + 32, :], dst=kb.t[0:32, :])
        p.op("gpsimd", lambda h, r1=r1: h.dma_start(kb.t[32:64, :], kvm[r1:r1 + 32, :]),
             writes=(kb,), dma=True, dma_key=kb)
        p.dma_in(vf, kvm[1024 + hh * 64:1024 + (hh + 1) * 64, :])
        p.dma_in(qb, qb_d[r0:r0 + 32, :], dst=qb.t[0:32, :])
        p.op("gpsimd", lambda h, r1=r1: h.dma_start(qb.t[32:64, :], qb_d[r1:r1 + 32, :]),
             writes=(qb,), dma=True, dma_key=qb)
        p.op("gpsimd", lambda h: h.memset(vext.ap, 1.0), writes=(vext,))
        for ts in range(32):
            ps = ps_tr.next()
            p.chain("tensor", [lambda h, ts=ts, ps=ps: h.matmul(
                ps[0:128, 0:64], vf[0:64, ts * 128:(ts + 1) * 128], idb.ap, start=True, stop=True)],
                reads=(vf, idb), writes=(ps,))
            p.op("vector", lambda h, ts=ts, ps=ps: h.tensor_copy(
                vext[0:128, ts * 65:ts * 65 + 64], ps[0:128, 0:64]), reads=(ps,), writes=(vext,))
        for tq in range(TC // TQ):
            q_ap = qb[0:64, tq * TQ:(tq + 1) * TQ]
            for ts in range(32):
                ps = ps_qk.next()
                p.chain("tensor", [lambda h, ts=ts, q_ap=q_ap, ps=ps: h.matmul(
                    ps[0:128, 0:TQ], kb[0:64, ts * 128:(ts + 1) * 128], q_ap,
                    start=True, stop=True)], reads=(kb, qb), writes=(ps,))
                pbuf = pb[ts % 2]
                p.op("scalar", lambda h, ps=ps, pbuf=pbuf: h.activation(
                    pbuf.ap, ps[0:128, 0:TQ], AF.Exp, scale=0.125), reads=(ps,), writes=(pbuf,))
                p.chain("tensor", [lambda h, ts=ts, pbuf=pbuf: h.matmul(
                    ps_av[0:65, 0:TQ], vext[0:128, ts * 65:(ts + 1) * 65], pbuf.ap,
                    start=(ts == 0), stop=(ts == 31))], reads=(vext, pbuf), writes=(ps_av,))
            p.op("vector", lambda h: h.tensor_copy(dn.ap, ps_av[64:65, 0:TQ]), reads=(ps_av,), writes=(dn,))
            p.chain("scalar", [
                lambda h: h.activation(dn.ap, dn.ap, AF.Ln),
                lambda h: h.activation(dn.ap, dn.ap, AF.Exp, scale=-1.0),
            ], reads=(), writes=(dn,))
            p.op("scalar", lambda h: h.copy(af.ap, ps_av[0:64, 0:TQ]), reads=(ps_av,), writes=(af,))
            p.op("tensor", lambda h: h.matmul(ps_bc[0:64, 0:TQ], cm.ones_row_f[0:1, 0:64], dn.ap,
                                              start=True, stop=True),
                 reads=(cm.ones_row_f, dn), writes=(ps_bc,))
            p.op("vector", lambda h: h.tensor_tensor(aob.ap, af.ap, ps_bc[0:64, 0:TQ], op=ALU.mult),
                 reads=(af, ps_bc), writes=(aob,))
            p.dma_out(ao_d[hh * 64:(hh + 1) * 64, tq * TQ:(tq + 1) * TQ], aob)


def build_fused(debug=False):
    nc = bass.Bass("TRN2", target_bir_lowering=False)
    p = Prog(nc)
    cm = Common(p)
    TCE = TC + 1
    DH = 4096
    dI = lambda n, s, dt=F32: nc.dram_tensor(n, s, dt, kind="ExternalInput").ap()
    _no_tap = {"pay_d", "payg", "kv_d", "kvg", "fence1_in", "fence2_in"}
    dN = lambda n, s, dt=F32: nc.dram_tensor(
        n, s, dt, kind="ExternalOutput" if (debug and n not in _no_tap) else "Internal").ap()
    xe = dI("xe", [D, TCE])
    pval = dI("pval", [128, 1])
    fmask = dI("fmask", [128, 1])
    vecs = dI("vecs", [128, 48])
    wk, wv, wr = dI("wk", [D, D]), dI("wv", [D, D]), dI("wr", [D, D])
    tdv = dI("tdv", [128, 8])
    wo0 = dI("wo0", [D, D])
    vecs2 = dI("vecs2", [128, 16])
    wfk, wfr, wfv = dI("wfk", [D, 4 * D]), dI("wfr", [D, D]), dI("wfv", [4 * D, D])
    w1t, w2t = dI("w1t", [D, DH]), dI("w2t", [D, DH])
    h1t, h2t = dI("h1t", [DH, DH]), dI("h2t", [DH, DH])
    w3t = dI("w3t", [DH, D])
    scl = dI("scl", [128, 136])
    wqkv = dI("wqkv", [D, 3 * D])
    cost, sint = dI("cost", [128, TC]), dI("sint", [128, TC])
    selm, selt = dI("selm", [128, 128]), dI("selt", [16, 1024])
    idt64 = dI("idt64", [64, 64], BF16)
    wo1 = dI("wo1", [D, D])
    bo_v = dI("bo_v", [128, 8])
    idt_i = dI("idt_i", [128, 128])
    payoff = dI("payoff", [1, 1], mybir.dt.int32)
    kvoff = dI("kvoff", [1, 2], mybir.dt.int32)
    hw = []
    for l in range(4):
        hw.append(dict(rwt=dI(f"rwt{l}", [128, 32]), rbt=dI(f"rbt{l}", [1, 4]),
                       lgb=dI(f"lgb{l}", [128, 64]), b1t=dI(f"b1t{l}", [128, 128]),
                       b2t=dI(f"b2t{l}", [128, 32]), w1_i=dI(f"w1i{l}", [4 * D, DH]),
                       w2_i=dI(f"w2i{l}", [4 * DH, D])))
    x5_o = nc.dram_tensor("x5_o", [D, TC], F32, kind="ExternalOutput").ap()
    ln0_e, r_e, wkv_e = dN("ln0_e", [D, TCE]), dN("r_e", [D, TCE]), dN("wkv_e", [D, TCE])
    u_d, ek_d = dN("u_d", [D, TC]), dN("ek_d", [D, TC])
    a_d, b_d = dN("a_d", [D, TC]), dN("b_d", [D, TC])
    pay_d = dN("pay_d", [128, 32])
    payg = nc.dram_tensor("payg", [1024, 32], F32, kind="Internal", addr_space="Shared").ap()
    x1_d = dN("x1_d", [D, TCE])
    x2_d, x3_d = dN("x2_d", [D, TC]), dN("x3_d", [D, TC])
    q_d = dN("q_d", [D, TC], BF16)
    kv_d = dN("kv_d", [2 * D, TC], BF16)
    kvg = nc.dram_tensor("kvg", [16 * D, TC], BF16, kind="Internal", addr_space="Shared").ap()
    kv_my = dN("kv_my", [2 * D, S], BF16)
    fence1_in = dN("fence1_in", [128, 1])
    fence1_out = nc.dram_tensor("fence1_out", [128, 1], F32, kind="Internal", addr_space="Shared").ap()
    fence2_in = dN("fence2_in", [128, 1])
    fence2_out = nc.dram_tensor("fence2_out", [128, 1], F32, kind="Internal", addr_space="Shared").ap()
    ao_d = dN("ao_d", [D, TC], BF16)
    x4_d = dN("x4_d", [D, TC])
    xhat_d = dN("xhat_d", [D, TC], BF16)
    hh_d = [dN(f"hh{l}", [D, TC]) for l in range(5)]
    cc2_sem = p.sem("cc_kv")

    with p.stage("A"):
        build_stageA(p, io=dict(cm=cm, xe=xe, pval=pval, fmask=fmask, vecs=vecs,
                                wk=wk, wv=wv, wr=wr,
                                ln0_o=ln0_e[:, 1:1 + TC], u_o=u_d, ek_o=ek_d,
                                r_o=r_e[:, 1:1 + TC]))
    with p.stage("scan"):
        sc_io = dict(u_d=u_d, ek_d=ek_d, tdv=tdv, pval=pval,
                     pay_d=pay_d, payg=payg, payoff=payoff,
                     fence1_in=fence1_in, fence1_out=fence1_out,
                     ln0_e=ln0_e, r_e=r_e,
                     wkv_e=wkv_e, a_d=a_d, b_d=b_d)
        if debug:
            sc_io.update(debug_taps=True,
                         pay_tap=nc.dram_tensor("pay_tap", [128, 32], F32, kind="ExternalOutput").ap(),
                         payg_tap=nc.dram_tensor("payg_tap", [128, 256], F32, kind="ExternalOutput").ap(),
                         pin_tap=nc.dram_tensor("pin_tap", [128, 32], F32, kind="ExternalOutput").ap())
        emit_scan_fused(p, cm, sc_io)
    with p.stage("C1"):
        build_stageC1(p, io=dict(ln0_e=ln0_e, r_e=r_e, wkv_e=wkv_e, wo=wo0, x1_o=x1_d))
    with p.stage("C2"):
        build_stageC2(p, io=dict(cm=cm, x1_e=x1_d, x_fm=xe[:, 1:1 + TC], pval=pval,
                                 vecs=vecs2, wfk=wfk, wfr=wfr, wfv=wfv, x2_o=x2_d))
    with p.stage("D"):
        build_stageD(p, io=dict(cm=cm, x2_i=x2_d, w1t=w1t, w2t=w2t, h1t=h1t, h2t=h2t,
                                w3t=w3t, scl=scl, x3_o=x3_d))
    with p.stage("E"):
        build_stageE(p, io=dict(cm=cm, x3_i=x3_d, wqkv=wqkv, cost=cost, sint=sint,
                                selm=selm, selt=selt, qb_o=q_d,
                                kb_o=kv_d[0:D, :], vb_o=kv_d[D:2 * D, :]))
    with p.stage("cc2"):
        p.op("gpsimd", lambda h: h.collective_compute(
            "AllGather", ALU.bypass, replica_groups=RG_ALL, ins=[kv_d.opt()], outs=[kvg.opt()]),
            sem=cc2_sem, k=1)
        p.op("gpsimd", lambda h: h.collective_compute(
            "AllReduce", ALU.add, replica_groups=RG_ALL, ins=[fence2_in.opt()], outs=[fence2_out.opt()]),
            sem=cc2_sem, k=1)
        p._emit_wait("gpsimd", Ev(cc2_sem, cc2_sem.n, "cc"))
        kofft = p.sbuf("cc2_koff", [1, 2], mybir.dt.int32)
        p.dma_in(kofft, kvoff)
        kvm_buf = Buf(None, "kv_my")
        for half in range(2):
            def cpy(h, half=half):
                with h.register(f"cc2_r{half}") as r0:
                    h.reg_load(r0, kofft[0:1, half:half + 1])
                    return h.dma_start(
                        bass.AP(kv_my.tensor, half * TC, [[2 * TC, 2 * D], [1, TC]]),
                        bass.AP(kvg.tensor, r0, [[TC, 2 * D], [1, TC]]))
            p.op("gpsimd", cpy, reads=(kofft,), writes=(kvm_buf,), dma=True, dma_key=kvm_buf)
    with p.stage("F"):
        emit_attnF2(p, cm, dict(qb_d=q_d, kv_my=kv_my, ao_d=ao_d, idt64=idt64))
    with p.stage("G"):
        build_stageG(p, io=dict(cm=cm, ao_i=ao_d, x3_i=x3_d, wo=wo1, bo_v=bo_v,
                                x4_o=x4_d, h0_o=hh_d[0]))
    for l in range(4):
        with p.stage(f"H{l}"):
            build_stageH(p, io=dict(cm=cm, xh_i=hh_d[l], xo_o=hh_d[l + 1], xhat_d=xhat_d,
                                    idt_i=idt_i, **hw[l]))
    with p.stage("I"):
        build_stageI(p, io=dict(cm=cm, x4_i=x4_d, h4_i=hh_d[4], x5_o=x5_o))
    return p


def _fused_maps(inp):
    f32 = lambda a: np.ascontiguousarray(np.asarray(a, np.float32))
    x = f32(inp["L0_attn_x"])
    vecs = np.concatenate([_v8(inp[f"L0_attn_{n}"]) for n in
                           ("ln1_g", "ln1_b", "mix_k", "mix_v", "mix_r", "time_first")], axis=1)
    vecs2 = np.concatenate([_v8(inp["L0_attn_ln2_g"]), _v8(inp["L0_attn_ln2_b"])], axis=1)
    scl = np.concatenate([_v32(inp["L0_ffn_w1_s"]), _v32(inp["L0_ffn_w2_s"]),
                          _v32(inp["L0_ffn_h1_s"]), _v32(inp["L0_ffn_h2_s"]),
                          _v8(inp["L0_ffn_w3_s"])], axis=1)
    wqkv = f32(inp["L1_attn_Wqkv"])
    perm = _perm1024()
    wperm = np.ascontiguousarray(np.concatenate(
        [wqkv[:, 0:1024][:, perm], wqkv[:, 1024:2048][:, perm], wqkv[:, 2048:3072]], axis=1))
    sel = np.zeros((128, 16 * 8), np.float32)
    selt = np.zeros((16, 128 * 8), np.float32)
    for j in range(8):
        for r in range(128):
            h = 2 * j + (1 if (32 <= r < 64 or r >= 96) else 0)
            sel[r, j * 16 + h] = 1.0
            selt[h, j * 128 + r] = 1.0
    common = dict(
        vecs=np.ascontiguousarray(vecs), vecs2=np.ascontiguousarray(vecs2),
        wk=f32(inp["L0_attn_Wk"]), wv=f32(inp["L0_attn_Wv"]), wr=f32(inp["L0_attn_Wr"]),
        tdv=_v8(inp["L0_attn_time_decay"]), wo0=f32(inp["L0_attn_Wo"]),
        wfk=f32(inp["L0_attn_Wfk"]), wfr=f32(inp["L0_attn_Wfr"]), wfv=f32(inp["L0_attn_Wfv"]),
        w1t=np.ascontiguousarray(f32(inp["L0_ffn_w1_w"]).T),
        w2t=np.ascontiguousarray(f32(inp["L0_ffn_w2_w"]).T),
        h1t=np.ascontiguousarray(f32(inp["L0_ffn_h1_w"]).T),
        h2t=np.ascontiguousarray(f32(inp["L0_ffn_h2_w"]).T),
        w3t=np.ascontiguousarray(f32(inp["L0_ffn_w3_w"]).T),
        scl=np.ascontiguousarray(scl), wqkv=wperm, selm=sel, selt=selt,
        idt64=np.eye(64, dtype=ml_dtypes.bfloat16),
        wo1=f32(inp["L1_attn_Wo"]), bo_v=_v8(inp["L1_attn_bo"]),
        idt_i=np.eye(128, dtype=np.float32),
    )
    for l in range(4):
        rw = f32(inp["L1_ffn_router_w"])[l]
        g_ = f32(inp["L1_ffn_ln_g"])[l]
        b_ = f32(inp["L1_ffn_ln_b"])[l]
        lgb = np.zeros((128, 64), np.float32)
        for e in range(4):
            lgb[:, (2 * e) * 8:(2 * e) * 8 + 8] = _v8(g_[e])
            lgb[:, (2 * e + 1) * 8:(2 * e + 1) * 8 + 8] = _v8(b_[e])
        common.update({
            f"rwt{l}": np.ascontiguousarray(rw.reshape(8, 128, 4).transpose(1, 0, 2).reshape(128, 32)),
            f"rbt{l}": np.ascontiguousarray(f32(inp["L1_ffn_router_b"])[l].reshape(1, 4)),
            f"lgb{l}": lgb,
            f"b1t{l}": np.ascontiguousarray(np.concatenate(
                [_v32(f32(inp["L1_ffn_b1"])[l][e]) for e in range(4)], axis=1)),
            f"b2t{l}": np.ascontiguousarray(np.concatenate(
                [_v8(f32(inp["L1_ffn_b2"])[l][e]) for e in range(4)], axis=1)),
            f"w1i{l}": np.ascontiguousarray(f32(inp["L1_ffn_w1"])[l].reshape(4 * D, 4096)),
            f"w2i{l}": np.ascontiguousarray(f32(inp["L1_ffn_w2"])[l].reshape(4 * 4096, D)),
        })
    maps = []
    for c in range(NCORES):
        b, hh = c // 2, c % 2
        xs = x[b, hh * TC:(hh + 1) * TC, :].T
        bound = np.zeros((D, 1), np.float32) if hh == 0 else x[b, hh * TC - 1, :].reshape(D, 1)
        cost, sint = _rope_tables(hh * TC)
        m = dict(common)
        m.update(xe=np.ascontiguousarray(np.concatenate([bound, xs], axis=1)),
                 pval=np.full((128, 1), float(hh), np.float32),
                 fmask=np.full((128, 1), 1.0 - hh, np.float32),
                 cost=cost, sint=sint,
                 payoff=np.array([[(c ^ 1) * 128 * 32]], np.int32),
                 kvoff=np.array([[(2 * b) * 2 * D * TC, (2 * b + 1) * 2 * D * TC]], np.int32))
        maps.append(m)
    return maps


def kernel(**inp):
    maps = _fused_maps(inp)
    p = build_fused()
    nc = _finish(p)
    res = run_bass_kernel_spmd(nc, maps, core_ids=list(range(NCORES))).results
    out = np.empty((B, S, D), np.float32)
    for c in range(NCORES):
        b, hh = c // 2, c % 2
        out[b, hh * TC:(hh + 1) * TC, :] = res[c]["x5_o"].T
    return out


# version 04
# speedup vs baseline: 1.3083x; 1.2826x over previous
import contextlib
import math
import numpy as np
import ml_dtypes
import concourse.bass as bass
import concourse.mybir as mybir
from concourse.bass_utils import run_bass_kernel_spmd

F32 = mybir.dt.float32
BF16 = mybir.dt.bfloat16
AF = mybir.ActivationFunctionType
ALU = mybir.AluOpType
AX = mybir.AxisListType

NCORES = 8
D = 1024
B = 4
S = 4096
TC = 2048
ND = D // 128
LN_EPS = 1e-5
WKV_EPS = 1e-8

ENGINES = ("tensor", "vector", "scalar", "gpsimd", "sync")
FENCE_ALL = False
_DMA_ENG = ("gpsimd", "sync", "scalar")


class Ev:
    __slots__ = ("sem", "n", "eng")

    def __init__(self, sem, n, eng):
        self.sem, self.n, self.eng = sem, n, eng


class Sem:
    def __init__(self, handle, name):
        self.h = handle
        self.name = name
        self.n = 0


class Buf:
    __slots__ = ("t", "ready", "users", "name", "dma_sem")

    def __init__(self, t, name=""):
        self.t = t
        self.name = name
        self.ready = None
        self.users = {}
        self.dma_sem = None

    def __getitem__(self, idx):
        return self.t[idx]

    @property
    def ap(self):
        return self.t[:]


class Prog:
    def __init__(self, nc):
        self.nc = nc
        self.q = {e: [] for e in ENGINES}
        self.ctx = contextlib.ExitStack()
        self._waited = {e: {} for e in ENGINES}
        self._fence_cache = {}
        self.sems = {}
        for e in ENGINES:
            self.sems[e] = self.sem("sm_" + e)
        self._fence_scratch = self.ctx.enter_context(self.nc.sbuf_tensor("fence_scr", [1, 1], F32))
        self.n_ops = 0
        _gs = self.sems["gpsimd"]
        _gs.n += 1
        scr = self._fence_scratch
        self._emit("gpsimd", lambda h, s=_gs.h, scr=scr: h.memset(scr[:], 0.0).then_inc(s, 1))
        self._fence_init_ev = Ev(_gs, _gs.n, "gpsimd")
        self._fence_init_done = set()
        self._dma_pool = [self.sem(f"dma{i}") for i in range(64)]
        self._dma_i = 0

    def _dma_sem_for(self, buf):
        if buf.dma_sem is None:
            buf.dma_sem = self._dma_pool[self._dma_i % len(self._dma_pool)]
            self._dma_i += 1
        return buf.dma_sem

    def sem(self, name):
        h = self.ctx.enter_context(self.nc.semaphore(name))
        s = Sem(h, name)
        if not hasattr(self, "_misc_sems"):
            self._misc_sems = []
        if name.startswith("cc_"):
            self._misc_sems.append(s)
        return s

    def sbuf(self, name, shape, dtype):
        st = getattr(self, "_cur_stack", None) or self.ctx
        name = getattr(self, "_prefix", "") + name
        return Buf(st.enter_context(self.nc.sbuf_tensor(name, shape, dtype)), name)

    def psum(self, name, shape, dtype=F32):
        st = getattr(self, "_cur_stack", None) or self.ctx
        name = getattr(self, "_prefix", "") + name
        return Buf(st.enter_context(self.nc.psum_tensor(name, shape, dtype)), name)

    def barrier(self):
        evs = []
        for e in ENGINES:
            s = self.sems[e]
            if s.n:
                evs.append(Ev(s, s.n, e))
        for s in self._dma_pool:
            if s.n:
                evs.append(Ev(s, s.n, "dma"))
        for s in getattr(self, "_misc_sems", []):
            if s.n:
                evs.append(Ev(s, s.n, "cc"))
        for e in ENGINES:
            for ev in evs:
                self._emit_wait(e, ev)

    @contextlib.contextmanager
    def stage(self, name):
        st = contextlib.ExitStack()
        old = getattr(self, "_cur_stack", None)
        oldp = getattr(self, "_prefix", "")
        self._cur_stack = st
        self._prefix = name + "_"
        try:
            yield
        finally:
            self._cur_stack = old
            self._prefix = oldp
            self.barrier()
            st.close()

    def raw(self, eng, fn):
        self._emit(eng, fn)

    def _emit(self, eng, fn):
        self.q[eng].append(fn)
        self.n_ops += 1

    def _emit_wait(self, eng, ev):
        if ev is None or ev.eng == eng:
            return
        w = self._waited[eng]
        if w.get(ev.sem, 0) >= ev.n:
            return
        w[ev.sem] = ev.n
        self._emit(eng, lambda h, s=ev.sem.h, n=ev.n: h.wait_ge(s, n))

    def _fenced(self, ev):
        if ev.eng not in ("vector", "scalar"):
            return ev
        key = (ev.sem, ev.n, ev.eng)
        f = self._fence_cache.get(key)
        if f is not None:
            return f
        scr = self._fence_scratch
        if ev.eng not in self._fence_init_done:
            self._fence_init_done.add(ev.eng)
            self._emit_wait(ev.eng, self._fence_init_ev)
        sem = self.sems[ev.eng]
        sem.n += 1
        n = sem.n
        if ev.eng == "vector":
            self._emit("vector", lambda h, s=sem.h, scr=scr: h.tensor_copy(scr[:], scr[:]).then_inc(s, 1))
        else:
            self._emit("scalar", lambda h, s=sem.h, scr=scr: h.copy(scr[:], scr[:]).then_inc(s, 1))
        f = Ev(sem, n, ev.eng)
        self._fence_cache[key] = f
        return f

    def op(self, eng, fn, reads=(), writes=(), sem=None, k=1, dma=False, dma_key=None):
        ev_eng = eng
        if dma:
            assert dma_key is not None
            sem = self._dma_sem_for(dma_key)
            k = 16
            ev_eng = "dma"
            self._emit_wait(eng, Ev(sem, sem.n, "dma"))
        elif sem is None:
            sem = self.sems[eng]
        for b in reads:
            ev = b.ready
            if ev is not None and (dma or FENCE_ALL) and ev.eng != eng:
                ev = self._fenced(ev)
            self._emit_wait(eng, ev)
        for b in writes:
            self._emit_wait(eng, b.ready)
            for uev in b.users.values():
                self._emit_wait(eng, uev)
        sem.n += k
        n = sem.n
        self._emit(eng, lambda h, fn=fn, s=sem.h, k=k: fn(h).then_inc(s, k))
        ev = Ev(sem, n, ev_eng)
        for b in writes:
            b.ready = ev
            b.users = {}
        for b in reads:
            b.users[eng if not dma else "dma:" + eng] = ev
        return ev

    def chain(self, eng, fns, reads=(), writes=(), sem=None):
        if sem is None:
            sem = self.sems[eng]
        for b in reads:
            ev = b.ready
            if ev is not None and FENCE_ALL and ev.eng != eng:
                ev = self._fenced(ev)
            self._emit_wait(eng, ev)
        for b in writes:
            self._emit_wait(eng, b.ready)
            for uev in b.users.values():
                self._emit_wait(eng, uev)
        for fn in fns[:-1]:
            self._emit(eng, fn)
        sem.n += 1
        n = sem.n
        self._emit(eng, lambda h, fn=fns[-1], s=sem.h: fn(h).then_inc(s, 1))
        ev = Ev(sem, n, eng)
        for b in writes:
            b.ready = ev
            b.users = {}
        for b in reads:
            b.users[eng] = ev
        return ev

    def dma_in(self, buf, dram_ap, eng="gpsimd", dst=None):
        d = buf.ap if dst is None else dst
        return self.op(eng, lambda h, d=d, s=dram_ap: h.dma_start(d, s), writes=(buf,), dma=True,
                       dma_key=buf)

    def dma_out(self, dram_ap, buf, eng="sync", src=None, accum=False):
        s = buf.ap if src is None else src
        if accum:
            return self.op(eng, lambda h, d=dram_ap, s=s: h.dma_start(d, s, accum_op=ALU.add),
                           reads=(buf,), dma=True, dma_key=buf)
        return self.op(eng, lambda h, d=dram_ap, s=s: h.dma_start(d, s), reads=(buf,), dma=True,
                       dma_key=buf)

    def barrier_ev(self, eng):
        sem = self.sems[eng]
        return Ev(sem, sem.n, eng)

    def finalize(self):
        with self.nc.Block() as block:
            for eng in ENGINES:
                ops = self.q[eng]
                if not ops:
                    continue
                section = getattr(block, eng)

                def body(handle, ops=ops):
                    for fn in ops:
                        fn(handle)
                section(body)
        self.ctx.close()
        return self.nc


class PsumPool:
    def __init__(self, p, name, n, cols=512, parts=128, dtype=F32):
        self.bufs = [p.psum(f"{name}{i}", [parts, cols], dtype) for i in range(n)]
        self.i = 0

    def next(self):
        b = self.bufs[self.i % len(self.bufs)]
        self.i += 1
        return b



class Common:

    def __init__(self, p):
        self.ones_col_f = p.sbuf("ones_col_f", [128, 1], F32)
        self.ones_row_f = p.sbuf("ones_row_f", [1, 128], F32)
        p.op("gpsimd", lambda h, t=self.ones_col_f: h.memset(t.ap, 1.0), writes=(self.ones_col_f,))
        p.op("gpsimd", lambda h, t=self.ones_row_f: h.memset(t.ap, 1.0), writes=(self.ones_row_f,))


def emit_ln(p, cm, T, src_tile, dst_tile, scratch, stat, ps_s0, ps_s1, ps_bc0, ps_bc1,
           gb=None, nd=ND, eps=LN_EPS, feat=None):
    src_bufs, dst_bufs = feat
    nfeat = nd * 128
    for j in range(nd):
        p.op("scalar", lambda h, j=j: h.activation(scratch[0:128, j * T:(j + 1) * T], src_tile(j), AF.Square),
             reads=tuple(src_bufs), writes=(scratch,))
    for j in range(nd):
        p.op("tensor", lambda h, j=j: h.matmul(ps_s0[0:1, 0:T], cm.ones_col_f[0:128, 0:1], src_tile(j),
                                               start=(j == 0), stop=(j == nd - 1)),
             reads=tuple(src_bufs) + (cm.ones_col_f,), writes=(ps_s0,))
    for j in range(nd):
        p.op("tensor", lambda h, j=j: h.matmul(ps_s1[0:1, 0:T], cm.ones_col_f[0:128, 0:1],
                                               scratch[0:128, j * T:(j + 1) * T],
                                               start=(j == 0), stop=(j == nd - 1)),
             reads=(scratch, cm.ones_col_f), writes=(ps_s1,))
    p.op("vector", lambda h: h.tensor_scalar(stat[0:1, 0:T], ps_s0[0:1, 0:T], 1.0 / nfeat, None, op0=ALU.mult),
         reads=(ps_s0,), writes=(stat,))
    p.chain("vector", [
        lambda h: h.tensor_scalar(stat[0:1, T:2 * T], ps_s1[0:1, 0:T], 1.0 / nfeat, eps, op0=ALU.mult, op1=ALU.add),
        lambda h: h.tensor_tensor(stat[0:1, 2 * T:3 * T], stat[0:1, 0:T], stat[0:1, 0:T], op=ALU.mult),
        lambda h: h.tensor_tensor(stat[0:1, T:2 * T], stat[0:1, T:2 * T], stat[0:1, 2 * T:3 * T], op=ALU.subtract),
    ], reads=(ps_s1,), writes=(stat,))
    p.chain("scalar", [
        lambda h: h.activation(stat[0:1, 2 * T:3 * T], stat[0:1, T:2 * T], AF.Ln),
        lambda h: h.activation(stat[0:1, 2 * T:3 * T], stat[0:1, 2 * T:3 * T], AF.Exp, scale=-0.5),
    ], reads=(stat,), writes=(stat,))
    p.op("tensor", lambda h: h.matmul(ps_bc0[0:128, 0:T], cm.ones_row_f[0:1, 0:128], stat[0:1, 0:T],
                                      start=True, stop=True),
         reads=(stat, cm.ones_row_f), writes=(ps_bc0,))
    p.op("tensor", lambda h: h.matmul(ps_bc1[0:128, 0:T], cm.ones_row_f[0:1, 0:128], stat[0:1, 2 * T:3 * T],
                                      start=True, stop=True),
         reads=(stat, cm.ones_row_f), writes=(ps_bc1,))
    for j in range(nd):
        p.op("vector", lambda h, j=j: h.tensor_tensor(scratch[0:128, j * T:(j + 1) * T], src_tile(j),
                                                      ps_bc0[0:128, 0:T], op=ALU.subtract),
             reads=tuple(src_bufs) + (ps_bc0,), writes=(scratch,))
        if gb is None:
            p.op("vector", lambda h, j=j: h.tensor_tensor(dst_tile(j), scratch[0:128, j * T:(j + 1) * T],
                                                          ps_bc1[0:128, 0:T], op=ALU.mult),
                 reads=(scratch, ps_bc1), writes=tuple(dst_bufs))
        else:
            g_ap, b_ap, gb_bufs = gb
            p.op("vector", lambda h, j=j: h.tensor_tensor(scratch[0:128, j * T:(j + 1) * T],
                                                          scratch[0:128, j * T:(j + 1) * T],
                                                          ps_bc1[0:128, 0:T], op=ALU.mult),
                 reads=(ps_bc1,), writes=(scratch,))
            p.op("vector", lambda h, j=j: h.tensor_scalar(dst_tile(j), scratch[0:128, j * T:(j + 1) * T],
                                                          g_ap(j), b_ap(j), op0=ALU.mult, op1=ALU.add),
                 reads=(scratch,) + tuple(gb_bufs), writes=tuple(dst_bufs))


def emit_mm(p, pool, n_k, rhs_tile, w_tile, w_bufs, rhs_bufs, evac, m_tiles, T):
    for mj in m_tiles:
        ps = pool.next()
        fns = [
            (lambda h, k=k, mj=mj, ps=ps: h.matmul(
                ps[0:128, 0:T], w_tile(k, mj), rhs_tile(k), start=(k == 0), stop=(k == n_k - 1)))
            for k in range(n_k)
        ]
        p.chain("tensor", fns, reads=tuple(w_bufs) + tuple(rhs_bufs), writes=(ps,))
        evac(mj, ps)


CH_A = 256


def fm3(dram_ap, c0, cw):
    return dram_ap.rearrange("(j p) t -> p j t", p=128)[:, :, c0:c0 + cw]


def build_stageA(p=None, io=None, debug=False):
    own = p is None
    if own:
        nc = bass.Bass("TRN2", target_bir_lowering=False)
        p = Prog(nc)
    else:
        nc = p.nc
    io = io or {}
    def dram(name, shape, dt, kind):
        if name in io:
            return io[name]
        return nc.dram_tensor(name, shape, dt, kind=kind).ap()
    TCE = TC + 1
    xe = dram("xe", [D, TCE], F32, "ExternalInput")
    pval = dram("pval", [128, 1], F32, "ExternalInput")
    fmask = dram("fmask", [128, 1], F32, "ExternalInput")
    vecs = dram("vecs", [128, 8 * 6], F32, "ExternalInput")
    wk = dram("wk", [D, D], F32, "ExternalInput")
    wv = dram("wv", [D, D], F32, "ExternalInput")
    wr = dram("wr", [D, D], F32, "ExternalInput")
    ln0_o = dram("ln0_o", [D, TC], F32, "ExternalOutput")
    u_o = dram("u_o", [D, TC], F32, "ExternalOutput")
    ek_o = dram("ek_o", [D, TC], F32, "ExternalOutput")
    r_o = dram("r_o", [D, TC], F32, "ExternalOutput")

    cm = io.get("cm") or Common(p)
    CH = CH_A
    NCH = TC // CH

    wkb = [p.sbuf(f"wk{k}", [128, D], BF16) for k in range(ND)]
    wvb = [p.sbuf(f"wv{k}", [128, D], BF16) for k in range(ND)]
    wrb = [p.sbuf(f"wr{k}", [128, D], BF16) for k in range(ND)]
    vec = p.sbuf("vec", [128, 8 * 6], F32)
    pv = p.sbuf("pv", [128, 1], F32)
    fm = p.sbuf("fm", [128, 1], F32)
    deriv = p.sbuf("deriv", [128, 8 * 4], F32)
    CHP = CH_A + 1
    xn_ch = p.sbuf("xn_ch", [128, ND * CHP], F32)

    CHE = CH + 1
    xb = p.sbuf("xb", [128, ND * CHE], F32)
    ln0b = p.sbuf("ln0b", [128, ND * CHE], F32)
    sqs = p.sbuf("sqs", [128, ND * CHE], F32)
    stat = p.sbuf("stat", [1, 3 * CHE], F32)
    ekf = p.sbuf("ekf", [128, ND * CH], F32)
    vf = p.sbuf("vf", [128, ND * CH], F32)
    uf = p.sbuf("uf", [128, ND * CH], F32)
    rf = p.sbuf("rf", [128, ND * CH], F32)
    kin = p.sbuf("kin", [128, ND * CH], BF16)
    vin = p.sbuf("vin", [128, ND * CH], BF16)
    rin = p.sbuf("rin", [128, ND * CH], BF16)
    tmp_f = p.sbuf("tmp_f", [128, CH], F32)

    ps_s0 = p.psum("ps_s0", [1, 512])
    ps_s1 = p.psum("ps_s1", [1, 512])
    ps_bc0 = p.psum("ps_bc0", [128, 512])
    ps_bc1 = p.psum("ps_bc1", [128, 512])
    pool = PsumPool(p, "pp", 4)

    for k in range(ND):
        p.dma_in(wkb[k], wk[k * 128:(k + 1) * 128, :])
        p.dma_in(wvb[k], wv[k * 128:(k + 1) * 128, :])
        p.dma_in(wrb[k], wr[k * 128:(k + 1) * 128, :])
    p.dma_in(vec, vecs)
    p.dma_in(pv, pval)
    p.dma_in(fm, fmask)
    for i, src_off in enumerate((16, 24, 32)):
        p.op("vector", lambda h, i=i, o=src_off: h.tensor_scalar(
            deriv[0:128, i * 8:(i + 1) * 8], vec[0:128, o:o + 8], -1.0, 1.0, op0=ALU.mult, op1=ALU.add),
            reads=(vec,), writes=(deriv,))
    p.op("vector", lambda h: h.tensor_scalar(deriv[0:128, 24:32], vec[0:128, 40:48], fm[0:128, 0:1], None, op0=ALU.mult),
         reads=(vec, fm), writes=(deriv,))

    g_ap = lambda j: vec[0:128, j:j + 1]
    b_ap = lambda j: vec[0:128, 8 + j:8 + j + 1]

    def do_ln_pair(T, xsrc_buf, xsrc_tile, ln0_dst_buf, ln0_dst_tile, xn_dst_tile, sq_b, st_b):
        emit_ln(p, cm, T, xsrc_tile, ln0_dst_tile, sq_b, st_b, ps_s0, ps_s1, ps_bc0, ps_bc1,
                gb=None, feat=((xsrc_buf,), (ln0_dst_buf,)))
        emit_ln(p, cm, T, ln0_dst_tile, xn_dst_tile, sq_b, st_b, ps_s0, ps_s1, ps_bc0, ps_bc1,
                gb=(g_ap, b_ap, (vec,)), feat=((ln0_dst_buf,), (xn_ch,)))

    mix_specs = [
        (kin, 16, 0), (vin, 24, 8), (rin, 32, 16)]

    for ci in range(NCH):
        c0 = ci * CH
        e0 = 0 if ci == 0 else ci * CH + 1
        W = CH + 1 if ci == 0 else CH
        lo = 1 if ci == 0 else 0
        xo = 0 if ci == 0 else 1
        if ci > 0:
            for j in range(ND):
                p.op("vector", lambda h, j=j: h.tensor_copy(
                    xn_ch[0:128, j * CHP:j * CHP + 1], xn_ch[0:128, j * CHP + CH:j * CHP + CH + 1]),
                    reads=(), writes=(xn_ch,))
            p.raw("vector", lambda h: h.drain())
        p.dma_in(xb, fm3(xe, e0, W), dst=xb.t[:, 0:ND * W].rearrange("p (j t) -> p j t", j=ND))
        do_ln_pair(W, xb, lambda j, W=W: xb[0:128, j * W:(j + 1) * W],
                   ln0b, lambda j, W=W: ln0b[0:128, j * W:(j + 1) * W],
                   lambda j, W=W, xo=xo: xn_ch[0:128, j * CHP + xo:j * CHP + xo + W], sqs, stat)
        if ci == 0:
            for j in range(ND):
                p.op("vector", lambda h, j=j: h.tensor_scalar(
                    xn_ch[0:128, j * CHP:j * CHP + 1], xn_ch[0:128, j * CHP:j * CHP + 1],
                    pv[0:128, 0:1], None, op0=ALU.mult),
                    reads=(pv,), writes=(xn_ch,))
            p.raw("vector", lambda h: h.drain())
        p.dma_out(fm3(ln0_o, c0, CH), ln0b,
                  src=ln0b.t[:, 0:ND * W].rearrange("p (j t) -> p j t", j=ND)[:, :, lo:lo + CH])
        for dst, mixo, invo in mix_specs:
            for j in range(ND):
                cur = xn_ch[0:128, j * CHP + 1:j * CHP + 1 + CH]
                prv = xn_ch[0:128, j * CHP:j * CHP + CH]
                p.op("vector", lambda h, prv=prv, invo=invo, j=j: h.tensor_scalar(
                    tmp_f[0:128, 0:CH], prv, deriv[0:128, invo + j:invo + j + 1], None, op0=ALU.mult),
                    reads=(xn_ch, deriv), writes=(tmp_f,))
                p.op("vector", lambda h, cur=cur, mixo=mixo, j=j, dst=dst: h.scalar_tensor_tensor(
                    dst[0:128, j * CH:(j + 1) * CH], cur, vec[0:128, mixo + j:mixo + j + 1],
                    tmp_f[0:128, 0:CH], op0=ALU.mult, op1=ALU.add),
                    reads=(xn_ch, vec, tmp_f), writes=(dst,))

        def evac_k(mj, ps):
            if ci == 0:
                p.op("vector", lambda h, mj=mj, ps=ps: h.tensor_tensor(
                    ps[0:128, 0:1], ps[0:128, 0:1], deriv[0:128, 24 + mj:24 + mj + 1], op=ALU.add),
                    reads=(deriv, ps), writes=(ps,))
            p.op("scalar", lambda h, mj=mj, ps=ps: h.activation(
                ekf[0:128, mj * CH:(mj + 1) * CH], ps[0:128, 0:CH], AF.Exp),
                reads=(ps,), writes=(ekf,))
        emit_mm(p, pool, ND, lambda k: kin[0:128, k * CH:(k + 1) * CH],
                lambda k, mj: wkb[k][0:128, mj * 128:(mj + 1) * 128],
                wkb, (kin,), evac_k, range(ND), CH)

        def evac_v(mj, ps):
            p.op("vector", lambda h, mj=mj, ps=ps: h.tensor_copy(
                vf[0:128, mj * CH:(mj + 1) * CH], ps[0:128, 0:CH]), reads=(ps,), writes=(vf,))
        emit_mm(p, pool, ND, lambda k: vin[0:128, k * CH:(k + 1) * CH],
                lambda k, mj: wvb[k][0:128, mj * 128:(mj + 1) * 128],
                wvb, (vin,), evac_v, range(ND), CH)

        def evac_r(mj, ps):
            p.op("scalar", lambda h, mj=mj, ps=ps: h.activation(
                rf[0:128, mj * CH:(mj + 1) * CH], ps[0:128, 0:CH], AF.Sigmoid),
                reads=(ps,), writes=(rf,))
        emit_mm(p, pool, ND, lambda k: rin[0:128, k * CH:(k + 1) * CH],
                lambda k, mj: wrb[k][0:128, mj * 128:(mj + 1) * 128],
                wrb, (rin,), evac_r, range(ND), CH)

        for j in range(ND):
            p.op("vector", lambda h, j=j: h.tensor_tensor(
                uf[0:128, j * CH:(j + 1) * CH], ekf[0:128, j * CH:(j + 1) * CH],
                vf[0:128, j * CH:(j + 1) * CH], op=ALU.mult), reads=(ekf, vf), writes=(uf,))

        for dram, buf in ((u_o, uf), (ek_o, ekf), (r_o, rf)):
            p.dma_out(fm3(dram, c0, CH), buf, src=buf.t[:].rearrange("p (j t) -> p j t", j=ND))

        if debug and ci == 0:
            xn_o = dram("xn_o", [D, CH + 1], F32, "ExternalOutput")
            kin_o = dram("kin_o", [D, CH], F32, "ExternalOutput")
            xnf = p.sbuf("xnf_dbg", [128, ND * (CH + 1)], F32)
            kinf = p.sbuf("kinf_dbg", [128, ND * CH], F32)
            for j in range(ND):
                p.op("vector", lambda h, j=j: h.tensor_copy(
                    xnf[0:128, j * (CH + 1):(j + 1) * (CH + 1)], xn_ch[0:128, j * CHP:j * CHP + CH + 1]),
                    reads=(xn_ch,), writes=(xnf,))
                p.op("vector", lambda h, j=j: h.tensor_copy(
                    kinf[0:128, j * CH:(j + 1) * CH], kin[0:128, j * CH:(j + 1) * CH]),
                    reads=(kin,), writes=(kinf,))
            p.dma_out(fm3(xn_o, 0, CH + 1), xnf, src=xnf.t[:].rearrange("p (j t) -> p j t", j=ND))
            p.dma_out(fm3(kin_o, 0, CH), kinf, src=kinf.t[:].rearrange("p (j t) -> p j t", j=ND))

    return p


def _finish(p):
    for s in p._dma_pool:
        if s.n > 0:
            p._emit("sync", lambda h, s=s.h, n=s.n: h.wait_ge(s, n))
    return p.finalize()


def build_stageB(p=None, io=None):
    own = p is None
    if own:
        nc = bass.Bass("TRN2", target_bir_lowering=False)
        p = Prog(nc)
    else:
        nc = p.nc
    io = io or {}
    def dram(name, shape, dt, kind):
        if name in io:
            return io[name]
        return nc.dram_tensor(name, shape, dt, kind=kind).ap()
    NCH_T = 4
    u_s = dram("u_s", [512, S], F32, "ExternalInput")
    ek_s = dram("ek_s", [512, S], F32, "ExternalInput")
    td_s = dram("td_s", [128, 4], F32, "ExternalInput")
    wkv_o = dram("wkv_o", [512, S], F32, "ExternalOutput")

    ones_t = p.sbuf("ones_t", [128, S], F32)
    p.op("gpsimd", lambda h: h.memset(ones_t.ap, 1.0), writes=(ones_t,))
    td = p.sbuf("td", [128, 4], F32)
    ew_col = p.sbuf("ew_col", [128, 4], F32)
    p.dma_in(td, td_s)
    p.chain("scalar", [
        lambda h: h.activation(ew_col[0:128, 0:4], td[0:128, 0:4], AF.Exp),
        lambda h: h.activation(ew_col[0:128, 0:4], ew_col[0:128, 0:4], AF.Exp, scale=-1.0),
    ], reads=(td,), writes=(ew_col,))

    ew_t = p.sbuf("ew_t", [128, S], F32)
    ut = p.sbuf("ut", [128, S], F32)
    ekt = p.sbuf("ekt", [128, S], F32)
    at = p.sbuf("at", [128, S], F32)
    bt = p.sbuf("bt", [128, S], F32)
    wt = p.sbuf("wt", [128, S], F32)

    for i in range(NCH_T):
        p.dma_in(ut, u_s[i * 128:(i + 1) * 128, :])
        p.dma_in(ekt, ek_s[i * 128:(i + 1) * 128, :])
        p.op("vector", lambda h, i=i: h.tensor_scalar(
            ew_t.ap, ones_t.ap, ew_col[0:128, i:i + 1], None, op0=ALU.mult),
            reads=(ones_t, ew_col), writes=(ew_t,))
        p.op("vector", lambda h: h.tensor_tensor_scan(
            at.ap, ew_t.ap, ut.ap, 0.0, op0=ALU.mult, op1=ALU.add),
            reads=(ew_t, ut), writes=(at,))
        p.op("vector", lambda h: h.tensor_tensor_scan(
            bt.ap, ew_t.ap, ekt.ap, 0.0, op0=ALU.mult, op1=ALU.add),
            reads=(ew_t, ekt), writes=(bt,))
        p.op("vector", lambda h: h.tensor_scalar(bt.ap, bt.ap, WKV_EPS, None, op0=ALU.add),
             reads=(), writes=(bt,))
        p.chain("scalar", [
            lambda h: h.activation(bt.ap, bt.ap, AF.Ln),
            lambda h: h.activation(bt.ap, bt.ap, AF.Exp, scale=-1.0),
        ], reads=(), writes=(bt,))
        p.op("vector", lambda h: h.tensor_tensor(wt.ap, at.ap, bt.ap, op=ALU.mult),
             reads=(at, bt), writes=(wt,))
        p.dma_out(wkv_o[i * 128:(i + 1) * 128, :], wt)
    return p


def build_stageC1(p=None, io=None):
    own = p is None
    if own:
        nc = bass.Bass("TRN2", target_bir_lowering=False)
        p = Prog(nc)
    else:
        nc = p.nc
    io = io or {}
    def dram(name, shape, dt, kind):
        if name in io:
            return io[name]
        return nc.dram_tensor(name, shape, dt, kind=kind).ap()
    TCE = TC + 1
    CH = 256
    ln0_e = dram("ln0_e", [D, TCE], F32, "ExternalInput")
    r_e = dram("r_e", [D, TCE], F32, "ExternalInput")
    wkv_e = dram("wkv_e", [D, TCE], F32, "ExternalInput")
    wo = dram("wo", [D, D], F32, "ExternalInput")
    x1_o = dram("x1_o", [D, TCE], F32, "ExternalOutput")

    wob = [p.sbuf(f"wo{k}", [128, D], BF16) for k in range(ND)]
    for k in range(ND):
        p.dma_in(wob[k], wo[k * 128:(k + 1) * 128, :])
    rb = p.sbuf("rb", [128, ND * (CH + 1)], F32)
    wb = p.sbuf("wb", [128, ND * (CH + 1)], F32)
    l0b = p.sbuf("l0b", [128, ND * (CH + 1)], F32)
    zb = p.sbuf("zb", [128, ND * (CH + 1)], BF16)
    x1b = p.sbuf("x1b", [128, ND * (CH + 1)], F32)
    pool = PsumPool(p, "pp", 4)

    nch = TCE // CH
    for ci in range(nch):
        c0 = ci * CH
        W = CH + 1 if ci == nch - 1 else CH
        p.dma_in(rb, fm3(r_e, c0, W), dst=rb.t[:, 0:ND * W].rearrange("p (j t) -> p j t", j=ND))
        p.dma_in(wb, fm3(wkv_e, c0, W), dst=wb.t[:, 0:ND * W].rearrange("p (j t) -> p j t", j=ND))
        p.dma_in(l0b, fm3(ln0_e, c0, W), dst=l0b.t[:, 0:ND * W].rearrange("p (j t) -> p j t", j=ND))
        for j in range(ND):
            p.op("vector", lambda h, j=j, W=W: h.tensor_tensor(
                zb[0:128, j * W:(j + 1) * W], rb[0:128, j * W:(j + 1) * W],
                wb[0:128, j * W:(j + 1) * W], op=ALU.mult), reads=(rb, wb), writes=(zb,))

        def evac(mj, ps, W=W):
            p.op("vector", lambda h, mj=mj, ps=ps, W=W: h.tensor_tensor(
                x1b[0:128, mj * W:(mj + 1) * W], ps[0:128, 0:W],
                l0b[0:128, mj * W:(mj + 1) * W], op=ALU.add), reads=(ps, l0b), writes=(x1b,))
        emit_mm(p, pool, ND, lambda k, W=W: zb[0:128, k * W:(k + 1) * W],
                lambda k, mj: wob[k][0:128, mj * 128:(mj + 1) * 128],
                wob, (zb,), evac, range(ND), W)
        p.dma_out(fm3(x1_o, c0, W), x1b, src=x1b.t[:, 0:ND * W].rearrange("p (j t) -> p j t", j=ND))
    return p


def build_stageC2(p=None, io=None):
    own = p is None
    if own:
        nc = bass.Bass("TRN2", target_bir_lowering=False)
        p = Prog(nc)
    else:
        nc = p.nc
    io = io or {}
    def dram(name, shape, dt, kind):
        if name in io:
            return io[name]
        return nc.dram_tensor(name, shape, dt, kind=kind).ap()
    TCE = TC + 1
    CH = 256
    CHP = CH + 1
    x1_e = dram("x1_e", [D, TCE], F32, "ExternalInput")
    x_fm = dram("x_fm", [D, TC], F32, "ExternalInput")
    pval = dram("pval", [128, 1], F32, "ExternalInput")
    vecs = dram("vecs", [128, 16], F32, "ExternalInput")
    wfk = dram("wfk", [D, 4 * D], F32, "ExternalInput")
    wfr = dram("wfr", [D, D], F32, "ExternalInput")
    wfv = dram("wfv", [4 * D, D], F32, "ExternalInput")
    x2_o = dram("x2_o", [D, TC], F32, "ExternalOutput")

    cm = io.get("cm") or Common(p)
    wfkb = [p.sbuf(f"wfk{k}", [128, 4 * D], BF16) for k in range(ND)]
    wfrb = [p.sbuf(f"wfr{k}", [128, D], BF16) for k in range(ND)]
    for k in range(ND):
        p.dma_in(wfkb[k], wfk[k * 128:(k + 1) * 128, :])
        p.dma_in(wfrb[k], wfr[k * 128:(k + 1) * 128, :])
    wfvb = [p.sbuf("wfv_a", [128, D], BF16), p.sbuf("wfv_b", [128, D], BF16)]
    vec = p.sbuf("vec", [128, 16], F32)
    pv = p.sbuf("pv", [128, 1], F32)
    p.dma_in(vec, vecs)
    p.dma_in(pv, pval)
    g_ap = lambda j: vec[0:128, j:j + 1]
    b_ap = lambda j: vec[0:128, 8 + j:8 + j + 1]

    x1w = p.sbuf("x1w", [128, ND * CHP], F32)
    yn_ch = p.sbuf("yn_ch", [128, ND * CHP], F32)
    sqs = p.sbuf("sqs", [128, ND * CHP], F32)
    stat = p.sbuf("stat", [1, 3 * CHP], F32)
    ynb = p.sbuf("ynb", [128, ND * CH], BF16)
    z2b = p.sbuf("z2b", [128, ND * CH], BF16)
    tmp_f = p.sbuf("tmp_f", [128, CH], F32)
    rrf = p.sbuf("rrf", [128, ND * CH], F32)
    kkt = p.sbuf("kkt", [128, CH], F32)
    kkb = p.sbuf("kkb", [128, 32 * CH], BF16)
    xc = p.sbuf("xc", [128, ND * CH], F32)
    bo = p.sbuf("bo", [128, ND * CH], F32)
    x2b = p.sbuf("x2b", [128, ND * CH], F32)

    ps_s0 = p.psum("ps_s0", [1, 512])
    ps_s1 = p.psum("ps_s1", [1, 512])
    ps_bc0 = p.psum("ps_bc0", [128, 512])
    ps_bc1 = p.psum("ps_bc1", [128, 512])
    pool = PsumPool(p, "pp", 4)

    NCH = TC // CH
    for ci in range(NCH):
        c0 = ci * CH
        e0 = 0 if ci == 0 else c0 + 1
        W = CH + 1 if ci == 0 else CH
        xo = 0 if ci == 0 else 1
        if ci > 0:
            for j in range(ND):
                p.op("vector", lambda h, j=j: h.tensor_copy(
                    yn_ch[0:128, j * CHP:j * CHP + 1], yn_ch[0:128, j * CHP + CH:j * CHP + CH + 1]),
                    reads=(), writes=(yn_ch,))
            p.raw("vector", lambda h: h.drain())
        p.dma_in(x1w, fm3(x1_e, e0, W), dst=x1w.t[:, 0:ND * W].rearrange("p (j t) -> p j t", j=ND))
        emit_ln(p, cm, W, lambda j, W=W: x1w[0:128, j * W:(j + 1) * W],
                lambda j, W=W, xo=xo: yn_ch[0:128, j * CHP + xo:j * CHP + xo + W],
                sqs, stat, ps_s0, ps_s1, ps_bc0, ps_bc1,
                gb=(g_ap, b_ap, (vec,)), feat=((x1w,), (yn_ch,)))
        if ci == 0:
            for j in range(ND):
                p.op("vector", lambda h, j=j: h.tensor_scalar(
                    yn_ch[0:128, j * CHP:j * CHP + 1], yn_ch[0:128, j * CHP:j * CHP + 1],
                    pv[0:128, 0:1], None, op0=ALU.mult), reads=(pv,), writes=(yn_ch,))
            p.raw("vector", lambda h: h.drain())
        for j in range(ND):
            p.op("vector", lambda h, j=j: h.tensor_copy(
                ynb[0:128, j * CH:(j + 1) * CH], yn_ch[0:128, j * CHP + 1:j * CHP + 1 + CH]),
                reads=(yn_ch,), writes=(ynb,))
            p.op("vector", lambda h, j=j: h.tensor_scalar(
                tmp_f[0:128, 0:CH], yn_ch[0:128, j * CHP:j * CHP + CH], 0.5, None, op0=ALU.mult),
                reads=(yn_ch,), writes=(tmp_f,))
            p.op("vector", lambda h, j=j: h.scalar_tensor_tensor(
                z2b[0:128, j * CH:(j + 1) * CH], yn_ch[0:128, j * CHP + 1:j * CHP + 1 + CH],
                0.5, tmp_f[0:128, 0:CH], op0=ALU.mult, op1=ALU.add),
                reads=(yn_ch, tmp_f), writes=(z2b,))

        def evac_rr(mj, ps):
            p.op("scalar", lambda h, mj=mj, ps=ps: h.activation(
                rrf[0:128, mj * CH:(mj + 1) * CH], ps[0:128, 0:CH], AF.Sigmoid),
                reads=(ps,), writes=(rrf,))
        emit_mm(p, pool, ND, lambda k: ynb[0:128, k * CH:(k + 1) * CH],
                lambda k, mj: wfrb[k][0:128, mj * 128:(mj + 1) * 128],
                wfrb, (ynb,), evac_rr, range(ND), CH)

        def evac_kk(mj, ps):
            p.op("scalar", lambda h, mj=mj, ps=ps: h.activation(
                kkt[0:128, 0:CH], ps[0:128, 0:CH], AF.Relu), reads=(ps,), writes=(kkt,))
            p.op("scalar", lambda h, mj=mj: h.activation(
                kkb[0:128, mj * CH:(mj + 1) * CH], kkt[0:128, 0:CH], AF.Square),
                reads=(kkt,), writes=(kkb,))
        emit_mm(p, pool, ND, lambda k: z2b[0:128, k * CH:(k + 1) * CH],
                lambda k, mj: wfkb[k][0:128, mj * 128:(mj + 1) * 128],
                wfkb, (z2b,), evac_kk, range(32), CH)

        p.dma_in(xc, fm3(x1_e, c0 + 1, CH), dst=xc.t[:].rearrange("p (j t) -> p j t", j=ND))
        for half in range(2):
            mjs = range(half * 4, half * 4 + 4)
            pss = {}
            for mj in mjs:
                pss[mj] = pool.next()
            for k in range(32):
                wb_ = wfvb[k % 2]
                p.dma_in(wb_, wfv[k * 128:(k + 1) * 128, :])
                for mj in mjs:
                    p.chain("tensor", [
                        lambda h, k=k, mj=mj, wb_=wb_, ps_=pss[mj]: h.matmul(
                            ps_[0:128, 0:CH], wb_[0:128, mj * 128:(mj + 1) * 128],
                            kkb[0:128, k * CH:(k + 1) * CH], start=(k == 0), stop=(k == 31))],
                        reads=(wb_, kkb), writes=(pss[mj],))
            for mj in mjs:
                p.op("vector", lambda h, mj=mj, ps_=pss[mj]: h.tensor_tensor(
                    bo[0:128, mj * CH:(mj + 1) * CH], rrf[0:128, mj * CH:(mj + 1) * CH],
                    ps_[0:128, 0:CH], op=ALU.mult), reads=(rrf, pss[mj]), writes=(bo,))
                p.op("vector", lambda h, mj=mj: h.tensor_tensor(
                    bo[0:128, mj * CH:(mj + 1) * CH], bo[0:128, mj * CH:(mj + 1) * CH],
                    xc[0:128, mj * CH:(mj + 1) * CH], op=ALU.add), reads=(xc,), writes=(bo,))
        emit_ln(p, cm, CH, lambda j: bo[0:128, j * CH:(j + 1) * CH],
                lambda j: x2b[0:128, j * CH:(j + 1) * CH],
                sqs, stat, ps_s0, ps_s1, ps_bc0, ps_bc1, gb=None, feat=((bo,), (x2b,)))
        p.dma_in(xc, fm3(x_fm, c0, CH), dst=xc.t[:].rearrange("p (j t) -> p j t", j=ND))
        for j in range(ND):
            p.op("vector", lambda h, j=j: h.tensor_tensor(
                x2b[0:128, j * CH:(j + 1) * CH], x2b[0:128, j * CH:(j + 1) * CH],
                xc[0:128, j * CH:(j + 1) * CH], op=ALU.add), reads=(xc,), writes=(x2b,))
        p.dma_out(fm3(x2_o, c0, CH), x2b, src=x2b.t[:].rearrange("p (j t) -> p j t", j=ND))
    return p


def build_stageD(p=None, io=None):
    own = p is None
    if own:
        nc = bass.Bass("TRN2", target_bir_lowering=False)
        p = Prog(nc)
    else:
        nc = p.nc
    io = io or {}
    def dram(name, shape, dt, kind):
        if name in io:
            return io[name]
        return nc.dram_tensor(name, shape, dt, kind=kind).ap()
    CH = 256
    DH = 4 * D
    x2_i = dram("x2_i", [D, TC], F32, "ExternalInput")
    w1t = dram("w1t", [D, DH], F32, "ExternalInput")
    w2t = dram("w2t", [D, DH], F32, "ExternalInput")
    h1t = dram("h1t", [DH, DH], F32, "ExternalInput")
    h2t = dram("h2t", [DH, DH], F32, "ExternalInput")
    w3t = dram("w3t", [DH, D], F32, "ExternalInput")
    scl = dram("scl", [128, 32 * 4 + 8], F32, "ExternalInput")
    x3_o = dram("x3_o", [D, TC], F32, "ExternalOutput")

    cm = io.get("cm") or Common(p)
    slab = [p.sbuf(f"slab{i}", [128, 512], BF16) for i in range(4)]
    sc = p.sbuf("sc", [128, 32 * 4 + 8], F32)
    p.dma_in(sc, scl)

    xcb = p.sbuf("xcb", [128, ND * CH], F32)
    lnb = p.sbuf("lnb", [128, ND * CH], BF16)
    lnf = p.sbuf("lnf", [128, ND * CH], F32)
    sqs = p.sbuf("sqs", [128, ND * CH], F32)
    stat = p.sbuf("stat", [1, 3 * CH], F32)
    slf = p.sbuf("slf", [128, 32 * CH], F32)
    t2f = p.sbuf("t2f", [128, CH], F32)
    hb = p.sbuf("hb", [128, 32 * CH], BF16)
    g1b = p.sbuf("g1b", [128, 32 * CH], BF16)
    ob = p.sbuf("ob", [128, ND * CH], F32)
    x3b = p.sbuf("x3b", [128, ND * CH], F32)

    ps_s0 = p.psum("ps_s0", [1, 512])
    ps_s1 = p.psum("ps_s1", [1, 512])
    ps_bc0 = p.psum("ps_bc0", [128, 512])
    ps_bc1 = p.psum("ps_bc1", [128, 512])
    pool = PsumPool(p, "pp", 4)

    wgb = [p.sbuf(f"wgb{i}", [128, 32 * 512], BF16) for i in range(2)]

    def big_mm(src_b, n_k, wsrc, dst_fn, scale_off, act):
        for g in range(8):
            wb_ = wgb[g % 2]
            p.dma_in(wb_, wsrc.rearrange("(k p) n -> p k n", p=128)[:, 0:n_k, g * 512:(g + 1) * 512],
                     dst=wb_.t[:, 0:n_k * 512].rearrange("p (k n) -> p k n", k=n_k))
            pss = [pool.next() for _ in range(4)]
            for k in range(n_k):
                for mi in range(4):
                    mj = g * 4 + mi
                    p.chain("tensor", [
                        lambda h, k=k, mi=mi, wb_=wb_, ps_=pss[mi]: h.matmul(
                            ps_[0:128, 0:CH], wb_[0:128, k * 512 + mi * 128:k * 512 + (mi + 1) * 128],
                            src_b[0:128, k * CH:(k + 1) * CH], start=(k == 0), stop=(k == n_k - 1))],
                        reads=(wb_, src_b), writes=(pss[mi],))
            for mi in range(4):
                mj = g * 4 + mi
                dst_fn(mj, pss[mi], scale_off, act)

    def evac_gelu(dst_b):
        def f(mj, ps, so, act):
            p.op("scalar", lambda h, mj=mj, ps=ps, so=so: h.activation(
                dst_b[0:128, mj * CH:(mj + 1) * CH], ps[0:128, 0:CH], AF.Gelu,
                scale=sc[0:128, so + mj:so + mj + 1]), reads=(ps, sc), writes=(dst_b,))
        return f

    NCH = TC // CH
    for ci in range(NCH):
        c0 = ci * CH
        p.dma_in(xcb, fm3(x2_i, c0, CH), dst=xcb.t[:].rearrange("p (j t) -> p j t", j=ND))
        emit_ln(p, cm, CH, lambda j: xcb[0:128, j * CH:(j + 1) * CH],
                lambda j: lnf[0:128, j * CH:(j + 1) * CH],
                sqs, stat, ps_s0, ps_s1, ps_bc0, ps_bc1, gb=None, feat=((xcb,), (lnf,)))
        for j in range(ND):
            p.op("vector", lambda h, j=j: h.tensor_copy(
                lnb[0:128, j * CH:(j + 1) * CH], lnf[0:128, j * CH:(j + 1) * CH]),
                reads=(lnf,), writes=(lnb,))
        def evac_t1(mj, ps, so=None, act=None):
            p.op("scalar", lambda h, mj=mj, ps=ps: h.activation(
                slf[0:128, mj * CH:(mj + 1) * CH], ps[0:128, 0:CH], AF.Silu,
                scale=sc[0:128, mj:mj + 1]), reads=(ps, sc), writes=(slf,))
        big_mm(lnb, ND, w1t, evac_t1, 0, AF.Silu)
        def evac_t2(mj, ps, so=None, act=None):
            p.op("vector", lambda h, mj=mj, ps=ps: h.tensor_scalar(
                t2f[0:128, 0:CH], ps[0:128, 0:CH], sc[0:128, 32 + mj:32 + mj + 1], None, op0=ALU.mult),
                reads=(ps, sc), writes=(t2f,))
            p.op("vector", lambda h, mj=mj: h.tensor_tensor(
                hb[0:128, mj * CH:(mj + 1) * CH], slf[0:128, mj * CH:(mj + 1) * CH],
                t2f[0:128, 0:CH], op=ALU.mult), reads=(slf, t2f), writes=(hb,))
        big_mm(lnb, ND, w2t, evac_t2, 32, AF.Silu)
        big_mm(hb, 32, h1t, evac_gelu(g1b), 64, AF.Gelu)
        big_mm(g1b, 32, h2t, evac_gelu(hb), 96, AF.Gelu)
        for g in range(2):
            pss = [pool.next() for _ in range(4)]
            for k in range(32):
                sb = slab[2 + k % 2]
                p.dma_in(sb, w3t[k * 128:(k + 1) * 128, g * 512:(g + 1) * 512])
                for mi in range(4):
                    mj = g * 4 + mi
                    p.chain("tensor", [
                        lambda h, k=k, mi=mi, sb=sb, ps_=pss[mi]: h.matmul(
                            ps_[0:128, 0:CH], sb[0:128, mi * 128:(mi + 1) * 128],
                            hb[0:128, k * CH:(k + 1) * CH], start=(k == 0), stop=(k == 31))],
                        reads=(sb, hb), writes=(pss[mi],))
            for mi in range(4):
                mj = g * 4 + mi
                p.op("vector", lambda h, mj=mj, ps=pss[mi]: h.tensor_scalar(
                    ob[0:128, mj * CH:(mj + 1) * CH], ps[0:128, 0:CH],
                    sc[0:128, 128 + mj:128 + mj + 1], None, op0=ALU.mult),
                    reads=(pss[mi], sc), writes=(ob,))
        emit_ln(p, cm, CH, lambda j: ob[0:128, j * CH:(j + 1) * CH],
                lambda j: x3b[0:128, j * CH:(j + 1) * CH],
                sqs, stat, ps_s0, ps_s1, ps_bc0, ps_bc1, gb=None, feat=((ob,), (x3b,)))
        for j in range(ND):
            p.op("vector", lambda h, j=j: h.tensor_tensor(
                x3b[0:128, j * CH:(j + 1) * CH], x3b[0:128, j * CH:(j + 1) * CH],
                xcb[0:128, j * CH:(j + 1) * CH], op=ALU.add), reads=(xcb,), writes=(x3b,))
        p.dma_out(fm3(x3_o, c0, CH), x3b, src=x3b.t[:].rearrange("p (j t) -> p j t", j=ND))
    return p


def build_stageE(p=None, io=None):
    own = p is None
    if own:
        nc = bass.Bass("TRN2", target_bir_lowering=False)
        p = Prog(nc)
    else:
        nc = p.nc
    io = io or {}
    def dram(name, shape, dt, kind):
        if name in io:
            return io[name]
        return nc.dram_tensor(name, shape, dt, kind=kind).ap()
    CH = 256
    x3_i = dram("x3_i", [D, TC], F32, "ExternalInput")
    wqkv = dram("wqkv", [D, 3 * D], F32, "ExternalInput")
    cost = dram("cost", [128, TC], F32, "ExternalInput")
    sint = dram("sint", [128, TC], F32, "ExternalInput")
    selm = dram("selm", [128, 16 * ND], F32, "ExternalInput")
    selt = dram("selt", [16, 128 * ND], F32, "ExternalInput")
    qb_o = dram("qb_o", [D, TC], BF16, "ExternalOutput")
    kb_o = dram("kb_o", [D, TC], BF16, "ExternalOutput")
    vb_o = dram("vb_o", [D, TC], BF16, "ExternalOutput")

    cm = io.get("cm") or Common(p)
    wb = [p.sbuf(f"wq{k}", [128, 3 * D], BF16) for k in range(ND)]
    for k in range(ND):
        p.dma_in(wb[k], wqkv[k * 128:(k + 1) * 128, :])
    cosb = p.sbuf("cosb", [128, TC], F32)
    sinb = p.sbuf("sinb", [128, TC], F32)
    p.dma_in(cosb, cost)
    p.dma_in(sinb, sint)
    selb = p.sbuf("selb", [128, 16 * ND], F32)
    seltb = p.sbuf("seltb", [16, 128 * ND], F32)
    p.dma_in(selb, selm)
    p.dma_in(seltb, selt)

    xcb = p.sbuf("xcb", [128, ND * CH], F32)
    lnb = p.sbuf("lnb", [128, ND * CH], BF16)
    lnf = p.sbuf("lnf", [128, ND * CH], F32)
    sqs = p.sbuf("sqs", [128, ND * CH], F32)
    stat = p.sbuf("stat", [1, 3 * CH], F32)
    qf = p.sbuf("qf", [128, ND * CH], F32)
    kf = p.sbuf("kf", [128, ND * CH], F32)
    qr = p.sbuf("qr", [128, ND * CH], F32)
    rt = p.sbuf("rt", [128, 2 * CH], F32)
    sqv = p.sbuf("sqv", [128, CH], F32)
    inv = p.sbuf("inv", [16, 512], F32)
    outb = p.sbuf("outb", [128, ND * CH], BF16)

    ps_s0 = p.psum("ps_s0", [1, 512])
    ps_s1 = p.psum("ps_s1", [1, 512])
    ps_bc0 = p.psum("ps_bc0", [128, 512])
    ps_bc1 = p.psum("ps_bc1", [128, 512])
    ps_n = p.psum("ps_n", [16, 512])
    pool = PsumPool(p, "pp", 3)

    NCH = TC // CH
    for ci in range(NCH):
        c0 = ci * CH
        p.dma_in(xcb, fm3(x3_i, c0, CH), dst=xcb.t[:].rearrange("p (j t) -> p j t", j=ND))
        emit_ln(p, cm, CH, lambda j: xcb[0:128, j * CH:(j + 1) * CH],
                lambda j: lnf[0:128, j * CH:(j + 1) * CH],
                sqs, stat, ps_s0, ps_s1, ps_bc0, ps_bc1, gb=None, feat=((xcb,), (lnf,)))
        for j in range(ND):
            p.op("vector", lambda h, j=j: h.tensor_copy(
                lnb[0:128, j * CH:(j + 1) * CH], lnf[0:128, j * CH:(j + 1) * CH]),
                reads=(lnf,), writes=(lnb,))

        def evac_qkv(mj, ps):
            if mj < 8:
                p.op("vector", lambda h, mj=mj, ps=ps: h.tensor_copy(
                    qf[0:128, mj * CH:(mj + 1) * CH], ps[0:128, 0:CH]), reads=(ps,), writes=(qf,))
            elif mj < 16:
                p.op("vector", lambda h, mj=mj, ps=ps: h.tensor_copy(
                    kf[0:128, (mj - 8) * CH:(mj - 7) * CH], ps[0:128, 0:CH]), reads=(ps,), writes=(kf,))
            else:
                p.op("vector", lambda h, mj=mj, ps=ps: h.tensor_copy(
                    outb[0:128, (mj - 16) * CH:(mj - 15) * CH], ps[0:128, 0:CH]), reads=(ps,), writes=(outb,))
        emit_mm(p, pool, ND, lambda k: lnb[0:128, k * CH:(k + 1) * CH],
                lambda k, mj: wb[k][0:128, mj * 128:(mj + 1) * 128],
                wb, (lnb,), evac_qkv, range(24), CH)
        p.dma_out(fm3(vb_o, c0, CH), outb, src=outb.t[:].rearrange("p (j t) -> p j t", j=ND))

        for src_f, dst_o in ((qf, qb_o), (kf, kb_o)):
            for j in range(ND):
                q1 = src_f[0:64, j * CH:(j + 1) * CH]
                q2 = src_f[64:128, j * CH:(j + 1) * CH]
                cs0 = cosb[0:64, c0:c0 + CH]
                sn0 = sinb[0:64, c0:c0 + CH]
                cs1 = cosb[64:128, c0:c0 + CH]
                sn1 = sinb[64:128, c0:c0 + CH]
                p.chain("vector", [
                    lambda h, q1=q1, cs0=cs0: h.tensor_tensor(rt[0:64, 0:CH], q1, cs0, op=ALU.mult),
                    lambda h, q2=q2, sn1=sn1: h.tensor_tensor(rt[0:64, CH:2 * CH], q2, sn1, op=ALU.mult),
                    lambda h, j=j: h.tensor_tensor(qr[0:64, j * CH:(j + 1) * CH],
                                                   rt[0:64, 0:CH], rt[0:64, CH:2 * CH], op=ALU.subtract),
                    lambda h, q1=q1, sn0=sn0: h.tensor_tensor(rt[64:128, 0:CH], q1, sn0, op=ALU.mult),
                    lambda h, q2=q2, cs1=cs1: h.tensor_tensor(rt[64:128, CH:2 * CH], q2, cs1, op=ALU.mult),
                    lambda h, j=j: h.tensor_tensor(qr[64:128, j * CH:(j + 1) * CH],
                                                   rt[64:128, 0:CH], rt[64:128, CH:2 * CH], op=ALU.add),
                ], reads=(src_f, cosb, sinb), writes=(qr, rt))
            for j in range(ND):
                p.op("scalar", lambda h, j=j: h.activation(sqv[0:128, 0:CH],
                                                           qr[0:128, j * CH:(j + 1) * CH], AF.Square),
                     reads=(qr,), writes=(sqv,))
                p.chain("tensor", [lambda h, j=j: h.matmul(
                    ps_n[0:16, 0:CH], selb[0:128, j * 16:(j + 1) * 16], sqv[0:128, 0:CH],
                    start=(j == 0), stop=(j == ND - 1))], reads=(selb, sqv), writes=(ps_n,))
            p.op("vector", lambda h: h.tensor_scalar(inv[0:16, 0:CH], ps_n[0:16, 0:CH], 1e-6, None, op0=ALU.add),
                 reads=(ps_n,), writes=(inv,))
            p.chain("scalar", [
                lambda h: h.activation(inv[0:16, 0:CH], inv[0:16, 0:CH], AF.Ln),
                lambda h: h.activation(inv[0:16, 0:CH], inv[0:16, 0:CH], AF.Exp, scale=-0.5),
            ], reads=(), writes=(inv,))
            for j in range(ND):
                p.op("tensor", lambda h, j=j: h.matmul(
                    ps_bc0[0:128, 0:CH], seltb[0:16, j * 128:(j + 1) * 128], inv[0:16, 0:CH],
                    start=True, stop=True), reads=(seltb, inv), writes=(ps_bc0,))
                p.op("vector", lambda h, j=j: h.tensor_tensor(
                    outb[0:128, j * CH:(j + 1) * CH], qr[0:128, j * CH:(j + 1) * CH],
                    ps_bc0[0:128, 0:CH], op=ALU.mult), reads=(qr, ps_bc0), writes=(outb,))
            p.dma_out(fm3(dst_o, c0, CH), outb, src=outb.t[:].rearrange("p (j t) -> p j t", j=ND))
    return p


def build_stageF(p=None, io=None):
    own = p is None
    if own:
        nc = bass.Bass("TRN2", target_bir_lowering=False)
        p = Prog(nc)
    else:
        nc = p.nc
    io = io or {}
    def dram(name, shape, dt, kind):
        if name in io:
            return io[name]
        return nc.dram_tensor(name, shape, dt, kind=kind).ap()
    TQ = 512
    qh_i = dram("qh_i", [512, S], BF16, "ExternalInput")
    kh_i = dram("kh_i", [512, S], BF16, "ExternalInput")
    vx_i = dram("vx_i", [128, 8 * 32 * 65], BF16, "ExternalInput")
    ao_o = dram("ao_o", [512, S], BF16, "ExternalOutput")

    cm = io.get("cm") or Common(p)
    kb = p.sbuf("kb", [64, S], BF16)
    qb = p.sbuf("qb", [64, S], BF16)
    vxb = p.sbuf("vxb", [128, 32 * 65], BF16)
    pb = [p.sbuf(f"pb{i}", [128, TQ], BF16) for i in range(2)]
    dn = p.sbuf("dn", [1, TQ], F32)
    af = p.sbuf("af", [64, TQ], F32)
    aob = p.sbuf("aob", [64, TQ], BF16)
    ps_qk = PsumPool(p, "qk", 2)
    ps_av = p.psum("ps_av", [65, 512])
    ps_bc = p.psum("ps_bc", [64, 512])

    for hh in range(8):
        p.dma_in(kb, kh_i[hh * 64:(hh + 1) * 64, :])
        p.dma_in(qb, qh_i[hh * 64:(hh + 1) * 64, :])
        p.dma_in(vxb, vx_i[0:128, hh * 2080:(hh + 1) * 2080])
        for tq in range(S // TQ):
            q_ap = qb[0:64, tq * TQ:(tq + 1) * TQ]
            for ts in range(32):
                ps = ps_qk.next()
                p.chain("tensor", [lambda h, ts=ts, q_ap=q_ap, ps=ps: h.matmul(
                    ps[0:128, 0:TQ], kb[0:64, ts * 128:(ts + 1) * 128], q_ap,
                    start=True, stop=True)], reads=(kb, qb), writes=(ps,))
                pbuf = pb[ts % 2]
                p.op("scalar", lambda h, ps=ps, pbuf=pbuf: h.activation(
                    pbuf.ap, ps[0:128, 0:TQ], AF.Exp, scale=0.125), reads=(ps,), writes=(pbuf,))
                p.chain("tensor", [lambda h, ts=ts, pbuf=pbuf: h.matmul(
                    ps_av[0:65, 0:TQ], vxb[0:128, ts * 65:(ts + 1) * 65], pbuf.ap,
                    start=(ts == 0), stop=(ts == 31))], reads=(vxb, pbuf), writes=(ps_av,))
            p.op("vector", lambda h: h.tensor_copy(dn.ap, ps_av[64:65, 0:TQ]), reads=(ps_av,), writes=(dn,))
            p.chain("scalar", [
                lambda h: h.activation(dn.ap, dn.ap, AF.Ln),
                lambda h: h.activation(dn.ap, dn.ap, AF.Exp, scale=-1.0),
            ], reads=(), writes=(dn,))
            p.op("scalar", lambda h: h.copy(af.ap, ps_av[0:64, 0:TQ]), reads=(ps_av,), writes=(af,))
            p.op("tensor", lambda h: h.matmul(ps_bc[0:64, 0:TQ], cm.ones_row_f[0:1, 0:64], dn.ap,
                                              start=True, stop=True),
                 reads=(cm.ones_row_f, dn), writes=(ps_bc,))
            p.op("vector", lambda h: h.tensor_tensor(aob.ap, af.ap, ps_bc[0:64, 0:TQ], op=ALU.mult),
                 reads=(af, ps_bc), writes=(aob,))
            p.dma_out(ao_o[hh * 64:(hh + 1) * 64, tq * TQ:(tq + 1) * TQ], aob)
    return p


def build_stageG(p=None, io=None):
    own = p is None
    if own:
        nc = bass.Bass("TRN2", target_bir_lowering=False)
        p = Prog(nc)
    else:
        nc = p.nc
    io = io or {}
    def dram(name, shape, dt, kind):
        if name in io:
            return io[name]
        return nc.dram_tensor(name, shape, dt, kind=kind).ap()
    CH = 256
    ao_i = dram("ao_i", [D, TC], BF16, "ExternalInput")
    x3_i = dram("x3_i", [D, TC], F32, "ExternalInput")
    wo = dram("wo", [D, D], F32, "ExternalInput")
    bo_v = dram("bo_v", [128, 8], F32, "ExternalInput")
    x4_o = dram("x4_o", [D, TC], F32, "ExternalOutput")
    h0_o = dram("h0_o", [D, TC], F32, "ExternalOutput")

    cm = io.get("cm") or Common(p)
    wob = [p.sbuf(f"wo{k}", [128, D], BF16) for k in range(ND)]
    for k in range(ND):
        p.dma_in(wob[k], wo[k * 128:(k + 1) * 128, :])
    bov = p.sbuf("bov", [128, 8], F32)
    p.dma_in(bov, bo_v)
    aob = p.sbuf("aob", [128, ND * CH], BF16)
    xcb = p.sbuf("xcb", [128, ND * CH], F32)
    of = p.sbuf("of", [128, ND * CH], F32)
    x4b = p.sbuf("x4b", [128, ND * CH], F32)
    h0b = p.sbuf("h0b", [128, ND * CH], F32)
    sqs = p.sbuf("sqs", [128, ND * CH], F32)
    stat = p.sbuf("stat", [1, 3 * CH], F32)
    ps_s0 = p.psum("ps_s0", [1, 512])
    ps_s1 = p.psum("ps_s1", [1, 512])
    ps_bc0 = p.psum("ps_bc0", [128, 512])
    ps_bc1 = p.psum("ps_bc1", [128, 512])
    pool = PsumPool(p, "pp", 4)

    NCH = TC // CH
    for ci in range(NCH):
        c0 = ci * CH
        p.dma_in(aob, fm3(ao_i, c0, CH), dst=aob.t[:].rearrange("p (j t) -> p j t", j=ND))
        p.dma_in(xcb, fm3(x3_i, c0, CH), dst=xcb.t[:].rearrange("p (j t) -> p j t", j=ND))

        def evac_o(mj, ps):
            p.op("vector", lambda h, mj=mj, ps=ps: h.tensor_scalar(
                of[0:128, mj * CH:(mj + 1) * CH], ps[0:128, 0:CH],
                bov[0:128, mj:mj + 1], None, op0=ALU.add), reads=(ps, bov), writes=(of,))
        emit_mm(p, pool, ND, lambda k: aob[0:128, k * CH:(k + 1) * CH],
                lambda k, mj: wob[k][0:128, mj * 128:(mj + 1) * 128],
                wob, (aob,), evac_o, range(ND), CH)
        emit_ln(p, cm, CH, lambda j: of[0:128, j * CH:(j + 1) * CH],
                lambda j: x4b[0:128, j * CH:(j + 1) * CH],
                sqs, stat, ps_s0, ps_s1, ps_bc0, ps_bc1, gb=None, feat=((of,), (x4b,)))
        for j in range(ND):
            p.op("vector", lambda h, j=j: h.tensor_tensor(
                x4b[0:128, j * CH:(j + 1) * CH], x4b[0:128, j * CH:(j + 1) * CH],
                xcb[0:128, j * CH:(j + 1) * CH], op=ALU.add), reads=(xcb,), writes=(x4b,))
        emit_ln(p, cm, CH, lambda j: x4b[0:128, j * CH:(j + 1) * CH],
                lambda j: h0b[0:128, j * CH:(j + 1) * CH],
                sqs, stat, ps_s0, ps_s1, ps_bc0, ps_bc1, gb=None, feat=((x4b,), (h0b,)))
        p.dma_out(fm3(x4_o, c0, CH), x4b, src=x4b.t[:].rearrange("p (j t) -> p j t", j=ND))
        p.dma_out(fm3(h0_o, c0, CH), h0b, src=h0b.t[:].rearrange("p (j t) -> p j t", j=ND))
    return p


def build_stageH(p=None, io=None, debug=False):
    own = p is None
    if own:
        nc = bass.Bass("TRN2", target_bir_lowering=False)
        p = Prog(nc)
    else:
        nc = p.nc
    io = io or {}
    def dram(name, shape, dt, kind):
        if name in io:
            return io[name]
        return nc.dram_tensor(name, shape, dt, kind=kind).ap()
    CH = 256
    DH = 4096
    xh_i = dram("xh_i", [D, TC], F32, "ExternalInput")
    rwt = dram("rwt", [128, 32], F32, "ExternalInput")
    rbt = dram("rbt", [1, 4], F32, "ExternalInput")
    lgb = dram("lgb", [128, 8 * 8], F32, "ExternalInput")
    b1t = dram("b1t", [128, 32 * 4], F32, "ExternalInput")
    b2t = dram("b2t", [128, 8 * 4], F32, "ExternalInput")
    w1_i = dram("w1_i", [4 * D, DH], F32, "ExternalInput")
    w2_i = dram("w2_i", [4 * DH, D], F32, "ExternalInput")
    idt_i = dram("idt_i", [128, 128], F32, "ExternalInput")
    xo_o = dram("xo_o", [D, TC], F32, "ExternalOutput")
    xhat_d = dram("xhat_d", [D, TC], BF16, "Internal")
    xhat_db = Buf(None, "xhat_d")

    cm = io.get("cm") or Common(p)
    rwb = p.sbuf("rwb", [128, 32], F32)
    rbb = p.sbuf("rbb", [128, 4], F32)
    rbs = p.sbuf("rbs", [1, 4], F32)
    lgbv = p.sbuf("lgbv", [128, 8 * 8], F32)
    b1v = p.sbuf("b1v", [128, 32 * 4], F32)
    b2v = p.sbuf("b2v", [128, 8 * 4], F32)
    idt = p.sbuf("idt", [128, 128], F32)
    for buf, src in ((rwb, rwt), (rbs, rbt), (lgbv, lgb), (b1v, b1t), (b2v, b2t), (idt, idt_i)):
        p.dma_in(buf, src)
    crow = [p.sbuf(f"crow{e}", [1, TC], BF16) for e in range(4)]
    ones_row_b = p.sbuf("ones_row_b", [1, 128], BF16)
    p.op("gpsimd", lambda h: h.memset(ones_row_b.ap, 1.0), writes=(ones_row_b,))

    xcb = p.sbuf("xcb", [128, ND * CH], F32)
    xhatb = p.sbuf("xhatb", [128, ND * CH], BF16)
    sqs = p.sbuf("sqs", [128, ND * CH], F32)
    stat = p.sbuf("stat", [1, 3 * CH], F32)
    lg = p.sbuf("lg", [128, 4], F32)
    m1 = p.sbuf("m1", [128, 1], F32)
    el = p.sbuf("el", [128, 4], F32)
    msk = p.sbuf("msk", [128, 4], F32)
    p2t = p.sbuf("p2t", [128, 4], F32)
    comb = p.sbuf("comb", [128, 4], F32)
    den = p.sbuf("den", [128, 1], F32)

    w1b = [p.sbuf(f"w1b{k}", [128, DH], BF16) for k in range(ND)]
    w2b = [p.sbuf(f"w2b{k}", [128, D], BF16) for k in range(32)]
    xeb = p.sbuf("xeb", [128, ND * CH], BF16)
    hb = p.sbuf("hb", [128, 32 * CH], BF16)
    gout = p.sbuf("gout", [128, ND * CH], F32)
    gateb = p.sbuf("gateb", [128, CH], F32)

    ps_s0 = p.psum("ps_s0", [1, 512])
    ps_s1 = p.psum("ps_s1", [1, 512])
    ps_bc0 = p.psum("ps_bc0", [128, 512])
    ps_bc1 = p.psum("ps_bc1", [128, 512])
    pool = PsumPool(p, "pp", 4)

    psb = pool.next()
    p.op("tensor", lambda h, psb=psb: h.matmul(psb[0:128, 0:4], cm.ones_row_f[0:1, 0:128],
                                               rbs[0:1, 0:4], start=True, stop=True),
         reads=(cm.ones_row_f, rbs), writes=(psb,))
    p.op("vector", lambda h, psb=psb: h.tensor_copy(rbb.ap, psb[0:128, 0:4]), reads=(psb,), writes=(rbb,))

    NCH = TC // CH
    for ci in range(NCH):
        c0 = ci * CH
        p.dma_in(xcb, fm3(xh_i, c0, CH), dst=xcb.t[:].rearrange("p (j t) -> p j t", j=ND))
        emit_ln(p, cm, CH, lambda j: xcb[0:128, j * CH:(j + 1) * CH],
                lambda j: xhatb[0:128, j * CH:(j + 1) * CH],
                sqs, stat, ps_s0, ps_s1, ps_bc0, ps_bc1, gb=None, feat=((xcb,), (xhatb,)))
        p.dma_out(fm3(xhat_d, c0, CH), xhatb, src=xhatb.t[:].rearrange("p (j t) -> p j t", j=ND))
        xhat_db.ready = xhatb.users.get("dma:sync") or xhat_db.ready
        for tt in range(CH // 128):
            ps_l = pool.next()
            fns = [(lambda h, ki=ki, tt=tt, ps_l=ps_l: h.matmul(
                ps_l[0:128, 0:4], xcb[0:128, ki * CH + tt * 128:ki * CH + (tt + 1) * 128],
                rwb[0:128, ki * 4:(ki + 1) * 4], start=(ki == 0), stop=(ki == ND - 1)))
                for ki in range(ND)]
            p.chain("tensor", fns, reads=(xcb, rwb), writes=(ps_l,))
            p.op("vector", lambda h, ps_l=ps_l: h.tensor_tensor(lg.ap, ps_l[0:128, 0:4], rbb.ap, op=ALU.add),
                 reads=(ps_l, rbb), writes=(lg,))
            p.raw("vector", lambda h: h.drain())
            p.op("vector", lambda h: h.tensor_reduce(m1.ap, lg.ap, axis=AX.X, op=ALU.max),
                 reads=(lg,), writes=(m1,))
            p.raw("vector", lambda h: h.drain())
            p.op("vector", lambda h: h.tensor_scalar(msk.ap, lg.ap, m1.ap, None, op0=ALU.subtract),
                 reads=(lg, m1), writes=(msk,))
            p.raw("vector", lambda h: h.drain())
            if debug and ci == 0 and tt == 0:
                dbg_r = dram("dbg_r", [128, 8], F32, "ExternalOutput")
                dbgr = p.sbuf("dbgr", [128, 8], F32)
                p.op("vector", lambda h: h.tensor_copy(dbgr[0:128, 0:1], m1.ap), reads=(m1,), writes=(dbgr,))
                p.op("vector", lambda h: h.tensor_copy(dbgr[0:128, 1:5], msk.ap), reads=(msk,), writes=(dbgr,))
                p.dma_out(dbg_r, dbgr)
            p.op("scalar", lambda h: h.activation(el.ap, msk.ap, AF.Exp), reads=(msk,), writes=(el,))
            p.chain("scalar", [
                lambda h: h.activation(msk.ap, msk.ap, AF.Relu, scale=-1.0),
                lambda h: h.activation(msk.ap, msk.ap, AF.Sign),
            ], reads=(), writes=(msk,))
            p.op("vector", lambda h: h.tensor_tensor(p2t.ap, el.ap, msk.ap, op=ALU.mult),
                 reads=(el, msk), writes=(p2t,))
            p.raw("vector", lambda h: h.drain())
            p.op("vector", lambda h: h.tensor_reduce(m1.ap, p2t.ap, axis=AX.X, op=ALU.max),
                 reads=(p2t,), writes=(m1,))
            p.raw("vector", lambda h: h.drain())
            p.op("vector", lambda h: h.tensor_scalar(msk.ap, el.ap, m1.ap, None, op0=ALU.subtract),
                 reads=(el, m1), writes=(msk,))
            p.raw("vector", lambda h: h.drain())
            p.chain("scalar", [
                lambda h: h.activation(msk.ap, msk.ap, AF.Relu, scale=-1.0),
                lambda h: h.activation(msk.ap, msk.ap, AF.Sign),
            ], reads=(), writes=(msk,))
            p.op("vector", lambda h: h.tensor_tensor(p2t.ap, el.ap, msk.ap, op=ALU.mult),
                 reads=(el, msk), writes=(p2t,))
            p.raw("vector", lambda h: h.drain())
            p.op("vector", lambda h: h.tensor_tensor(comb.ap, el.ap, p2t.ap, op=ALU.subtract),
                 reads=(el, p2t), writes=(comb,))
            p.raw("vector", lambda h: h.drain())
            p.op("vector", lambda h: h.tensor_reduce(den.ap, comb.ap, axis=AX.X, op=ALU.add),
                 reads=(comb,), writes=(den,))
            p.raw("vector", lambda h: h.drain())
            p.chain("scalar", [
                lambda h: h.activation(den.ap, den.ap, AF.Ln),
                lambda h: h.activation(den.ap, den.ap, AF.Exp, scale=-1.0),
            ], reads=(), writes=(den,))
            p.op("vector", lambda h: h.tensor_scalar(comb.ap, comb.ap, den.ap, None, op0=ALU.mult),
                 reads=(den,), writes=(comb,))
            p.raw("vector", lambda h: h.drain())
            if debug and ci == 0 and tt == 0:
                lg_o = dram("lg_o", [128, 16], F32, "ExternalOutput")
                dbg0 = p.sbuf("dbg0", [128, 16], F32)
                p.op("vector", lambda h: h.tensor_copy(dbg0[0:128, 0:4], lg.ap), reads=(lg,), writes=(dbg0,))
                p.op("vector", lambda h: h.tensor_copy(dbg0[0:128, 4:8], el.ap), reads=(el,), writes=(dbg0,))
                p.op("vector", lambda h: h.tensor_copy(dbg0[0:128, 8:12], comb.ap), reads=(comb,), writes=(dbg0,))
                p.op("vector", lambda h: h.tensor_copy(dbg0[0:128, 12:13], den.ap), reads=(den,), writes=(dbg0,))
                p.dma_out(lg_o, dbg0)
            for e in range(4):
                ps_t = pool.next()
                p.op("tensor", lambda h, e=e, ps_t=ps_t: h.matmul(
                    ps_t[0:1, 0:128], comb[0:128, e:e + 1], idt.ap, start=True, stop=True),
                    reads=(comb, idt), writes=(ps_t,))
                p.op("vector", lambda h, e=e, ps_t=ps_t, c0=c0, tt=tt: h.tensor_copy(
                    crow[e][0:1, c0 + tt * 128:c0 + (tt + 1) * 128], ps_t[0:1, 0:128]),
                    reads=(ps_t,), writes=(crow[e],))

    for e in range(4):
        for k in range(ND):
            p.dma_in(w1b[k], w1_i[e * D + k * 128:e * D + (k + 1) * 128, :])
        for k in range(32):
            p.dma_in(w2b[k], w2_i[e * DH + k * 128:e * DH + (k + 1) * 128, :])
        for ci in range(NCH):
            c0 = ci * CH
            p.dma_in(xhatb, fm3(xhat_d, c0, CH), dst=xhatb.t[:].rearrange("p (j t) -> p j t", j=ND))
            p.op("gpsimd", lambda h: h.engine_nop(), reads=(xhat_db,)) if False else None
            p.dma_in(xcb, fm3(xh_i, c0, CH), dst=xcb.t[:].rearrange("p (j t) -> p j t", j=ND))
            for j in range(ND):
                p.op("vector", lambda h, j=j, e=e: h.tensor_scalar(
                    xeb[0:128, j * CH:(j + 1) * CH], xhatb[0:128, j * CH:(j + 1) * CH],
                    lgbv[0:128, (2 * e) * 8 + j:(2 * e) * 8 + j + 1],
                    lgbv[0:128, (2 * e + 1) * 8 + j:(2 * e + 1) * 8 + j + 1],
                    op0=ALU.mult, op1=ALU.add), reads=(xhatb, lgbv), writes=(xeb,))

            def evac_h(mj, ps, e=e):
                p.op("scalar", lambda h, mj=mj, ps=ps, e=e: h.activation(
                    hb[0:128, mj * CH:(mj + 1) * CH], ps[0:128, 0:CH], AF.Gelu,
                    bias=b1v[0:128, e * 32 + mj:e * 32 + mj + 1]), reads=(ps, b1v), writes=(hb,))
            emit_mm(p, pool, ND, lambda k: xeb[0:128, k * CH:(k + 1) * CH],
                    lambda k, mj: w1b[k][0:128, mj * 128:(mj + 1) * 128],
                    w1b, (xeb,), evac_h, range(32), CH)
            ps_g = pool.next()
            p.op("tensor", lambda h, e=e, c0=c0, ps_g=ps_g: h.matmul(
                ps_g[0:128, 0:CH], ones_row_b[0:1, 0:128], crow[e][0:1, c0:c0 + CH],
                start=True, stop=True), reads=(ones_row_b, crow[e]), writes=(ps_g,))
            p.op("vector", lambda h, ps_g=ps_g: h.tensor_copy(gateb[0:128, 0:CH], ps_g[0:128, 0:CH]),
                 reads=(ps_g,), writes=(gateb,))

            def evac_y(mj, ps, e=e):
                p.op("vector", lambda h, mj=mj, ps=ps, e=e: h.tensor_scalar(
                    gout[0:128, mj * CH:(mj + 1) * CH], ps[0:128, 0:CH],
                    b2v[0:128, e * 8 + mj:e * 8 + mj + 1], None, op0=ALU.add),
                    reads=(ps, b2v), writes=(gout,))
                p.op("vector", lambda h, mj=mj: h.tensor_tensor(
                    gout[0:128, mj * CH:(mj + 1) * CH], gout[0:128, mj * CH:(mj + 1) * CH],
                    xcb[0:128, mj * CH:(mj + 1) * CH], op=ALU.add), reads=(xcb,), writes=(gout,))
                p.op("vector", lambda h, mj=mj: h.tensor_tensor(
                    gout[0:128, mj * CH:(mj + 1) * CH], gout[0:128, mj * CH:(mj + 1) * CH],
                    gateb[0:128, 0:CH], op=ALU.mult), reads=(gateb,), writes=(gout,))
            emit_mm(p, pool, 32, lambda k: hb[0:128, k * CH:(k + 1) * CH],
                    lambda k, mj: w2b[k][0:128, mj * 128:(mj + 1) * 128],
                    w2b, (hb,), evac_y, range(ND), CH)
            p.dma_out(fm3(xo_o, c0, CH), gout, src=gout.t[:].rearrange("p (j t) -> p j t", j=ND),
                      accum=(e > 0), eng="gpsimd" if e > 0 else "sync")
            if debug and e == 0 and ci == 0:
                xhd_o = dram("xhd_o", [D, CH], F32, "ExternalOutput")
                xeb_o = dram("xeb_o", [D, CH], F32, "ExternalOutput")
                hb_o = dram("hb_o", [DH, CH], F32, "ExternalOutput")
                gate_o = dram("gate_o", [128, CH], F32, "ExternalOutput")
                dbg1 = p.sbuf("dbg1", [128, ND * CH], F32)

                for j in range(ND):
                    p.op("vector", lambda h, j=j: h.tensor_copy(
                        dbg1[0:128, j * CH:(j + 1) * CH], xhatb[0:128, j * CH:(j + 1) * CH]),
                        reads=(xhatb,), writes=(dbg1,))
                p.dma_out(fm3(xhd_o, 0, CH), dbg1, src=dbg1.t[:].rearrange("p (j t) -> p j t", j=ND))
                for j in range(ND):
                    p.op("vector", lambda h, j=j: h.tensor_copy(
                        dbg1[0:128, j * CH:(j + 1) * CH], xeb[0:128, j * CH:(j + 1) * CH]),
                        reads=(xeb,), writes=(dbg1,))
                p.dma_out(fm3(xeb_o, 0, CH), dbg1, src=dbg1.t[:].rearrange("p (j t) -> p j t", j=ND))
                for j in range(4):
                    p.op("vector", lambda h, j=j: h.tensor_copy(
                        dbg1[0:128, j * CH:(j + 1) * CH], hb[0:128, j * CH:(j + 1) * CH]),
                        reads=(hb,), writes=(dbg1,))
                p.dma_out(hb_o.rearrange("(j p) t -> p j t", p=128)[:, 0:4, :], dbg1,
                          src=dbg1.t[:, 0:4 * CH].rearrange("p (j t) -> p j t", j=4))
                p.dma_out(gate_o, gateb)
    return p


def build_stageI(p=None, io=None):
    own = p is None
    if own:
        nc = bass.Bass("TRN2", target_bir_lowering=False)
        p = Prog(nc)
    else:
        nc = p.nc
    io = io or {}
    def dram(name, shape, dt, kind):
        if name in io:
            return io[name]
        return nc.dram_tensor(name, shape, dt, kind=kind).ap()
    CH = 512
    x4_i = dram("x4_i", [D, TC], F32, "ExternalInput")
    h4_i = dram("h4_i", [D, TC], F32, "ExternalInput")
    x5_o = dram("x5_o", [D, TC], F32, "ExternalOutput")
    cm = io.get("cm") or Common(p)
    x4b = p.sbuf("x4b", [128, ND * CH], F32)
    h4b = p.sbuf("h4b", [128, ND * CH], F32)
    ob = p.sbuf("ob", [128, ND * CH], F32)
    sqs = p.sbuf("sqs", [128, ND * CH], F32)
    stat = p.sbuf("stat", [1, 3 * CH], F32)
    ps_s0 = p.psum("ps_s0", [1, 512])
    ps_s1 = p.psum("ps_s1", [1, 512])
    ps_bc0 = p.psum("ps_bc0", [128, 512])
    ps_bc1 = p.psum("ps_bc1", [128, 512])
    for ci in range(TC // CH):
        c0 = ci * CH
        p.dma_in(x4b, fm3(x4_i, c0, CH), dst=x4b.t[:].rearrange("p (j t) -> p j t", j=ND))
        p.dma_in(h4b, fm3(h4_i, c0, CH), dst=h4b.t[:].rearrange("p (j t) -> p j t", j=ND))
        emit_ln(p, cm, CH, lambda j: h4b[0:128, j * CH:(j + 1) * CH],
                lambda j: ob[0:128, j * CH:(j + 1) * CH],
                sqs, stat, ps_s0, ps_s1, ps_bc0, ps_bc1, gb=None, feat=((h4b,), (ob,)))
        for j in range(ND):
            p.op("vector", lambda h, j=j: h.tensor_tensor(
                ob[0:128, j * CH:(j + 1) * CH], ob[0:128, j * CH:(j + 1) * CH],
                x4b[0:128, j * CH:(j + 1) * CH], op=ALU.add), reads=(x4b,), writes=(ob,))
        p.dma_out(fm3(x5_o, c0, CH), ob, src=ob.t[:].rearrange("p (j t) -> p j t", j=ND))
    return p


def _perm1024():
    perm = np.zeros(1024, np.int64)
    for g in range(8):
        for r in range(128):
            if r < 32:
                h, d = 2 * g, r
            elif r < 64:
                h, d = 2 * g + 1, r - 32
            elif r < 96:
                h, d = 2 * g, 32 + r - 64
            else:
                h, d = 2 * g + 1, 32 + r - 96
            perm[g * 128 + r] = h * 64 + d
    return perm


def _head_rows(h):
    g, sub = h // 2, h % 2
    return np.concatenate([np.arange(g * 128 + sub * 32, g * 128 + sub * 32 + 32),
                           np.arange(g * 128 + 64 + sub * 32, g * 128 + 64 + sub * 32 + 32)])


def _rope_tables(s0):
    inv = 1.0 / (10000.0 ** (np.arange(32, dtype=np.float64) / 32.0))
    pos = np.arange(s0, s0 + TC, dtype=np.float64)
    ang = (pos[None, :] * inv[:, None])
    cost = np.cos(ang).astype(np.float32)
    sint = np.sin(ang).astype(np.float32)
    cost = np.concatenate([cost, cost, cost, cost], axis=0)
    sint = np.concatenate([sint, sint, sint, sint], axis=0)
    return np.ascontiguousarray(cost), np.ascontiguousarray(sint)


def _v8(v):
    return np.ascontiguousarray(np.asarray(v, np.float32).reshape(8, 128).T)


def _v32(v):
    return np.ascontiguousarray(np.asarray(v, np.float32).reshape(32, 128).T)


def _run(build_fn, maps):
    p = build_fn()
    nc = _finish(p)
    return run_bass_kernel_spmd(nc, maps, core_ids=list(range(NCORES))).results


def kernel_unfused(**inp):
    f32 = lambda a: np.ascontiguousarray(np.asarray(a, np.float32))
    x = f32(inp["L0_attn_x"])

    vecs = np.concatenate([_v8(inp[f"L0_attn_{n}"]) for n in
                           ("ln1_g", "ln1_b", "mix_k", "mix_v", "mix_r", "time_first")], axis=1)
    wk, wv, wr = f32(inp["L0_attn_Wk"]), f32(inp["L0_attn_Wv"]), f32(inp["L0_attn_Wr"])
    mapsA = []
    for c in range(NCORES):
        b, hh = c // 2, c % 2
        xs = x[b, hh * TC:(hh + 1) * TC, :].T
        bound = np.zeros((D, 1), np.float32) if hh == 0 else x[b, hh * TC - 1, :].reshape(D, 1)
        pval = np.full((128, 1), float(hh), np.float32)
        fmask = np.full((128, 1), 1.0 - hh, np.float32)
        mapsA.append(dict(xe=np.ascontiguousarray(np.concatenate([bound, xs], axis=1)),
                          pval=pval, fmask=fmask, vecs=np.ascontiguousarray(vecs),
                          wk=wk, wv=wv, wr=wr))
    resA = _run(build_stageA, mapsA)

    td = f32(inp["L0_attn_time_decay"])
    mapsB = []
    for c in range(NCORES):
        b, hh = c // 2, c % 2
        u_full = np.concatenate([resA[2 * b]["u_o"], resA[2 * b + 1]["u_o"]], axis=1)
        ek_full = np.concatenate([resA[2 * b]["ek_o"], resA[2 * b + 1]["ek_o"]], axis=1)
        lo = hh * 512
        mapsB.append(dict(u_s=np.ascontiguousarray(u_full[lo:lo + 512]),
                          ek_s=np.ascontiguousarray(ek_full[lo:lo + 512]),
                          td_s=np.ascontiguousarray(td[lo:lo + 512].reshape(4, 128).T)))
    resB = _run(build_stageB, mapsB)

    wo0 = f32(inp["L0_attn_Wo"])
    wkv_fm = {}
    for c in range(NCORES):
        b, hh = c // 2, c % 2
        full = np.concatenate([resB[2 * b]["wkv_o"], resB[2 * b + 1]["wkv_o"]], axis=0)
        wkv_fm[c] = full[:, hh * TC:(hh + 1) * TC]
    mapsC1 = []
    for c in range(NCORES):
        b, hh = c // 2, c % 2
        z = np.zeros((D, 1), np.float32)
        if hh == 0:
            l0e = np.concatenate([z, resA[c]["ln0_o"]], axis=1)
            re_ = np.concatenate([z, resA[c]["r_o"]], axis=1)
            we_ = np.concatenate([z, wkv_fm[c]], axis=1)
        else:
            l0e = np.concatenate([resA[c - 1]["ln0_o"][:, -1:], resA[c]["ln0_o"]], axis=1)
            re_ = np.concatenate([resA[c - 1]["r_o"][:, -1:], resA[c]["r_o"]], axis=1)
            we_ = np.concatenate([wkv_fm[c - 1][:, -1:], wkv_fm[c]], axis=1)
        mapsC1.append(dict(ln0_e=np.ascontiguousarray(l0e), r_e=np.ascontiguousarray(re_),
                           wkv_e=np.ascontiguousarray(we_), wo=wo0))
    resC1 = _run(build_stageC1, mapsC1)

    vecs2 = np.concatenate([_v8(inp["L0_attn_ln2_g"]), _v8(inp["L0_attn_ln2_b"])], axis=1)
    wfk, wfr, wfv = f32(inp["L0_attn_Wfk"]), f32(inp["L0_attn_Wfr"]), f32(inp["L0_attn_Wfv"])
    mapsC2 = []
    for c in range(NCORES):
        b, hh = c // 2, c % 2
        mapsC2.append(dict(x1_e=np.ascontiguousarray(resC1[c]["x1_o"]),
                           x_fm=np.ascontiguousarray(x[b, hh * TC:(hh + 1) * TC, :].T),
                           pval=np.full((128, 1), float(hh), np.float32),
                           vecs=np.ascontiguousarray(vecs2), wfk=wfk, wfr=wfr, wfv=wfv))
    resC2 = _run(build_stageC2, mapsC2)

    scl = np.concatenate([_v32(inp["L0_ffn_w1_s"]), _v32(inp["L0_ffn_w2_s"]),
                          _v32(inp["L0_ffn_h1_s"]), _v32(inp["L0_ffn_h2_s"]),
                          _v8(inp["L0_ffn_w3_s"])], axis=1)
    w1t = np.ascontiguousarray(f32(inp["L0_ffn_w1_w"]).T)
    w2t = np.ascontiguousarray(f32(inp["L0_ffn_w2_w"]).T)
    h1t = np.ascontiguousarray(f32(inp["L0_ffn_h1_w"]).T)
    h2t = np.ascontiguousarray(f32(inp["L0_ffn_h2_w"]).T)
    w3t = np.ascontiguousarray(f32(inp["L0_ffn_w3_w"]).T)
    mapsD = [dict(x2_i=np.ascontiguousarray(resC2[c]["x2_o"]), w1t=w1t, w2t=w2t,
                  h1t=h1t, h2t=h2t, w3t=w3t, scl=np.ascontiguousarray(scl))
             for c in range(NCORES)]
    resD = _run(build_stageD, mapsD)

    wqkv = f32(inp["L1_attn_Wqkv"])
    perm = _perm1024()
    wperm = np.ascontiguousarray(np.concatenate(
        [wqkv[:, 0:1024][:, perm], wqkv[:, 1024:2048][:, perm], wqkv[:, 2048:3072]], axis=1))
    sel = np.zeros((128, 16 * 8), np.float32)
    selt = np.zeros((16, 128 * 8), np.float32)
    for j in range(8):
        for r in range(128):
            h = 2 * j + (1 if (32 <= r < 64 or r >= 96) else 0)
            sel[r, j * 16 + h] = 1.0
            selt[h, j * 128 + r] = 1.0
    mapsE = []
    for c in range(NCORES):
        hh = c % 2
        cost, sint = _rope_tables(hh * TC)
        mapsE.append(dict(x3_i=np.ascontiguousarray(resD[c]["x3_o"]), wqkv=wperm,
                          cost=cost, sint=sint, selm=sel, selt=selt))
    resE = _run(build_stageE, mapsE)

    mapsF = []
    for c in range(NCORES):
        b, hg = c // 2, c % 2
        qfull = np.concatenate([np.asarray(resE[2 * b]["qb_o"]), np.asarray(resE[2 * b + 1]["qb_o"])], axis=1)
        kfull = np.concatenate([np.asarray(resE[2 * b]["kb_o"]), np.asarray(resE[2 * b + 1]["kb_o"])], axis=1)
        vfull = np.concatenate([np.asarray(resE[2 * b]["vb_o"]), np.asarray(resE[2 * b + 1]["vb_o"])], axis=1)
        qh = np.zeros((512, S), ml_dtypes.bfloat16)
        kh = np.zeros((512, S), ml_dtypes.bfloat16)
        vx = np.zeros((128, 8 * 32 * 65), ml_dtypes.bfloat16)
        ones_col = np.ones((128, 1), ml_dtypes.bfloat16)
        for i in range(8):
            h = hg * 8 + i
            rows = _head_rows(h)
            qh[i * 64:(i + 1) * 64] = qfull[rows]
            kh[i * 64:(i + 1) * 64] = kfull[rows]
            vh = vfull[h * 64:(h + 1) * 64]
            for ts in range(32):
                base = (i * 32 + ts) * 65
                vx[:, base:base + 64] = vh[:, ts * 128:(ts + 1) * 128].T
                vx[:, base + 64:base + 65] = ones_col
        mapsF.append(dict(qh_i=qh, kh_i=kh, vx_i=vx))
    resF = _run(build_stageF, mapsF)

    wo1 = f32(inp["L1_attn_Wo"])
    bo1 = _v8(inp["L1_attn_bo"])
    mapsG = []
    for c in range(NCORES):
        b, hh = c // 2, c % 2
        ao_fm = np.concatenate([np.asarray(resF[2 * b]["ao_o"]), np.asarray(resF[2 * b + 1]["ao_o"])], axis=0)
        mapsG.append(dict(ao_i=np.ascontiguousarray(ao_fm[:, hh * TC:(hh + 1) * TC]),
                          x3_i=np.ascontiguousarray(resD[c]["x3_o"]), wo=wo1, bo_v=bo1))
    resG = _run(build_stageG, mapsG)

    idt = np.eye(128, dtype=np.float32)
    xh = {c: resG[c]["h0_o"] for c in range(NCORES)}
    for l in range(4):
        rw = f32(inp["L1_ffn_router_w"])[l]
        rb = f32(inp["L1_ffn_router_b"])[l]
        g_ = f32(inp["L1_ffn_ln_g"])[l]
        b_ = f32(inp["L1_ffn_ln_b"])[l]
        w1 = f32(inp["L1_ffn_w1"])[l]
        b1 = f32(inp["L1_ffn_b1"])[l]
        w2 = f32(inp["L1_ffn_w2"])[l]
        b2 = f32(inp["L1_ffn_b2"])[l]
        rwt = np.ascontiguousarray(rw.reshape(8, 128, 4).transpose(1, 0, 2).reshape(128, 32))
        lgb = np.zeros((128, 64), np.float32)
        for e in range(4):
            lgb[:, (2 * e) * 8:(2 * e) * 8 + 8] = _v8(g_[e])
            lgb[:, (2 * e + 1) * 8:(2 * e + 1) * 8 + 8] = _v8(b_[e])
        b1t = np.ascontiguousarray(np.concatenate([_v32(b1[e]) for e in range(4)], axis=1))
        b2t = np.ascontiguousarray(np.concatenate([_v8(b2[e]) for e in range(4)], axis=1))
        w1_i = np.ascontiguousarray(w1.reshape(4 * D, 4096))
        w2_i = np.ascontiguousarray(w2.reshape(4 * 4096, D))
        mapsH = [dict(xh_i=np.ascontiguousarray(xh[c]), rwt=rwt,
                      rbt=np.ascontiguousarray(rb.reshape(1, 4)), lgb=lgb, b1t=b1t, b2t=b2t,
                      w1_i=w1_i, w2_i=w2_i, idt_i=idt) for c in range(NCORES)]
        resH = _run(build_stageH, mapsH)
        xh = {c: resH[c]["xo_o"] for c in range(NCORES)}

    mapsI = [dict(x4_i=np.ascontiguousarray(resG[c]["x4_o"]), h4_i=np.ascontiguousarray(xh[c]))
             for c in range(NCORES)]
    resI = _run(build_stageI, mapsI)

    out = np.empty((B, S, D), np.float32)
    for c in range(NCORES):
        b, hh = c // 2, c % 2
        out[b, hh * TC:(hh + 1) * TC, :] = resI[c]["x5_o"].T
    return out


RG_PAIR = [[0, 1], [2, 3], [4, 5], [6, 7]]
RG_ALL = [[0, 1, 2, 3, 4, 5, 6, 7]]


def emit_scan_fused(p, cm, io):
    nc = p.nc
    u_d, ek_d, tdv, pval = io["u_d"], io["ek_d"], io["tdv"], io["pval"]
    pay_d, payg, payoff = io["pay_d"], io["payg"], io["payoff"]
    ln0_e, r_e, wkv_e = io["ln0_e"], io["r_e"], io["wkv_e"]
    a_d, b_d = io["a_d"], io["b_d"]
    cc_sem = p.sem("cc_scan")

    td = p.sbuf("sc_td", [128, 8], F32)
    pv = p.sbuf("sc_pv", [128, 1], F32)
    ew8 = p.sbuf("sc_ew8", [128, 8], F32)
    ntd = p.sbuf("sc_ntd", [128, 8], F32)
    pay = p.sbuf("sc_pay", [128, 32], F32)
    ones_t = p.sbuf("sc_ones", [128, TC], F32)
    ew_t = p.sbuf("sc_ewt", [128, TC], F32)
    ut = p.sbuf("sc_u", [128, TC], F32)
    ekt = p.sbuf("sc_ek", [128, TC], F32)
    at = p.sbuf("sc_a", [128, TC], F32)
    bt = p.sbuf("sc_b", [128, TC], F32)

    p.dma_in(td, tdv)
    p.dma_in(pv, pval)
    p.op("gpsimd", lambda h: h.memset(ones_t.ap, 1.0), writes=(ones_t,))
    p.chain("scalar", [
        lambda h: h.activation(ntd.ap, td.ap, AF.Exp),
    ], reads=(td,), writes=(ntd,))
    p.chain("scalar", [
        lambda h: h.activation(ew8.ap, ntd.ap, AF.Exp, scale=-1.0),
    ], reads=(ntd,), writes=(ew8,))
    p.op("vector", lambda h: h.tensor_scalar(ntd.ap, ntd.ap, -1.0, None, op0=ALU.mult),
         reads=(), writes=(ntd,))
    p.raw("vector", lambda h: h.drain())

    for j in range(ND):
        p.dma_in(ut, u_d[j * 128:(j + 1) * 128, :])
        p.dma_in(ekt, ek_d[j * 128:(j + 1) * 128, :])
        p.op("vector", lambda h, j=j: h.tensor_scalar(ew_t.ap, ones_t.ap, ew8[0:128, j:j + 1], None, op0=ALU.mult),
             reads=(ones_t, ew8), writes=(ew_t,))
        p.op("vector", lambda h: h.tensor_tensor_scan(at.ap, ew_t.ap, ut.ap, 0.0, op0=ALU.mult, op1=ALU.add),
             reads=(ew_t, ut), writes=(at,))
        p.op("vector", lambda h: h.tensor_tensor_scan(bt.ap, ew_t.ap, ekt.ap, 0.0, op0=ALU.mult, op1=ALU.add),
             reads=(ew_t, ekt), writes=(bt,))
        p.op("vector", lambda h, j=j: h.tensor_copy(pay[0:128, j:j + 1], at[0:128, TC - 1:TC]),
             reads=(at,), writes=(pay,))
        p.op("vector", lambda h, j=j: h.tensor_copy(pay[0:128, 8 + j:8 + j + 1], bt[0:128, TC - 1:TC]),
             reads=(bt,), writes=(pay,))
        p.dma_out(a_d[j * 128:(j + 1) * 128, :], at)
        p.dma_out(b_d[j * 128:(j + 1) * 128, :], bt)
    lr = p.sbuf("sc_lr", [128, 16], F32)
    p.op("gpsimd", lambda h: h.dma_start(
        lr[0:128, 0:8], ln0_e.rearrange("(j p) t -> p j t", p=128)[:, :, TC:TC + 1].rearrange("p j o -> p (j o)"),
        allow_slow_non_contiguous=True), writes=(lr,), dma=True, dma_key=lr)
    p.op("gpsimd", lambda h: h.dma_start(
        lr[0:128, 8:16], r_e.rearrange("(j p) t -> p j t", p=128)[:, :, TC:TC + 1].rearrange("p j o -> p (j o)"),
        allow_slow_non_contiguous=True), writes=(lr,), dma=True, dma_key=lr)
    p.op("vector", lambda h: h.tensor_copy(pay[0:128, 16:32], lr.ap), reads=(lr,), writes=(pay,))
    p.raw("vector", lambda h: h.drain())
    ev_pay = p.dma_out(pay_d, pay)
    p._emit_wait("gpsimd", ev_pay)
    p.op("gpsimd", lambda h: h.collective_compute(
        "AllGather", ALU.bypass, replica_groups=RG_ALL, ins=[pay_d.opt()], outs=[payg.opt()]),
        sem=cc_sem, k=1)
    fl_in, fl_out = io["fence1_in"], io["fence1_out"]
    p.op("gpsimd", lambda h: h.collective_compute(
        "AllReduce", ALU.add, replica_groups=RG_ALL, ins=[fl_in.opt()], outs=[fl_out.opt()]),
        sem=cc_sem, k=1)
    ev_cc = Ev(cc_sem, cc_sem.n, "cc")
    pin = p.sbuf("sc_pin", [128, 32], F32)
    pofft = p.sbuf("sc_poff", [1, 1], mybir.dt.int32)
    p.dma_in(pofft, payoff)
    p._emit_wait("gpsimd", ev_cc)

    def _pin_dma(h):
        with h.register("sc_r0") as r0:
            h.reg_load(r0, pofft[0:1, 0:1])
            return h.dma_start(pin.ap, bass.AP(payg.tensor, r0, [[32, 128], [1, 32]]))
    p.op("gpsimd", _pin_dma, reads=(pofft,), writes=(pin,), dma=True, dma_key=pin)
    pvinv = p.sbuf("sc_pvinv", [128, 1], F32)
    p.op("vector", lambda h: h.tensor_scalar(pvinv.ap, pv.ap, -1.0, 1.0, op0=ALU.mult, op1=ALU.add),
         reads=(pv,), writes=(pvinv,))
    p.raw("vector", lambda h: h.drain())
    bsafe = p.sbuf("sc_bsafe", [128, 8], F32)
    p.op("vector", lambda h: h.tensor_scalar(bsafe.ap, pin[0:128, 8:16], pv[0:128, 0:1], pvinv[0:128, 0:1],
                                             op0=ALU.mult, op1=ALU.add),
         reads=(pin, pv, pvinv), writes=(bsafe,))
    p.op("vector", lambda h: h.tensor_scalar(pin.ap, pin.ap, pv[0:128, 0:1], None, op0=ALU.mult),
         reads=(pv,), writes=(pin,))
    p.raw("vector", lambda h: h.drain())
    p.op("sync", lambda h: h.dma_start(
        ln0_e.rearrange("(j p) t -> p j t", p=128)[:, :, 0:1].rearrange("p j o -> p (j o)"),
        pin[0:128, 16:24], allow_slow_non_contiguous=True), reads=(pin,), dma=True, dma_key=pin)
    p.op("sync", lambda h: h.dma_start(
        r_e.rearrange("(j p) t -> p j t", p=128)[:, :, 0:1].rearrange("p j o -> p (j o)"),
        pin[0:128, 8 + 16:16 + 16], allow_slow_non_contiguous=True), reads=(pin,), dma=True, dma_key=pin)
    wb0 = p.sbuf("sc_wb0", [128, 8], F32)
    p.op("vector", lambda h: h.tensor_scalar(wb0.ap, bsafe.ap, WKV_EPS, None, op0=ALU.add),
         reads=(bsafe,), writes=(wb0,))
    p.raw("vector", lambda h: h.drain())
    p.chain("scalar", [
        lambda h: h.activation(wb0.ap, wb0.ap, AF.Ln),
        lambda h: h.activation(wb0.ap, wb0.ap, AF.Exp, scale=-1.0),
    ], reads=(), writes=(wb0,))
    p.op("vector", lambda h: h.tensor_tensor(wb0.ap, wb0.ap, pin[0:128, 0:8], op=ALU.mult),
         reads=(pin,), writes=(wb0,))
    p.raw("vector", lambda h: h.drain())
    p.op("sync", lambda h: h.dma_start(
        wkv_e.rearrange("(j p) t -> p j t", p=128)[:, :, 0:1].rearrange("p j o -> p (j o)"),
        wb0.ap, allow_slow_non_contiguous=True), reads=(wb0,), dma=True, dma_key=wb0)
    if io.get("debug_taps"):
        pay_tap = io["pay_tap"]
        payg_tap = io["payg_tap"]
        p.dma_out(pay_tap, pay)
        pgt = p.sbuf("sc_pgt", [128, 8 * 32], F32)
        for r in range(8):
            p.dma_in(pgt, payg[r * 128:(r + 1) * 128, :], dst=pgt.t[0:128, r * 32:(r + 1) * 32])
        p.dma_out(payg_tap, pgt)
        pin_tap = io["pin_tap"]
        p.dma_out(pin_tap, pin)
    iot = p.sbuf("sc_iot", [128, TC], F32)
    p.op("gpsimd", lambda h: h.iota(iot.ap, pattern=[[1, TC]], base=1, channel_multiplier=0,
                                    allow_small_or_imprecise_dtypes=True), writes=(iot,))
    etab = p.sbuf("sc_etab", [128, TC], F32)
    ca = p.sbuf("sc_ca", [128, 16], F32)
    p.op("vector", lambda h: h.tensor_copy(ca.ap, pin[0:128, 0:16]), reads=(pin,), writes=(ca,))
    p.raw("vector", lambda h: h.drain())
    wt = p.sbuf("sc_wkv", [128, TC], F32)
    for j in range(ND):
        p.dma_in(at, a_d[j * 128:(j + 1) * 128, :])
        p.dma_in(bt, b_d[j * 128:(j + 1) * 128, :])
        p.op("vector", lambda h, j=j: h.tensor_scalar(etab.ap, iot.ap, ntd[0:128, j:j + 1], None, op0=ALU.mult),
             reads=(iot, ntd), writes=(etab,))
        p.op("scalar", lambda h: h.activation(etab.ap, etab.ap, AF.Exp), reads=(), writes=(etab,))
        p.op("vector", lambda h, j=j: h.scalar_tensor_tensor(
            at.ap, etab.ap, ca[0:128, j:j + 1], at.ap, op0=ALU.mult, op1=ALU.add),
            reads=(etab, ca), writes=(at,))
        p.op("vector", lambda h, j=j: h.scalar_tensor_tensor(
            bt.ap, etab.ap, ca[0:128, 8 + j:8 + j + 1], bt.ap, op0=ALU.mult, op1=ALU.add),
            reads=(etab, ca), writes=(bt,))
        p.op("vector", lambda h: h.tensor_scalar(bt.ap, bt.ap, WKV_EPS, None, op0=ALU.add),
             reads=(), writes=(bt,))
        p.chain("scalar", [
            lambda h: h.activation(bt.ap, bt.ap, AF.Ln),
            lambda h: h.activation(bt.ap, bt.ap, AF.Exp, scale=-1.0),
        ], reads=(), writes=(bt,))
        p.op("vector", lambda h: h.tensor_tensor(wt.ap, at.ap, bt.ap, op=ALU.mult),
             reads=(at, bt), writes=(wt,))
        p.dma_out(wkv_e[j * 128:(j + 1) * 128, 1:1 + TC], wt)


def emit_attnF2(p, cm, io):
    nc = p.nc
    qb_d, kvm, ao_d, idt64 = io["qb_d"], io["kv_my"], io["ao_d"], io["idt64"]
    TQ = 512
    idb = p.sbuf("f2_id", [64, 64], BF16)
    p.dma_in(idb, idt64)
    kb = p.sbuf("f2_k", [64, S], BF16)
    qb = p.sbuf("f2_q", [64, TC], BF16)
    vf = p.sbuf("f2_v", [64, S], BF16)
    vext = p.sbuf("f2_vx", [128, 32 * 65], BF16)
    pb = [p.sbuf(f"f2_p{i}", [128, TQ], BF16) for i in range(2)]
    dn = p.sbuf("f2_dn", [1, TQ], F32)
    af = p.sbuf("f2_af", [64, TQ], F32)
    aob = p.sbuf("f2_ao", [64, TQ], BF16)
    ps_qk = PsumPool(p, "f2qk", 2)
    ps_tr = PsumPool(p, "f2tr", 2)
    ps_av = p.psum("f2_av", [65, 512])
    ps_bc = p.psum("f2_bc", [64, 512])

    for hh in range(16):
        g, sub = hh // 2, hh % 2
        r0 = g * 128 + sub * 32
        r1 = g * 128 + 64 + sub * 32
        p.dma_in(kb, kvm[r0:r0 + 32, :], dst=kb.t[0:32, :])
        p.op("gpsimd", lambda h, r1=r1: h.dma_start(kb.t[32:64, :], kvm[r1:r1 + 32, :]),
             writes=(kb,), dma=True, dma_key=kb)
        p.dma_in(vf, kvm[1024 + hh * 64:1024 + (hh + 1) * 64, :])
        p.dma_in(qb, qb_d[r0:r0 + 32, :], dst=qb.t[0:32, :])
        p.op("gpsimd", lambda h, r1=r1: h.dma_start(qb.t[32:64, :], qb_d[r1:r1 + 32, :]),
             writes=(qb,), dma=True, dma_key=qb)
        p.op("gpsimd", lambda h: h.memset(vext.ap, 1.0), writes=(vext,))
        for ts in range(32):
            ps = ps_tr.next()
            p.chain("tensor", [lambda h, ts=ts, ps=ps: h.matmul(
                ps[0:128, 0:64], vf[0:64, ts * 128:(ts + 1) * 128], idb.ap, start=True, stop=True)],
                reads=(vf, idb), writes=(ps,))
            p.op("vector", lambda h, ts=ts, ps=ps: h.tensor_copy(
                vext[0:128, ts * 65:ts * 65 + 64], ps[0:128, 0:64]), reads=(ps,), writes=(vext,))
        for tq in range(TC // TQ):
            q_ap = qb[0:64, tq * TQ:(tq + 1) * TQ]
            for ts in range(32):
                ps = ps_qk.next()
                p.chain("tensor", [lambda h, ts=ts, q_ap=q_ap, ps=ps: h.matmul(
                    ps[0:128, 0:TQ], kb[0:64, ts * 128:(ts + 1) * 128], q_ap,
                    start=True, stop=True)], reads=(kb, qb), writes=(ps,))
                pbuf = pb[ts % 2]
                p.op("scalar", lambda h, ps=ps, pbuf=pbuf: h.activation(
                    pbuf.ap, ps[0:128, 0:TQ], AF.Exp, scale=0.125), reads=(ps,), writes=(pbuf,))
                p.chain("tensor", [lambda h, ts=ts, pbuf=pbuf: h.matmul(
                    ps_av[0:65, 0:TQ], vext[0:128, ts * 65:(ts + 1) * 65], pbuf.ap,
                    start=(ts == 0), stop=(ts == 31))], reads=(vext, pbuf), writes=(ps_av,))
            p.op("vector", lambda h: h.tensor_copy(dn.ap, ps_av[64:65, 0:TQ]), reads=(ps_av,), writes=(dn,))
            p.chain("scalar", [
                lambda h: h.activation(dn.ap, dn.ap, AF.Ln),
                lambda h: h.activation(dn.ap, dn.ap, AF.Exp, scale=-1.0),
            ], reads=(), writes=(dn,))
            p.op("scalar", lambda h: h.copy(af.ap, ps_av[0:64, 0:TQ]), reads=(ps_av,), writes=(af,))
            p.op("tensor", lambda h: h.matmul(ps_bc[0:64, 0:TQ], cm.ones_row_f[0:1, 0:64], dn.ap,
                                              start=True, stop=True),
                 reads=(cm.ones_row_f, dn), writes=(ps_bc,))
            p.op("vector", lambda h: h.tensor_tensor(aob.ap, af.ap, ps_bc[0:64, 0:TQ], op=ALU.mult),
                 reads=(af, ps_bc), writes=(aob,))
            p.dma_out(ao_d[hh * 64:(hh + 1) * 64, tq * TQ:(tq + 1) * TQ], aob)


def build_fused(debug=False):
    nc = bass.Bass("TRN2", target_bir_lowering=False)
    p = Prog(nc)
    cm = Common(p)
    TCE = TC + 1
    DH = 4096
    dI = lambda n, s, dt=F32: nc.dram_tensor(n, s, dt, kind="ExternalInput").ap()
    _no_tap = {"pay_d", "payg", "kv_d", "kvg", "fence1_in", "fence2_in"}
    dN = lambda n, s, dt=F32: nc.dram_tensor(
        n, s, dt, kind="ExternalOutput" if (debug and n not in _no_tap) else "Internal").ap()
    xe = dI("xe", [D, TCE])
    pval = dI("pval", [128, 1])
    fmask = dI("fmask", [128, 1])
    vecs = dI("vecs", [128, 48])
    wk, wv, wr = dI("wk", [D, D]), dI("wv", [D, D]), dI("wr", [D, D])
    tdv = dI("tdv", [128, 8])
    wo0 = dI("wo0", [D, D])
    vecs2 = dI("vecs2", [128, 16])
    wfk, wfr, wfv = dI("wfk", [D, 4 * D]), dI("wfr", [D, D]), dI("wfv", [4 * D, D])
    w1t, w2t = dI("w1t", [D, DH]), dI("w2t", [D, DH])
    h1t, h2t = dI("h1t", [DH, DH]), dI("h2t", [DH, DH])
    w3t = dI("w3t", [DH, D])
    scl = dI("scl", [128, 136])
    wqkv = dI("wqkv", [D, 3 * D])
    cost, sint = dI("cost", [128, TC]), dI("sint", [128, TC])
    selm, selt = dI("selm", [128, 128]), dI("selt", [16, 1024])
    idt64 = dI("idt64", [64, 64], BF16)
    wo1 = dI("wo1", [D, D])
    bo_v = dI("bo_v", [128, 8])
    idt_i = dI("idt_i", [128, 128])
    payoff = dI("payoff", [1, 1], mybir.dt.int32)
    kvoff = dI("kvoff", [1, 2], mybir.dt.int32)
    hw = []
    for l in range(4):
        hw.append(dict(rwt=dI(f"rwt{l}", [128, 32]), rbt=dI(f"rbt{l}", [1, 4]),
                       lgb=dI(f"lgb{l}", [128, 64]), b1t=dI(f"b1t{l}", [128, 128]),
                       b2t=dI(f"b2t{l}", [128, 32]), w1_i=dI(f"w1i{l}", [4 * D, DH]),
                       w2_i=dI(f"w2i{l}", [4 * DH, D])))
    x5_o = nc.dram_tensor("x5_o", [D, TC], F32, kind="ExternalOutput").ap()
    ln0_e, r_e, wkv_e = dN("ln0_e", [D, TCE]), dN("r_e", [D, TCE]), dN("wkv_e", [D, TCE])
    u_d, ek_d = dN("u_d", [D, TC]), dN("ek_d", [D, TC])
    a_d, b_d = dN("a_d", [D, TC]), dN("b_d", [D, TC])
    pay_d = dN("pay_d", [128, 32])
    payg = nc.dram_tensor("payg", [1024, 32], F32, kind="Internal", addr_space="Shared").ap()
    x1_d = dN("x1_d", [D, TCE])
    x2_d, x3_d = dN("x2_d", [D, TC]), dN("x3_d", [D, TC])
    q_d = dN("q_d", [D, TC], BF16)
    kv_d = dN("kv_d", [2 * D, TC], BF16)
    kvg = nc.dram_tensor("kvg", [16 * D, TC], BF16, kind="Internal", addr_space="Shared").ap()
    kv_my = dN("kv_my", [2 * D, S], BF16)
    fence1_in = dN("fence1_in", [128, 1])
    fence1_out = nc.dram_tensor("fence1_out", [128, 1], F32, kind="Internal", addr_space="Shared").ap()
    fence2_in = dN("fence2_in", [128, 1])
    fence2_out = nc.dram_tensor("fence2_out", [128, 1], F32, kind="Internal", addr_space="Shared").ap()
    ao_d = dN("ao_d", [D, TC], BF16)
    x4_d = dN("x4_d", [D, TC])
    xhat_d = dN("xhat_d", [D, TC], BF16)
    hh_d = [dN(f"hh{l}", [D, TC]) for l in range(5)]
    cc2_sem = p.sem("cc_kv")

    with p.stage("A"):
        build_stageA(p, io=dict(cm=cm, xe=xe, pval=pval, fmask=fmask, vecs=vecs,
                                wk=wk, wv=wv, wr=wr,
                                ln0_o=ln0_e[:, 1:1 + TC], u_o=u_d, ek_o=ek_d,
                                r_o=r_e[:, 1:1 + TC]))
    with p.stage("scan"):
        sc_io = dict(u_d=u_d, ek_d=ek_d, tdv=tdv, pval=pval,
                     pay_d=pay_d, payg=payg, payoff=payoff,
                     fence1_in=fence1_in, fence1_out=fence1_out,
                     ln0_e=ln0_e, r_e=r_e,
                     wkv_e=wkv_e, a_d=a_d, b_d=b_d)
        if debug:
            sc_io.update(debug_taps=True,
                         pay_tap=nc.dram_tensor("pay_tap", [128, 32], F32, kind="ExternalOutput").ap(),
                         payg_tap=nc.dram_tensor("payg_tap", [128, 256], F32, kind="ExternalOutput").ap(),
                         pin_tap=nc.dram_tensor("pin_tap", [128, 32], F32, kind="ExternalOutput").ap())
        emit_scan_fused(p, cm, sc_io)
    with p.stage("C1"):
        build_stageC1(p, io=dict(ln0_e=ln0_e, r_e=r_e, wkv_e=wkv_e, wo=wo0, x1_o=x1_d))
    with p.stage("C2"):
        build_stageC2(p, io=dict(cm=cm, x1_e=x1_d, x_fm=xe[:, 1:1 + TC], pval=pval,
                                 vecs=vecs2, wfk=wfk, wfr=wfr, wfv=wfv, x2_o=x2_d))
    with p.stage("D"):
        build_stageD(p, io=dict(cm=cm, x2_i=x2_d, w1t=w1t, w2t=w2t, h1t=h1t, h2t=h2t,
                                w3t=w3t, scl=scl, x3_o=x3_d))
    with p.stage("E"):
        build_stageE(p, io=dict(cm=cm, x3_i=x3_d, wqkv=wqkv, cost=cost, sint=sint,
                                selm=selm, selt=selt, qb_o=q_d,
                                kb_o=kv_d[0:D, :], vb_o=kv_d[D:2 * D, :]))
    with p.stage("cc2"):
        p.op("gpsimd", lambda h: h.collective_compute(
            "AllGather", ALU.bypass, replica_groups=RG_ALL, ins=[kv_d.opt()], outs=[kvg.opt()]),
            sem=cc2_sem, k=1)
        p.op("gpsimd", lambda h: h.collective_compute(
            "AllReduce", ALU.add, replica_groups=RG_ALL, ins=[fence2_in.opt()], outs=[fence2_out.opt()]),
            sem=cc2_sem, k=1)
        p._emit_wait("gpsimd", Ev(cc2_sem, cc2_sem.n, "cc"))
        kofft = p.sbuf("cc2_koff", [1, 2], mybir.dt.int32)
        p.dma_in(kofft, kvoff)
        kvm_buf = Buf(None, "kv_my")
        for half in range(2):
            def cpy(h, half=half):
                with h.register(f"cc2_r{half}") as r0:
                    h.reg_load(r0, kofft[0:1, half:half + 1])
                    return h.dma_start(
                        bass.AP(kv_my.tensor, half * TC, [[2 * TC, 2 * D], [1, TC]]),
                        bass.AP(kvg.tensor, r0, [[TC, 2 * D], [1, TC]]))
            p.op("gpsimd", cpy, reads=(kofft,), writes=(kvm_buf,), dma=True, dma_key=kvm_buf)
    with p.stage("F"):
        emit_attnF2(p, cm, dict(qb_d=q_d, kv_my=kv_my, ao_d=ao_d, idt64=idt64))
    with p.stage("G"):
        build_stageG(p, io=dict(cm=cm, ao_i=ao_d, x3_i=x3_d, wo=wo1, bo_v=bo_v,
                                x4_o=x4_d, h0_o=hh_d[0]))
    for l in range(4):
        with p.stage(f"H{l}"):
            build_stageH(p, io=dict(cm=cm, xh_i=hh_d[l], xo_o=hh_d[l + 1], xhat_d=xhat_d,
                                    idt_i=idt_i, **hw[l]))
    with p.stage("I"):
        build_stageI(p, io=dict(cm=cm, x4_i=x4_d, h4_i=hh_d[4], x5_o=x5_o))
    return p


def _fused_maps(inp):
    f32 = lambda a: np.ascontiguousarray(np.asarray(a, np.float32))
    x = f32(inp["L0_attn_x"])
    vecs = np.concatenate([_v8(inp[f"L0_attn_{n}"]) for n in
                           ("ln1_g", "ln1_b", "mix_k", "mix_v", "mix_r", "time_first")], axis=1)
    vecs2 = np.concatenate([_v8(inp["L0_attn_ln2_g"]), _v8(inp["L0_attn_ln2_b"])], axis=1)
    scl = np.concatenate([_v32(inp["L0_ffn_w1_s"]), _v32(inp["L0_ffn_w2_s"]),
                          _v32(inp["L0_ffn_h1_s"]), _v32(inp["L0_ffn_h2_s"]),
                          _v8(inp["L0_ffn_w3_s"])], axis=1)
    wqkv = f32(inp["L1_attn_Wqkv"])
    perm = _perm1024()
    wperm = np.ascontiguousarray(np.concatenate(
        [wqkv[:, 0:1024][:, perm], wqkv[:, 1024:2048][:, perm], wqkv[:, 2048:3072]], axis=1))
    sel = np.zeros((128, 16 * 8), np.float32)
    selt = np.zeros((16, 128 * 8), np.float32)
    for j in range(8):
        for r in range(128):
            h = 2 * j + (1 if (32 <= r < 64 or r >= 96) else 0)
            sel[r, j * 16 + h] = 1.0
            selt[h, j * 128 + r] = 1.0
    common = dict(
        vecs=np.ascontiguousarray(vecs), vecs2=np.ascontiguousarray(vecs2),
        wk=f32(inp["L0_attn_Wk"]), wv=f32(inp["L0_attn_Wv"]), wr=f32(inp["L0_attn_Wr"]),
        tdv=_v8(inp["L0_attn_time_decay"]), wo0=f32(inp["L0_attn_Wo"]),
        wfk=f32(inp["L0_attn_Wfk"]), wfr=f32(inp["L0_attn_Wfr"]), wfv=f32(inp["L0_attn_Wfv"]),
        w1t=np.ascontiguousarray(f32(inp["L0_ffn_w1_w"]).T),
        w2t=np.ascontiguousarray(f32(inp["L0_ffn_w2_w"]).T),
        h1t=np.ascontiguousarray(f32(inp["L0_ffn_h1_w"]).T),
        h2t=np.ascontiguousarray(f32(inp["L0_ffn_h2_w"]).T),
        w3t=np.ascontiguousarray(f32(inp["L0_ffn_w3_w"]).T),
        scl=np.ascontiguousarray(scl), wqkv=wperm, selm=sel, selt=selt,
        idt64=np.eye(64, dtype=ml_dtypes.bfloat16),
        wo1=f32(inp["L1_attn_Wo"]), bo_v=_v8(inp["L1_attn_bo"]),
        idt_i=np.eye(128, dtype=np.float32),
    )
    for l in range(4):
        rw = f32(inp["L1_ffn_router_w"])[l]
        g_ = f32(inp["L1_ffn_ln_g"])[l]
        b_ = f32(inp["L1_ffn_ln_b"])[l]
        lgb = np.zeros((128, 64), np.float32)
        for e in range(4):
            lgb[:, (2 * e) * 8:(2 * e) * 8 + 8] = _v8(g_[e])
            lgb[:, (2 * e + 1) * 8:(2 * e + 1) * 8 + 8] = _v8(b_[e])
        common.update({
            f"rwt{l}": np.ascontiguousarray(rw.reshape(8, 128, 4).transpose(1, 0, 2).reshape(128, 32)),
            f"rbt{l}": np.ascontiguousarray(f32(inp["L1_ffn_router_b"])[l].reshape(1, 4)),
            f"lgb{l}": lgb,
            f"b1t{l}": np.ascontiguousarray(np.concatenate(
                [_v32(f32(inp["L1_ffn_b1"])[l][e]) for e in range(4)], axis=1)),
            f"b2t{l}": np.ascontiguousarray(np.concatenate(
                [_v8(f32(inp["L1_ffn_b2"])[l][e]) for e in range(4)], axis=1)),
            f"w1i{l}": np.ascontiguousarray(f32(inp["L1_ffn_w1"])[l].reshape(4 * D, 4096)),
            f"w2i{l}": np.ascontiguousarray(f32(inp["L1_ffn_w2"])[l].reshape(4 * 4096, D)),
        })
    maps = []
    for c in range(NCORES):
        b, hh = c // 2, c % 2
        xs = x[b, hh * TC:(hh + 1) * TC, :].T
        bound = np.zeros((D, 1), np.float32) if hh == 0 else x[b, hh * TC - 1, :].reshape(D, 1)
        cost, sint = _rope_tables(hh * TC)
        m = dict(common)
        m.update(xe=np.ascontiguousarray(np.concatenate([bound, xs], axis=1)),
                 pval=np.full((128, 1), float(hh), np.float32),
                 fmask=np.full((128, 1), 1.0 - hh, np.float32),
                 cost=cost, sint=sint,
                 payoff=np.array([[(c ^ 1) * 128 * 32]], np.int32),
                 kvoff=np.array([[(2 * b) * 2 * D * TC, (2 * b + 1) * 2 * D * TC]], np.int32))
        maps.append(m)
    return maps


def kernel(**inp):
    maps = _fused_maps(inp)
    p = build_fused()
    nc = _finish(p)
    res = run_bass_kernel_spmd(nc, maps, core_ids=list(range(NCORES))).results
    out = np.empty((B, S, D), np.float32)
    for c in range(NCORES):
        b, hh = c // 2, c % 2
        out[b, hh * TC:(hh + 1) * TC, :] = res[c]["x5_o"].T
    return out


# version 05 (final)
# speedup vs baseline: 1.3607x; 1.0400x over previous
import contextlib
import math
import numpy as np
import ml_dtypes
import concourse.bass as bass
import concourse.mybir as mybir
from concourse.bass_utils import run_bass_kernel_spmd

F32 = mybir.dt.float32
BF16 = mybir.dt.bfloat16
AF = mybir.ActivationFunctionType
ALU = mybir.AluOpType
AX = mybir.AxisListType

NCORES = 8
D = 1024
B = 4
S = 4096
TC = 2048
ND = D // 128
LN_EPS = 1e-5
WKV_EPS = 1e-8

ENGINES = ("tensor", "vector", "scalar", "gpsimd", "sync")
FENCE_ALL = False
_DMA_ENG = ("gpsimd", "sync", "scalar")


class Ev:
    __slots__ = ("sem", "n", "eng")

    def __init__(self, sem, n, eng):
        self.sem, self.n, self.eng = sem, n, eng


class Sem:
    def __init__(self, handle, name):
        self.h = handle
        self.name = name
        self.n = 0


class Buf:
    __slots__ = ("t", "ready", "users", "name", "dma_sem")

    def __init__(self, t, name=""):
        self.t = t
        self.name = name
        self.ready = None
        self.users = {}
        self.dma_sem = None

    def __getitem__(self, idx):
        return self.t[idx]

    @property
    def ap(self):
        return self.t[:]


class Prog:
    def __init__(self, nc):
        self.nc = nc
        self.q = {e: [] for e in ENGINES}
        self.ctx = contextlib.ExitStack()
        self._waited = {e: {} for e in ENGINES}
        self._fence_cache = {}
        self.sems = {}
        for e in ENGINES:
            self.sems[e] = self.sem("sm_" + e)
        self._fence_scratch = self.ctx.enter_context(self.nc.sbuf_tensor("fence_scr", [1, 1], F32))
        self.n_ops = 0
        _gs = self.sems["gpsimd"]
        _gs.n += 1
        scr = self._fence_scratch
        self._emit("gpsimd", lambda h, s=_gs.h, scr=scr: h.memset(scr[:], 0.0).then_inc(s, 1))
        self._fence_init_ev = Ev(_gs, _gs.n, "gpsimd")
        self._fence_init_done = set()
        self._dma_pool = [self.sem(f"dma{i}") for i in range(64)]
        self._dma_i = 0

    def _dma_sem_for(self, buf):
        if buf.dma_sem is None:
            buf.dma_sem = self._dma_pool[self._dma_i % len(self._dma_pool)]
            self._dma_i += 1
        return buf.dma_sem

    def sem(self, name):
        h = self.ctx.enter_context(self.nc.semaphore(name))
        s = Sem(h, name)
        if not hasattr(self, "_misc_sems"):
            self._misc_sems = []
        if name.startswith("cc_"):
            self._misc_sems.append(s)
        return s

    def sbuf(self, name, shape, dtype):
        st = getattr(self, "_cur_stack", None) or self.ctx
        name = getattr(self, "_prefix", "") + name
        return Buf(st.enter_context(self.nc.sbuf_tensor(name, shape, dtype)), name)

    def psum(self, name, shape, dtype=F32):
        st = getattr(self, "_cur_stack", None) or self.ctx
        name = getattr(self, "_prefix", "") + name
        return Buf(st.enter_context(self.nc.psum_tensor(name, shape, dtype)), name)

    def barrier(self):
        evs = []
        for e in ENGINES:
            s = self.sems[e]
            if s.n:
                evs.append(Ev(s, s.n, e))
        for s in self._dma_pool:
            if s.n:
                evs.append(Ev(s, s.n, "dma"))
        for s in getattr(self, "_misc_sems", []):
            if s.n:
                evs.append(Ev(s, s.n, "cc"))
        for e in ENGINES:
            for ev in evs:
                self._emit_wait(e, ev)

    @contextlib.contextmanager
    def stage(self, name):
        st = contextlib.ExitStack()
        old = getattr(self, "_cur_stack", None)
        oldp = getattr(self, "_prefix", "")
        self._cur_stack = st
        self._prefix = name + "_"
        try:
            yield
        finally:
            self._cur_stack = old
            self._prefix = oldp
            self.barrier()
            st.close()

    def raw(self, eng, fn):
        self._emit(eng, fn)

    def _emit(self, eng, fn):
        self.q[eng].append(fn)
        self.n_ops += 1

    def _emit_wait(self, eng, ev):
        if ev is None or ev.eng == eng:
            return
        w = self._waited[eng]
        if w.get(ev.sem, 0) >= ev.n:
            return
        w[ev.sem] = ev.n
        self._emit(eng, lambda h, s=ev.sem.h, n=ev.n: h.wait_ge(s, n))

    def _fenced(self, ev):
        if ev.eng not in ("vector", "scalar"):
            return ev
        key = (ev.sem, ev.n, ev.eng)
        f = self._fence_cache.get(key)
        if f is not None:
            return f
        scr = self._fence_scratch
        if ev.eng not in self._fence_init_done:
            self._fence_init_done.add(ev.eng)
            self._emit_wait(ev.eng, self._fence_init_ev)
        sem = self.sems[ev.eng]
        sem.n += 1
        n = sem.n
        if ev.eng == "vector":
            self._emit("vector", lambda h, s=sem.h, scr=scr: h.tensor_copy(scr[:], scr[:]).then_inc(s, 1))
        else:
            self._emit("scalar", lambda h, s=sem.h, scr=scr: h.copy(scr[:], scr[:]).then_inc(s, 1))
        f = Ev(sem, n, ev.eng)
        self._fence_cache[key] = f
        return f

    def op(self, eng, fn, reads=(), writes=(), sem=None, k=1, dma=False, dma_key=None):
        ev_eng = eng
        if dma:
            assert dma_key is not None
            sem = self._dma_sem_for(dma_key)
            k = 16
            ev_eng = "dma"
            self._emit_wait(eng, Ev(sem, sem.n, "dma"))
        elif sem is None:
            sem = self.sems[eng]
        for b in reads:
            ev = b.ready
            if ev is not None and (dma or FENCE_ALL) and ev.eng != eng:
                ev = self._fenced(ev)
            self._emit_wait(eng, ev)
        for b in writes:
            self._emit_wait(eng, b.ready)
            for uev in b.users.values():
                self._emit_wait(eng, uev)
        sem.n += k
        n = sem.n
        self._emit(eng, lambda h, fn=fn, s=sem.h, k=k: fn(h).then_inc(s, k))
        ev = Ev(sem, n, ev_eng)
        for b in writes:
            b.ready = ev
            b.users = {}
        for b in reads:
            b.users[eng if not dma else "dma:" + eng] = ev
        return ev

    def chain(self, eng, fns, reads=(), writes=(), sem=None):
        if sem is None:
            sem = self.sems[eng]
        for b in reads:
            ev = b.ready
            if ev is not None and FENCE_ALL and ev.eng != eng:
                ev = self._fenced(ev)
            self._emit_wait(eng, ev)
        for b in writes:
            self._emit_wait(eng, b.ready)
            for uev in b.users.values():
                self._emit_wait(eng, uev)
        for fn in fns[:-1]:
            self._emit(eng, fn)
        sem.n += 1
        n = sem.n
        self._emit(eng, lambda h, fn=fns[-1], s=sem.h: fn(h).then_inc(s, 1))
        ev = Ev(sem, n, eng)
        for b in writes:
            b.ready = ev
            b.users = {}
        for b in reads:
            b.users[eng] = ev
        return ev

    def dma_in(self, buf, dram_ap, eng="gpsimd", dst=None):
        d = buf.ap if dst is None else dst
        return self.op(eng, lambda h, d=d, s=dram_ap: h.dma_start(d, s), writes=(buf,), dma=True,
                       dma_key=buf)

    def dma_out(self, dram_ap, buf, eng="sync", src=None, accum=False):
        s = buf.ap if src is None else src
        if accum:
            return self.op(eng, lambda h, d=dram_ap, s=s: h.dma_start(d, s, accum_op=ALU.add),
                           reads=(buf,), dma=True, dma_key=buf)
        return self.op(eng, lambda h, d=dram_ap, s=s: h.dma_start(d, s), reads=(buf,), dma=True,
                       dma_key=buf)

    def barrier_ev(self, eng):
        sem = self.sems[eng]
        return Ev(sem, sem.n, eng)

    def finalize(self):
        with self.nc.Block() as block:
            for eng in ENGINES:
                ops = self.q[eng]
                if not ops:
                    continue
                section = getattr(block, eng)

                def body(handle, ops=ops):
                    for fn in ops:
                        fn(handle)
                section(body)
        self.ctx.close()
        return self.nc


class PsumPool:
    def __init__(self, p, name, n, cols=512, parts=128, dtype=F32):
        self.bufs = [p.psum(f"{name}{i}", [parts, cols], dtype) for i in range(n)]
        self.i = 0

    def next(self):
        b = self.bufs[self.i % len(self.bufs)]
        self.i += 1
        return b



class Common:

    def __init__(self, p):
        self.ones_col_f = p.sbuf("ones_col_f", [128, 1], F32)
        self.ones_row_f = p.sbuf("ones_row_f", [1, 128], F32)
        p.op("gpsimd", lambda h, t=self.ones_col_f: h.memset(t.ap, 1.0), writes=(self.ones_col_f,))
        p.op("gpsimd", lambda h, t=self.ones_row_f: h.memset(t.ap, 1.0), writes=(self.ones_row_f,))


def emit_ln(p, cm, T, src_tile, dst_tile, scratch, stat, ps_s0, ps_s1, ps_bc0, ps_bc1,
           gb=None, nd=ND, eps=LN_EPS, feat=None):
    src_bufs, dst_bufs = feat
    nfeat = nd * 128
    for j in range(nd):
        p.op("scalar", lambda h, j=j: h.activation(scratch[0:128, j * T:(j + 1) * T], src_tile(j), AF.Square),
             reads=tuple(src_bufs), writes=(scratch,))
    for j in range(nd):
        p.op("tensor", lambda h, j=j: h.matmul(ps_s0[0:1, 0:T], cm.ones_col_f[0:128, 0:1], src_tile(j),
                                               start=(j == 0), stop=(j == nd - 1)),
             reads=tuple(src_bufs) + (cm.ones_col_f,), writes=(ps_s0,))
    for j in range(nd):
        p.op("tensor", lambda h, j=j: h.matmul(ps_s1[0:1, 0:T], cm.ones_col_f[0:128, 0:1],
                                               scratch[0:128, j * T:(j + 1) * T],
                                               start=(j == 0), stop=(j == nd - 1)),
             reads=(scratch, cm.ones_col_f), writes=(ps_s1,))
    p.op("vector", lambda h: h.tensor_scalar(stat[0:1, 0:T], ps_s0[0:1, 0:T], 1.0 / nfeat, None, op0=ALU.mult),
         reads=(ps_s0,), writes=(stat,))
    p.chain("vector", [
        lambda h: h.tensor_scalar(stat[0:1, T:2 * T], ps_s1[0:1, 0:T], 1.0 / nfeat, eps, op0=ALU.mult, op1=ALU.add),
        lambda h: h.tensor_tensor(stat[0:1, 2 * T:3 * T], stat[0:1, 0:T], stat[0:1, 0:T], op=ALU.mult),
        lambda h: h.tensor_tensor(stat[0:1, T:2 * T], stat[0:1, T:2 * T], stat[0:1, 2 * T:3 * T], op=ALU.subtract),
    ], reads=(ps_s1,), writes=(stat,))
    p.chain("scalar", [
        lambda h: h.activation(stat[0:1, 2 * T:3 * T], stat[0:1, T:2 * T], AF.Ln),
        lambda h: h.activation(stat[0:1, 2 * T:3 * T], stat[0:1, 2 * T:3 * T], AF.Exp, scale=-0.5),
    ], reads=(stat,), writes=(stat,))
    p.op("tensor", lambda h: h.matmul(ps_bc0[0:128, 0:T], cm.ones_row_f[0:1, 0:128], stat[0:1, 0:T],
                                      start=True, stop=True),
         reads=(stat, cm.ones_row_f), writes=(ps_bc0,))
    p.op("tensor", lambda h: h.matmul(ps_bc1[0:128, 0:T], cm.ones_row_f[0:1, 0:128], stat[0:1, 2 * T:3 * T],
                                      start=True, stop=True),
         reads=(stat, cm.ones_row_f), writes=(ps_bc1,))
    for j in range(nd):
        p.op("vector", lambda h, j=j: h.tensor_tensor(scratch[0:128, j * T:(j + 1) * T], src_tile(j),
                                                      ps_bc0[0:128, 0:T], op=ALU.subtract),
             reads=tuple(src_bufs) + (ps_bc0,), writes=(scratch,))
        if gb is None:
            p.op("vector", lambda h, j=j: h.tensor_tensor(dst_tile(j), scratch[0:128, j * T:(j + 1) * T],
                                                          ps_bc1[0:128, 0:T], op=ALU.mult),
                 reads=(scratch, ps_bc1), writes=tuple(dst_bufs))
        else:
            g_ap, b_ap, gb_bufs = gb
            p.op("vector", lambda h, j=j: h.tensor_tensor(scratch[0:128, j * T:(j + 1) * T],
                                                          scratch[0:128, j * T:(j + 1) * T],
                                                          ps_bc1[0:128, 0:T], op=ALU.mult),
                 reads=(ps_bc1,), writes=(scratch,))
            p.op("vector", lambda h, j=j: h.tensor_scalar(dst_tile(j), scratch[0:128, j * T:(j + 1) * T],
                                                          g_ap(j), b_ap(j), op0=ALU.mult, op1=ALU.add),
                 reads=(scratch,) + tuple(gb_bufs), writes=tuple(dst_bufs))


def emit_mm(p, pool, n_k, rhs_tile, w_tile, w_bufs, rhs_bufs, evac, m_tiles, T):
    for mj in m_tiles:
        ps = pool.next()
        fns = [
            (lambda h, k=k, mj=mj, ps=ps: h.matmul(
                ps[0:128, 0:T], w_tile(k, mj), rhs_tile(k), start=(k == 0), stop=(k == n_k - 1)))
            for k in range(n_k)
        ]
        p.chain("tensor", fns, reads=tuple(w_bufs) + tuple(rhs_bufs), writes=(ps,))
        evac(mj, ps)


CH_A = 256


def fm3(dram_ap, c0, cw):
    return dram_ap.rearrange("(j p) t -> p j t", p=128)[:, :, c0:c0 + cw]


def build_stageA(p=None, io=None, debug=False):
    own = p is None
    if own:
        nc = bass.Bass("TRN2", target_bir_lowering=False)
        p = Prog(nc)
    else:
        nc = p.nc
    io = io or {}
    def dram(name, shape, dt, kind):
        if name in io:
            return io[name]
        return nc.dram_tensor(name, shape, dt, kind=kind).ap()
    TCE = TC + 1
    xe = dram("xe", [D, TCE], F32, "ExternalInput")
    pval = dram("pval", [128, 1], F32, "ExternalInput")
    fmask = dram("fmask", [128, 1], F32, "ExternalInput")
    vecs = dram("vecs", [128, 8 * 6], F32, "ExternalInput")
    wk = dram("wk", [D, D], F32, "ExternalInput")
    wv = dram("wv", [D, D], F32, "ExternalInput")
    wr = dram("wr", [D, D], F32, "ExternalInput")
    ln0_o = dram("ln0_o", [D, TC], F32, "ExternalOutput")
    u_o = dram("u_o", [D, TC], F32, "ExternalOutput")
    ek_o = dram("ek_o", [D, TC], F32, "ExternalOutput")
    r_o = dram("r_o", [D, TC], F32, "ExternalOutput")

    cm = io.get("cm") or Common(p)
    CH = CH_A
    NCH = TC // CH

    wkb = [p.sbuf(f"wk{k}", [128, D], BF16) for k in range(ND)]
    wvb = [p.sbuf(f"wv{k}", [128, D], BF16) for k in range(ND)]
    wrb = [p.sbuf(f"wr{k}", [128, D], BF16) for k in range(ND)]
    vec = p.sbuf("vec", [128, 8 * 6], F32)
    pv = p.sbuf("pv", [128, 1], F32)
    fm = p.sbuf("fm", [128, 1], F32)
    deriv = p.sbuf("deriv", [128, 8 * 4], F32)
    CHP = CH_A + 1
    xn_ch = p.sbuf("xn_ch", [128, ND * CHP], F32)

    CHE = CH + 1
    xb = p.sbuf("xb", [128, ND * CHE], F32)
    ln0b = p.sbuf("ln0b", [128, ND * CHE], F32)
    sqs = p.sbuf("sqs", [128, ND * CHE], F32)
    stat = p.sbuf("stat", [1, 3 * CHE], F32)
    ekf = p.sbuf("ekf", [128, ND * CH], F32)
    vf = p.sbuf("vf", [128, ND * CH], F32)
    uf = p.sbuf("uf", [128, ND * CH], F32)
    rf = p.sbuf("rf", [128, ND * CH], F32)
    kin = p.sbuf("kin", [128, ND * CH], BF16)
    vin = p.sbuf("vin", [128, ND * CH], BF16)
    rin = p.sbuf("rin", [128, ND * CH], BF16)
    tmp_f = p.sbuf("tmp_f", [128, CH], F32)

    ps_s0 = p.psum("ps_s0", [1, 512])
    ps_s1 = p.psum("ps_s1", [1, 512])
    ps_bc0 = p.psum("ps_bc0", [128, 512])
    ps_bc1 = p.psum("ps_bc1", [128, 512])
    pool = PsumPool(p, "pp", 4)

    for k in range(ND):
        p.dma_in(wkb[k], wk[k * 128:(k + 1) * 128, :])
        p.dma_in(wvb[k], wv[k * 128:(k + 1) * 128, :])
        p.dma_in(wrb[k], wr[k * 128:(k + 1) * 128, :])
    p.dma_in(vec, vecs)
    p.dma_in(pv, pval)
    p.dma_in(fm, fmask)
    for i, src_off in enumerate((16, 24, 32)):
        p.op("vector", lambda h, i=i, o=src_off: h.tensor_scalar(
            deriv[0:128, i * 8:(i + 1) * 8], vec[0:128, o:o + 8], -1.0, 1.0, op0=ALU.mult, op1=ALU.add),
            reads=(vec,), writes=(deriv,))
    p.op("vector", lambda h: h.tensor_scalar(deriv[0:128, 24:32], vec[0:128, 40:48], fm[0:128, 0:1], None, op0=ALU.mult),
         reads=(vec, fm), writes=(deriv,))

    g_ap = lambda j: vec[0:128, j:j + 1]
    b_ap = lambda j: vec[0:128, 8 + j:8 + j + 1]

    def do_ln_pair(T, xsrc_buf, xsrc_tile, ln0_dst_buf, ln0_dst_tile, xn_dst_tile, sq_b, st_b):
        emit_ln(p, cm, T, xsrc_tile, ln0_dst_tile, sq_b, st_b, ps_s0, ps_s1, ps_bc0, ps_bc1,
                gb=None, feat=((xsrc_buf,), (ln0_dst_buf,)))
        emit_ln(p, cm, T, ln0_dst_tile, xn_dst_tile, sq_b, st_b, ps_s0, ps_s1, ps_bc0, ps_bc1,
                gb=(g_ap, b_ap, (vec,)), feat=((ln0_dst_buf,), (xn_ch,)))

    mix_specs = [
        (kin, 16, 0), (vin, 24, 8), (rin, 32, 16)]

    for ci in range(NCH):
        c0 = ci * CH
        e0 = 0 if ci == 0 else ci * CH + 1
        W = CH + 1 if ci == 0 else CH
        lo = 1 if ci == 0 else 0
        xo = 0 if ci == 0 else 1
        if ci > 0:
            for j in range(ND):
                p.op("vector", lambda h, j=j: h.tensor_copy(
                    xn_ch[0:128, j * CHP:j * CHP + 1], xn_ch[0:128, j * CHP + CH:j * CHP + CH + 1]),
                    reads=(), writes=(xn_ch,))
            p.raw("vector", lambda h: h.drain())
        p.dma_in(xb, fm3(xe, e0, W), dst=xb.t[:, 0:ND * W].rearrange("p (j t) -> p j t", j=ND))
        do_ln_pair(W, xb, lambda j, W=W: xb[0:128, j * W:(j + 1) * W],
                   ln0b, lambda j, W=W: ln0b[0:128, j * W:(j + 1) * W],
                   lambda j, W=W, xo=xo: xn_ch[0:128, j * CHP + xo:j * CHP + xo + W], sqs, stat)
        if ci == 0:
            for j in range(ND):
                p.op("vector", lambda h, j=j: h.tensor_scalar(
                    xn_ch[0:128, j * CHP:j * CHP + 1], xn_ch[0:128, j * CHP:j * CHP + 1],
                    pv[0:128, 0:1], None, op0=ALU.mult),
                    reads=(pv,), writes=(xn_ch,))
            p.raw("vector", lambda h: h.drain())
        p.dma_out(fm3(ln0_o, c0, CH), ln0b,
                  src=ln0b.t[:, 0:ND * W].rearrange("p (j t) -> p j t", j=ND)[:, :, lo:lo + CH])
        for dst, mixo, invo in mix_specs:
            for j in range(ND):
                cur = xn_ch[0:128, j * CHP + 1:j * CHP + 1 + CH]
                prv = xn_ch[0:128, j * CHP:j * CHP + CH]
                p.op("vector", lambda h, prv=prv, invo=invo, j=j: h.tensor_scalar(
                    tmp_f[0:128, 0:CH], prv, deriv[0:128, invo + j:invo + j + 1], None, op0=ALU.mult),
                    reads=(xn_ch, deriv), writes=(tmp_f,))
                p.op("vector", lambda h, cur=cur, mixo=mixo, j=j, dst=dst: h.scalar_tensor_tensor(
                    dst[0:128, j * CH:(j + 1) * CH], cur, vec[0:128, mixo + j:mixo + j + 1],
                    tmp_f[0:128, 0:CH], op0=ALU.mult, op1=ALU.add),
                    reads=(xn_ch, vec, tmp_f), writes=(dst,))

        def evac_k(mj, ps):
            if ci == 0:
                p.op("vector", lambda h, mj=mj, ps=ps: h.tensor_tensor(
                    ps[0:128, 0:1], ps[0:128, 0:1], deriv[0:128, 24 + mj:24 + mj + 1], op=ALU.add),
                    reads=(deriv, ps), writes=(ps,))
            p.op("scalar", lambda h, mj=mj, ps=ps: h.activation(
                ekf[0:128, mj * CH:(mj + 1) * CH], ps[0:128, 0:CH], AF.Exp),
                reads=(ps,), writes=(ekf,))
        emit_mm(p, pool, ND, lambda k: kin[0:128, k * CH:(k + 1) * CH],
                lambda k, mj: wkb[k][0:128, mj * 128:(mj + 1) * 128],
                wkb, (kin,), evac_k, range(ND), CH)

        def evac_v(mj, ps):
            p.op("vector", lambda h, mj=mj, ps=ps: h.tensor_copy(
                vf[0:128, mj * CH:(mj + 1) * CH], ps[0:128, 0:CH]), reads=(ps,), writes=(vf,))
        emit_mm(p, pool, ND, lambda k: vin[0:128, k * CH:(k + 1) * CH],
                lambda k, mj: wvb[k][0:128, mj * 128:(mj + 1) * 128],
                wvb, (vin,), evac_v, range(ND), CH)

        def evac_r(mj, ps):
            p.op("scalar", lambda h, mj=mj, ps=ps: h.activation(
                rf[0:128, mj * CH:(mj + 1) * CH], ps[0:128, 0:CH], AF.Sigmoid),
                reads=(ps,), writes=(rf,))
        emit_mm(p, pool, ND, lambda k: rin[0:128, k * CH:(k + 1) * CH],
                lambda k, mj: wrb[k][0:128, mj * 128:(mj + 1) * 128],
                wrb, (rin,), evac_r, range(ND), CH)

        for j in range(ND):
            p.op("vector", lambda h, j=j: h.tensor_tensor(
                uf[0:128, j * CH:(j + 1) * CH], ekf[0:128, j * CH:(j + 1) * CH],
                vf[0:128, j * CH:(j + 1) * CH], op=ALU.mult), reads=(ekf, vf), writes=(uf,))

        for dram, buf in ((u_o, uf), (ek_o, ekf), (r_o, rf)):
            p.dma_out(fm3(dram, c0, CH), buf, src=buf.t[:].rearrange("p (j t) -> p j t", j=ND))

        if debug and ci == 0:
            xn_o = dram("xn_o", [D, CH + 1], F32, "ExternalOutput")
            kin_o = dram("kin_o", [D, CH], F32, "ExternalOutput")
            xnf = p.sbuf("xnf_dbg", [128, ND * (CH + 1)], F32)
            kinf = p.sbuf("kinf_dbg", [128, ND * CH], F32)
            for j in range(ND):
                p.op("vector", lambda h, j=j: h.tensor_copy(
                    xnf[0:128, j * (CH + 1):(j + 1) * (CH + 1)], xn_ch[0:128, j * CHP:j * CHP + CH + 1]),
                    reads=(xn_ch,), writes=(xnf,))
                p.op("vector", lambda h, j=j: h.tensor_copy(
                    kinf[0:128, j * CH:(j + 1) * CH], kin[0:128, j * CH:(j + 1) * CH]),
                    reads=(kin,), writes=(kinf,))
            p.dma_out(fm3(xn_o, 0, CH + 1), xnf, src=xnf.t[:].rearrange("p (j t) -> p j t", j=ND))
            p.dma_out(fm3(kin_o, 0, CH), kinf, src=kinf.t[:].rearrange("p (j t) -> p j t", j=ND))

    return p


def _finish(p):
    for s in p._dma_pool:
        if s.n > 0:
            p._emit("sync", lambda h, s=s.h, n=s.n: h.wait_ge(s, n))
    return p.finalize()


def build_stageB(p=None, io=None):
    own = p is None
    if own:
        nc = bass.Bass("TRN2", target_bir_lowering=False)
        p = Prog(nc)
    else:
        nc = p.nc
    io = io or {}
    def dram(name, shape, dt, kind):
        if name in io:
            return io[name]
        return nc.dram_tensor(name, shape, dt, kind=kind).ap()
    NCH_T = 4
    u_s = dram("u_s", [512, S], F32, "ExternalInput")
    ek_s = dram("ek_s", [512, S], F32, "ExternalInput")
    td_s = dram("td_s", [128, 4], F32, "ExternalInput")
    wkv_o = dram("wkv_o", [512, S], F32, "ExternalOutput")

    ones_t = p.sbuf("ones_t", [128, S], F32)
    p.op("gpsimd", lambda h: h.memset(ones_t.ap, 1.0), writes=(ones_t,))
    td = p.sbuf("td", [128, 4], F32)
    ew_col = p.sbuf("ew_col", [128, 4], F32)
    p.dma_in(td, td_s)
    p.chain("scalar", [
        lambda h: h.activation(ew_col[0:128, 0:4], td[0:128, 0:4], AF.Exp),
        lambda h: h.activation(ew_col[0:128, 0:4], ew_col[0:128, 0:4], AF.Exp, scale=-1.0),
    ], reads=(td,), writes=(ew_col,))

    ew_t = p.sbuf("ew_t", [128, S], F32)
    ut = p.sbuf("ut", [128, S], F32)
    ekt = p.sbuf("ekt", [128, S], F32)
    at = p.sbuf("at", [128, S], F32)
    bt = p.sbuf("bt", [128, S], F32)
    wt = p.sbuf("wt", [128, S], F32)

    for i in range(NCH_T):
        p.dma_in(ut, u_s[i * 128:(i + 1) * 128, :])
        p.dma_in(ekt, ek_s[i * 128:(i + 1) * 128, :])
        p.op("vector", lambda h, i=i: h.tensor_scalar(
            ew_t.ap, ones_t.ap, ew_col[0:128, i:i + 1], None, op0=ALU.mult),
            reads=(ones_t, ew_col), writes=(ew_t,))
        p.op("vector", lambda h: h.tensor_tensor_scan(
            at.ap, ew_t.ap, ut.ap, 0.0, op0=ALU.mult, op1=ALU.add),
            reads=(ew_t, ut), writes=(at,))
        p.op("vector", lambda h: h.tensor_tensor_scan(
            bt.ap, ew_t.ap, ekt.ap, 0.0, op0=ALU.mult, op1=ALU.add),
            reads=(ew_t, ekt), writes=(bt,))
        p.op("vector", lambda h: h.tensor_scalar(bt.ap, bt.ap, WKV_EPS, None, op0=ALU.add),
             reads=(), writes=(bt,))
        p.chain("scalar", [
            lambda h: h.activation(bt.ap, bt.ap, AF.Ln),
            lambda h: h.activation(bt.ap, bt.ap, AF.Exp, scale=-1.0),
        ], reads=(), writes=(bt,))
        p.op("vector", lambda h: h.tensor_tensor(wt.ap, at.ap, bt.ap, op=ALU.mult),
             reads=(at, bt), writes=(wt,))
        p.dma_out(wkv_o[i * 128:(i + 1) * 128, :], wt)
    return p


def build_stageC1(p=None, io=None):
    own = p is None
    if own:
        nc = bass.Bass("TRN2", target_bir_lowering=False)
        p = Prog(nc)
    else:
        nc = p.nc
    io = io or {}
    def dram(name, shape, dt, kind):
        if name in io:
            return io[name]
        return nc.dram_tensor(name, shape, dt, kind=kind).ap()
    TCE = TC + 1
    CH = 256
    ln0_e = dram("ln0_e", [D, TCE], F32, "ExternalInput")
    r_e = dram("r_e", [D, TCE], F32, "ExternalInput")
    wkv_e = dram("wkv_e", [D, TCE], F32, "ExternalInput")
    wo = dram("wo", [D, D], F32, "ExternalInput")
    x1_o = dram("x1_o", [D, TCE], F32, "ExternalOutput")

    wob = [p.sbuf(f"wo{k}", [128, D], BF16) for k in range(ND)]
    for k in range(ND):
        p.dma_in(wob[k], wo[k * 128:(k + 1) * 128, :])
    rb = p.sbuf("rb", [128, ND * (CH + 1)], F32)
    wb = p.sbuf("wb", [128, ND * (CH + 1)], F32)
    l0b = p.sbuf("l0b", [128, ND * (CH + 1)], F32)
    zb = p.sbuf("zb", [128, ND * (CH + 1)], BF16)
    x1b = p.sbuf("x1b", [128, ND * (CH + 1)], F32)
    pool = PsumPool(p, "pp", 4)

    nch = TCE // CH
    for ci in range(nch):
        c0 = ci * CH
        W = CH + 1 if ci == nch - 1 else CH
        p.dma_in(rb, fm3(r_e, c0, W), dst=rb.t[:, 0:ND * W].rearrange("p (j t) -> p j t", j=ND))
        p.dma_in(wb, fm3(wkv_e, c0, W), dst=wb.t[:, 0:ND * W].rearrange("p (j t) -> p j t", j=ND))
        p.dma_in(l0b, fm3(ln0_e, c0, W), dst=l0b.t[:, 0:ND * W].rearrange("p (j t) -> p j t", j=ND))
        for j in range(ND):
            p.op("vector", lambda h, j=j, W=W: h.tensor_tensor(
                zb[0:128, j * W:(j + 1) * W], rb[0:128, j * W:(j + 1) * W],
                wb[0:128, j * W:(j + 1) * W], op=ALU.mult), reads=(rb, wb), writes=(zb,))

        def evac(mj, ps, W=W):
            p.op("vector", lambda h, mj=mj, ps=ps, W=W: h.tensor_tensor(
                x1b[0:128, mj * W:(mj + 1) * W], ps[0:128, 0:W],
                l0b[0:128, mj * W:(mj + 1) * W], op=ALU.add), reads=(ps, l0b), writes=(x1b,))
        emit_mm(p, pool, ND, lambda k, W=W: zb[0:128, k * W:(k + 1) * W],
                lambda k, mj: wob[k][0:128, mj * 128:(mj + 1) * 128],
                wob, (zb,), evac, range(ND), W)
        p.dma_out(fm3(x1_o, c0, W), x1b, src=x1b.t[:, 0:ND * W].rearrange("p (j t) -> p j t", j=ND))
    return p


def build_stageC2(p=None, io=None):
    own = p is None
    if own:
        nc = bass.Bass("TRN2", target_bir_lowering=False)
        p = Prog(nc)
    else:
        nc = p.nc
    io = io or {}
    def dram(name, shape, dt, kind):
        if name in io:
            return io[name]
        return nc.dram_tensor(name, shape, dt, kind=kind).ap()
    TCE = TC + 1
    CH = 256
    CHP = CH + 1
    x1_e = dram("x1_e", [D, TCE], F32, "ExternalInput")
    x_fm = dram("x_fm", [D, TC], F32, "ExternalInput")
    pval = dram("pval", [128, 1], F32, "ExternalInput")
    vecs = dram("vecs", [128, 16], F32, "ExternalInput")
    wfk = dram("wfk", [D, 4 * D], F32, "ExternalInput")
    wfr = dram("wfr", [D, D], F32, "ExternalInput")
    wfv = dram("wfv", [4 * D, D], F32, "ExternalInput")
    x2_o = dram("x2_o", [D, TC], F32, "ExternalOutput")

    cm = io.get("cm") or Common(p)
    wfkb = [p.sbuf(f"wfk{k}", [128, 4 * D], BF16) for k in range(ND)]
    wfrb = [p.sbuf(f"wfr{k}", [128, D], BF16) for k in range(ND)]
    for k in range(ND):
        p.dma_in(wfkb[k], wfk[k * 128:(k + 1) * 128, :])
        p.dma_in(wfrb[k], wfr[k * 128:(k + 1) * 128, :])
    wfvb = [p.sbuf("wfv_a", [128, 16 * 512], BF16), p.sbuf("wfv_b", [128, 16 * 512], BF16)]
    vec = p.sbuf("vec", [128, 16], F32)
    pv = p.sbuf("pv", [128, 1], F32)
    p.dma_in(vec, vecs)
    p.dma_in(pv, pval)
    g_ap = lambda j: vec[0:128, j:j + 1]
    b_ap = lambda j: vec[0:128, 8 + j:8 + j + 1]

    x1w = p.sbuf("x1w", [128, ND * CHP], F32)
    yn_ch = p.sbuf("yn_ch", [128, ND * CHP], F32)
    sqs = p.sbuf("sqs", [128, ND * CHP], F32)
    stat = p.sbuf("stat", [1, 3 * CHP], F32)
    ynb = p.sbuf("ynb", [128, ND * CH], BF16)
    z2b = p.sbuf("z2b", [128, ND * CH], BF16)
    tmp_f = p.sbuf("tmp_f", [128, CH], F32)
    rrf = p.sbuf("rrf", [128, ND * CH], F32)
    kkt = p.sbuf("kkt", [128, CH], F32)
    kkb = p.sbuf("kkb", [128, 32 * CH], BF16)
    xc = p.sbuf("xc", [128, ND * CH], F32)
    bo = p.sbuf("bo", [128, ND * CH], F32)
    x2b = p.sbuf("x2b", [128, ND * CH], F32)

    ps_s0 = p.psum("ps_s0", [1, 512])
    ps_s1 = p.psum("ps_s1", [1, 512])
    ps_bc0 = p.psum("ps_bc0", [128, 512])
    ps_bc1 = p.psum("ps_bc1", [128, 512])
    pool = PsumPool(p, "pp", 4)

    NCH = TC // CH
    for ci in range(NCH):
        c0 = ci * CH
        e0 = 0 if ci == 0 else c0 + 1
        W = CH + 1 if ci == 0 else CH
        xo = 0 if ci == 0 else 1
        if ci > 0:
            for j in range(ND):
                p.op("vector", lambda h, j=j: h.tensor_copy(
                    yn_ch[0:128, j * CHP:j * CHP + 1], yn_ch[0:128, j * CHP + CH:j * CHP + CH + 1]),
                    reads=(), writes=(yn_ch,))
            p.raw("vector", lambda h: h.drain())
        p.dma_in(x1w, fm3(x1_e, e0, W), dst=x1w.t[:, 0:ND * W].rearrange("p (j t) -> p j t", j=ND))
        emit_ln(p, cm, W, lambda j, W=W: x1w[0:128, j * W:(j + 1) * W],
                lambda j, W=W, xo=xo: yn_ch[0:128, j * CHP + xo:j * CHP + xo + W],
                sqs, stat, ps_s0, ps_s1, ps_bc0, ps_bc1,
                gb=(g_ap, b_ap, (vec,)), feat=((x1w,), (yn_ch,)))
        if ci == 0:
            for j in range(ND):
                p.op("vector", lambda h, j=j: h.tensor_scalar(
                    yn_ch[0:128, j * CHP:j * CHP + 1], yn_ch[0:128, j * CHP:j * CHP + 1],
                    pv[0:128, 0:1], None, op0=ALU.mult), reads=(pv,), writes=(yn_ch,))
            p.raw("vector", lambda h: h.drain())
        for j in range(ND):
            p.op("vector", lambda h, j=j: h.tensor_copy(
                ynb[0:128, j * CH:(j + 1) * CH], yn_ch[0:128, j * CHP + 1:j * CHP + 1 + CH]),
                reads=(yn_ch,), writes=(ynb,))
            p.op("vector", lambda h, j=j: h.tensor_scalar(
                tmp_f[0:128, 0:CH], yn_ch[0:128, j * CHP:j * CHP + CH], 0.5, None, op0=ALU.mult),
                reads=(yn_ch,), writes=(tmp_f,))
            p.op("vector", lambda h, j=j: h.scalar_tensor_tensor(
                z2b[0:128, j * CH:(j + 1) * CH], yn_ch[0:128, j * CHP + 1:j * CHP + 1 + CH],
                0.5, tmp_f[0:128, 0:CH], op0=ALU.mult, op1=ALU.add),
                reads=(yn_ch, tmp_f), writes=(z2b,))

        def evac_rr(mj, ps):
            p.op("scalar", lambda h, mj=mj, ps=ps: h.activation(
                rrf[0:128, mj * CH:(mj + 1) * CH], ps[0:128, 0:CH], AF.Sigmoid),
                reads=(ps,), writes=(rrf,))
        emit_mm(p, pool, ND, lambda k: ynb[0:128, k * CH:(k + 1) * CH],
                lambda k, mj: wfrb[k][0:128, mj * 128:(mj + 1) * 128],
                wfrb, (ynb,), evac_rr, range(ND), CH)

        def evac_kk(mj, ps):
            p.op("scalar", lambda h, mj=mj, ps=ps: h.activation(
                kkt[0:128, 0:CH], ps[0:128, 0:CH], AF.Relu), reads=(ps,), writes=(kkt,))
            p.op("scalar", lambda h, mj=mj: h.activation(
                kkb[0:128, mj * CH:(mj + 1) * CH], kkt[0:128, 0:CH], AF.Square),
                reads=(kkt,), writes=(kkb,))
        emit_mm(p, pool, ND, lambda k: z2b[0:128, k * CH:(k + 1) * CH],
                lambda k, mj: wfkb[k][0:128, mj * 128:(mj + 1) * 128],
                wfkb, (z2b,), evac_kk, range(32), CH)

        p.dma_in(xc, fm3(x1_e, c0 + 1, CH), dst=xc.t[:].rearrange("p (j t) -> p j t", j=ND))
        for half in range(2):
            mjs = range(half * 4, half * 4 + 4)
            pss = {}
            for mj in mjs:
                pss[mj] = pool.next()
            for kb in range(2):
                wb_ = wfvb[kb % 2]
                p.dma_in(wb_, wfv.rearrange("(k p) n -> p k n", p=128)[:, kb * 16:(kb + 1) * 16,
                                                                       half * 512:(half + 1) * 512],
                         dst=wb_.t[:].rearrange("p (k n) -> p k n", k=16))
                for kk_ in range(16):
                    k = kb * 16 + kk_
                    for mj in mjs:
                        p.chain("tensor", [
                            lambda h, k=k, kk_=kk_, mj=mj, wb_=wb_, ps_=pss[mj]: h.matmul(
                                ps_[0:128, 0:CH], wb_[0:128, kk_ * 512 + (mj - (mj // 4) * 4) * 128:
                                                      kk_ * 512 + (mj - (mj // 4) * 4 + 1) * 128],
                                kkb[0:128, k * CH:(k + 1) * CH], start=(k == 0), stop=(k == 31))],
                            reads=(wb_, kkb), writes=(pss[mj],))
            for mj in mjs:
                p.op("vector", lambda h, mj=mj, ps_=pss[mj]: h.tensor_tensor(
                    bo[0:128, mj * CH:(mj + 1) * CH], rrf[0:128, mj * CH:(mj + 1) * CH],
                    ps_[0:128, 0:CH], op=ALU.mult), reads=(rrf, pss[mj]), writes=(bo,))
                p.op("vector", lambda h, mj=mj: h.tensor_tensor(
                    bo[0:128, mj * CH:(mj + 1) * CH], bo[0:128, mj * CH:(mj + 1) * CH],
                    xc[0:128, mj * CH:(mj + 1) * CH], op=ALU.add), reads=(xc,), writes=(bo,))
        emit_ln(p, cm, CH, lambda j: bo[0:128, j * CH:(j + 1) * CH],
                lambda j: x2b[0:128, j * CH:(j + 1) * CH],
                sqs, stat, ps_s0, ps_s1, ps_bc0, ps_bc1, gb=None, feat=((bo,), (x2b,)))
        p.dma_in(xc, fm3(x_fm, c0, CH), dst=xc.t[:].rearrange("p (j t) -> p j t", j=ND))
        for j in range(ND):
            p.op("vector", lambda h, j=j: h.tensor_tensor(
                x2b[0:128, j * CH:(j + 1) * CH], x2b[0:128, j * CH:(j + 1) * CH],
                xc[0:128, j * CH:(j + 1) * CH], op=ALU.add), reads=(xc,), writes=(x2b,))
        p.dma_out(fm3(x2_o, c0, CH), x2b, src=x2b.t[:].rearrange("p (j t) -> p j t", j=ND))
    return p


def build_stageD(p=None, io=None):
    own = p is None
    if own:
        nc = bass.Bass("TRN2", target_bir_lowering=False)
        p = Prog(nc)
    else:
        nc = p.nc
    io = io or {}
    def dram(name, shape, dt, kind):
        if name in io:
            return io[name]
        return nc.dram_tensor(name, shape, dt, kind=kind).ap()
    CH = 256
    DH = 4 * D
    x2_i = dram("x2_i", [D, TC], F32, "ExternalInput")
    w1t = dram("w1t", [D, DH], F32, "ExternalInput")
    w2t = dram("w2t", [D, DH], F32, "ExternalInput")
    h1t = dram("h1t", [DH, DH], F32, "ExternalInput")
    h2t = dram("h2t", [DH, DH], F32, "ExternalInput")
    w3t = dram("w3t", [DH, D], F32, "ExternalInput")
    scl = dram("scl", [128, 32 * 4 + 8], F32, "ExternalInput")
    x3_o = dram("x3_o", [D, TC], F32, "ExternalOutput")

    cm = io.get("cm") or Common(p)
    slab = [p.sbuf(f"slab{i}", [128, 512], BF16) for i in range(4)]
    sc = p.sbuf("sc", [128, 32 * 4 + 8], F32)
    p.dma_in(sc, scl)

    xcb = p.sbuf("xcb", [128, ND * CH], F32)
    lnb = p.sbuf("lnb", [128, ND * CH], BF16)
    lnf = p.sbuf("lnf", [128, ND * CH], F32)
    sqs = p.sbuf("sqs", [128, ND * CH], F32)
    stat = p.sbuf("stat", [1, 3 * CH], F32)
    slf = p.sbuf("slf", [128, 32 * CH], F32)
    t2f = p.sbuf("t2f", [128, CH], F32)
    hb = p.sbuf("hb", [128, 32 * CH], BF16)
    g1b = p.sbuf("g1b", [128, 32 * CH], BF16)
    ob = p.sbuf("ob", [128, ND * CH], F32)
    x3b = p.sbuf("x3b", [128, ND * CH], F32)

    ps_s0 = p.psum("ps_s0", [1, 512])
    ps_s1 = p.psum("ps_s1", [1, 512])
    ps_bc0 = p.psum("ps_bc0", [128, 512])
    ps_bc1 = p.psum("ps_bc1", [128, 512])
    pool = PsumPool(p, "pp", 4)

    wgb = [p.sbuf(f"wgb{i}", [128, 32 * 512], BF16) for i in range(2)]

    def big_mm(src_b, n_k, wsrc, dst_fn, scale_off, act):
        for g in range(8):
            wb_ = wgb[g % 2]
            p.dma_in(wb_, wsrc.rearrange("(k p) n -> p k n", p=128)[:, 0:n_k, g * 512:(g + 1) * 512],
                     dst=wb_.t[:, 0:n_k * 512].rearrange("p (k n) -> p k n", k=n_k))
            pss = [pool.next() for _ in range(4)]
            for k in range(n_k):
                for mi in range(4):
                    mj = g * 4 + mi
                    p.chain("tensor", [
                        lambda h, k=k, mi=mi, wb_=wb_, ps_=pss[mi]: h.matmul(
                            ps_[0:128, 0:CH], wb_[0:128, k * 512 + mi * 128:k * 512 + (mi + 1) * 128],
                            src_b[0:128, k * CH:(k + 1) * CH], start=(k == 0), stop=(k == n_k - 1))],
                        reads=(wb_, src_b), writes=(pss[mi],))
            for mi in range(4):
                mj = g * 4 + mi
                dst_fn(mj, pss[mi], scale_off, act)

    def evac_gelu(dst_b):
        def f(mj, ps, so, act):
            p.op("scalar", lambda h, mj=mj, ps=ps, so=so: h.activation(
                dst_b[0:128, mj * CH:(mj + 1) * CH], ps[0:128, 0:CH], AF.Gelu,
                scale=sc[0:128, so + mj:so + mj + 1]), reads=(ps, sc), writes=(dst_b,))
        return f

    NCH = TC // CH
    for ci in range(NCH):
        c0 = ci * CH
        p.dma_in(xcb, fm3(x2_i, c0, CH), dst=xcb.t[:].rearrange("p (j t) -> p j t", j=ND))
        emit_ln(p, cm, CH, lambda j: xcb[0:128, j * CH:(j + 1) * CH],
                lambda j: lnf[0:128, j * CH:(j + 1) * CH],
                sqs, stat, ps_s0, ps_s1, ps_bc0, ps_bc1, gb=None, feat=((xcb,), (lnf,)))
        for j in range(ND):
            p.op("vector", lambda h, j=j: h.tensor_copy(
                lnb[0:128, j * CH:(j + 1) * CH], lnf[0:128, j * CH:(j + 1) * CH]),
                reads=(lnf,), writes=(lnb,))
        def evac_t1(mj, ps, so=None, act=None):
            p.op("scalar", lambda h, mj=mj, ps=ps: h.activation(
                slf[0:128, mj * CH:(mj + 1) * CH], ps[0:128, 0:CH], AF.Silu,
                scale=sc[0:128, mj:mj + 1]), reads=(ps, sc), writes=(slf,))
        big_mm(lnb, ND, w1t, evac_t1, 0, AF.Silu)
        def evac_t2(mj, ps, so=None, act=None):
            p.op("vector", lambda h, mj=mj, ps=ps: h.tensor_scalar(
                t2f[0:128, 0:CH], ps[0:128, 0:CH], sc[0:128, 32 + mj:32 + mj + 1], None, op0=ALU.mult),
                reads=(ps, sc), writes=(t2f,))
            p.op("vector", lambda h, mj=mj: h.tensor_tensor(
                hb[0:128, mj * CH:(mj + 1) * CH], slf[0:128, mj * CH:(mj + 1) * CH],
                t2f[0:128, 0:CH], op=ALU.mult), reads=(slf, t2f), writes=(hb,))
        big_mm(lnb, ND, w2t, evac_t2, 32, AF.Silu)
        big_mm(hb, 32, h1t, evac_gelu(g1b), 64, AF.Gelu)
        big_mm(g1b, 32, h2t, evac_gelu(hb), 96, AF.Gelu)
        for g in range(2):
            pss = [pool.next() for _ in range(4)]
            for k in range(32):
                sb = slab[2 + k % 2]
                p.dma_in(sb, w3t[k * 128:(k + 1) * 128, g * 512:(g + 1) * 512])
                for mi in range(4):
                    mj = g * 4 + mi
                    p.chain("tensor", [
                        lambda h, k=k, mi=mi, sb=sb, ps_=pss[mi]: h.matmul(
                            ps_[0:128, 0:CH], sb[0:128, mi * 128:(mi + 1) * 128],
                            hb[0:128, k * CH:(k + 1) * CH], start=(k == 0), stop=(k == 31))],
                        reads=(sb, hb), writes=(pss[mi],))
            for mi in range(4):
                mj = g * 4 + mi
                p.op("vector", lambda h, mj=mj, ps=pss[mi]: h.tensor_scalar(
                    ob[0:128, mj * CH:(mj + 1) * CH], ps[0:128, 0:CH],
                    sc[0:128, 128 + mj:128 + mj + 1], None, op0=ALU.mult),
                    reads=(pss[mi], sc), writes=(ob,))
        emit_ln(p, cm, CH, lambda j: ob[0:128, j * CH:(j + 1) * CH],
                lambda j: x3b[0:128, j * CH:(j + 1) * CH],
                sqs, stat, ps_s0, ps_s1, ps_bc0, ps_bc1, gb=None, feat=((ob,), (x3b,)))
        for j in range(ND):
            p.op("vector", lambda h, j=j: h.tensor_tensor(
                x3b[0:128, j * CH:(j + 1) * CH], x3b[0:128, j * CH:(j + 1) * CH],
                xcb[0:128, j * CH:(j + 1) * CH], op=ALU.add), reads=(xcb,), writes=(x3b,))
        p.dma_out(fm3(x3_o, c0, CH), x3b, src=x3b.t[:].rearrange("p (j t) -> p j t", j=ND))
    return p


def build_stageE(p=None, io=None):
    own = p is None
    if own:
        nc = bass.Bass("TRN2", target_bir_lowering=False)
        p = Prog(nc)
    else:
        nc = p.nc
    io = io or {}
    def dram(name, shape, dt, kind):
        if name in io:
            return io[name]
        return nc.dram_tensor(name, shape, dt, kind=kind).ap()
    CH = 256
    x3_i = dram("x3_i", [D, TC], F32, "ExternalInput")
    wqkv = dram("wqkv", [D, 3 * D], F32, "ExternalInput")
    cost = dram("cost", [128, TC], F32, "ExternalInput")
    sint = dram("sint", [128, TC], F32, "ExternalInput")
    selm = dram("selm", [128, 16 * ND], F32, "ExternalInput")
    selt = dram("selt", [16, 128 * ND], F32, "ExternalInput")
    qb_o = dram("qb_o", [D, TC], BF16, "ExternalOutput")
    kb_o = dram("kb_o", [D, TC], BF16, "ExternalOutput")
    vb_o = dram("vb_o", [D, TC], BF16, "ExternalOutput")

    cm = io.get("cm") or Common(p)
    wb = [p.sbuf(f"wq{k}", [128, 3 * D], BF16) for k in range(ND)]
    for k in range(ND):
        p.dma_in(wb[k], wqkv[k * 128:(k + 1) * 128, :])
    cosb = p.sbuf("cosb", [128, TC], F32)
    sinb = p.sbuf("sinb", [128, TC], F32)
    p.dma_in(cosb, cost)
    p.dma_in(sinb, sint)
    selb = p.sbuf("selb", [128, 16 * ND], F32)
    seltb = p.sbuf("seltb", [16, 128 * ND], F32)
    p.dma_in(selb, selm)
    p.dma_in(seltb, selt)

    xcb = p.sbuf("xcb", [128, ND * CH], F32)
    lnb = p.sbuf("lnb", [128, ND * CH], BF16)
    lnf = p.sbuf("lnf", [128, ND * CH], F32)
    sqs = p.sbuf("sqs", [128, ND * CH], F32)
    stat = p.sbuf("stat", [1, 3 * CH], F32)
    qf = p.sbuf("qf", [128, ND * CH], F32)
    kf = p.sbuf("kf", [128, ND * CH], F32)
    qr = p.sbuf("qr", [128, ND * CH], F32)
    rt = p.sbuf("rt", [128, 2 * CH], F32)
    sqv = p.sbuf("sqv", [128, CH], F32)
    inv = p.sbuf("inv", [16, 512], F32)
    outb = p.sbuf("outb", [128, ND * CH], BF16)

    ps_s0 = p.psum("ps_s0", [1, 512])
    ps_s1 = p.psum("ps_s1", [1, 512])
    ps_bc0 = p.psum("ps_bc0", [128, 512])
    ps_bc1 = p.psum("ps_bc1", [128, 512])
    ps_n = p.psum("ps_n", [16, 512])
    pool = PsumPool(p, "pp", 3)

    NCH = TC // CH
    for ci in range(NCH):
        c0 = ci * CH
        p.dma_in(xcb, fm3(x3_i, c0, CH), dst=xcb.t[:].rearrange("p (j t) -> p j t", j=ND))
        emit_ln(p, cm, CH, lambda j: xcb[0:128, j * CH:(j + 1) * CH],
                lambda j: lnf[0:128, j * CH:(j + 1) * CH],
                sqs, stat, ps_s0, ps_s1, ps_bc0, ps_bc1, gb=None, feat=((xcb,), (lnf,)))
        for j in range(ND):
            p.op("vector", lambda h, j=j: h.tensor_copy(
                lnb[0:128, j * CH:(j + 1) * CH], lnf[0:128, j * CH:(j + 1) * CH]),
                reads=(lnf,), writes=(lnb,))

        def evac_qkv(mj, ps):
            if mj < 8:
                p.op("vector", lambda h, mj=mj, ps=ps: h.tensor_copy(
                    qf[0:128, mj * CH:(mj + 1) * CH], ps[0:128, 0:CH]), reads=(ps,), writes=(qf,))
            elif mj < 16:
                p.op("vector", lambda h, mj=mj, ps=ps: h.tensor_copy(
                    kf[0:128, (mj - 8) * CH:(mj - 7) * CH], ps[0:128, 0:CH]), reads=(ps,), writes=(kf,))
            else:
                p.op("vector", lambda h, mj=mj, ps=ps: h.tensor_copy(
                    outb[0:128, (mj - 16) * CH:(mj - 15) * CH], ps[0:128, 0:CH]), reads=(ps,), writes=(outb,))
        emit_mm(p, pool, ND, lambda k: lnb[0:128, k * CH:(k + 1) * CH],
                lambda k, mj: wb[k][0:128, mj * 128:(mj + 1) * 128],
                wb, (lnb,), evac_qkv, range(24), CH)
        p.dma_out(fm3(vb_o, c0, CH), outb, src=outb.t[:].rearrange("p (j t) -> p j t", j=ND))

        for src_f, dst_o in ((qf, qb_o), (kf, kb_o)):
            for j in range(ND):
                q1 = src_f[0:64, j * CH:(j + 1) * CH]
                q2 = src_f[64:128, j * CH:(j + 1) * CH]
                cs0 = cosb[0:64, c0:c0 + CH]
                sn0 = sinb[0:64, c0:c0 + CH]
                cs1 = cosb[64:128, c0:c0 + CH]
                sn1 = sinb[64:128, c0:c0 + CH]
                p.chain("vector", [
                    lambda h, q1=q1, cs0=cs0: h.tensor_tensor(rt[0:64, 0:CH], q1, cs0, op=ALU.mult),
                    lambda h, q2=q2, sn1=sn1: h.tensor_tensor(rt[0:64, CH:2 * CH], q2, sn1, op=ALU.mult),
                    lambda h, j=j: h.tensor_tensor(qr[0:64, j * CH:(j + 1) * CH],
                                                   rt[0:64, 0:CH], rt[0:64, CH:2 * CH], op=ALU.subtract),
                    lambda h, q1=q1, sn0=sn0: h.tensor_tensor(rt[64:128, 0:CH], q1, sn0, op=ALU.mult),
                    lambda h, q2=q2, cs1=cs1: h.tensor_tensor(rt[64:128, CH:2 * CH], q2, cs1, op=ALU.mult),
                    lambda h, j=j: h.tensor_tensor(qr[64:128, j * CH:(j + 1) * CH],
                                                   rt[64:128, 0:CH], rt[64:128, CH:2 * CH], op=ALU.add),
                ], reads=(src_f, cosb, sinb), writes=(qr, rt))
            for j in range(ND):
                p.op("scalar", lambda h, j=j: h.activation(sqv[0:128, 0:CH],
                                                           qr[0:128, j * CH:(j + 1) * CH], AF.Square),
                     reads=(qr,), writes=(sqv,))
                p.chain("tensor", [lambda h, j=j: h.matmul(
                    ps_n[0:16, 0:CH], selb[0:128, j * 16:(j + 1) * 16], sqv[0:128, 0:CH],
                    start=(j == 0), stop=(j == ND - 1))], reads=(selb, sqv), writes=(ps_n,))
            p.op("vector", lambda h: h.tensor_scalar(inv[0:16, 0:CH], ps_n[0:16, 0:CH], 1e-6, None, op0=ALU.add),
                 reads=(ps_n,), writes=(inv,))
            p.chain("scalar", [
                lambda h: h.activation(inv[0:16, 0:CH], inv[0:16, 0:CH], AF.Ln),
                lambda h: h.activation(inv[0:16, 0:CH], inv[0:16, 0:CH], AF.Exp, scale=-0.5),
            ], reads=(), writes=(inv,))
            for j in range(ND):
                p.op("tensor", lambda h, j=j: h.matmul(
                    ps_bc0[0:128, 0:CH], seltb[0:16, j * 128:(j + 1) * 128], inv[0:16, 0:CH],
                    start=True, stop=True), reads=(seltb, inv), writes=(ps_bc0,))
                p.op("vector", lambda h, j=j: h.tensor_tensor(
                    outb[0:128, j * CH:(j + 1) * CH], qr[0:128, j * CH:(j + 1) * CH],
                    ps_bc0[0:128, 0:CH], op=ALU.mult), reads=(qr, ps_bc0), writes=(outb,))
            p.dma_out(fm3(dst_o, c0, CH), outb, src=outb.t[:].rearrange("p (j t) -> p j t", j=ND))
    return p


def build_stageF(p=None, io=None):
    own = p is None
    if own:
        nc = bass.Bass("TRN2", target_bir_lowering=False)
        p = Prog(nc)
    else:
        nc = p.nc
    io = io or {}
    def dram(name, shape, dt, kind):
        if name in io:
            return io[name]
        return nc.dram_tensor(name, shape, dt, kind=kind).ap()
    TQ = 512
    qh_i = dram("qh_i", [512, S], BF16, "ExternalInput")
    kh_i = dram("kh_i", [512, S], BF16, "ExternalInput")
    vx_i = dram("vx_i", [128, 8 * 32 * 65], BF16, "ExternalInput")
    ao_o = dram("ao_o", [512, S], BF16, "ExternalOutput")

    cm = io.get("cm") or Common(p)
    kb = p.sbuf("kb", [64, S], BF16)
    qb = p.sbuf("qb", [64, S], BF16)
    vxb = p.sbuf("vxb", [128, 32 * 65], BF16)
    pb = [p.sbuf(f"pb{i}", [128, TQ], BF16) for i in range(2)]
    dn = p.sbuf("dn", [1, TQ], F32)
    af = p.sbuf("af", [64, TQ], F32)
    aob = p.sbuf("aob", [64, TQ], BF16)
    ps_qk = PsumPool(p, "qk", 2)
    ps_av = p.psum("ps_av", [65, 512])
    ps_bc = p.psum("ps_bc", [64, 512])

    for hh in range(8):
        p.dma_in(kb, kh_i[hh * 64:(hh + 1) * 64, :])
        p.dma_in(qb, qh_i[hh * 64:(hh + 1) * 64, :])
        p.dma_in(vxb, vx_i[0:128, hh * 2080:(hh + 1) * 2080])
        for tq in range(S // TQ):
            q_ap = qb[0:64, tq * TQ:(tq + 1) * TQ]
            for ts in range(32):
                ps = ps_qk.next()
                p.chain("tensor", [lambda h, ts=ts, q_ap=q_ap, ps=ps: h.matmul(
                    ps[0:128, 0:TQ], kb[0:64, ts * 128:(ts + 1) * 128], q_ap,
                    start=True, stop=True)], reads=(kb, qb), writes=(ps,))
                pbuf = pb[ts % 2]
                p.op("scalar", lambda h, ps=ps, pbuf=pbuf: h.activation(
                    pbuf.ap, ps[0:128, 0:TQ], AF.Exp, scale=0.125), reads=(ps,), writes=(pbuf,))
                p.chain("tensor", [lambda h, ts=ts, pbuf=pbuf: h.matmul(
                    ps_av[0:65, 0:TQ], vxb[0:128, ts * 65:(ts + 1) * 65], pbuf.ap,
                    start=(ts == 0), stop=(ts == 31))], reads=(vxb, pbuf), writes=(ps_av,))
            p.op("vector", lambda h: h.tensor_copy(dn.ap, ps_av[64:65, 0:TQ]), reads=(ps_av,), writes=(dn,))
            p.chain("scalar", [
                lambda h: h.activation(dn.ap, dn.ap, AF.Ln),
                lambda h: h.activation(dn.ap, dn.ap, AF.Exp, scale=-1.0),
            ], reads=(), writes=(dn,))
            p.op("scalar", lambda h: h.copy(af.ap, ps_av[0:64, 0:TQ]), reads=(ps_av,), writes=(af,))
            p.op("tensor", lambda h: h.matmul(ps_bc[0:64, 0:TQ], cm.ones_row_f[0:1, 0:64], dn.ap,
                                              start=True, stop=True),
                 reads=(cm.ones_row_f, dn), writes=(ps_bc,))
            p.op("vector", lambda h: h.tensor_tensor(aob.ap, af.ap, ps_bc[0:64, 0:TQ], op=ALU.mult),
                 reads=(af, ps_bc), writes=(aob,))
            p.dma_out(ao_o[hh * 64:(hh + 1) * 64, tq * TQ:(tq + 1) * TQ], aob)
    return p


def build_stageG(p=None, io=None):
    own = p is None
    if own:
        nc = bass.Bass("TRN2", target_bir_lowering=False)
        p = Prog(nc)
    else:
        nc = p.nc
    io = io or {}
    def dram(name, shape, dt, kind):
        if name in io:
            return io[name]
        return nc.dram_tensor(name, shape, dt, kind=kind).ap()
    CH = 256
    ao_i = dram("ao_i", [D, TC], BF16, "ExternalInput")
    x3_i = dram("x3_i", [D, TC], F32, "ExternalInput")
    wo = dram("wo", [D, D], F32, "ExternalInput")
    bo_v = dram("bo_v", [128, 8], F32, "ExternalInput")
    x4_o = dram("x4_o", [D, TC], F32, "ExternalOutput")
    h0_o = dram("h0_o", [D, TC], F32, "ExternalOutput")

    cm = io.get("cm") or Common(p)
    wob = [p.sbuf(f"wo{k}", [128, D], BF16) for k in range(ND)]
    for k in range(ND):
        p.dma_in(wob[k], wo[k * 128:(k + 1) * 128, :])
    bov = p.sbuf("bov", [128, 8], F32)
    p.dma_in(bov, bo_v)
    aob = p.sbuf("aob", [128, ND * CH], BF16)
    xcb = p.sbuf("xcb", [128, ND * CH], F32)
    of = p.sbuf("of", [128, ND * CH], F32)
    x4b = p.sbuf("x4b", [128, ND * CH], F32)
    h0b = p.sbuf("h0b", [128, ND * CH], F32)
    sqs = p.sbuf("sqs", [128, ND * CH], F32)
    stat = p.sbuf("stat", [1, 3 * CH], F32)
    ps_s0 = p.psum("ps_s0", [1, 512])
    ps_s1 = p.psum("ps_s1", [1, 512])
    ps_bc0 = p.psum("ps_bc0", [128, 512])
    ps_bc1 = p.psum("ps_bc1", [128, 512])
    pool = PsumPool(p, "pp", 4)

    NCH = TC // CH
    for ci in range(NCH):
        c0 = ci * CH
        p.dma_in(aob, fm3(ao_i, c0, CH), dst=aob.t[:].rearrange("p (j t) -> p j t", j=ND))
        p.dma_in(xcb, fm3(x3_i, c0, CH), dst=xcb.t[:].rearrange("p (j t) -> p j t", j=ND))

        def evac_o(mj, ps):
            p.op("vector", lambda h, mj=mj, ps=ps: h.tensor_scalar(
                of[0:128, mj * CH:(mj + 1) * CH], ps[0:128, 0:CH],
                bov[0:128, mj:mj + 1], None, op0=ALU.add), reads=(ps, bov), writes=(of,))
        emit_mm(p, pool, ND, lambda k: aob[0:128, k * CH:(k + 1) * CH],
                lambda k, mj: wob[k][0:128, mj * 128:(mj + 1) * 128],
                wob, (aob,), evac_o, range(ND), CH)
        emit_ln(p, cm, CH, lambda j: of[0:128, j * CH:(j + 1) * CH],
                lambda j: x4b[0:128, j * CH:(j + 1) * CH],
                sqs, stat, ps_s0, ps_s1, ps_bc0, ps_bc1, gb=None, feat=((of,), (x4b,)))
        for j in range(ND):
            p.op("vector", lambda h, j=j: h.tensor_tensor(
                x4b[0:128, j * CH:(j + 1) * CH], x4b[0:128, j * CH:(j + 1) * CH],
                xcb[0:128, j * CH:(j + 1) * CH], op=ALU.add), reads=(xcb,), writes=(x4b,))
        emit_ln(p, cm, CH, lambda j: x4b[0:128, j * CH:(j + 1) * CH],
                lambda j: h0b[0:128, j * CH:(j + 1) * CH],
                sqs, stat, ps_s0, ps_s1, ps_bc0, ps_bc1, gb=None, feat=((x4b,), (h0b,)))
        p.dma_out(fm3(x4_o, c0, CH), x4b, src=x4b.t[:].rearrange("p (j t) -> p j t", j=ND))
        p.dma_out(fm3(h0_o, c0, CH), h0b, src=h0b.t[:].rearrange("p (j t) -> p j t", j=ND))
    return p


def build_stageH(p=None, io=None, debug=False):
    own = p is None
    if own:
        nc = bass.Bass("TRN2", target_bir_lowering=False)
        p = Prog(nc)
    else:
        nc = p.nc
    io = io or {}
    def dram(name, shape, dt, kind):
        if name in io:
            return io[name]
        return nc.dram_tensor(name, shape, dt, kind=kind).ap()
    CH = 256
    DH = 4096
    xh_i = dram("xh_i", [D, TC], F32, "ExternalInput")
    rwt = dram("rwt", [128, 32], F32, "ExternalInput")
    rbt = dram("rbt", [1, 4], F32, "ExternalInput")
    lgb = dram("lgb", [128, 8 * 8], F32, "ExternalInput")
    b1t = dram("b1t", [128, 32 * 4], F32, "ExternalInput")
    b2t = dram("b2t", [128, 8 * 4], F32, "ExternalInput")
    w1_i = dram("w1_i", [4 * D, DH], F32, "ExternalInput")
    w2_i = dram("w2_i", [4 * DH, D], F32, "ExternalInput")
    idt_i = dram("idt_i", [128, 128], F32, "ExternalInput")
    xo_o = dram("xo_o", [D, TC], F32, "ExternalOutput")
    xhat_d = dram("xhat_d", [D, TC], BF16, "Internal")
    xhat_db = Buf(None, "xhat_d")

    cm = io.get("cm") or Common(p)
    rwb = p.sbuf("rwb", [128, 32], F32)
    rbb = p.sbuf("rbb", [128, 4], F32)
    rbs = p.sbuf("rbs", [1, 4], F32)
    lgbv = p.sbuf("lgbv", [128, 8 * 8], F32)
    b1v = p.sbuf("b1v", [128, 32 * 4], F32)
    b2v = p.sbuf("b2v", [128, 8 * 4], F32)
    idt = p.sbuf("idt", [128, 128], F32)
    for buf, src in ((rwb, rwt), (rbs, rbt), (lgbv, lgb), (b1v, b1t), (b2v, b2t), (idt, idt_i)):
        p.dma_in(buf, src)
    crow = [p.sbuf(f"crow{e}", [1, TC], BF16) for e in range(4)]
    ones_row_b = p.sbuf("ones_row_b", [1, 128], BF16)
    p.op("gpsimd", lambda h: h.memset(ones_row_b.ap, 1.0), writes=(ones_row_b,))

    xcb = p.sbuf("xcb", [128, ND * CH], F32)
    xhatb = p.sbuf("xhatb", [128, ND * CH], BF16)
    sqs = p.sbuf("sqs", [128, ND * CH], F32)
    stat = p.sbuf("stat", [1, 3 * CH], F32)
    lg = p.sbuf("lg", [128, 4], F32)
    m1 = p.sbuf("m1", [128, 1], F32)
    el = p.sbuf("el", [128, 4], F32)
    msk = p.sbuf("msk", [128, 4], F32)
    p2t = p.sbuf("p2t", [128, 4], F32)
    comb = p.sbuf("comb", [128, 4], F32)
    den = p.sbuf("den", [128, 1], F32)

    w1b = [p.sbuf(f"w1b{k}", [128, DH], BF16) for k in range(ND)]
    w2b = [p.sbuf(f"w2b{k}", [128, D], BF16) for k in range(32)]
    xeb = p.sbuf("xeb", [128, ND * CH], BF16)
    hb = p.sbuf("hb", [128, 32 * CH], BF16)
    gout = p.sbuf("gout", [128, ND * CH], F32)
    gateb = p.sbuf("gateb", [128, CH], F32)

    ps_s0 = p.psum("ps_s0", [1, 512])
    ps_s1 = p.psum("ps_s1", [1, 512])
    ps_bc0 = p.psum("ps_bc0", [128, 512])
    ps_bc1 = p.psum("ps_bc1", [128, 512])
    pool = PsumPool(p, "pp", 4)

    psb = pool.next()
    p.op("tensor", lambda h, psb=psb: h.matmul(psb[0:128, 0:4], cm.ones_row_f[0:1, 0:128],
                                               rbs[0:1, 0:4], start=True, stop=True),
         reads=(cm.ones_row_f, rbs), writes=(psb,))
    p.op("vector", lambda h, psb=psb: h.tensor_copy(rbb.ap, psb[0:128, 0:4]), reads=(psb,), writes=(rbb,))

    NCH = TC // CH
    for ci in range(NCH):
        c0 = ci * CH
        p.dma_in(xcb, fm3(xh_i, c0, CH), dst=xcb.t[:].rearrange("p (j t) -> p j t", j=ND))
        emit_ln(p, cm, CH, lambda j: xcb[0:128, j * CH:(j + 1) * CH],
                lambda j: xhatb[0:128, j * CH:(j + 1) * CH],
                sqs, stat, ps_s0, ps_s1, ps_bc0, ps_bc1, gb=None, feat=((xcb,), (xhatb,)))
        p.dma_out(fm3(xhat_d, c0, CH), xhatb, src=xhatb.t[:].rearrange("p (j t) -> p j t", j=ND))
        xhat_db.ready = xhatb.users.get("dma:sync") or xhat_db.ready
        for tt in range(CH // 128):
            ps_l = pool.next()
            fns = [(lambda h, ki=ki, tt=tt, ps_l=ps_l: h.matmul(
                ps_l[0:128, 0:4], xcb[0:128, ki * CH + tt * 128:ki * CH + (tt + 1) * 128],
                rwb[0:128, ki * 4:(ki + 1) * 4], start=(ki == 0), stop=(ki == ND - 1)))
                for ki in range(ND)]
            p.chain("tensor", fns, reads=(xcb, rwb), writes=(ps_l,))
            p.op("vector", lambda h, ps_l=ps_l: h.tensor_tensor(lg.ap, ps_l[0:128, 0:4], rbb.ap, op=ALU.add),
                 reads=(ps_l, rbb), writes=(lg,))
            p.raw("vector", lambda h: h.drain())
            p.op("vector", lambda h: h.tensor_reduce(m1.ap, lg.ap, axis=AX.X, op=ALU.max),
                 reads=(lg,), writes=(m1,))
            p.raw("vector", lambda h: h.drain())
            p.op("vector", lambda h: h.tensor_scalar(msk.ap, lg.ap, m1.ap, None, op0=ALU.subtract),
                 reads=(lg, m1), writes=(msk,))
            p.raw("vector", lambda h: h.drain())
            if debug and ci == 0 and tt == 0:
                dbg_r = dram("dbg_r", [128, 8], F32, "ExternalOutput")
                dbgr = p.sbuf("dbgr", [128, 8], F32)
                p.op("vector", lambda h: h.tensor_copy(dbgr[0:128, 0:1], m1.ap), reads=(m1,), writes=(dbgr,))
                p.op("vector", lambda h: h.tensor_copy(dbgr[0:128, 1:5], msk.ap), reads=(msk,), writes=(dbgr,))
                p.dma_out(dbg_r, dbgr)
            p.op("scalar", lambda h: h.activation(el.ap, msk.ap, AF.Exp), reads=(msk,), writes=(el,))
            p.chain("scalar", [
                lambda h: h.activation(msk.ap, msk.ap, AF.Relu, scale=-1.0),
                lambda h: h.activation(msk.ap, msk.ap, AF.Sign),
            ], reads=(), writes=(msk,))
            p.op("vector", lambda h: h.tensor_tensor(p2t.ap, el.ap, msk.ap, op=ALU.mult),
                 reads=(el, msk), writes=(p2t,))
            p.raw("vector", lambda h: h.drain())
            p.op("vector", lambda h: h.tensor_reduce(m1.ap, p2t.ap, axis=AX.X, op=ALU.max),
                 reads=(p2t,), writes=(m1,))
            p.raw("vector", lambda h: h.drain())
            p.op("vector", lambda h: h.tensor_scalar(msk.ap, el.ap, m1.ap, None, op0=ALU.subtract),
                 reads=(el, m1), writes=(msk,))
            p.raw("vector", lambda h: h.drain())
            p.chain("scalar", [
                lambda h: h.activation(msk.ap, msk.ap, AF.Relu, scale=-1.0),
                lambda h: h.activation(msk.ap, msk.ap, AF.Sign),
            ], reads=(), writes=(msk,))
            p.op("vector", lambda h: h.tensor_tensor(p2t.ap, el.ap, msk.ap, op=ALU.mult),
                 reads=(el, msk), writes=(p2t,))
            p.raw("vector", lambda h: h.drain())
            p.op("vector", lambda h: h.tensor_tensor(comb.ap, el.ap, p2t.ap, op=ALU.subtract),
                 reads=(el, p2t), writes=(comb,))
            p.raw("vector", lambda h: h.drain())
            p.op("vector", lambda h: h.tensor_reduce(den.ap, comb.ap, axis=AX.X, op=ALU.add),
                 reads=(comb,), writes=(den,))
            p.raw("vector", lambda h: h.drain())
            p.chain("scalar", [
                lambda h: h.activation(den.ap, den.ap, AF.Ln),
                lambda h: h.activation(den.ap, den.ap, AF.Exp, scale=-1.0),
            ], reads=(), writes=(den,))
            p.op("vector", lambda h: h.tensor_scalar(comb.ap, comb.ap, den.ap, None, op0=ALU.mult),
                 reads=(den,), writes=(comb,))
            p.raw("vector", lambda h: h.drain())
            if debug and ci == 0 and tt == 0:
                lg_o = dram("lg_o", [128, 16], F32, "ExternalOutput")
                dbg0 = p.sbuf("dbg0", [128, 16], F32)
                p.op("vector", lambda h: h.tensor_copy(dbg0[0:128, 0:4], lg.ap), reads=(lg,), writes=(dbg0,))
                p.op("vector", lambda h: h.tensor_copy(dbg0[0:128, 4:8], el.ap), reads=(el,), writes=(dbg0,))
                p.op("vector", lambda h: h.tensor_copy(dbg0[0:128, 8:12], comb.ap), reads=(comb,), writes=(dbg0,))
                p.op("vector", lambda h: h.tensor_copy(dbg0[0:128, 12:13], den.ap), reads=(den,), writes=(dbg0,))
                p.dma_out(lg_o, dbg0)
            for e in range(4):
                ps_t = pool.next()
                p.op("tensor", lambda h, e=e, ps_t=ps_t: h.matmul(
                    ps_t[0:1, 0:128], comb[0:128, e:e + 1], idt.ap, start=True, stop=True),
                    reads=(comb, idt), writes=(ps_t,))
                p.op("vector", lambda h, e=e, ps_t=ps_t, c0=c0, tt=tt: h.tensor_copy(
                    crow[e][0:1, c0 + tt * 128:c0 + (tt + 1) * 128], ps_t[0:1, 0:128]),
                    reads=(ps_t,), writes=(crow[e],))

    for e in range(4):
        for k in range(ND):
            p.dma_in(w1b[k], w1_i[e * D + k * 128:e * D + (k + 1) * 128, :])
        for k in range(32):
            p.dma_in(w2b[k], w2_i[e * DH + k * 128:e * DH + (k + 1) * 128, :])
        for ci in range(NCH):
            c0 = ci * CH
            p.dma_in(xhatb, fm3(xhat_d, c0, CH), dst=xhatb.t[:].rearrange("p (j t) -> p j t", j=ND))
            p.op("gpsimd", lambda h: h.engine_nop(), reads=(xhat_db,)) if False else None
            p.dma_in(xcb, fm3(xh_i, c0, CH), dst=xcb.t[:].rearrange("p (j t) -> p j t", j=ND))
            for j in range(ND):
                p.op("vector", lambda h, j=j, e=e: h.tensor_scalar(
                    xeb[0:128, j * CH:(j + 1) * CH], xhatb[0:128, j * CH:(j + 1) * CH],
                    lgbv[0:128, (2 * e) * 8 + j:(2 * e) * 8 + j + 1],
                    lgbv[0:128, (2 * e + 1) * 8 + j:(2 * e + 1) * 8 + j + 1],
                    op0=ALU.mult, op1=ALU.add), reads=(xhatb, lgbv), writes=(xeb,))

            def evac_h(mj, ps, e=e):
                p.op("scalar", lambda h, mj=mj, ps=ps, e=e: h.activation(
                    hb[0:128, mj * CH:(mj + 1) * CH], ps[0:128, 0:CH], AF.Gelu,
                    bias=b1v[0:128, e * 32 + mj:e * 32 + mj + 1]), reads=(ps, b1v), writes=(hb,))
            emit_mm(p, pool, ND, lambda k: xeb[0:128, k * CH:(k + 1) * CH],
                    lambda k, mj: w1b[k][0:128, mj * 128:(mj + 1) * 128],
                    w1b, (xeb,), evac_h, range(32), CH)
            ps_g = pool.next()
            p.op("tensor", lambda h, e=e, c0=c0, ps_g=ps_g: h.matmul(
                ps_g[0:128, 0:CH], ones_row_b[0:1, 0:128], crow[e][0:1, c0:c0 + CH],
                start=True, stop=True), reads=(ones_row_b, crow[e]), writes=(ps_g,))
            p.op("vector", lambda h, ps_g=ps_g: h.tensor_copy(gateb[0:128, 0:CH], ps_g[0:128, 0:CH]),
                 reads=(ps_g,), writes=(gateb,))

            def evac_y(mj, ps, e=e):
                p.op("vector", lambda h, mj=mj, ps=ps, e=e: h.tensor_scalar(
                    gout[0:128, mj * CH:(mj + 1) * CH], ps[0:128, 0:CH],
                    b2v[0:128, e * 8 + mj:e * 8 + mj + 1], None, op0=ALU.add),
                    reads=(ps, b2v), writes=(gout,))
                p.op("vector", lambda h, mj=mj: h.tensor_tensor(
                    gout[0:128, mj * CH:(mj + 1) * CH], gout[0:128, mj * CH:(mj + 1) * CH],
                    xcb[0:128, mj * CH:(mj + 1) * CH], op=ALU.add), reads=(xcb,), writes=(gout,))
                p.op("vector", lambda h, mj=mj: h.tensor_tensor(
                    gout[0:128, mj * CH:(mj + 1) * CH], gout[0:128, mj * CH:(mj + 1) * CH],
                    gateb[0:128, 0:CH], op=ALU.mult), reads=(gateb,), writes=(gout,))
            emit_mm(p, pool, 32, lambda k: hb[0:128, k * CH:(k + 1) * CH],
                    lambda k, mj: w2b[k][0:128, mj * 128:(mj + 1) * 128],
                    w2b, (hb,), evac_y, range(ND), CH)
            p.dma_out(fm3(xo_o, c0, CH), gout, src=gout.t[:].rearrange("p (j t) -> p j t", j=ND),
                      accum=(e > 0), eng="gpsimd" if e > 0 else "sync")
            if debug and e == 0 and ci == 0:
                xhd_o = dram("xhd_o", [D, CH], F32, "ExternalOutput")
                xeb_o = dram("xeb_o", [D, CH], F32, "ExternalOutput")
                hb_o = dram("hb_o", [DH, CH], F32, "ExternalOutput")
                gate_o = dram("gate_o", [128, CH], F32, "ExternalOutput")
                dbg1 = p.sbuf("dbg1", [128, ND * CH], F32)

                for j in range(ND):
                    p.op("vector", lambda h, j=j: h.tensor_copy(
                        dbg1[0:128, j * CH:(j + 1) * CH], xhatb[0:128, j * CH:(j + 1) * CH]),
                        reads=(xhatb,), writes=(dbg1,))
                p.dma_out(fm3(xhd_o, 0, CH), dbg1, src=dbg1.t[:].rearrange("p (j t) -> p j t", j=ND))
                for j in range(ND):
                    p.op("vector", lambda h, j=j: h.tensor_copy(
                        dbg1[0:128, j * CH:(j + 1) * CH], xeb[0:128, j * CH:(j + 1) * CH]),
                        reads=(xeb,), writes=(dbg1,))
                p.dma_out(fm3(xeb_o, 0, CH), dbg1, src=dbg1.t[:].rearrange("p (j t) -> p j t", j=ND))
                for j in range(4):
                    p.op("vector", lambda h, j=j: h.tensor_copy(
                        dbg1[0:128, j * CH:(j + 1) * CH], hb[0:128, j * CH:(j + 1) * CH]),
                        reads=(hb,), writes=(dbg1,))
                p.dma_out(hb_o.rearrange("(j p) t -> p j t", p=128)[:, 0:4, :], dbg1,
                          src=dbg1.t[:, 0:4 * CH].rearrange("p (j t) -> p j t", j=4))
                p.dma_out(gate_o, gateb)
    return p


def build_stageI(p=None, io=None):
    own = p is None
    if own:
        nc = bass.Bass("TRN2", target_bir_lowering=False)
        p = Prog(nc)
    else:
        nc = p.nc
    io = io or {}
    def dram(name, shape, dt, kind):
        if name in io:
            return io[name]
        return nc.dram_tensor(name, shape, dt, kind=kind).ap()
    CH = 512
    x4_i = dram("x4_i", [D, TC], F32, "ExternalInput")
    h4_i = dram("h4_i", [D, TC], F32, "ExternalInput")
    x5_o = dram("x5_o", [D, TC], F32, "ExternalOutput")
    cm = io.get("cm") or Common(p)
    x4b = p.sbuf("x4b", [128, ND * CH], F32)
    h4b = p.sbuf("h4b", [128, ND * CH], F32)
    ob = p.sbuf("ob", [128, ND * CH], F32)
    sqs = p.sbuf("sqs", [128, ND * CH], F32)
    stat = p.sbuf("stat", [1, 3 * CH], F32)
    ps_s0 = p.psum("ps_s0", [1, 512])
    ps_s1 = p.psum("ps_s1", [1, 512])
    ps_bc0 = p.psum("ps_bc0", [128, 512])
    ps_bc1 = p.psum("ps_bc1", [128, 512])
    for ci in range(TC // CH):
        c0 = ci * CH
        p.dma_in(x4b, fm3(x4_i, c0, CH), dst=x4b.t[:].rearrange("p (j t) -> p j t", j=ND))
        p.dma_in(h4b, fm3(h4_i, c0, CH), dst=h4b.t[:].rearrange("p (j t) -> p j t", j=ND))
        emit_ln(p, cm, CH, lambda j: h4b[0:128, j * CH:(j + 1) * CH],
                lambda j: ob[0:128, j * CH:(j + 1) * CH],
                sqs, stat, ps_s0, ps_s1, ps_bc0, ps_bc1, gb=None, feat=((h4b,), (ob,)))
        for j in range(ND):
            p.op("vector", lambda h, j=j: h.tensor_tensor(
                ob[0:128, j * CH:(j + 1) * CH], ob[0:128, j * CH:(j + 1) * CH],
                x4b[0:128, j * CH:(j + 1) * CH], op=ALU.add), reads=(x4b,), writes=(ob,))
        p.dma_out(fm3(x5_o, c0, CH), ob, src=ob.t[:].rearrange("p (j t) -> p j t", j=ND))
    return p


def _perm1024():
    perm = np.zeros(1024, np.int64)
    for g in range(8):
        for r in range(128):
            if r < 32:
                h, d = 2 * g, r
            elif r < 64:
                h, d = 2 * g + 1, r - 32
            elif r < 96:
                h, d = 2 * g, 32 + r - 64
            else:
                h, d = 2 * g + 1, 32 + r - 96
            perm[g * 128 + r] = h * 64 + d
    return perm


def _head_rows(h):
    g, sub = h // 2, h % 2
    return np.concatenate([np.arange(g * 128 + sub * 32, g * 128 + sub * 32 + 32),
                           np.arange(g * 128 + 64 + sub * 32, g * 128 + 64 + sub * 32 + 32)])


def _rope_tables(s0):
    inv = 1.0 / (10000.0 ** (np.arange(32, dtype=np.float64) / 32.0))
    pos = np.arange(s0, s0 + TC, dtype=np.float64)
    ang = (pos[None, :] * inv[:, None])
    cost = np.cos(ang).astype(np.float32)
    sint = np.sin(ang).astype(np.float32)
    cost = np.concatenate([cost, cost, cost, cost], axis=0)
    sint = np.concatenate([sint, sint, sint, sint], axis=0)
    return np.ascontiguousarray(cost), np.ascontiguousarray(sint)


def _v8(v):
    return np.ascontiguousarray(np.asarray(v, np.float32).reshape(8, 128).T)


def _v32(v):
    return np.ascontiguousarray(np.asarray(v, np.float32).reshape(32, 128).T)


def _run(build_fn, maps):
    p = build_fn()
    nc = _finish(p)
    return run_bass_kernel_spmd(nc, maps, core_ids=list(range(NCORES))).results


def kernel_unfused(**inp):
    f32 = lambda a: np.ascontiguousarray(np.asarray(a, np.float32))
    x = f32(inp["L0_attn_x"])

    vecs = np.concatenate([_v8(inp[f"L0_attn_{n}"]) for n in
                           ("ln1_g", "ln1_b", "mix_k", "mix_v", "mix_r", "time_first")], axis=1)
    wk, wv, wr = f32(inp["L0_attn_Wk"]), f32(inp["L0_attn_Wv"]), f32(inp["L0_attn_Wr"])
    mapsA = []
    for c in range(NCORES):
        b, hh = c // 2, c % 2
        xs = x[b, hh * TC:(hh + 1) * TC, :].T
        bound = np.zeros((D, 1), np.float32) if hh == 0 else x[b, hh * TC - 1, :].reshape(D, 1)
        pval = np.full((128, 1), float(hh), np.float32)
        fmask = np.full((128, 1), 1.0 - hh, np.float32)
        mapsA.append(dict(xe=np.ascontiguousarray(np.concatenate([bound, xs], axis=1)),
                          pval=pval, fmask=fmask, vecs=np.ascontiguousarray(vecs),
                          wk=wk, wv=wv, wr=wr))
    resA = _run(build_stageA, mapsA)

    td = f32(inp["L0_attn_time_decay"])
    mapsB = []
    for c in range(NCORES):
        b, hh = c // 2, c % 2
        u_full = np.concatenate([resA[2 * b]["u_o"], resA[2 * b + 1]["u_o"]], axis=1)
        ek_full = np.concatenate([resA[2 * b]["ek_o"], resA[2 * b + 1]["ek_o"]], axis=1)
        lo = hh * 512
        mapsB.append(dict(u_s=np.ascontiguousarray(u_full[lo:lo + 512]),
                          ek_s=np.ascontiguousarray(ek_full[lo:lo + 512]),
                          td_s=np.ascontiguousarray(td[lo:lo + 512].reshape(4, 128).T)))
    resB = _run(build_stageB, mapsB)

    wo0 = f32(inp["L0_attn_Wo"])
    wkv_fm = {}
    for c in range(NCORES):
        b, hh = c // 2, c % 2
        full = np.concatenate([resB[2 * b]["wkv_o"], resB[2 * b + 1]["wkv_o"]], axis=0)
        wkv_fm[c] = full[:, hh * TC:(hh + 1) * TC]
    mapsC1 = []
    for c in range(NCORES):
        b, hh = c // 2, c % 2
        z = np.zeros((D, 1), np.float32)
        if hh == 0:
            l0e = np.concatenate([z, resA[c]["ln0_o"]], axis=1)
            re_ = np.concatenate([z, resA[c]["r_o"]], axis=1)
            we_ = np.concatenate([z, wkv_fm[c]], axis=1)
        else:
            l0e = np.concatenate([resA[c - 1]["ln0_o"][:, -1:], resA[c]["ln0_o"]], axis=1)
            re_ = np.concatenate([resA[c - 1]["r_o"][:, -1:], resA[c]["r_o"]], axis=1)
            we_ = np.concatenate([wkv_fm[c - 1][:, -1:], wkv_fm[c]], axis=1)
        mapsC1.append(dict(ln0_e=np.ascontiguousarray(l0e), r_e=np.ascontiguousarray(re_),
                           wkv_e=np.ascontiguousarray(we_), wo=wo0))
    resC1 = _run(build_stageC1, mapsC1)

    vecs2 = np.concatenate([_v8(inp["L0_attn_ln2_g"]), _v8(inp["L0_attn_ln2_b"])], axis=1)
    wfk, wfr, wfv = f32(inp["L0_attn_Wfk"]), f32(inp["L0_attn_Wfr"]), f32(inp["L0_attn_Wfv"])
    mapsC2 = []
    for c in range(NCORES):
        b, hh = c // 2, c % 2
        mapsC2.append(dict(x1_e=np.ascontiguousarray(resC1[c]["x1_o"]),
                           x_fm=np.ascontiguousarray(x[b, hh * TC:(hh + 1) * TC, :].T),
                           pval=np.full((128, 1), float(hh), np.float32),
                           vecs=np.ascontiguousarray(vecs2), wfk=wfk, wfr=wfr, wfv=wfv))
    resC2 = _run(build_stageC2, mapsC2)

    scl = np.concatenate([_v32(inp["L0_ffn_w1_s"]), _v32(inp["L0_ffn_w2_s"]),
                          _v32(inp["L0_ffn_h1_s"]), _v32(inp["L0_ffn_h2_s"]),
                          _v8(inp["L0_ffn_w3_s"])], axis=1)
    w1t = np.ascontiguousarray(f32(inp["L0_ffn_w1_w"]).T)
    w2t = np.ascontiguousarray(f32(inp["L0_ffn_w2_w"]).T)
    h1t = np.ascontiguousarray(f32(inp["L0_ffn_h1_w"]).T)
    h2t = np.ascontiguousarray(f32(inp["L0_ffn_h2_w"]).T)
    w3t = np.ascontiguousarray(f32(inp["L0_ffn_w3_w"]).T)
    mapsD = [dict(x2_i=np.ascontiguousarray(resC2[c]["x2_o"]), w1t=w1t, w2t=w2t,
                  h1t=h1t, h2t=h2t, w3t=w3t, scl=np.ascontiguousarray(scl))
             for c in range(NCORES)]
    resD = _run(build_stageD, mapsD)

    wqkv = f32(inp["L1_attn_Wqkv"])
    perm = _perm1024()
    wperm = np.ascontiguousarray(np.concatenate(
        [wqkv[:, 0:1024][:, perm], wqkv[:, 1024:2048][:, perm], wqkv[:, 2048:3072]], axis=1))
    sel = np.zeros((128, 16 * 8), np.float32)
    selt = np.zeros((16, 128 * 8), np.float32)
    for j in range(8):
        for r in range(128):
            h = 2 * j + (1 if (32 <= r < 64 or r >= 96) else 0)
            sel[r, j * 16 + h] = 1.0
            selt[h, j * 128 + r] = 1.0
    mapsE = []
    for c in range(NCORES):
        hh = c % 2
        cost, sint = _rope_tables(hh * TC)
        mapsE.append(dict(x3_i=np.ascontiguousarray(resD[c]["x3_o"]), wqkv=wperm,
                          cost=cost, sint=sint, selm=sel, selt=selt))
    resE = _run(build_stageE, mapsE)

    mapsF = []
    for c in range(NCORES):
        b, hg = c // 2, c % 2
        qfull = np.concatenate([np.asarray(resE[2 * b]["qb_o"]), np.asarray(resE[2 * b + 1]["qb_o"])], axis=1)
        kfull = np.concatenate([np.asarray(resE[2 * b]["kb_o"]), np.asarray(resE[2 * b + 1]["kb_o"])], axis=1)
        vfull = np.concatenate([np.asarray(resE[2 * b]["vb_o"]), np.asarray(resE[2 * b + 1]["vb_o"])], axis=1)
        qh = np.zeros((512, S), ml_dtypes.bfloat16)
        kh = np.zeros((512, S), ml_dtypes.bfloat16)
        vx = np.zeros((128, 8 * 32 * 65), ml_dtypes.bfloat16)
        ones_col = np.ones((128, 1), ml_dtypes.bfloat16)
        for i in range(8):
            h = hg * 8 + i
            rows = _head_rows(h)
            qh[i * 64:(i + 1) * 64] = qfull[rows]
            kh[i * 64:(i + 1) * 64] = kfull[rows]
            vh = vfull[h * 64:(h + 1) * 64]
            for ts in range(32):
                base = (i * 32 + ts) * 65
                vx[:, base:base + 64] = vh[:, ts * 128:(ts + 1) * 128].T
                vx[:, base + 64:base + 65] = ones_col
        mapsF.append(dict(qh_i=qh, kh_i=kh, vx_i=vx))
    resF = _run(build_stageF, mapsF)

    wo1 = f32(inp["L1_attn_Wo"])
    bo1 = _v8(inp["L1_attn_bo"])
    mapsG = []
    for c in range(NCORES):
        b, hh = c // 2, c % 2
        ao_fm = np.concatenate([np.asarray(resF[2 * b]["ao_o"]), np.asarray(resF[2 * b + 1]["ao_o"])], axis=0)
        mapsG.append(dict(ao_i=np.ascontiguousarray(ao_fm[:, hh * TC:(hh + 1) * TC]),
                          x3_i=np.ascontiguousarray(resD[c]["x3_o"]), wo=wo1, bo_v=bo1))
    resG = _run(build_stageG, mapsG)

    idt = np.eye(128, dtype=np.float32)
    xh = {c: resG[c]["h0_o"] for c in range(NCORES)}
    for l in range(4):
        rw = f32(inp["L1_ffn_router_w"])[l]
        rb = f32(inp["L1_ffn_router_b"])[l]
        g_ = f32(inp["L1_ffn_ln_g"])[l]
        b_ = f32(inp["L1_ffn_ln_b"])[l]
        w1 = f32(inp["L1_ffn_w1"])[l]
        b1 = f32(inp["L1_ffn_b1"])[l]
        w2 = f32(inp["L1_ffn_w2"])[l]
        b2 = f32(inp["L1_ffn_b2"])[l]
        rwt = np.ascontiguousarray(rw.reshape(8, 128, 4).transpose(1, 0, 2).reshape(128, 32))
        lgb = np.zeros((128, 64), np.float32)
        for e in range(4):
            lgb[:, (2 * e) * 8:(2 * e) * 8 + 8] = _v8(g_[e])
            lgb[:, (2 * e + 1) * 8:(2 * e + 1) * 8 + 8] = _v8(b_[e])
        b1t = np.ascontiguousarray(np.concatenate([_v32(b1[e]) for e in range(4)], axis=1))
        b2t = np.ascontiguousarray(np.concatenate([_v8(b2[e]) for e in range(4)], axis=1))
        w1_i = np.ascontiguousarray(w1.reshape(4 * D, 4096))
        w2_i = np.ascontiguousarray(w2.reshape(4 * 4096, D))
        mapsH = [dict(xh_i=np.ascontiguousarray(xh[c]), rwt=rwt,
                      rbt=np.ascontiguousarray(rb.reshape(1, 4)), lgb=lgb, b1t=b1t, b2t=b2t,
                      w1_i=w1_i, w2_i=w2_i, idt_i=idt) for c in range(NCORES)]
        resH = _run(build_stageH, mapsH)
        xh = {c: resH[c]["xo_o"] for c in range(NCORES)}

    mapsI = [dict(x4_i=np.ascontiguousarray(resG[c]["x4_o"]), h4_i=np.ascontiguousarray(xh[c]))
             for c in range(NCORES)]
    resI = _run(build_stageI, mapsI)

    out = np.empty((B, S, D), np.float32)
    for c in range(NCORES):
        b, hh = c // 2, c % 2
        out[b, hh * TC:(hh + 1) * TC, :] = resI[c]["x5_o"].T
    return out


RG_PAIR = [[0, 1], [2, 3], [4, 5], [6, 7]]
RG_ALL = [[0, 1, 2, 3, 4, 5, 6, 7]]


def emit_scan_fused(p, cm, io):
    nc = p.nc
    u_d, ek_d, tdv, pval = io["u_d"], io["ek_d"], io["tdv"], io["pval"]
    pay_d, payg, payoff = io["pay_d"], io["payg"], io["payoff"]
    ln0_e, r_e, wkv_e = io["ln0_e"], io["r_e"], io["wkv_e"]
    a_d, b_d = io["a_d"], io["b_d"]
    cc_sem = p.sem("cc_scan")

    td = p.sbuf("sc_td", [128, 8], F32)
    pv = p.sbuf("sc_pv", [128, 1], F32)
    ew8 = p.sbuf("sc_ew8", [128, 8], F32)
    ntd = p.sbuf("sc_ntd", [128, 8], F32)
    pay = p.sbuf("sc_pay", [128, 32], F32)
    ones_t = p.sbuf("sc_ones", [128, TC], F32)
    ew_t = p.sbuf("sc_ewt", [128, TC], F32)
    ut = p.sbuf("sc_u", [128, TC], F32)
    ekt = p.sbuf("sc_ek", [128, TC], F32)
    at = p.sbuf("sc_a", [128, TC], F32)
    bt = p.sbuf("sc_b", [128, TC], F32)

    p.dma_in(td, tdv)
    p.dma_in(pv, pval)
    p.op("gpsimd", lambda h: h.memset(ones_t.ap, 1.0), writes=(ones_t,))
    p.chain("scalar", [
        lambda h: h.activation(ntd.ap, td.ap, AF.Exp),
    ], reads=(td,), writes=(ntd,))
    p.chain("scalar", [
        lambda h: h.activation(ew8.ap, ntd.ap, AF.Exp, scale=-1.0),
    ], reads=(ntd,), writes=(ew8,))
    p.op("vector", lambda h: h.tensor_scalar(ntd.ap, ntd.ap, -1.0, None, op0=ALU.mult),
         reads=(), writes=(ntd,))
    p.raw("vector", lambda h: h.drain())

    for j in range(ND):
        p.dma_in(ut, u_d[j * 128:(j + 1) * 128, :])
        p.dma_in(ekt, ek_d[j * 128:(j + 1) * 128, :])
        p.op("vector", lambda h, j=j: h.tensor_scalar(ew_t.ap, ones_t.ap, ew8[0:128, j:j + 1], None, op0=ALU.mult),
             reads=(ones_t, ew8), writes=(ew_t,))
        p.op("vector", lambda h: h.tensor_tensor_scan(at.ap, ew_t.ap, ut.ap, 0.0, op0=ALU.mult, op1=ALU.add),
             reads=(ew_t, ut), writes=(at,))
        p.op("vector", lambda h: h.tensor_tensor_scan(bt.ap, ew_t.ap, ekt.ap, 0.0, op0=ALU.mult, op1=ALU.add),
             reads=(ew_t, ekt), writes=(bt,))
        p.op("vector", lambda h, j=j: h.tensor_copy(pay[0:128, j:j + 1], at[0:128, TC - 1:TC]),
             reads=(at,), writes=(pay,))
        p.op("vector", lambda h, j=j: h.tensor_copy(pay[0:128, 8 + j:8 + j + 1], bt[0:128, TC - 1:TC]),
             reads=(bt,), writes=(pay,))
        p.dma_out(a_d[j * 128:(j + 1) * 128, :], at)
        p.dma_out(b_d[j * 128:(j + 1) * 128, :], bt)
    lr = p.sbuf("sc_lr", [128, 16], F32)
    p.op("gpsimd", lambda h: h.dma_start(
        lr[0:128, 0:8], ln0_e.rearrange("(j p) t -> p j t", p=128)[:, :, TC:TC + 1].rearrange("p j o -> p (j o)"),
        allow_slow_non_contiguous=True), writes=(lr,), dma=True, dma_key=lr)
    p.op("gpsimd", lambda h: h.dma_start(
        lr[0:128, 8:16], r_e.rearrange("(j p) t -> p j t", p=128)[:, :, TC:TC + 1].rearrange("p j o -> p (j o)"),
        allow_slow_non_contiguous=True), writes=(lr,), dma=True, dma_key=lr)
    p.op("vector", lambda h: h.tensor_copy(pay[0:128, 16:32], lr.ap), reads=(lr,), writes=(pay,))
    p.raw("vector", lambda h: h.drain())
    ev_pay = p.dma_out(pay_d, pay)
    p._emit_wait("gpsimd", ev_pay)
    p.op("gpsimd", lambda h: h.collective_compute(
        "AllGather", ALU.bypass, replica_groups=RG_ALL, ins=[pay_d.opt()], outs=[payg.opt()]),
        sem=cc_sem, k=1)
    fl_in, fl_out = io["fence1_in"], io["fence1_out"]
    p.op("gpsimd", lambda h: h.collective_compute(
        "AllReduce", ALU.add, replica_groups=RG_ALL, ins=[fl_in.opt()], outs=[fl_out.opt()]),
        sem=cc_sem, k=1)
    ev_cc = Ev(cc_sem, cc_sem.n, "cc")
    pin = p.sbuf("sc_pin", [128, 32], F32)
    pofft = p.sbuf("sc_poff", [1, 1], mybir.dt.int32)
    p.dma_in(pofft, payoff)
    p._emit_wait("gpsimd", ev_cc)

    def _pin_dma(h):
        with h.register("sc_r0") as r0:
            h.reg_load(r0, pofft[0:1, 0:1])
            return h.dma_start(pin.ap, bass.AP(payg.tensor, r0, [[32, 128], [1, 32]]))
    p.op("gpsimd", _pin_dma, reads=(pofft,), writes=(pin,), dma=True, dma_key=pin)
    pvinv = p.sbuf("sc_pvinv", [128, 1], F32)
    p.op("vector", lambda h: h.tensor_scalar(pvinv.ap, pv.ap, -1.0, 1.0, op0=ALU.mult, op1=ALU.add),
         reads=(pv,), writes=(pvinv,))
    p.raw("vector", lambda h: h.drain())
    bsafe = p.sbuf("sc_bsafe", [128, 8], F32)
    p.op("vector", lambda h: h.tensor_scalar(bsafe.ap, pin[0:128, 8:16], pv[0:128, 0:1], pvinv[0:128, 0:1],
                                             op0=ALU.mult, op1=ALU.add),
         reads=(pin, pv, pvinv), writes=(bsafe,))
    p.op("vector", lambda h: h.tensor_scalar(pin.ap, pin.ap, pv[0:128, 0:1], None, op0=ALU.mult),
         reads=(pv,), writes=(pin,))
    p.raw("vector", lambda h: h.drain())
    p.op("sync", lambda h: h.dma_start(
        ln0_e.rearrange("(j p) t -> p j t", p=128)[:, :, 0:1].rearrange("p j o -> p (j o)"),
        pin[0:128, 16:24], allow_slow_non_contiguous=True), reads=(pin,), dma=True, dma_key=pin)
    p.op("sync", lambda h: h.dma_start(
        r_e.rearrange("(j p) t -> p j t", p=128)[:, :, 0:1].rearrange("p j o -> p (j o)"),
        pin[0:128, 8 + 16:16 + 16], allow_slow_non_contiguous=True), reads=(pin,), dma=True, dma_key=pin)
    wb0 = p.sbuf("sc_wb0", [128, 8], F32)
    p.op("vector", lambda h: h.tensor_scalar(wb0.ap, bsafe.ap, WKV_EPS, None, op0=ALU.add),
         reads=(bsafe,), writes=(wb0,))
    p.raw("vector", lambda h: h.drain())
    p.chain("scalar", [
        lambda h: h.activation(wb0.ap, wb0.ap, AF.Ln),
        lambda h: h.activation(wb0.ap, wb0.ap, AF.Exp, scale=-1.0),
    ], reads=(), writes=(wb0,))
    p.op("vector", lambda h: h.tensor_tensor(wb0.ap, wb0.ap, pin[0:128, 0:8], op=ALU.mult),
         reads=(pin,), writes=(wb0,))
    p.raw("vector", lambda h: h.drain())
    p.op("sync", lambda h: h.dma_start(
        wkv_e.rearrange("(j p) t -> p j t", p=128)[:, :, 0:1].rearrange("p j o -> p (j o)"),
        wb0.ap, allow_slow_non_contiguous=True), reads=(wb0,), dma=True, dma_key=wb0)
    if io.get("debug_taps"):
        pay_tap = io["pay_tap"]
        payg_tap = io["payg_tap"]
        p.dma_out(pay_tap, pay)
        pgt = p.sbuf("sc_pgt", [128, 8 * 32], F32)
        for r in range(8):
            p.dma_in(pgt, payg[r * 128:(r + 1) * 128, :], dst=pgt.t[0:128, r * 32:(r + 1) * 32])
        p.dma_out(payg_tap, pgt)
        pin_tap = io["pin_tap"]
        p.dma_out(pin_tap, pin)
    iot = p.sbuf("sc_iot", [128, TC], F32)
    p.op("gpsimd", lambda h: h.iota(iot.ap, pattern=[[1, TC]], base=1, channel_multiplier=0,
                                    allow_small_or_imprecise_dtypes=True), writes=(iot,))
    etab = p.sbuf("sc_etab", [128, TC], F32)
    ca = p.sbuf("sc_ca", [128, 16], F32)
    p.op("vector", lambda h: h.tensor_copy(ca.ap, pin[0:128, 0:16]), reads=(pin,), writes=(ca,))
    p.raw("vector", lambda h: h.drain())
    wt = p.sbuf("sc_wkv", [128, TC], F32)
    for j in range(ND):
        p.dma_in(at, a_d[j * 128:(j + 1) * 128, :])
        p.dma_in(bt, b_d[j * 128:(j + 1) * 128, :])
        p.op("vector", lambda h, j=j: h.tensor_scalar(etab.ap, iot.ap, ntd[0:128, j:j + 1], None, op0=ALU.mult),
             reads=(iot, ntd), writes=(etab,))
        p.op("scalar", lambda h: h.activation(etab.ap, etab.ap, AF.Exp), reads=(), writes=(etab,))
        p.op("vector", lambda h, j=j: h.scalar_tensor_tensor(
            at.ap, etab.ap, ca[0:128, j:j + 1], at.ap, op0=ALU.mult, op1=ALU.add),
            reads=(etab, ca), writes=(at,))
        p.op("vector", lambda h, j=j: h.scalar_tensor_tensor(
            bt.ap, etab.ap, ca[0:128, 8 + j:8 + j + 1], bt.ap, op0=ALU.mult, op1=ALU.add),
            reads=(etab, ca), writes=(bt,))
        p.op("vector", lambda h: h.tensor_scalar(bt.ap, bt.ap, WKV_EPS, None, op0=ALU.add),
             reads=(), writes=(bt,))
        p.chain("scalar", [
            lambda h: h.activation(bt.ap, bt.ap, AF.Ln),
            lambda h: h.activation(bt.ap, bt.ap, AF.Exp, scale=-1.0),
        ], reads=(), writes=(bt,))
        p.op("vector", lambda h: h.tensor_tensor(wt.ap, at.ap, bt.ap, op=ALU.mult),
             reads=(at, bt), writes=(wt,))
        p.dma_out(wkv_e[j * 128:(j + 1) * 128, 1:1 + TC], wt)


def emit_attnF2(p, cm, io):
    nc = p.nc
    qb_d, kvm, ao_d, idt64 = io["qb_d"], io["kv_my"], io["ao_d"], io["idt64"]
    TQ = 512
    idb = p.sbuf("f2_id", [64, 64], BF16)
    p.dma_in(idb, idt64)
    kb = p.sbuf("f2_k", [64, S], BF16)
    qb = p.sbuf("f2_q", [64, TC], BF16)
    vf = p.sbuf("f2_v", [64, S], BF16)
    vext = p.sbuf("f2_vx", [128, 32 * 65], BF16)
    pb = [p.sbuf(f"f2_p{i}", [128, TQ], BF16) for i in range(2)]
    dn = p.sbuf("f2_dn", [1, TQ], F32)
    af = p.sbuf("f2_af", [64, TQ], F32)
    aob = p.sbuf("f2_ao", [64, TQ], BF16)
    ps_qk = PsumPool(p, "f2qk", 2)
    ps_tr = PsumPool(p, "f2tr", 2)
    ps_av = p.psum("f2_av", [65, 512])
    ps_bc = p.psum("f2_bc", [64, 512])

    for hh in range(16):
        g, sub = hh // 2, hh % 2
        r0 = g * 128 + sub * 32
        r1 = g * 128 + 64 + sub * 32
        p.dma_in(kb, kvm[r0:r0 + 32, :], dst=kb.t[0:32, :])
        p.op("gpsimd", lambda h, r1=r1: h.dma_start(kb.t[32:64, :], kvm[r1:r1 + 32, :]),
             writes=(kb,), dma=True, dma_key=kb)
        p.dma_in(vf, kvm[1024 + hh * 64:1024 + (hh + 1) * 64, :])
        p.dma_in(qb, qb_d[r0:r0 + 32, :], dst=qb.t[0:32, :])
        p.op("gpsimd", lambda h, r1=r1: h.dma_start(qb.t[32:64, :], qb_d[r1:r1 + 32, :]),
             writes=(qb,), dma=True, dma_key=qb)
        p.op("gpsimd", lambda h: h.memset(vext.ap, 1.0), writes=(vext,))
        for ts in range(32):
            ps = ps_tr.next()
            p.chain("tensor", [lambda h, ts=ts, ps=ps: h.matmul(
                ps[0:128, 0:64], vf[0:64, ts * 128:(ts + 1) * 128], idb.ap, start=True, stop=True)],
                reads=(vf, idb), writes=(ps,))
            p.op("vector", lambda h, ts=ts, ps=ps: h.tensor_copy(
                vext[0:128, ts * 65:ts * 65 + 64], ps[0:128, 0:64]), reads=(ps,), writes=(vext,))
        for tq in range(TC // TQ):
            q_ap = qb[0:64, tq * TQ:(tq + 1) * TQ]
            for ts in range(32):
                ps = ps_qk.next()
                p.chain("tensor", [lambda h, ts=ts, q_ap=q_ap, ps=ps: h.matmul(
                    ps[0:128, 0:TQ], kb[0:64, ts * 128:(ts + 1) * 128], q_ap,
                    start=True, stop=True)], reads=(kb, qb), writes=(ps,))
                pbuf = pb[ts % 2]
                p.op("scalar", lambda h, ps=ps, pbuf=pbuf: h.activation(
                    pbuf.ap, ps[0:128, 0:TQ], AF.Exp, scale=0.125), reads=(ps,), writes=(pbuf,))
                p.chain("tensor", [lambda h, ts=ts, pbuf=pbuf: h.matmul(
                    ps_av[0:65, 0:TQ], vext[0:128, ts * 65:(ts + 1) * 65], pbuf.ap,
                    start=(ts == 0), stop=(ts == 31))], reads=(vext, pbuf), writes=(ps_av,))
            p.op("vector", lambda h: h.tensor_copy(dn.ap, ps_av[64:65, 0:TQ]), reads=(ps_av,), writes=(dn,))
            p.chain("scalar", [
                lambda h: h.activation(dn.ap, dn.ap, AF.Ln),
                lambda h: h.activation(dn.ap, dn.ap, AF.Exp, scale=-1.0),
            ], reads=(), writes=(dn,))
            p.op("scalar", lambda h: h.copy(af.ap, ps_av[0:64, 0:TQ]), reads=(ps_av,), writes=(af,))
            p.op("tensor", lambda h: h.matmul(ps_bc[0:64, 0:TQ], cm.ones_row_f[0:1, 0:64], dn.ap,
                                              start=True, stop=True),
                 reads=(cm.ones_row_f, dn), writes=(ps_bc,))
            p.op("vector", lambda h: h.tensor_tensor(aob.ap, af.ap, ps_bc[0:64, 0:TQ], op=ALU.mult),
                 reads=(af, ps_bc), writes=(aob,))
            p.dma_out(ao_d[hh * 64:(hh + 1) * 64, tq * TQ:(tq + 1) * TQ], aob)


def build_fused(debug=False):
    nc = bass.Bass("TRN2", target_bir_lowering=False)
    p = Prog(nc)
    cm = Common(p)
    TCE = TC + 1
    DH = 4096
    dI = lambda n, s, dt=F32: nc.dram_tensor(n, s, dt, kind="ExternalInput").ap()
    _no_tap = {"pay_d", "payg", "kv_d", "kvg", "fence1_in", "fence2_in"}
    dN = lambda n, s, dt=F32: nc.dram_tensor(
        n, s, dt, kind="ExternalOutput" if (debug and n not in _no_tap) else "Internal").ap()
    xe = dI("xe", [D, TCE])
    pval = dI("pval", [128, 1])
    fmask = dI("fmask", [128, 1])
    vecs = dI("vecs", [128, 48])
    wk, wv, wr = dI("wk", [D, D]), dI("wv", [D, D]), dI("wr", [D, D])
    tdv = dI("tdv", [128, 8])
    wo0 = dI("wo0", [D, D])
    vecs2 = dI("vecs2", [128, 16])
    wfk, wfr, wfv = dI("wfk", [D, 4 * D]), dI("wfr", [D, D]), dI("wfv", [4 * D, D])
    w1t, w2t = dI("w1t", [D, DH]), dI("w2t", [D, DH])
    h1t, h2t = dI("h1t", [DH, DH]), dI("h2t", [DH, DH])
    w3t = dI("w3t", [DH, D])
    scl = dI("scl", [128, 136])
    wqkv = dI("wqkv", [D, 3 * D])
    cost, sint = dI("cost", [128, TC]), dI("sint", [128, TC])
    selm, selt = dI("selm", [128, 128]), dI("selt", [16, 1024])
    idt64 = dI("idt64", [64, 64], BF16)
    wo1 = dI("wo1", [D, D])
    bo_v = dI("bo_v", [128, 8])
    idt_i = dI("idt_i", [128, 128])
    payoff = dI("payoff", [1, 1], mybir.dt.int32)
    kvoff = dI("kvoff", [1, 2], mybir.dt.int32)
    hw = []
    for l in range(4):
        hw.append(dict(rwt=dI(f"rwt{l}", [128, 32]), rbt=dI(f"rbt{l}", [1, 4]),
                       lgb=dI(f"lgb{l}", [128, 64]), b1t=dI(f"b1t{l}", [128, 128]),
                       b2t=dI(f"b2t{l}", [128, 32]), w1_i=dI(f"w1i{l}", [4 * D, DH]),
                       w2_i=dI(f"w2i{l}", [4 * DH, D])))
    x5_o = nc.dram_tensor("x5_o", [D, TC], F32, kind="ExternalOutput").ap()
    ln0_e, r_e, wkv_e = dN("ln0_e", [D, TCE]), dN("r_e", [D, TCE]), dN("wkv_e", [D, TCE])
    u_d, ek_d = dN("u_d", [D, TC]), dN("ek_d", [D, TC])
    a_d, b_d = dN("a_d", [D, TC]), dN("b_d", [D, TC])
    pay_d = dN("pay_d", [128, 32])
    payg = nc.dram_tensor("payg", [1024, 32], F32, kind="Internal", addr_space="Shared").ap()
    x1_d = dN("x1_d", [D, TCE])
    x2_d, x3_d = dN("x2_d", [D, TC]), dN("x3_d", [D, TC])
    q_d = dN("q_d", [D, TC], BF16)
    kv_d = dN("kv_d", [2 * D, TC], BF16)
    kvg = nc.dram_tensor("kvg", [16 * D, TC], BF16, kind="Internal", addr_space="Shared").ap()
    kv_my = dN("kv_my", [2 * D, S], BF16)
    fence1_in = dN("fence1_in", [128, 1])
    fence1_out = nc.dram_tensor("fence1_out", [128, 1], F32, kind="Internal", addr_space="Shared").ap()
    fence2_in = dN("fence2_in", [128, 1])
    fence2_out = nc.dram_tensor("fence2_out", [128, 1], F32, kind="Internal", addr_space="Shared").ap()
    ao_d = dN("ao_d", [D, TC], BF16)
    x4_d = dN("x4_d", [D, TC])
    xhat_d = dN("xhat_d", [D, TC], BF16)
    hh_d = [dN(f"hh{l}", [D, TC]) for l in range(5)]
    cc2_sem = p.sem("cc_kv")

    with p.stage("A"):
        build_stageA(p, io=dict(cm=cm, xe=xe, pval=pval, fmask=fmask, vecs=vecs,
                                wk=wk, wv=wv, wr=wr,
                                ln0_o=ln0_e[:, 1:1 + TC], u_o=u_d, ek_o=ek_d,
                                r_o=r_e[:, 1:1 + TC]))
    with p.stage("scan"):
        sc_io = dict(u_d=u_d, ek_d=ek_d, tdv=tdv, pval=pval,
                     pay_d=pay_d, payg=payg, payoff=payoff,
                     fence1_in=fence1_in, fence1_out=fence1_out,
                     ln0_e=ln0_e, r_e=r_e,
                     wkv_e=wkv_e, a_d=a_d, b_d=b_d)
        if debug:
            sc_io.update(debug_taps=True,
                         pay_tap=nc.dram_tensor("pay_tap", [128, 32], F32, kind="ExternalOutput").ap(),
                         payg_tap=nc.dram_tensor("payg_tap", [128, 256], F32, kind="ExternalOutput").ap(),
                         pin_tap=nc.dram_tensor("pin_tap", [128, 32], F32, kind="ExternalOutput").ap())
        emit_scan_fused(p, cm, sc_io)
    with p.stage("C1"):
        build_stageC1(p, io=dict(ln0_e=ln0_e, r_e=r_e, wkv_e=wkv_e, wo=wo0, x1_o=x1_d))
    with p.stage("C2"):
        build_stageC2(p, io=dict(cm=cm, x1_e=x1_d, x_fm=xe[:, 1:1 + TC], pval=pval,
                                 vecs=vecs2, wfk=wfk, wfr=wfr, wfv=wfv, x2_o=x2_d))
    with p.stage("D"):
        build_stageD(p, io=dict(cm=cm, x2_i=x2_d, w1t=w1t, w2t=w2t, h1t=h1t, h2t=h2t,
                                w3t=w3t, scl=scl, x3_o=x3_d))
    with p.stage("E"):
        build_stageE(p, io=dict(cm=cm, x3_i=x3_d, wqkv=wqkv, cost=cost, sint=sint,
                                selm=selm, selt=selt, qb_o=q_d,
                                kb_o=kv_d[0:D, :], vb_o=kv_d[D:2 * D, :]))
    with p.stage("cc2"):
        p.op("gpsimd", lambda h: h.collective_compute(
            "AllGather", ALU.bypass, replica_groups=RG_ALL, ins=[kv_d.opt()], outs=[kvg.opt()]),
            sem=cc2_sem, k=1)
        p.op("gpsimd", lambda h: h.collective_compute(
            "AllReduce", ALU.add, replica_groups=RG_ALL, ins=[fence2_in.opt()], outs=[fence2_out.opt()]),
            sem=cc2_sem, k=1)
        p._emit_wait("gpsimd", Ev(cc2_sem, cc2_sem.n, "cc"))
        kofft = p.sbuf("cc2_koff", [1, 2], mybir.dt.int32)
        p.dma_in(kofft, kvoff)
        kvm_buf = Buf(None, "kv_my")
        for half in range(2):
            def cpy(h, half=half):
                with h.register(f"cc2_r{half}") as r0:
                    h.reg_load(r0, kofft[0:1, half:half + 1])
                    return h.dma_start(
                        bass.AP(kv_my.tensor, half * TC, [[2 * TC, 2 * D], [1, TC]]),
                        bass.AP(kvg.tensor, r0, [[TC, 2 * D], [1, TC]]))
            p.op("gpsimd", cpy, reads=(kofft,), writes=(kvm_buf,), dma=True, dma_key=kvm_buf)
    with p.stage("F"):
        emit_attnF2(p, cm, dict(qb_d=q_d, kv_my=kv_my, ao_d=ao_d, idt64=idt64))
    with p.stage("G"):
        build_stageG(p, io=dict(cm=cm, ao_i=ao_d, x3_i=x3_d, wo=wo1, bo_v=bo_v,
                                x4_o=x4_d, h0_o=hh_d[0]))
    for l in range(4):
        with p.stage(f"H{l}"):
            build_stageH(p, io=dict(cm=cm, xh_i=hh_d[l], xo_o=hh_d[l + 1], xhat_d=xhat_d,
                                    idt_i=idt_i, **hw[l]))
    with p.stage("I"):
        build_stageI(p, io=dict(cm=cm, x4_i=x4_d, h4_i=hh_d[4], x5_o=x5_o))
    return p


def _fused_maps(inp):
    f32 = lambda a: np.ascontiguousarray(np.asarray(a, np.float32))
    x = f32(inp["L0_attn_x"])
    vecs = np.concatenate([_v8(inp[f"L0_attn_{n}"]) for n in
                           ("ln1_g", "ln1_b", "mix_k", "mix_v", "mix_r", "time_first")], axis=1)
    vecs2 = np.concatenate([_v8(inp["L0_attn_ln2_g"]), _v8(inp["L0_attn_ln2_b"])], axis=1)
    scl = np.concatenate([_v32(inp["L0_ffn_w1_s"]), _v32(inp["L0_ffn_w2_s"]),
                          _v32(inp["L0_ffn_h1_s"]), _v32(inp["L0_ffn_h2_s"]),
                          _v8(inp["L0_ffn_w3_s"])], axis=1)
    wqkv = f32(inp["L1_attn_Wqkv"])
    perm = _perm1024()
    wperm = np.ascontiguousarray(np.concatenate(
        [wqkv[:, 0:1024][:, perm], wqkv[:, 1024:2048][:, perm], wqkv[:, 2048:3072]], axis=1))
    sel = np.zeros((128, 16 * 8), np.float32)
    selt = np.zeros((16, 128 * 8), np.float32)
    for j in range(8):
        for r in range(128):
            h = 2 * j + (1 if (32 <= r < 64 or r >= 96) else 0)
            sel[r, j * 16 + h] = 1.0
            selt[h, j * 128 + r] = 1.0
    common = dict(
        vecs=np.ascontiguousarray(vecs), vecs2=np.ascontiguousarray(vecs2),
        wk=f32(inp["L0_attn_Wk"]), wv=f32(inp["L0_attn_Wv"]), wr=f32(inp["L0_attn_Wr"]),
        tdv=_v8(inp["L0_attn_time_decay"]), wo0=f32(inp["L0_attn_Wo"]),
        wfk=f32(inp["L0_attn_Wfk"]), wfr=f32(inp["L0_attn_Wfr"]), wfv=f32(inp["L0_attn_Wfv"]),
        w1t=np.ascontiguousarray(f32(inp["L0_ffn_w1_w"]).T),
        w2t=np.ascontiguousarray(f32(inp["L0_ffn_w2_w"]).T),
        h1t=np.ascontiguousarray(f32(inp["L0_ffn_h1_w"]).T),
        h2t=np.ascontiguousarray(f32(inp["L0_ffn_h2_w"]).T),
        w3t=np.ascontiguousarray(f32(inp["L0_ffn_w3_w"]).T),
        scl=np.ascontiguousarray(scl), wqkv=wperm, selm=sel, selt=selt,
        idt64=np.eye(64, dtype=ml_dtypes.bfloat16),
        wo1=f32(inp["L1_attn_Wo"]), bo_v=_v8(inp["L1_attn_bo"]),
        idt_i=np.eye(128, dtype=np.float32),
    )
    for l in range(4):
        rw = f32(inp["L1_ffn_router_w"])[l]
        g_ = f32(inp["L1_ffn_ln_g"])[l]
        b_ = f32(inp["L1_ffn_ln_b"])[l]
        lgb = np.zeros((128, 64), np.float32)
        for e in range(4):
            lgb[:, (2 * e) * 8:(2 * e) * 8 + 8] = _v8(g_[e])
            lgb[:, (2 * e + 1) * 8:(2 * e + 1) * 8 + 8] = _v8(b_[e])
        common.update({
            f"rwt{l}": np.ascontiguousarray(rw.reshape(8, 128, 4).transpose(1, 0, 2).reshape(128, 32)),
            f"rbt{l}": np.ascontiguousarray(f32(inp["L1_ffn_router_b"])[l].reshape(1, 4)),
            f"lgb{l}": lgb,
            f"b1t{l}": np.ascontiguousarray(np.concatenate(
                [_v32(f32(inp["L1_ffn_b1"])[l][e]) for e in range(4)], axis=1)),
            f"b2t{l}": np.ascontiguousarray(np.concatenate(
                [_v8(f32(inp["L1_ffn_b2"])[l][e]) for e in range(4)], axis=1)),
            f"w1i{l}": np.ascontiguousarray(f32(inp["L1_ffn_w1"])[l].reshape(4 * D, 4096)),
            f"w2i{l}": np.ascontiguousarray(f32(inp["L1_ffn_w2"])[l].reshape(4 * 4096, D)),
        })
    maps = []
    for c in range(NCORES):
        b, hh = c // 2, c % 2
        xs = x[b, hh * TC:(hh + 1) * TC, :].T
        bound = np.zeros((D, 1), np.float32) if hh == 0 else x[b, hh * TC - 1, :].reshape(D, 1)
        cost, sint = _rope_tables(hh * TC)
        m = dict(common)
        m.update(xe=np.ascontiguousarray(np.concatenate([bound, xs], axis=1)),
                 pval=np.full((128, 1), float(hh), np.float32),
                 fmask=np.full((128, 1), 1.0 - hh, np.float32),
                 cost=cost, sint=sint,
                 payoff=np.array([[(c ^ 1) * 128 * 32]], np.int32),
                 kvoff=np.array([[(2 * b) * 2 * D * TC, (2 * b + 1) * 2 * D * TC]], np.int32))
        maps.append(m)
    return maps


def kernel(**inp):
    maps = _fused_maps(inp)
    p = build_fused()
    nc = _finish(p)
    res = run_bass_kernel_spmd(nc, maps, core_ids=list(range(NCORES))).results
    out = np.empty((B, S, D), np.float32)
    for c in range(NCORES):
        b, hh = c // 2, c % 2
        out[b, hh * TC:(hh + 1) * TC, :] = res[c]["x5_o"].T
    return out
